# version 05
# baseline (speedup 1.0000x reference)
import numpy as np
from contextlib import ExitStack

import concourse.bass as bass
import concourse.mybir as mybir
from concourse.bass_utils import run_bass_kernel_spmd

F32 = mybir.dt.float32
BF16 = mybir.dt.bfloat16
I32 = mybir.dt.int32
AF = mybir.ActivationFunctionType
ALU = mybir.AluOpType
AX = mybir.AxisListType

T = 2048
D = 1024
NTT = 4
TT = 512
DC = 8

DEBUG_LAYERS = [0, 1, 2, 3]
DEBUG_SPILL = False
_LAST_RES = [None]


COMPUTE = ("T", "V", "S", "P", "Y")
DMA_LANES = {"Y": 10, "P": 10, "S": 4}
SAME_ENGINE_SYNC = True


class Buf:
    __slots__ = ("writer", "readers", "name")

    def __init__(self, name=""):
        self.writer = None
        self.readers = []
        self.name = name


class Ctx:
    def __init__(self, nc, stack):
        self.nc = nc
        self.stack = stack
        self.streams = {e: [] for e in COMPUTE}
        self.sems = {}
        self.cnt = {}
        self.lane_rr = {}
        keys = ["T", "V", "S", "P"]
        for q, n in DMA_LANES.items():
            keys += [f"d{q}{i}" for i in range(n)]
            self.lane_rr[q] = 0
        for key in keys:
            self.sems[key] = stack.enter_context(nc.semaphore("sem_" + key))
            self.cnt[key] = 0
        self.waited = {e: {} for e in COMPUTE}
        self.scopes = []
        self._names = {}

    def _uname(self, base):
        n = self._names.get(base, 0)
        self._names[base] = n + 1
        return f"{base}{n}" if n else base

    def sb(self, name, shape, dtype, scope=None):
        if scope is False:
            st = self.stack
        else:
            st = self.scopes[-1] if self.scopes and scope is None else self.stack
        t = st.enter_context(self.nc.sbuf_tensor(self._uname(name), shape, dtype))
        return t

    def ps(self, name, shape, dtype=F32, scope=None):
        st = self.scopes[-1] if self.scopes and scope is None else self.stack
        t = st.enter_context(self.nc.psum_tensor(self._uname(name), shape, dtype))
        return t

    def push_scope(self):
        self.scopes.append(ExitStack())

    def pop_scope(self):
        self.barrier()
        self.scopes.pop().close()

    def barrier(self):
        snap = dict(self.cnt)
        for e in COMPUTE:
            for key, v in snap.items():
                if v and self.waited[e].get(key, 0) < v:
                    tgt = v * 16 if key.startswith("d") else v
                    self.streams[e].append(("wait", key, tgt))
                    self.waited[e][key] = v

    def _collect_waits(self, eng, reads, writes):
        need = {}
        for b in reads:
            if b is None:
                continue
            t = b.writer
            if t:
                k, v = t
                need[k] = max(need.get(k, 0), v)
        for b in writes:
            if b is None:
                continue
            toks = list(b.readers)
            if b.writer:
                toks.append(b.writer)
            for k, v in toks:
                need[k] = max(need.get(k, 0), v)
        out = []
        for k, v in need.items():
            if not SAME_ENGINE_SYNC and k == eng:
                continue
            if k == eng and v >= self.cnt[k]:
                assert v <= self.cnt[k], f"bad same-engine wait {k} {v} > {self.cnt[k]}"
            if self.waited[eng].get(k, 0) >= v:
                continue
            out.append((k, v * 16 if k.startswith("d") else v))
            self.waited[eng][k] = v
        return out

    def _finish(self, eng, semkey, reads, writes):
        self.cnt[semkey] += 1
        tok = (semkey, self.cnt[semkey])
        for b in reads:
            if b is not None:
                b.readers.append(tok)
        for b in writes:
            if b is not None:
                b.writer = tok
                b.readers = []
        return tok

    def op(self, eng, build, reads=(), writes=()):
        reads = list(reads)
        writes = list(writes)
        waits = self._collect_waits(eng, reads, writes)
        for k, v in waits:
            self.streams[eng].append(("wait", k, v))
        tok_val = self.cnt[eng] + 1
        self.streams[eng].append(("op", build, eng, tok_val))
        return self._finish(eng, eng, reads, writes)

    def dma(self, q, dst_ap, src_ap, reads=(), writes=()):
        lane = self.lane_rr[q]
        self.lane_rr[q] = (lane + 1) % DMA_LANES[q]
        semkey = f"d{q}{lane}"
        reads = list(reads)
        writes = list(writes)
        waits = self._collect_waits(q, reads, writes)
        prev = self.cnt[semkey]
        if prev and self.waited[q].get(semkey, 0) < prev:
            waits.append((semkey, prev * 16))
            self.waited[q][semkey] = prev
        for k, v in waits:
            self.streams[q].append(("wait", k, v))
        self.cnt[semkey] += 1
        self.streams[q].append(("dma", dst_ap, src_ap, semkey, self.cnt[semkey] * 16))
        tok = (semkey, self.cnt[semkey])
        for b in reads:
            if b is not None:
                b.readers.append(tok)
        for b in writes:
            if b is not None:
                b.writer = tok
                b.readers = []
        return tok

    def mm_group(self, items, reads=(), writes=()):
        reads = list(reads)
        writes = list(writes)
        waits = self._collect_waits("T", reads, writes)
        for k, v in waits:
            self.streams["T"].append(("wait", k, v))
        n = len(items)
        for i, it in enumerate(items):
            last = i == n - 1
            self.streams["T"].append(("mm", it, "T", self.cnt["T"] + 1 if last else None))
        return self._finish("T", "T", reads, writes)

    def replay(self, eng_key, eng):
        sems = self.sems
        for entry in self.streams[eng_key]:
            kind = entry[0]
            if kind == "wait":
                eng.wait_ge(sems[entry[1]], entry[2])
            elif kind == "op":
                _, build, semkey, val = entry
                ins = build(eng)
                ins.then_inc(sems[semkey], 1)
            elif kind == "dma":
                _, dst, src, semkey, tgt = entry
                eng.dma_start(dst, src).then_inc(sems[semkey], 16)
            elif kind == "mm":
                _, it, semkey, val = entry
                tp = it[5] if len(it) > 5 else None
                ins = eng.matmul(it[0], it[1], it[2], start=it[3], stop=it[4],
                                 tile_position=tp)
                if val is not None:
                    ins.then_inc(sems[semkey], 1)
            elif kind == "tr":
                _, out_ap, in_ap, ident_ap, semkey, val = entry
                ins = eng.transpose(out_ap, in_ap, ident_ap)
                if val is not None:
                    ins.then_inc(sems[semkey], 1)

    def transpose(self, out_ap, in_ap, ident_ap, reads=(), writes=()):
        reads = list(reads)
        writes = list(writes)
        waits = self._collect_waits("T", reads, writes)
        for k, v in waits:
            self.streams["T"].append(("wait", k, v))
        self.streams["T"].append(("tr", out_ap, in_ap, ident_ap, "T", self.cnt["T"] + 1))
        return self._finish("T", "T", reads, writes)



def img_lhsT(w):
    K, M = w.shape
    assert K % 128 == 0 and M % 128 == 0, (K, M)
    Kc, Mc = K // 128, M // 128
    return np.ascontiguousarray(
        w.reshape(Kc, 128, Mc, 128).transpose(1, 2, 0, 3).reshape(128, Mc * Kc * 128)
    )


def img_rhs(w):
    K, N = w.shape
    assert K % 128 == 0
    Kc = K // 128
    return np.ascontiguousarray(w.reshape(Kc, 128, N).transpose(1, 0, 2).reshape(128, Kc * N))


def cols(v, parts=128):
    v = np.asarray(v)
    n = v.size // parts
    return np.ascontiguousarray(v.reshape(n, parts).T)


def prep_host(inputs):
    g = {}
    f = lambda k: np.asarray(inputs[k], np.float32)

    g["w_l0_in"] = img_lhsT(f("L0_attn_in_w"))
    g["w_l0_mix"] = img_lhsT(f("L0_attn_mix_w"))
    g["w_l0_out"] = img_lhsT(f("L0_attn_out_w"))
    g["c_l0_gamma"] = cols(f("L0_attn_norm_gamma"))
    g["c_l0_beta"] = cols(f("L0_attn_norm_beta"))
    g["c_l0_inb"] = cols(f("L0_attn_in_b"))
    g["c_l0_dwb"] = cols(f("L0_attn_dw_b"))
    g["c_l0_mixb"] = cols(f("L0_attn_mix_b"))
    g["c_l0_outb"] = cols(f("L0_attn_out_b"))
    g["c_l0_dec"] = cols(f("L0_attn_log_decay"))
    dw = f("L0_attn_dw_w")
    g["c_l0_dww"] = np.ascontiguousarray(
        dw.reshape(16, 128, 7).transpose(1, 0, 2).reshape(128, 112))

    g["w_l0f_w1"] = img_lhsT(np.ascontiguousarray(f("L0_ffn_w1_w").T))
    g["w_l0f_w2"] = img_lhsT(np.ascontiguousarray(f("L0_ffn_w2_w").T))
    g["w_l0f_h1"] = img_lhsT(np.ascontiguousarray(f("L0_ffn_h1_w").T))
    g["w_l0f_h2"] = img_lhsT(np.ascontiguousarray(f("L0_ffn_h2_w").T))
    g["w_l0f_w3"] = img_lhsT(np.ascontiguousarray(f("L0_ffn_w3_w").T))
    g["c_l0f_s1"] = cols(f("L0_ffn_w1_s"))
    g["c_l0f_s2"] = cols(f("L0_ffn_w2_s"))
    g["c_l0f_s3"] = cols(f("L0_ffn_h1_s"))
    g["c_l0f_s4"] = cols(f("L0_ffn_h2_s"))
    g["c_l0f_s5"] = cols(f("L0_ffn_w3_s"))

    g["w_l1_q"] = img_lhsT(f("L1_attn_Wq").reshape(1024, 1024))
    g["w_l1_kv"] = img_lhsT(f("L1_attn_Wkv_down"))
    g["w_l1_kup"] = img_lhsT(f("L1_attn_Wk_up").reshape(256, 1024))
    g["w_l1_vup"] = img_rhs(f("L1_attn_Wv_up").reshape(256, 1024))
    g["w_l1_out"] = img_lhsT(f("L1_attn_Wout").reshape(1024, 1024))
    g["w_l1_base"] = img_lhsT(f("L1_ffn_W_base"))
    g["w_l1_enc"] = np.concatenate([img_lhsT(f("L1_ffn_W_enc")[e]) for e in range(8)], axis=1)
    g["w_l1_dec"] = np.concatenate([img_lhsT(f("L1_ffn_W_dec")[e]) for e in range(8)], axis=1)
    wd = f("L1_ffn_W_down")
    g["w_l1_dn"] = np.ascontiguousarray(
        wd.reshape(8, 8, 128, 64).transpose(2, 0, 1, 3).reshape(128, 8 * 8 * 64))
    wu = f("L1_ffn_W_up")
    g["w_l1_up"] = np.ascontiguousarray(
        wu.reshape(8, 64, 8, 128).transpose(1, 0, 2, 3).reshape(64, 8 * 8 * 128))
    g["c_l1_bb"] = cols(f("L1_ffn_b_base"))
    g["c_l1_benc"] = np.ascontiguousarray(f("L1_ffn_b_enc").reshape(8, 2, 128).transpose(2, 0, 1).reshape(128, 16))
    g["c_l1_bdec"] = np.ascontiguousarray(f("L1_ffn_b_dec").reshape(8, 8, 128).transpose(2, 0, 1).reshape(128, 64))
    g["r_l1_bd"] = np.ascontiguousarray(f("L1_ffn_b_down"))
    g["r_l1_bu"] = np.ascontiguousarray(f("L1_ffn_b_up"))
    g["i_l1_conn"] = np.ascontiguousarray(np.asarray(inputs["L1_ffn_connected_idx"], np.int32).reshape(8, 1))

    wh = f("L2_attn_W_hidden")
    g["w_l2_hv"] = img_rhs(np.ascontiguousarray(wh[:, :2048]))
    g["w_l2_hg"] = img_lhsT(np.ascontiguousarray(wh[:, 2048:]))
    g["w_l2_qk"] = img_lhsT(f("L2_attn_W_qk"))
    g["w_l2_out"] = img_lhsT(f("L2_attn_W_out"))
    g["c_l2_scale"] = cols(f("L2_attn_ln_scale"))
    g["c_l2_bias"] = cols(f("L2_attn_ln_bias"))
    g["r_l2_bhv"] = np.ascontiguousarray(f("L2_attn_b_hidden")[:2048].reshape(1, 2048))
    g["c_l2_bhg"] = cols(f("L2_attn_b_hidden")[2048:])
    g["c_l2_bqk"] = cols(f("L2_attn_b_qk"))
    g["c_l2_osw"] = np.ascontiguousarray(f("L2_attn_os_weight").T)
    g["c_l2_osb"] = np.ascontiguousarray(f("L2_attn_os_bias").T)
    g["c_l2_bout"] = cols(f("L2_attn_b_out"))
    g["w_l2_gw"] = img_rhs(np.ascontiguousarray(f("L2_ffn_gate_w").T))
    g["r_l2_gb"] = np.ascontiguousarray(f("L2_ffn_gate_bias").reshape(1, 8))
    g["w_l2_w1"] = np.concatenate([img_lhsT(np.ascontiguousarray(f("L2_ffn_w1")[e].T)) for e in range(8)], axis=1)
    g["w_l2_w3"] = np.concatenate([img_lhsT(np.ascontiguousarray(f("L2_ffn_w3")[e].T)) for e in range(8)], axis=1)
    g["w_l2_w2"] = np.concatenate([img_lhsT(np.ascontiguousarray(f("L2_ffn_w2")[e].T)) for e in range(8)], axis=1)
    g["w_l2_sw1"] = img_lhsT(np.ascontiguousarray(f("L2_ffn_sw1").T))
    g["w_l2_sw3"] = img_lhsT(np.ascontiguousarray(f("L2_ffn_sw3").T))
    g["w_l2_sw2"] = img_lhsT(np.ascontiguousarray(f("L2_ffn_sw2").T))

    g["w_l3_q"] = img_lhsT(np.ascontiguousarray(f("L3_attn_Wq").T))
    g["w_l3_k"] = img_lhsT(np.ascontiguousarray(f("L3_attn_Wk").T))
    g["w_l3_v"] = img_lhsT(np.ascontiguousarray(f("L3_attn_Wv").T))
    g["w_l3_gate"] = img_lhsT(np.ascontiguousarray(f("L3_attn_Wgate").T))
    g["w_l3_proj"] = img_lhsT(np.ascontiguousarray(f("L3_attn_Wproj").T))
    wbg = np.zeros((1024, 64), np.float32)
    wbg[:, 0:4] = f("L3_attn_Wbeta").T
    wbg[:, 32:36] = f("L3_attn_Wgk").T
    g["w_l3_bg"] = np.ascontiguousarray(
        wbg.reshape(8, 128, 64).transpose(1, 0, 2).reshape(128, 512))
    g["c_l3_cwq"] = np.ascontiguousarray(f("L3_attn_cwq").reshape(8, 128, 4).transpose(1, 0, 2).reshape(128, 32))
    g["c_l3_cwk"] = np.ascontiguousarray(f("L3_attn_cwk").reshape(2, 128, 4).transpose(1, 0, 2).reshape(128, 8))
    g["c_l3_cwv"] = np.ascontiguousarray(f("L3_attn_cwv").reshape(2, 128, 4).transpose(1, 0, 2).reshape(128, 8))
    bb64 = np.zeros((64, 1), np.float32); bb64[0:4, 0] = f("L3_attn_b_beta")
    dt64 = np.zeros((64, 1), np.float32); dt64[32:36, 0] = f("L3_attn_dt_bias")
    al64 = np.zeros((64, 1), np.float32); al64[32:36, 0] = f("L3_attn_A_log")
    g["c_l3_bb64"] = bb64; g["c_l3_dt64"] = dt64; g["c_l3_al64"] = al64
    g["r_l3_rmsw"] = np.ascontiguousarray(np.tile(f("L3_attn_rms_w"), 2).reshape(128, 1))
    g["w_l3f_a1"] = img_lhsT(f("L3_ffn_aux_weight1"))
    g["w_l3f_wT"] = img_rhs(np.ascontiguousarray(f("L3_ffn_weight").T))
    g["w_l3f_a2"] = img_rhs(f("L3_ffn_aux_weight2"))
    g["c_l3f_b"] = cols(f("L3_ffn_bias"))

    per_core = []
    x = np.asarray(inputs["L0_attn_x"], np.float32)
    for c in range(8):
        per_core.append({"x_in": np.ascontiguousarray(x[c].T)})
    return g, per_core



class Prog:

    def __init__(self):
        self.nc = bass.Bass("TRN2", target_bir_lowering=False)
        self.stack = ExitStack()
        self.ctx = Ctx(self.nc, self.stack)
        self.hbm = {}
        self.hbuf = {}
        self.hbuf8 = {}
        self.hbuf8 = {}

    def dram_in(self, name, shape):
        ap = self.nc.dram_tensor(name, list(shape), F32, kind="ExternalInput").ap()
        self.hbm[name] = ap
        self.hbuf[name] = Buf(name)
        self.hbuf8[name] = [Buf(f"{name}.{i}") for i in range(DC)]
        return ap

    def dram_scratch(self, name, shape, dtype=F32):
        kind = "ExternalOutput" if DEBUG_SPILL else "Internal"
        ap = self.nc.dram_tensor(name, list(shape), dtype, kind=kind).ap()
        self.hbm[name] = ap
        self.hbuf[name] = Buf(name)
        self.hbuf8[name] = [Buf(f"{name}.{i}") for i in range(DC)]
        return ap

    def dram_out(self, name, shape):
        ap = self.nc.dram_tensor(name, list(shape), F32, kind="ExternalOutput").ap()
        self.hbm[name] = ap
        self.hbuf[name] = Buf(name)
        self.hbuf8[name] = [Buf(f"{name}.{i}") for i in range(DC)]
        return ap


def emit_constants(P):
    c = P.ctx
    P.ones_col = c.sb("ones_col", [128, 1], F32)
    P.ones_row = c.sb("ones_row", [1, 128], F32)
    P.ident_bf = c.sb("ident_bf", [128, 128], BF16)
    P.ln_tmp4 = [c.sb(f"ln_tmp{i}", [128, 2 * TT], F32) for i in range(4)]
    P.b_ln_tmp4 = [Buf() for _ in range(4)]
    P.ones_col_bf = c.sb("ones_col_bf", [128, 1], BF16)
    P.ones_row_bf = c.sb("ones_row_bf", [1, 128], BF16)
    P.conn_i = c.sb("conn_i", [8, 1], I32)
    P.b_conn = Buf("conn")
    c.push_scope()
    iot = c.sb("iota_i", [128, 128], I32)
    P.b_ones_col = Buf("ones_col")
    P.b_ones_row = Buf("ones_row")
    P.b_ident = Buf("ident")
    b_iot = Buf("iota")
    c.op("V", lambda e: e.memset(P.ones_col_bf[:, :], 1.0), writes=[P.b_ones_col])
    c.op("V", lambda e: e.memset(P.ones_col[:, :], 1.0), writes=[P.b_ones_col])
    c.op("V", lambda e: e.memset(P.ones_row[:, :], 1.0), writes=[P.b_ones_row])
    c.op("V", lambda e: e.memset(P.ones_row_bf[:, :], 1.0), writes=[P.b_ones_row])
    c.op("P", lambda e: e.iota(iot[:, :], [[1, 128]], channel_multiplier=-1,
                               allow_small_or_imprecise_dtypes=True), writes=[b_iot])
    c.op("V", lambda e: e.tensor_single_scalar(P.ident_bf[:, :], iot[:, :], 0, ALU.is_equal),
         reads=[b_iot], writes=[P.b_ident])
    c.pop_scope()


def load_const_cols(P, names_shapes):
    c = P.ctx
    for name, shape in names_shapes:
        t = c.sb("c_" + name, list(shape), F32)
        b = Buf(name)
        c.dma("Y", t[:, :] if len(shape) == 2 else t[:], P.hbm[name],
              reads=[P.hbuf[name]], writes=[b])
        setattr(P, "t_" + name, t)
        setattr(P, "b_" + name, b)



def emit_layernorm2(P, src_hbm, src_buf, xn_sb, xn_buf, gamma_col=None, beta_col=None,
                    src_sb=None, eps=1e-5):
    c = P.ctx
    c.push_scope()
    from_hbm = src_sb is None
    if from_hbm:
        xch = c.sb("ln_xch", [128, 2 * T], F32)
        b_xch = [Buf(), Buf()]
        xb, bxb = xn_sb, xn_buf
    else:
        xb, bxb = src_sb
    xsq = c.sb("ln_xsq", [128, 2 * T], BF16)
    b_xsq = [Buf(), Buf()]
    rows = c.sb("ln_rows", [1, 2 * T], F32)
    b_rows = Buf("ln_rows")
    mrow = c.sb("ln_mrow", [1, T], F32)
    b_mrow = Buf("ln_mrow")
    onesb = P.ones_col_bf[:, 0:1]

    c.push_scope()
    p_s = [c.ps(f"ln_ps{tt}", [1, TT]) for tt in range(NTT)]
    p_q = [c.ps(f"ln_pq{tt}", [1, TT]) for tt in range(NTT)]
    b_ps = [Buf() for _ in range(NTT)]
    b_pq = [Buf() for _ in range(NTT)]
    for ch in range(DC):
        pb = ch % 2
        if from_hbm:
            c.dma("Y", xch[:, pb * T:(pb + 1) * T], src_hbm[ch * 128:(ch + 1) * 128, :],
                  reads=[src_buf[ch]], writes=[b_xch[pb]])
            c.op("V", lambda e, i=ch, p=pb: e.tensor_copy(xb[:, i * T:(i + 1) * T],
                                                          xch[:, p * T:(p + 1) * T]),
                 reads=[b_xch[pb]], writes=[bxb[ch]])
        c.op("S", lambda e, i=ch, p=pb: e.activation(xsq[:, p * T:(p + 1) * T],
                                                     xb[:, i * T:(i + 1) * T], AF.Square),
             reads=[bxb[ch]], writes=[b_xsq[pb]])
        for tt in range(NTT):
            sl = slice(tt * TT, (tt + 1) * TT)
            c.mm_group([(p_s[tt][0:1, :], onesb, xb[:, ch * T:][:, sl], ch == 0, ch == DC - 1)],
                       reads=[bxb[ch], P.b_ones_col], writes=[b_ps[tt]])
            c.mm_group([(p_q[tt][0:1, :], onesb, xsq[:, pb * T:][:, sl], ch == 0, ch == DC - 1)],
                       reads=[b_xsq[pb], P.b_ones_col], writes=[b_pq[tt]])
    for tt in range(NTT):
        c.op("V", lambda e, t=tt: e.tensor_copy(rows[0:1, t * TT:(t + 1) * TT], p_s[t][0:1, :]),
             reads=[b_ps[tt]], writes=[b_rows])
        c.op("V", lambda e, t=tt: e.tensor_copy(rows[0:1, T + t * TT:T + (t + 1) * TT], p_q[t][0:1, :]),
             reads=[b_pq[tt]], writes=[b_rows])
    c.pop_scope()

    c.op("V", lambda e: e.tensor_scalar_mul(mrow[0:1, :], rows[0:1, 0:T], 1.0 / D),
         reads=[b_rows], writes=[b_mrow])
    c.op("S", lambda e: e.activation(rows[0:1, 0:T], mrow[0:1, :], AF.Square),
         reads=[b_mrow], writes=[b_rows])
    c.op("V", lambda e: e.scalar_tensor_tensor(rows[0:1, T:2 * T], rows[0:1, T:2 * T],
                                               1.0 / D, rows[0:1, 0:T], ALU.mult, ALU.subtract),
         reads=[b_rows], writes=[b_rows])
    c.op("V", lambda e: e.tensor_scalar_add(rows[0:1, T:2 * T], rows[0:1, T:2 * T], eps),
         reads=[b_rows], writes=[b_rows])
    c.op("S", lambda e: e.activation(rows[0:1, T:2 * T], rows[0:1, T:2 * T], AF.Ln),
         reads=[b_rows], writes=[b_rows])
    c.op("S", lambda e: e.activation(rows[0:1, 0:T], rows[0:1, T:2 * T], AF.Exp, scale=-0.5),
         reads=[b_rows], writes=[b_rows])
    c.op("V", lambda e: e.tensor_mul(rows[0:1, T:2 * T], mrow[0:1, :], rows[0:1, 0:T]),
         reads=[b_rows, b_mrow], writes=[b_rows])
    c.op("V", lambda e: e.tensor_scalar_mul(rows[0:1, T:2 * T], rows[0:1, T:2 * T], -1.0),
         reads=[b_rows], writes=[b_rows])

    if hasattr(P, "dbg_rows") and P.dbg_rows is not None:
        yr, b_yr = P.dbg_rows
        c.dma("Y", yr[0:1, 0:2 * T], rows[0:1, :], reads=[b_rows], writes=[b_yr])
        c.dma("Y", yr[0:1, 2 * T:3 * T], mrow[0:1, :], reads=[b_mrow], writes=[b_yr])
        P.dbg_rows = None
    p_bc = [c.ps(f"ln_bc{i}", [128, 2 * TT]) for i in range(2)]
    b_bc = [Buf(), Buf()]
    if from_hbm:
        xt = c.sb("ln_xt", [128, 2 * TT], F32)
        b_xt = [Buf(), Buf()]
    for tt in range(NTT):
        pb2 = tt % 2
        pbc, bbc = p_bc[pb2], b_bc[pb2]
        c.mm_group([(pbc[:, 0:TT], P.ones_row[0:1, :], rows[0:1, tt * TT:(tt + 1) * TT], True, True),
                    (pbc[:, TT:2 * TT], P.ones_row[0:1, :], rows[0:1, T + tt * TT:T + (tt + 1) * TT], True, True)],
                   reads=[b_rows, P.b_ones_row], writes=[bbc])
        for ch in range(DC):
            sl = slice(tt * TT, (tt + 1) * TT)
            lt = P.ln_tmp4[ch % 4]
            b_lt = P.b_ln_tmp4[ch % 4]
            if from_hbm:
                px = ch % 2
                c.dma("Y", xt[:, px * TT:(px + 1) * TT],
                      src_hbm[ch * 128:(ch + 1) * 128, tt * TT:(tt + 1) * TT],
                      reads=[src_buf[ch]], writes=[b_xt[px]])
                xsrc = xt[:, px * TT:(px + 1) * TT]
                bsrc = b_xt[px]
            else:
                xsrc = xb[:, ch * T:][:, sl]
                bsrc = bxb[ch]
            c.op("V", lambda e, xs=xsrc, p=pbc, L=lt: e.tensor_mul(L[:, 0:TT], xs, p[:, 0:TT]),
                 reads=[bsrc, bbc], writes=[b_lt])
            if gamma_col is None:
                c.op("V", lambda e, i=ch, t=tt, p=pbc, L=lt: e.tensor_add(
                    xn_sb[:, i * T + t * TT: i * T + (t + 1) * TT], L[:, 0:TT], p[:, TT:2 * TT]),
                    reads=[b_lt, bbc], writes=[xn_buf[ch]])
            else:
                gcol, gbuf = gamma_col
                bcol, bbuf = beta_col
                c.op("V", lambda e, p=pbc, L=lt: e.tensor_add(
                    L[:, TT:2 * TT], L[:, 0:TT], p[:, TT:2 * TT]),
                    reads=[b_lt, bbc], writes=[b_lt])
                c.op("V", lambda e, i=ch, t=tt, L=lt: e.tensor_scalar(
                    xn_sb[:, i * T + t * TT: i * T + (t + 1) * TT], L[:, TT:2 * TT],
                    gcol[:, i:i + 1], bcol[:, i:i + 1], ALU.mult, ALU.add),
                    reads=[b_lt, gbuf, bbuf], writes=[xn_buf[ch]])
    c.pop_scope()


def gemm_fm(P, wkey, Kc, mcs, rhs_fn, rhs_bufs, evac, wcols=128, ntt=NTT, ttw=TT,
            col_off=0, extra_reads=(), pool=None):
    c = P.ctx
    wbuf = c.sb("gw_" + wkey, [128, 2 * Kc * wcols], BF16)
    b_w = [Buf(), Buf()]
    if pool is None:
        pss = [c.ps("gp_" + wkey, [128, ttw]) for i in range(2)]
        b_ps = [Buf(), Buf()]
    else:
        pss, b_ps = pool
    wimg = P.hbm[wkey]
    b_wimg = P.hbuf[wkey]
    for i, mc in enumerate(mcs):
        pb = i % 2
        seg = Kc * wcols
        c.dma("P", wbuf[:, pb * seg:(pb + 1) * seg],
              wimg[:, col_off + mc * seg: col_off + (mc + 1) * seg],
              reads=[b_wimg], writes=[b_w[pb]])
        for tt in range(ntt):
            ps, bp = pss[tt % 2], b_ps[tt % 2]
            items = []
            for kc in range(Kc):
                items.append((ps[:, :], wbuf[:, pb * seg + kc * wcols: pb * seg + (kc + 1) * wcols],
                              rhs_fn(kc, tt), kc == 0, kc == Kc - 1))
            c.mm_group(items, reads=[b_w[pb]] + list(rhs_bufs) + list(extra_reads), writes=[bp])
            evac(mc, tt, ps, bp)



def emit_l0(P, x_hbm, x_buf, out_hbm, out_buf):
    c = P.ctx
    H = 2048
    HC = 16
    TP = T + 8
    KC = 8

    c.push_scope()
    xn2 = c.sb("l0_xn2", [128, DC * T], BF16)
    b_xn2 = [Buf() for _ in range(DC)]
    vgv = c.sb("l0_vgv", [128, HC * TP], BF16)
    b_v = [Buf(f"v{h}") for h in range(HC)]
    for h in range(HC):
        c.op("V", lambda e, hh=h: e.memset(vgv[:, hh * TP:hh * TP + 8], 0.0), writes=[b_v[h]])

    c.push_scope()
    xn_sb = c.sb("l0_xn0", [128, DC * T], BF16)
    xn_bufs = [Buf() for _ in range(DC)]
    emit_layernorm2(P, x_hbm, x_buf, xn_sb, xn_bufs)
    xn_hbm = P.dram_scratch("sp_xn0", [128, DC * T], BF16)
    b_xnh = P.hbuf["sp_xn0"]
    for ch in range(DC):
        c.dma("Y", P.hbm["sp_xn0"][:, ch * T:(ch + 1) * T], xn_sb[:, ch * T:(ch + 1) * T],
              reads=[xn_bufs[ch]], writes=[b_xnh])
    emit_layernorm2(P, None, None, xn2, b_xn2,
                    gamma_col=(P.t_c_l0_gamma[:, :], P.b_c_l0_gamma),
                    beta_col=(P.t_c_l0_beta[:, :], P.b_c_l0_beta),
                    src_sb=(xn_sb, xn_bufs))
    c.pop_scope()

    c.push_scope()
    wbuf = c.sb("l0_wbuf", [128, 2 * KC * 128], BF16)
    b_w = [Buf(), Buf()]
    p_slots = [c.ps(f"l0_pc{i}", [128, TT]) for i in range(4)]
    b_p = [Buf() for _ in range(4)]
    w_img = P.hbm["w_l0_in"]
    b_wimg = P.hbuf["w_l0_in"]
    for i, mc in enumerate(range(16, 32)):
        pb = i % 2
        c.dma("P", wbuf[:, pb * 1024:(pb + 1) * 1024],
              w_img[:, (mc * KC) * 128:(mc * KC + KC) * 128],
              reads=[b_wimg], writes=[b_w[pb]])
        hc = mc - 16
        for tt in range(NTT):
            sl = slice(tt * TT, (tt + 1) * TT)
            ps = p_slots[(i * NTT + tt) % 4]
            bp = b_p[(i * NTT + tt) % 4]
            items = []
            for kc in range(KC):
                items.append((ps[:, :], wbuf[:, pb * 1024 + kc * 128: pb * 1024 + (kc + 1) * 128],
                              xn2[:, kc * T:][:, sl], kc == 0, kc == KC - 1))
            c.mm_group(items, reads=[b_w[pb]] + b_xn2, writes=[bp])
            c.op("V", lambda e, h=hc, t=tt, p=ps, m=mc: e.tensor_scalar(
                vgv[:, h * TP + 8 + t * TT: h * TP + 8 + (t + 1) * TT], p[:, :],
                P.t_c_l0_inb[:, m:m + 1], None, ALU.add),
                reads=[bp, P.b_c_l0_inb], writes=[b_v[hc]])

    scan_bf = c.sb("l0_scan", [128, HC * T], BF16)
    b_scan = [Buf() for _ in range(HC)]
    dec = c.sb("l0_dec", [128, HC], F32)
    b_dec = Buf()
    scan_f = c.sb("l0_scanf", [128, T // 2], F32)
    b_scanf = Buf()
    c.op("S", lambda e: e.activation(dec[:, :], P.t_c_l0_dec[:, :], AF.Sigmoid),
         reads=[P.b_c_l0_dec], writes=[b_dec])
    carry = c.sb("l0_carry", [128, 1], F32)
    b_carry = Buf()
    TH2 = T // 2
    for hc in range(HC):
        for hf in range(2):
            init = 0.0 if hf == 0 else carry[:, 0:1]
            rd = [b_dec, b_v[hc]] + ([b_carry] if hf else [])
            c.op("V", lambda e, h=hc, f=hf, ini=init: e.tensor_tensor_scan(
                scan_f[:, :], dec[:, h:h + 1].broadcast_to([128, TH2]),
                vgv[:, h * TP + 8 + f * TH2: h * TP + 8 + (f + 1) * TH2],
                ini, ALU.mult, ALU.add),
                reads=rd, writes=[b_scanf])
            if hf == 0:
                c.op("V", lambda e: e.tensor_copy(carry[:, 0:1], scan_f[:, TH2 - 1:TH2]),
                     reads=[b_scanf], writes=[b_carry])
            c.op("P", lambda e, h=hc, f=hf: e.tensor_copy(
                scan_bf[:, h * T + f * TH2: h * T + (f + 1) * TH2], scan_f[:, :]),
                reads=[b_scanf], writes=[b_scan[hc]])

    diag = c.sb("l0_diag", [128, 7 * 128], BF16)
    b_diag = Buf()
    bsum = c.sb("l0_bsum", [128, HC], F32)
    b_bsum = Buf()
    c.op("V", lambda e: e.tensor_add(bsum[:, :], P.t_c_l0_dwb[:, :], P.t_c_l0_mixb[:, :]),
         reads=[P.b_c_l0_dwb, P.b_c_l0_mixb], writes=[b_bsum])

    wmix = P.hbm["w_l0_mix"]
    b_wmix = P.hbuf["w_l0_mix"]
    wbuf2 = c.sb("l0_wbuf2", [128, 2 * 16 * 128], BF16)
    b_w2 = [Buf(), Buf()]
    g_sb = c.sb("l0_g", [128, 2 * TT], F32)
    b_g = [Buf(), Buf()]
    KCH = 16
    for hc in range(HC):
        pb = hc % 2
        c.dma("P", wbuf2[:, pb * 2048:(pb + 1) * 2048],
              wmix[:, (hc * KCH) * 128:(hc * KCH + KCH) * 128],
              reads=[b_wmix], writes=[b_w2[pb]])
        c.dma("P", wbuf[:, pb * 1024:(pb + 1) * 1024],
              w_img[:, (hc * KC) * 128:(hc * KC + KC) * 128],
              reads=[b_wimg], writes=[b_w[pb]])
        for j in range(7):
            c.op("V", lambda e, h=hc, jj=j: e.tensor_scalar_mul(
                diag[:, jj * 128:(jj + 1) * 128], P.ident_bf[:, :],
                P.t_c_l0_dww[:, h * 7 + jj:h * 7 + jj + 1]),
                reads=[P.b_ident, P.b_c_l0_dww], writes=[b_diag])
        for tt in range(NTT):
            sl = slice(tt * TT, (tt + 1) * TT)
            items = []
            for kc in range(KCH):
                items.append((p_slots[tt][:, :],
                              wbuf2[:, pb * 2048 + kc * 128:pb * 2048 + (kc + 1) * 128],
                              scan_bf[:, kc * T:][:, sl], kc == 0, False))
            for j in range(7):
                sh = slice(hc * TP + 8 + tt * TT - (6 - j), hc * TP + 8 + (tt + 1) * TT - (6 - j))
                items.append((p_slots[tt][:, :], diag[:, j * 128:(j + 1) * 128], vgv[:, sh],
                              False, j == 6))
            c.mm_group(items, reads=[b_w2[pb], b_diag, b_v[hc]] + b_scan, writes=[b_p[tt]])
        if hc == 0:
            pg2 = [c.ps(f"l0_pg{i}", [128, TT]) for i in range(2)]
            bpg2 = [Buf(), Buf()]
        for tt in range(NTT):
            sl = slice(tt * TT, (tt + 1) * TT)
            pg, bpg = pg2[tt % 2], bpg2[tt % 2]
            wl = slice((tt % 2) * TT, (tt % 2 + 1) * TT)
            itg = []
            for kc in range(KC):
                itg.append((pg[:, :], wbuf[:, pb * 1024 + kc * 128:pb * 1024 + (kc + 1) * 128],
                            xn2[:, kc * T:][:, sl], kc == 0, kc == KC - 1))
            c.mm_group(itg, reads=[b_w[pb]] + b_xn2, writes=[bpg])
            c.op("S", lambda e, h=hc, p=pg, W=wl: e.activation(
                g_sb[:, W], p[:, :], AF.Sigmoid, bias=P.t_c_l0_inb[:, h:h + 1]),
                reads=[bpg, P.b_c_l0_inb], writes=[b_g[tt % 2]])
            c.op("V", lambda e, h=hc, t=tt, W=wl: e.scalar_tensor_tensor(
                vgv[:, h * TP + 8 + t * TT: h * TP + 8 + (t + 1) * TT], p_slots[t][:, :],
                bsum[:, h:h + 1], g_sb[:, W], ALU.add, ALU.mult),
                reads=[b_p[tt], b_g[tt % 2], b_bsum], writes=[b_v[hc]])
    c.pop_scope()

    c.push_scope()
    s_hbm = P.dram_scratch("sp_s0", [128, DC * T], F32)
    b_sh = P.hbuf8["sp_s0"]
    s_sb = c.sb("l0_s", [128, 2 * TT], F32)
    b_s = [Buf(), Buf()]
    xn_tile = c.sb("l0_xnt", [128, T], BF16)
    b_xnt = Buf()
    wout = P.hbm["w_l0_out"]
    b_wout = P.hbuf["w_l0_out"]
    wbuf3 = c.sb("l0_wbuf3", [128, 2 * 16 * 128], BF16)
    b_w3 = [Buf(), Buf()]
    p_f = [c.ps(f"l0_pf{i}", [128, TT]) for i in range(2)]
    b_pf = [Buf(), Buf()]
    for mc in range(DC):
        pb = mc % 2
        c.dma("P", wbuf3[:, pb * 2048:(pb + 1) * 2048],
              wout[:, (mc * 16) * 128:(mc * 16 + 16) * 128],
              reads=[b_wout], writes=[b_w3[pb]])
        c.dma("Y", xn_tile[:, :], P.hbm["sp_xn0"][:, mc * T:(mc + 1) * T],
              reads=[b_xnh], writes=[b_xnt])
        for tt in range(NTT):
            sl = slice(tt * TT, (tt + 1) * TT)
            ps = p_f[tt % 2]
            bp = b_pf[tt % 2]
            items = []
            for kc in range(16):
                items.append((ps[:, :], wbuf3[:, pb * 2048 + kc * 128:pb * 2048 + (kc + 1) * 128],
                              vgv[:, kc * TP + 8:][:, sl], kc == 0, kc == 15))
            c.mm_group(items, reads=[b_w3[pb]] + b_v, writes=[bp])
            wl = slice((tt % 2) * TT, (tt % 2 + 1) * TT)
            c.op("V", lambda e, p=ps, m=mc, t=tt, W=wl: e.scalar_tensor_tensor(
                s_sb[:, W], p[:, :], P.t_c_l0_outb[:, m:m + 1], xn_tile[:, t * TT:(t + 1) * TT],
                ALU.add, ALU.add),
                reads=[bp, b_xnt, P.b_c_l0_outb], writes=[b_s[tt % 2]])
            c.dma("Y", s_hbm[:, mc * T + tt * TT: mc * T + (tt + 1) * TT], s_sb[:, wl],
                  reads=[b_s[tt % 2]], writes=[b_sh[mc]])
    c.pop_scope()
    c.pop_scope()

    g_hbm = P.dram_scratch("sp_g0", [128, DC * T], BF16)
    b_gh = P.hbuf8["sp_g0"]
    emit_l0_ffn(P, g_hbm, b_gh)

    emit_combine(P, x_hbm, x_buf, P.hbm["sp_s0"], b_sh, P.hbm["sp_g0"], b_gh,
                 out_hbm, out_buf)


def emit_l0_ffn(P, g_hbm, b_gh):
    c = P.ctx
    HC = 32
    KC8 = 8
    KC32 = 32
    c.push_scope()
    TH = T // 2
    NT2 = TH // TT
    xn_t = c.sb("f_xn", [128, DC * TH], BF16)
    b_xn = [Buf() for _ in range(DC)]
    A = c.sb("f_A", [128, HC * TH], BF16)
    b_A = [Buf() for _ in range(HC)]
    Bt = c.sb("f_B", [128, HC * TH], BF16)
    b_B = [Buf() for _ in range(HC)]
    wb1 = c.sb("f_w1", [128, 2 * KC8 * 128], BF16)
    wb2 = c.sb("f_w2", [128, 2 * KC8 * 128], BF16)
    b_w1 = [Buf(), Buf()]
    b_w2 = [Buf(), Buf()]
    tmp = c.sb("f_tmp", [128, 2 * TT], F32)
    b_tmp = [Buf(), Buf()]
    wbig = c.sb("f_wbig", [128, 2 * KC32 * 128], BF16)
    b_wbig = [Buf(), Buf()]
    gt = c.sb("f_g", [128, 2 * TT], BF16)
    b_gt = [Buf(), Buf()]
    p_slots = [c.ps(f"f_p{i}", [128, TT]) for i in range(4)]
    b_p = [Buf() for _ in range(4)]
    for half in range(2):
        for ch in range(DC):
            c.dma("Y", xn_t[:, ch * TH:(ch + 1) * TH],
                  P.hbm["sp_xn0"][:, ch * T + half * TH: ch * T + (half + 1) * TH],
                  reads=[P.hbuf["sp_xn0"]], writes=[b_xn[ch]])
        for mc in range(HC):
            pb = mc % 2
            c.dma("P", wb1[:, pb * 1024:(pb + 1) * 1024],
                  P.hbm["w_l0f_w1"][:, mc * KC8 * 128:(mc * KC8 + KC8) * 128],
                  reads=[P.hbuf["w_l0f_w1"]], writes=[b_w1[pb]])
            c.dma("P", wb2[:, pb * 1024:(pb + 1) * 1024],
                  P.hbm["w_l0f_w2"][:, mc * KC8 * 128:(mc * KC8 + KC8) * 128],
                  reads=[P.hbuf["w_l0f_w2"]], writes=[b_w2[pb]])
            for tt in range(NT2):
                sl = slice(tt * TT, (tt + 1) * TT)
                ps1, bp1 = p_slots[2 * (tt % 2)], b_p[2 * (tt % 2)]
                ps2, bp2 = p_slots[2 * (tt % 2) + 1], b_p[2 * (tt % 2) + 1]
                it1, it2 = [], []
                for kc in range(KC8):
                    it1.append((ps1[:, :], wb1[:, pb * 1024 + kc * 128:pb * 1024 + (kc + 1) * 128],
                                xn_t[:, kc * TH:][:, sl], kc == 0, kc == KC8 - 1))
                    it2.append((ps2[:, :], wb2[:, pb * 1024 + kc * 128:pb * 1024 + (kc + 1) * 128],
                                xn_t[:, kc * TH:][:, sl], kc == 0, kc == KC8 - 1))
                c.mm_group(it1, reads=[b_w1[pb]] + b_xn, writes=[bp1])
                c.mm_group(it2, reads=[b_w2[pb]] + b_xn, writes=[bp2])
                wl = slice((tt % 2) * TT, (tt % 2 + 1) * TT)
                c.op("S", lambda e, p=ps1, m=mc, W=wl: e.activation(
                    tmp[:, W], p[:, :], AF.Silu, scale=P.t_c_l0f_s1[:, m:m + 1]),
                    reads=[bp1, P.b_c_l0f_s1], writes=[b_tmp[tt % 2]])
                c.op("V", lambda e, p=ps2, m=mc, t=tt, W=wl: e.scalar_tensor_tensor(
                    A[:, m * TH + t * TT: m * TH + (t + 1) * TT], p[:, :],
                    P.t_c_l0f_s2[:, m:m + 1], tmp[:, W], ALU.mult, ALU.mult),
                    reads=[bp2, b_tmp[tt % 2], P.b_c_l0f_s2], writes=[b_A[mc]])
        for stage, (wkey, skey, scol, src, bsrc, dst, bdst) in enumerate([
                ("w_l0f_h1", "s3", P.t_c_l0f_s3, A, b_A, Bt, b_B),
                ("w_l0f_h2", "s4", P.t_c_l0f_s4, Bt, b_B, A, b_A)]):
            for mc in range(HC):
                pb = mc % 2
                c.dma("P", wbig[:, pb * KC32 * 128:(pb + 1) * KC32 * 128],
                      P.hbm[wkey][:, mc * KC32 * 128:(mc * KC32 + KC32) * 128],
                      reads=[P.hbuf[wkey]], writes=[b_wbig[pb]])
                for tt in range(NT2):
                    sl = slice(tt * TT, (tt + 1) * TT)
                    ps, bp = p_slots[tt % 2], b_p[tt % 2]
                    items = []
                    for kc in range(KC32):
                        items.append((ps[:, :],
                                      wbig[:, pb * KC32 * 128 + kc * 128:pb * KC32 * 128 + (kc + 1) * 128],
                                      src[:, kc * TH:][:, sl], kc == 0, kc == KC32 - 1))
                    c.mm_group(items, reads=[b_wbig[pb]] + bsrc, writes=[bp])
                    c.op("S", lambda e, p=ps, m=mc, t=tt, sc=scol, d=dst: e.activation(
                        d[:, m * TH + t * TT: m * TH + (t + 1) * TT], p[:, :], AF.Gelu,
                        scale=sc[:, m:m + 1]),
                        reads=[bp], writes=[bdst[mc]])
        for mc in range(DC):
            pb = mc % 2
            c.dma("P", wbig[:, pb * KC32 * 128:(pb + 1) * KC32 * 128],
                  P.hbm["w_l0f_w3"][:, mc * KC32 * 128:(mc * KC32 + KC32) * 128],
                  reads=[P.hbuf["w_l0f_w3"]], writes=[b_wbig[pb]])
            for tt in range(NT2):
                sl = slice(tt * TT, (tt + 1) * TT)
                ps, bp = p_slots[tt % 2], b_p[tt % 2]
                items = []
                for kc in range(KC32):
                    items.append((ps[:, :],
                                  wbig[:, pb * KC32 * 128 + kc * 128:pb * KC32 * 128 + (kc + 1) * 128],
                                  A[:, kc * TH:][:, sl], kc == 0, kc == KC32 - 1))
                c.mm_group(items, reads=[b_wbig[pb]] + b_A, writes=[bp])
                wl = slice((tt % 2) * TT, (tt % 2 + 1) * TT)
                c.op("S", lambda e, p=ps, m=mc, W=wl: e.activation(
                    gt[:, W], p[:, :], AF.Silu, scale=P.t_c_l0f_s5[:, m:m + 1]),
                    reads=[bp, P.b_c_l0f_s5], writes=[b_gt[tt % 2]])
                c.dma("Y", g_hbm[:, mc * T + half * TH + tt * TT: mc * T + half * TH + (tt + 1) * TT],
                      gt[:, wl], reads=[b_gt[tt % 2]], writes=[b_gh[mc]])
    c.pop_scope()


def emit_combine(P, x_hbm, x_buf, s_hbm, b_sh, g_hbm, b_gh, out_hbm, out_buf):
    c = P.ctx
    c.push_scope()
    NL = 4
    xt = c.sb("cb_x", [128, NL * TT], F32)
    st = c.sb("cb_s", [128, NL * TT], F32)
    gt = c.sb("cb_g", [128, NL * TT], BF16)
    ot = c.sb("cb_o", [128, NL * TT], F32)
    bx = [Buf() for _ in range(NL)]
    bs = [Buf() for _ in range(NL)]
    bg = [Buf() for _ in range(NL)]
    bo = [Buf() for _ in range(NL)]
    i = 0
    for ch in range(DC):
        for tt in range(NTT):
            ln = i % NL
            i += 1
            w = slice(ln * TT, (ln + 1) * TT)
            sl = slice(ch * T + tt * TT, ch * T + (tt + 1) * TT)
            c.dma("Y", xt[:, w], x_hbm[ch * 128:(ch + 1) * 128, tt * TT:(tt + 1) * TT]
                  if x_hbm.shape[0] == D else x_hbm[:, sl],
                  reads=[x_buf[ch]], writes=[bx[ln]])
            c.dma("Y", st[:, w], s_hbm[:, sl], reads=[b_sh[ch]], writes=[bs[ln]])
            c.dma("P", gt[:, w], g_hbm[:, sl], reads=[b_gh[ch]], writes=[bg[ln]])
            c.op("V", lambda e, W=w: e.tensor_mul(ot[:, W], st[:, W], gt[:, W]),
                 reads=[bs[ln], bg[ln]], writes=[bo[ln]])
            c.op("P", lambda e, W=w: e.tensor_add(ot[:, W], ot[:, W], xt[:, W]),
                 reads=[bo[ln], bx[ln]], writes=[bo[ln]])
            c.dma("Y", out_hbm[ch * 128:(ch + 1) * 128, tt * TT:(tt + 1) * TT]
                  if out_hbm.shape[0] == D else out_hbm[:, sl],
                  ot[:, w], reads=[bo[ln]], writes=[out_buf[ch]])
    c.pop_scope()



def emit_l1(P, x_hbm, x_buf, out_hbm, out_buf):
    c = P.ctx
    H, HD = 16, 64
    c.push_scope()
    s_hbm = P.dram_scratch("sp_s1", [128, DC * T], F32)
    b_sh = P.hbuf8["sp_s1"]
    g_hbm = P.dram_scratch("sp_g1", [128, DC * T], BF16)
    b_gh = P.hbuf8["sp_g1"]
    xn_hbm = P.dram_scratch("sp_xn1", [128, DC * T], BF16)
    b_xnh = P.hbuf8["sp_xn1"]

    c.push_scope()
    q = c.sb("l1_q", [128, DC * T], BF16)
    b_q = [Buf() for _ in range(DC)]
    k = c.sb("l1_k", [128, DC * T], BF16)
    b_k = [Buf() for _ in range(DC)]
    v_tm = c.sb("l1_v", [128, 16 * 1040], BF16)
    b_v = [Buf() for _ in range(16)]

    c.push_scope()
    xn = c.sb("l1_xn", [128, DC * T], BF16)
    b_xn = [Buf() for _ in range(DC)]
    emit_layernorm2(P, x_hbm, x_buf, xn, b_xn)
    for ch in range(DC):
        c.dma("Y", xn_hbm[:, ch * T:(ch + 1) * T], xn[:, ch * T:(ch + 1) * T],
              reads=[b_xn[ch]], writes=[b_xnh[ch]])

    def rhs_xn(kc, tt):
        return xn[:, kc * T:][:, tt * TT:(tt + 1) * TT]
    gemm_fm(P, "w_l1_q", 8, range(8), rhs_xn, b_xn,
            lambda mc, tt, ps, bp: c.op("V", lambda e, m=mc, t=tt, p=ps: e.tensor_copy(
                q[:, m * T + t * TT: m * T + (t + 1) * TT], p[:, :]),
                reads=[bp], writes=[b_q[mc]]))
    c.push_scope()
    lat = c.sb("l1_lat", [128, 2 * T], BF16)
    b_lat = [Buf(), Buf()]
    gemm_fm(P, "w_l1_kv", 8, range(2), rhs_xn, b_xn,
            lambda mc, tt, ps, bp: c.op("V", lambda e, m=mc, t=tt, p=ps: e.tensor_copy(
                lat[:, m * T + t * TT: m * T + (t + 1) * TT], p[:, :]),
                reads=[bp], writes=[b_lat[mc]]))

    def rhs_lat(kc, tt):
        return lat[:, kc * T:][:, tt * TT:(tt + 1) * TT]
    gemm_fm(P, "w_l1_kup", 2, range(8), rhs_lat, b_lat,
            lambda mc, tt, ps, bp: c.op("V", lambda e, m=mc, t=tt, p=ps: e.tensor_copy(
                k[:, m * T + t * TT: m * T + (t + 1) * TT], p[:, :]),
                reads=[bp], writes=[b_k[mc]]))
    wv = c.sb("l1_wv", [128, 2 * 1024], BF16)
    b_wv = Buf()
    c.dma("P", wv[:, :], P.hbm["w_l1_vup"][:, :], reads=[P.hbuf["w_l1_vup"]], writes=[b_wv])
    pv = [c.ps(f"l1_pv{i}", [128, TT]) for i in range(2)]
    b_pv = [Buf(), Buf()]
    for kt in range(16):
        c.op("V", lambda e, K=kt: e.memset(
            v_tm[:, K * 1040:(K + 1) * 1040].rearrange("p (h f) -> p h f", h=16)[:, :, 64:65], 1.0),
            writes=[b_v[kt]])
        for nb in range(2):
            ps, bp = pv[nb], b_pv[nb]
            items = []
            for kc in range(2):
                items.append((ps[:, :], lat[:, kc * T + kt * 128: kc * T + (kt + 1) * 128],
                              wv[:, kc * 1024 + nb * 512: kc * 1024 + (nb + 1) * 512],
                              kc == 0, kc == 1))
            c.mm_group(items, reads=b_lat + [b_wv], writes=[bp])
            c.op("V", lambda e, K=kt, n=nb, p=ps: e.tensor_copy(
                v_tm[:, K * 1040 + n * 8 * 65:][:, 0:520].rearrange("p (h f) -> p h f", h=8)[:, :, 0:64],
                p[:, :].rearrange("p (h f) -> p h f", h=8)),
                reads=[bp], writes=[b_v[kt]])
    c.pop_scope()
    c.pop_scope()
    y = c.sb("l1_y", [128, DC * T], BF16)
    b_y = [Buf() for _ in range(DC)]

    cneg = c.sb("l1_cneg", [128, 1], F32)
    b_cn = Buf()
    c.push_scope()
    sq = c.sb("l1_sq", [128, T], BF16)
    b_sq = Buf()
    mxr = c.sb("l1_mx", [1, 64], F32)
    b_mx = Buf()
    p1 = c.ps("l1_p1", [1, TT])
    b_p1 = Buf()
    for half, (src, bufs) in enumerate([(q, b_q), (k, b_k)]):
        for ch in range(DC):
            c.op("S", lambda e, s=src, i=ch: e.activation(sq[:, :], s[:, i * T:(i + 1) * T], AF.Square),
                 reads=[bufs[ch]], writes=[b_sq])
            for tt in range(NTT):
                c.mm_group([(p1[0:1, :], P.ones_col_bf[:, 0:1], sq[:, tt * TT:(tt + 1) * TT], True, True)],
                           reads=[b_sq, P.b_ones_col], writes=[b_p1])
                c.op("V", lambda e, hf=half, i=ch, t=tt: e.reduce_max(
                    mxr[0:1, hf * 32 + i * 4 + t: hf * 32 + i * 4 + t + 1], p1[0:1, :], axis=AX.X),
                    reads=[b_p1], writes=[b_mx])
    crow = c.sb("l1_cr", [1, 4], F32)
    b_cr = Buf()
    c.op("V", lambda e: e.tensor_reduce(crow[0:1, 0:2], mxr[0:1, :].rearrange("p (a b) -> p a b", a=2),
                                        AX.X, ALU.max), reads=[b_mx], writes=[b_cr])
    c.op("S", lambda e: e.activation(crow[0:1, 2:4], crow[0:1, 0:2], AF.Ln), reads=[b_cr], writes=[b_cr])
    c.op("V", lambda e: e.reduce_sum(crow[0:1, 0:1], crow[0:1, 2:4], axis=AX.X), reads=[b_cr], writes=[b_cr])
    c.op("S", lambda e: e.activation(crow[0:1, 1:2], crow[0:1, 0:1], AF.Exp, scale=0.5),
         reads=[b_cr], writes=[b_cr])
    c.op("V", lambda e: e.tensor_scalar_mul(crow[0:1, 0:1], crow[0:1, 1:2], -0.125),
         reads=[b_cr], writes=[b_cr])
    pcb = c.ps("l1_pcb", [128, 1])
    b_pcb = Buf()
    c.mm_group([(pcb[:, 0:1], P.ones_row[0:1, :], crow[0:1, 0:1], True, True)],
               reads=[b_cr, P.b_ones_row], writes=[b_pcb])
    c.op("V", lambda e: e.tensor_copy(cneg[:, :], pcb[:, 0:1]), reads=[b_pcb], writes=[b_cn])
    c.pop_scope()

    masks = c.sb("l1_masks", [128, 4 * TT], BF16)
    b_mk = Buf()
    iot2 = c.sb("l1_iot2", [128, TT], I32)
    b_io = Buf()
    c.op("P", lambda e: e.iota(iot2[:, :], [[1, TT]], channel_multiplier=-1,
                               allow_small_or_imprecise_dtypes=True), writes=[b_io])
    for r in range(4):
        c.op("V", lambda e, rr=r: e.tensor_single_scalar(
            masks[:, rr * TT:(rr + 1) * TT], iot2[:, :], 128 * rr, ALU.is_ge),
            reads=[b_io], writes=[b_mk])

    expS2 = [c.sb(f"l1_es{i}", [128, 16 * TT], BF16) for i in range(2)]
    b_es2 = [Buf(), Buf()]
    av2 = [c.sb(f"l1_av{i}", [128, TT], F32) for i in range(2)]
    b_av2 = [Buf(), Buf()]
    rc2 = [c.sb(f"l1_rc{i}", [1, TT], F32) for i in range(2)]
    b_rc2 = [Buf(), Buf()]
    psc4 = [c.ps(f"l1_psc{i}", [128, TT]) for i in range(4)]
    b_psc4 = [Buf() for _ in range(4)]
    pav2 = [c.ps(f"l1_pav{i}", [128, TT]) for i in range(2)]
    b_pav2 = [Buf(), Buf()]
    pbc2_ = [c.ps(f"l1_pbc{i}", [128, TT]) for i in range(2)]
    b_pbc2_ = [Buf(), Buf()]

    def emit_head(qb, ch, hh):
        lane = hh
        expS, b_es = expS2[lane], b_es2[lane]
        av_sb, b_av = av2[lane], b_av2[lane]
        rc, b_rc = rc2[lane], b_rc2[lane]
        pav, b_pav = pav2[lane], b_pav2[lane]
        pbc, b_pbc = pbc2_[lane], b_pbc2_[lane]
        head = 2 * ch + hh
        rs = slice(64 * hh, 64 * hh + 64)
        tp = (64 * hh, 0)
        nkb = 4 * qb + 4
        for kb in range(nkb):
            ps, bp = psc4[2 * lane + kb % 2], b_psc4[2 * lane + kb % 2]
            c.mm_group([(ps[:, :], k[rs, ch * T + kb * 128: ch * T + (kb + 1) * 128],
                         q[rs, ch * T + qb * TT: ch * T + (qb + 1) * TT], True, True, tp)],
                       reads=[b_k[ch], b_q[ch]], writes=[bp])
            c.op("S", lambda e, K=kb, p=ps: e.activation(
                expS[:, K * TT:(K + 1) * TT], p[:, :], AF.Exp, bias=cneg[:, 0:1], scale=0.125),
                reads=[bp, b_cn], writes=[b_es])
            if kb >= 4 * qb:
                r = kb - 4 * qb
                c.op("V", lambda e, K=kb, rr=r: e.tensor_mul(
                    expS[:, K * TT:(K + 1) * TT], expS[:, K * TT:(K + 1) * TT],
                    masks[:, rr * TT:(rr + 1) * TT]),
                    reads=[b_es, b_mk], writes=[b_es])
            yield
        items = []
        for kb in range(nkb):
            items.append((pav[0:65, :], v_tm[:, kb * 1040 + head * 65: kb * 1040 + head * 65 + 65],
                          expS[:, kb * TT:(kb + 1) * TT], kb == 0, kb == nkb - 1))
        c.mm_group(items, reads=b_v + [b_es], writes=[b_pav])
        yield
        c.op("V", lambda e: e.tensor_copy(av_sb[0:65, :], pav[0:65, :]),
             reads=[b_pav], writes=[b_av])
        c.op("V", lambda e: e.reciprocal(rc[0:1, :], av_sb[64:65, :]),
             reads=[b_av], writes=[b_rc])
        yield
        c.mm_group([(pbc[0:64, :], P.ones_row[0:1, 0:64], rc[0:1, :], True, True)],
                   reads=[b_rc, P.b_ones_row], writes=[b_pbc])
        yield
        c.op("V", lambda e, i=ch, h=hh, t=qb: e.tensor_mul(
            y[64 * h:64 * h + 64, i * T + t * TT: i * T + (t + 1) * TT],
            av_sb[0:64, :], pbc[0:64, :]),
            reads=[b_av, b_pbc], writes=[b_y[ch]])

    for qb in range(NTT):
        for ch in range(DC):
            alive = [emit_head(qb, ch, 0), emit_head(qb, ch, 1)]
            while alive:
                nxt = []
                for gen in alive:
                    try:
                        next(gen)
                        nxt.append(gen)
                    except StopIteration:
                        pass
                alive = nxt

    s_sb = c.sb("l1_s", [128, TT], F32)
    b_s = Buf()

    def rhs_y(kc, tt):
        return y[:, kc * T:][:, tt * TT:(tt + 1) * TT]

    def ev_s(mc, tt, ps, bp):
        c.op("V", lambda e, m=mc, t=tt, p=ps: e.tensor_copy(s_sb[:, :], p[:, :]),
             reads=[bp], writes=[b_s])
        c.dma("Y", s_hbm[:, mc * T + tt * TT: mc * T + (tt + 1) * TT], s_sb[:, :],
              reads=[b_s], writes=[b_sh[mc]])
    gemm_fm(P, "w_l1_out", 8, range(8), rhs_y, b_y, ev_s,
            pool=([psc4[0], psc4[1]], [b_psc4[0], b_psc4[1]]))
    c.pop_scope()

    c.push_scope()
    xn = c.sb("l1f_xn", [128, DC * T], BF16)
    b_xn = [Buf() for _ in range(DC)]
    for ch in range(DC):
        c.dma("Y", xn[:, ch * T:(ch + 1) * T], xn_hbm[:, ch * T:(ch + 1) * T],
              reads=[b_xnh[ch]], writes=[b_xn[ch]])

    def rhs_xn(kc, tt):
        return xn[:, kc * T:][:, tt * TT:(tt + 1) * TT]
    gpool = ([c.ps(f"l1f_gp{i}", [128, TT]) for i in range(2)], [Buf(), Buf()])
    h_sb = c.sb("l1f_h", [128, 2 * T], BF16)
    b_h = [Buf(), Buf()]
    diff = c.sb("l1f_diff", [128, TT], F32)
    b_df = Buf()
    sqd = c.sb("l1f_sqd", [128, TT], BF16)
    b_sqd = Buf()
    lcols = c.sb("l1f_lc", [128, 256], F32)
    b_lc = Buf()
    for ei in range(8):
        def ev_h(mc, tt, ps, bp, ei=ei):
            c.op("S", lambda e, m=mc, t=tt, p=ps, E=ei: e.activation(
                h_sb[:, m * T + t * TT: m * T + (t + 1) * TT], p[:, :], AF.Relu,
                bias=P.t_c_l1_benc[:, E * 2 + m:E * 2 + m + 1]),
                reads=[bp, P.b_c_l1_benc], writes=[b_h[mc]])
        gemm_fm(P, "w_l1_enc", 8, range(2), rhs_xn, b_xn, ev_h, col_off=ei * 2048, pool=gpool)

        def rhs_h(kc, tt):
            return h_sb[:, kc * T:][:, tt * TT:(tt + 1) * TT]

        def ev_rec(mc, tt, ps, bp, ei=ei):
            c.op("V", lambda e, m=mc, t=tt, p=ps, E=ei: e.scalar_tensor_tensor(
                diff[:, :], p[:, :], P.t_c_l1_bdec[:, E * 8 + m:E * 8 + m + 1],
                xn[:, m * T + t * TT: m * T + (t + 1) * TT], ALU.add, ALU.subtract),
                reads=[bp, b_xn[mc], P.b_c_l1_bdec], writes=[b_df])
            c.op("S", lambda e, m=mc, t=tt, E=ei: e.activation(
                sqd[:, :], diff[:, :], AF.Square,
                accum_out=lcols[:, E * 32 + m * 4 + t: E * 32 + m * 4 + t + 1]),
                reads=[b_df], writes=[b_sqd, b_lc])
        gemm_fm(P, "w_l1_dec", 2, range(8), rhs_h, b_h, ev_rec, col_off=ei * 2048, pool=gpool)
    ohbc = c.sb("l1f_ohbc", [128, 8], F32)
    b_ohbc = Buf()
    bdu = c.sb("l1f_bdu", [128, 9], F32)
    b_bdu = Buf()
    c.push_scope()
    lsum = c.sb("l1f_ls", [128, 8], F32)
    b_lsum = Buf()
    c.op("V", lambda e: e.tensor_reduce(lsum[:, :], lcols[:, :].rearrange("p (a b) -> p a b", a=8),
                                        AX.X, ALU.add), reads=[b_lc], writes=[b_lsum])
    pl8 = c.ps("l1f_pl8", [1, 8])
    b_pl8 = Buf()
    c.mm_group([(pl8[0:1, :], P.ones_col[:, 0:1], lsum[:, :], True, True)],
               reads=[b_lsum, P.b_ones_col], writes=[b_pl8])
    small = c.sb("l1f_small", [1, 32], F32)
    b_sm = Buf()
    c.op("V", lambda e: e.tensor_copy(small[0:1, 0:8], pl8[0:1, :]), reads=[b_pl8], writes=[b_sm])
    c.op("V", lambda e: e.tensor_reduce(small[0:1, 8:9], small[0:1, 0:8], AX.X, ALU.min),
         reads=[b_sm], writes=[b_sm])
    c.op("V", lambda e: e.tensor_scalar(small[0:1, 9:17], small[0:1, 0:8], small[0:1, 8:9], None,
                                        ALU.is_le), reads=[b_sm], writes=[b_sm])
    oh_bf = c.sb("l1f_ohbf", [1, 8], BF16)
    b_ohbf = Buf()
    c.op("V", lambda e: e.tensor_copy(oh_bf[0:1, :], small[0:1, 9:17]), reads=[b_sm], writes=[b_ohbf])
    connf = c.sb("l1f_connf", [8, 1], F32)
    b_cf = Buf()
    c.dma("Y", P.conn_i[0:8, 0:1], P.hbm["i_l1_conn"], reads=[P.hbuf["i_l1_conn"]], writes=[P.b_conn])
    c.op("V", lambda e: e.tensor_copy(connf[:, :], P.conn_i[0:8, 0:1]), reads=[P.b_conn], writes=[b_cf])
    io8 = c.sb("l1f_io8", [8, 8], I32)
    b_io8 = Buf()
    c.op("P", lambda e: e.iota(io8[:, :], [[1, 8]], channel_multiplier=0,
                               allow_small_or_imprecise_dtypes=True), writes=[b_io8])
    iof = c.sb("l1f_iof", [8, 8], F32)
    b_iof = Buf()
    c.op("V", lambda e: e.tensor_copy(iof[:, :], io8[:, :]), reads=[b_io8], writes=[b_iof])
    P8 = c.sb("l1f_P8", [8, 8], BF16)
    b_P8 = Buf()
    c.op("V", lambda e: e.tensor_scalar(P8[:, :], iof[:, :], connf[0:8, 0:1], None, ALU.is_equal),
         reads=[b_iof, b_cf], writes=[b_P8])
    p81 = c.ps("l1f_p81", [8, 8], BF16)
    b_p81 = Buf()
    c.transpose(p81[0:8, 0:1], oh_bf[0:1, 0:8], P.ident_bf[0:1, 0:1], reads=[b_ohbf, P.b_ident],
                writes=[b_p81])
    ohc = c.sb("l1f_ohc", [8, 1], BF16)
    b_ohc = Buf()
    c.op("V", lambda e: e.tensor_copy(ohc[:, :], p81[0:8, 0:1]), reads=[b_p81], writes=[b_ohc])
    p82 = c.ps("l1f_p82", [8, 1])
    b_p82 = Buf()
    c.mm_group([(p82[0:8, 0:1], P8[:, :], ohc[:, :], True, True)], reads=[b_P8, b_ohc], writes=[b_p82])

    oh2f = c.sb("l1f_oh2f", [8, 1], F32)
    b_oh2f = Buf()
    c.op("V", lambda e: e.tensor_copy(oh2f[:, :], p82[0:8, 0:1]), reads=[b_p82], writes=[b_oh2f])
    d8 = c.sb("l1f_d8", [8, 8], BF16)
    b_d8 = Buf()
    c.op("V", lambda e: e.tensor_scalar_mul(d8[:, :], P.ident_bf[0:8, 0:8], oh2f[0:8, 0:1]),
         reads=[P.b_ident, b_oh2f], writes=[b_d8])
    ones8 = c.sb("l1f_o8", [8, 128], BF16)
    b_o8 = Buf()
    c.op("V", lambda e: e.memset(ones8[:, :], 1.0), writes=[b_o8])
    pbc8 = c.ps("l1f_pbc8", [128, 8])
    b_pbc8 = Buf()
    c.mm_group([(pbc8[:, :], ones8[:, :], d8[:, :], True, True)], reads=[b_o8, b_d8], writes=[b_pbc8])
    c.op("V", lambda e: e.tensor_copy(ohbc[:, :], pbc8[:, :]), reads=[b_pbc8], writes=[b_ohbc])
    bdn = c.sb("l1f_bdn", [8, 64], F32)
    b_bdn = Buf()
    bun = c.sb("l1f_bun", [8, 1024], F32)
    b_bun = Buf()
    c.dma("Y", bdn[:, :], P.hbm["r_l1_bd"], reads=[P.hbuf["r_l1_bd"]], writes=[b_bdn])
    c.dma("Y", bun[:, :], P.hbm["r_l1_bu"], reads=[P.hbuf["r_l1_bu"]], writes=[b_bun])
    pb1 = c.ps("l1f_pb1", [128, 2])
    b_pb1 = Buf()
    c.mm_group([(pb1[0:64, 0:1], bdn[:, :], oh2f[:, :], True, True)],
               reads=[b_bdn, b_oh2f], writes=[b_pb1])
    c.op("V", lambda e: e.tensor_copy(bdu[0:64, 0:1], pb1[0:64, 0:1]), reads=[b_pb1], writes=[b_bdu])
    for mc in range(8):
        c.mm_group([(pb1[:, 1:2], bun[:, mc * 128:(mc + 1) * 128], oh2f[:, :], True, True)],
                   reads=[b_bun, b_oh2f], writes=[b_pb1])
        c.op("V", lambda e, m=mc: e.tensor_copy(bdu[:, 1 + m:2 + m], pb1[:, 1:2]),
             reads=[b_pb1], writes=[b_bdu])
    c.pop_scope()
    wd_all = c.sb("l1f_wda", [128, 4096], BF16)
    b_wda = Buf()
    wu_all = c.sb("l1f_wua", [64, 8192], BF16)
    b_wua = Buf()
    c.dma("P", wd_all[:, :], P.hbm["w_l1_dn"][:, :], reads=[P.hbuf["w_l1_dn"]], writes=[b_wda])
    c.dma("P", wu_all[:, :], P.hbm["w_l1_up"][:, :], reads=[P.hbuf["w_l1_up"]], writes=[b_wua])
    for ei in range(8):
        c.op("V", lambda e, E=ei: e.tensor_scalar_mul(
            wd_all[:, E * 512:(E + 1) * 512], wd_all[:, E * 512:(E + 1) * 512], ohbc[:, E:E + 1]),
            reads=[b_ohbc], writes=[b_wda])
        c.op("V", lambda e, E=ei: e.tensor_scalar_mul(
            wu_all[:, E * 1024:(E + 1) * 1024], wu_all[:, E * 1024:(E + 1) * 1024], ohbc[0:64, E:E + 1]),
            reads=[b_ohbc], writes=[b_wua])

    base = c.sb("l1f_base", [128, DC * T], BF16)
    b_base = [Buf() for _ in range(DC)]
    gemm_fm(P, "w_l1_base", 8, range(8), rhs_xn, b_xn,
            lambda mc, tt, ps, bp: c.op("V", lambda e, m=mc, t=tt, p=ps: e.tensor_scalar(
                base[:, m * T + t * TT: m * T + (t + 1) * TT], p[:, :],
                P.t_c_l1_bb[:, m:m + 1], None, ALU.add),
                reads=[bp, P.b_c_l1_bb], writes=[b_base[mc]]), pool=gpool)

    mid = c.sb("l1f_mid", [64, T], BF16)
    b_mid = Buf()
    pm = [c.ps(f"l1f_pm{i}", [128, TT]) for i in range(2)]
    b_pm = [Buf(), Buf()]
    for tt in range(NTT):
        ps, bp = pm[tt % 2], b_pm[tt % 2]
        items = []
        n = 0
        for ei in range(8):
            for kc in range(8):
                items.append((ps[0:64, :], wd_all[:, (ei * 8 + kc) * 64:(ei * 8 + kc + 1) * 64],
                              rhs_xn(kc, tt), n == 0, n == 63))
                n += 1
        c.mm_group(items, reads=[b_wda] + b_xn, writes=[bp])
        c.op("S", lambda e, t=tt, p=ps: e.activation(
            mid[:, t * TT:(t + 1) * TT], p[0:64, :], AF.Relu, bias=bdu[0:64, 0:1]),
            reads=[bp, b_bdu], writes=[b_mid])

    tmp = c.sb("l1f_tmp", [128, TT], F32)
    b_tmp = Buf()
    gt = c.sb("l1f_gt", [128, TT], BF16)
    b_gt = Buf()
    for mc in range(8):
        for tt in range(NTT):
            ps, bp = pm[tt % 2], b_pm[tt % 2]
            items = []
            for ei in range(8):
                items.append((ps[:, :], wu_all[:, (ei * 8 + mc) * 128:(ei * 8 + mc + 1) * 128],
                              mid[:, tt * TT:(tt + 1) * TT], ei == 0, ei == 7))
            c.mm_group(items, reads=[b_wua, b_mid], writes=[bp])
            c.op("V", lambda e, m=mc, t=tt, p=ps: e.scalar_tensor_tensor(
                tmp[:, :], p[:, :], bdu[:, 1 + m:2 + m],
                base[:, m * T + t * TT: m * T + (t + 1) * TT], ALU.add, ALU.add),
                reads=[bp, b_base[mc], b_bdu], writes=[b_tmp])
            c.op("S", lambda e: e.activation(gt[:, :], tmp[:, :], AF.Silu),
                 reads=[b_tmp], writes=[b_gt])
            c.dma("Y", g_hbm[:, mc * T + tt * TT: mc * T + (tt + 1) * TT], gt[:, :],
                  reads=[b_gt], writes=[b_gh[mc]])
    c.pop_scope()
    c.pop_scope()

    emit_combine(P, x_hbm, x_buf, P.hbm["sp_s1"], b_sh, P.hbm["sp_g1"], b_gh,
                 out_hbm, out_buf)



def emit_l2(P, x_hbm, x_buf, out_hbm, out_buf):
    c = P.ctx
    c.push_scope()
    s_hbm = P.dram_scratch("sp_s2", [128, DC * T], F32)
    b_sh = P.hbuf8["sp_s2"]
    g_hbm = P.dram_scratch("sp_g2", [128, DC * T], BF16)
    b_gh = P.hbuf8["sp_g2"]
    xn_hbm = P.dram_scratch("sp_xn2", [128, DC * T], BF16)
    b_xnh = P.hbuf8["sp_xn2"]

    c.push_scope()
    xn2 = c.sb("l2_xn2", [128, DC * T], BF16)
    b_xn2 = [Buf() for _ in range(DC)]
    c.push_scope()
    xn = c.sb("l2_xn", [128, DC * T], BF16)
    b_xn = [Buf() for _ in range(DC)]
    emit_layernorm2(P, x_hbm, x_buf, xn, b_xn)
    for ch in range(DC):
        c.dma("Y", xn_hbm[:, ch * T:(ch + 1) * T], xn[:, ch * T:(ch + 1) * T],
              reads=[b_xn[ch]], writes=[b_xnh[ch]])
    emit_layernorm2(P, None, None, xn2, b_xn2,
                    gamma_col=(P.t_c_l2_scale[:, :], P.b_c_l2_scale),
                    beta_col=(P.t_c_l2_bias[:, :], P.b_c_l2_bias),
                    src_sb=(xn, b_xn))
    c.pop_scope()

    def rhs_xn2(kc, tt):
        return xn2[:, kc * T:][:, tt * TT:(tt + 1) * TT]

    v_tm = c.sb("l2_v", [128, 16 * T], BF16)
    b_v = [Buf() for _ in range(16)]
    bhv = c.sb("l2_bhv", [1, 2048], BF16)
    b_bhv = Buf()
    c.dma("P", bhv[:, :], P.hbm["r_l2_bhv"], reads=[P.hbuf["r_l2_bhv"]], writes=[b_bhv])
    c.push_scope()
    wv = c.sb("l2_wv", [128, 2 * 4096], BF16)
    b_wv = [Buf(), Buf()]
    pv = [c.ps(f"l2_pv{i}", [128, TT]) for i in range(2)]
    b_pv = [Buf(), Buf()]
    for nb in range(4):
        pb = nb % 2
        c.dma("P", wv[:, pb * 4096:(pb + 1) * 4096].rearrange("p (k n) -> p k n", k=8),
              P.hbm["w_l2_hv"].rearrange("p (k n) -> p k n", k=8)[:, :, nb * TT:(nb + 1) * TT],
              reads=[P.hbuf["w_l2_hv"]], writes=[b_wv[pb]])
        for kt in range(16):
            ps, bp = pv[kt % 2], b_pv[kt % 2]
            items = [(ps[:, :], P.ones_row_bf[0:1, 0:128],
                      bhv[0:1, nb * TT:(nb + 1) * TT], True, False)]
            for kc in range(DC):
                items.append((ps[:, :], xn2[:, kc * T + kt * 128: kc * T + (kt + 1) * 128],
                              wv[:, pb * 4096 + kc * TT: pb * 4096 + (kc + 1) * TT],
                              False, kc == DC - 1))
            c.mm_group(items, reads=b_xn2 + [b_wv[pb], b_bhv, P.b_ones_row], writes=[bp])
            c.op("S", lambda e, K=kt, n=nb, p=ps: e.activation(
                v_tm[:, K * T + n * TT: K * T + (n + 1) * TT], p[:, :], AF.Silu),
                reads=[bp], writes=[b_v[kt]])
    c.pop_scope()

    qb_sb = c.sb("l2_q", [128, T], BF16)
    kb_sb = c.sb("l2_k", [128, T], BF16)
    b_qb, b_kb = Buf(), Buf()
    qkf = c.sb("l2_qkf", [128, TT], F32)
    b_qkf = Buf()

    def ev_qk(mc, tt, ps, bp):
        c.op("S", lambda e, p=ps: e.activation(qkf[:, :], p[:, :], AF.Silu,
                                               bias=P.t_c_l2_bqk[:, 0:1]),
             reads=[bp, P.b_c_l2_bqk], writes=[b_qkf])
        c.op("V", lambda e, t=tt: e.tensor_scalar(
            qb_sb[:, t * TT:(t + 1) * TT], qkf[:, :], P.t_c_l2_osw[:, 0:1],
            P.t_c_l2_osb[:, 0:1], ALU.mult, ALU.add),
            reads=[b_qkf, P.b_c_l2_osw, P.b_c_l2_osb], writes=[b_qb])
        c.op("V", lambda e, t=tt: e.tensor_scalar(
            kb_sb[:, t * TT:(t + 1) * TT], qkf[:, :], P.t_c_l2_osw[:, 1:2],
            P.t_c_l2_osb[:, 1:2], ALU.mult, ALU.add),
            reads=[b_qkf, P.b_c_l2_osw, P.b_c_l2_osb], writes=[b_kb])
    c.push_scope()
    gemm_fm(P, "w_l2_qk", 8, range(1), rhs_xn2, b_xn2, ev_qk)
    c.pop_scope()

    relu2 = c.sb("l2_r2", [128, 16 * TT], BF16)
    b_r2 = [Buf() for _ in range(16)]
    og = c.sb("l2_og", [128, 16 * TT], BF16)
    b_og = [Buf() for _ in range(16)]
    tsc = c.sb("l2_tsc", [128, 2 * TT], BF16)
    b_tsc = [Buf(), Buf()]
    gsc = c.sb("l2_gsc", [128, 2 * TT], F32)
    b_gsc = [Buf(), Buf()]
    s_sb = c.sb("l2_s", [128, 2 * TT], F32)
    b_s = [Buf(), Buf()]
    psim = [c.ps(f"l2_psim{i}", [128, TT]) for i in range(2)]
    b_psim = [Buf(), Buf()]
    pav = c.ps("l2_pav", [128, TT])
    b_pav = Buf()
    pg = c.ps("l2_pg", [128, TT])
    b_pg = Buf()
    wgbuf = c.sb("l2_wg", [128, 2 * 1024], BF16)
    b_wg = [Buf(), Buf()]
    wobuf = c.sb("l2_wo", [128, 2 * 2048], BF16)
    b_wo = [Buf(), Buf()]
    pso = [c.ps(f"l2_pso{i}", [128, TT]) for i in range(2)]
    b_pso = [Buf(), Buf()]
    for qb in range(NTT):
        for kb in range(16):
            ps, bp = psim[kb % 2], b_psim[kb % 2]
            w2_ = slice((kb % 2) * TT, (kb % 2 + 1) * TT)
            c.mm_group([(ps[:, :], kb_sb[:, kb * 128:(kb + 1) * 128],
                         qb_sb[:, qb * TT:(qb + 1) * TT], True, True)],
                       reads=[b_kb, b_qb], writes=[bp])
            c.op("S", lambda e, p=ps, W=w2_: e.activation(tsc[:, W], p[:, :], AF.Relu,
                                                          scale=1.0 / T),
                 reads=[bp], writes=[b_tsc[kb % 2]])
            c.op("S", lambda e, K=kb, W=w2_: e.activation(relu2[:, K * TT:(K + 1) * TT],
                                                          tsc[:, W], AF.Square),
                 reads=[b_tsc[kb % 2]], writes=[b_r2[kb]])
        for vd in range(16):
            pb = vd % 2
            c.dma("P", wgbuf[:, pb * 1024:(pb + 1) * 1024],
                  P.hbm["w_l2_hg"][:, vd * 1024:(vd + 1) * 1024],
                  reads=[P.hbuf["w_l2_hg"]], writes=[b_wg[pb]])
            wv2 = slice((vd % 2) * TT, (vd % 2 + 1) * TT)
            items = []
            for kb in range(16):
                items.append((pav[:, :], v_tm[:, kb * T + vd * 128: kb * T + (vd + 1) * 128],
                              relu2[:, kb * TT:(kb + 1) * TT], kb == 0, kb == 15))
            c.mm_group(items, reads=b_v + b_r2, writes=[b_pav])
            itg = []
            for kc in range(DC):
                itg.append((pg[:, :], wgbuf[:, pb * 1024 + kc * 128: pb * 1024 + (kc + 1) * 128],
                            rhs_xn2(kc, qb), kc == 0, kc == DC - 1))
            c.mm_group(itg, reads=b_xn2 + [b_wg[pb]], writes=[b_pg])
            c.op("S", lambda e, m=vd, W=wv2: e.activation(gsc[:, W], pg[:, :], AF.Silu,
                                                          bias=P.t_c_l2_bhg[:, m:m + 1]),
                 reads=[b_pg, P.b_c_l2_bhg], writes=[b_gsc[vd % 2]])
            c.op("V", lambda e, m=vd, W=wv2: e.tensor_mul(og[:, m * TT:(m + 1) * TT],
                                                          pav[:, :], gsc[:, W]),
                 reads=[b_pav, b_gsc[vd % 2]], writes=[b_og[vd]])
        for mc in range(DC):
            pb = mc % 2
            c.dma("P", wobuf[:, pb * 2048:(pb + 1) * 2048],
                  P.hbm["w_l2_out"][:, mc * 2048:(mc + 1) * 2048],
                  reads=[P.hbuf["w_l2_out"]], writes=[b_wo[pb]])
            items = []
            for kc in range(16):
                items.append((pso[mc % 2][:, :], wobuf[:, pb * 2048 + kc * 128: pb * 2048 + (kc + 1) * 128],
                              og[:, kc * TT:(kc + 1) * TT], kc == 0, kc == 15))
            c.mm_group(items, reads=b_og + [b_wo[pb]], writes=[b_pso[mc % 2]])
            ws2 = slice((mc % 2) * TT, (mc % 2 + 1) * TT)
            c.op("V", lambda e, m=mc, p=pso[mc % 2], W=ws2: e.tensor_scalar(
                s_sb[:, W], p[:, :], P.t_c_l2_bout[:, m:m + 1], None, ALU.add),
                reads=[b_pso[mc % 2], P.b_c_l2_bout], writes=[b_s[mc % 2]])
            c.dma("Y", s_hbm[:, mc * T + qb * TT: mc * T + (qb + 1) * TT], s_sb[:, ws2],
                  reads=[b_s[mc % 2]], writes=[b_sh[mc]])
    c.pop_scope()

    c.push_scope()
    xn = c.sb("l2f_xn", [128, DC * T], BF16)
    b_xn = [Buf() for _ in range(DC)]
    for ch in range(DC):
        c.dma("Y", xn[:, ch * T:(ch + 1) * T], xn_hbm[:, ch * T:(ch + 1) * T],
              reads=[b_xnh[ch]], writes=[b_xn[ch]])
    gate_f8 = c.sb("l2f_g8", [8, 16 * 128], BF16)
    b_g8 = Buf()
    gate_hbm = P.dram_scratch("sp_l2gate", [8, T], BF16)
    b_ghb = P.hbuf["sp_l2gate"]
    grow = c.sb("l2f_grow", [1, 2 * TT], BF16)
    b_grow = [Buf(), Buf()]
    c.push_scope()
    wgt = c.sb("l2f_wgt", [128, 64], BF16)
    b_wgt = Buf()
    c.dma("P", wgt[:, :], P.hbm["w_l2_gw"], reads=[P.hbuf["w_l2_gw"]], writes=[b_wgt])
    gbr = c.sb("l2f_gbr", [1, 8], F32)
    b_gbr = Buf()
    c.dma("Y", gbr[:, :], P.hbm["r_l2_gb"], reads=[P.hbuf["r_l2_gb"]], writes=[b_gbr])
    pgl = c.ps("l2f_pgl", [128, 8])
    b_pgl = Buf()
    pbb = c.ps("l2f_pbb", [128, 8])
    b_pbb = Buf()
    biasbc = c.sb("l2f_bb", [128, 8], F32)
    b_bb = Buf()
    c.mm_group([(pbb[:, :], P.ones_row[0:1, :], gbr[0:1, :], True, True)],
               reads=[b_gbr, P.b_ones_row], writes=[b_pbb])
    c.op("V", lambda e: e.tensor_copy(biasbc[:, :], pbb[:, :]), reads=[b_pbb], writes=[b_bb])
    gsm = c.sb("l2f_gsm", [128, 32], F32)
    b_gs = Buf()
    gti = c.sb("l2f_gti", [128, 8], BF16)
    b_gti = Buf()
    ptr = c.ps("l2f_ptr", [8, 128], BF16)
    b_ptr = Buf()
    for tk in range(16):
        items = []
        for kc in range(DC):
            items.append((pgl[:, :], xn[:, kc * T + tk * 128: kc * T + (tk + 1) * 128],
                          wgt[:, kc * 8:(kc + 1) * 8], kc == 0, kc == DC - 1))
        c.mm_group(items, reads=b_xn + [b_wgt], writes=[b_pgl])
        c.op("V", lambda e: e.reduce_max(gsm[:, 16:17], pgl[:, :], axis=AX.X),
             reads=[b_pgl], writes=[b_gs])
        c.op("V", lambda e: e.tensor_scalar_mul(gsm[:, 17:18], gsm[:, 16:17], -1.0),
             reads=[b_gs], writes=[b_gs])
        c.op("S", lambda e: e.activation(gsm[:, 0:8], pgl[:, :], AF.Exp, bias=gsm[:, 17:18]),
             reads=[b_pgl, b_gs], writes=[b_gs])
        c.op("V", lambda e: e.reduce_sum(gsm[:, 18:19], gsm[:, 0:8], axis=AX.X),
             reads=[b_gs], writes=[b_gs])
        c.op("V", lambda e: e.reciprocal(gsm[:, 19:20], gsm[:, 18:19]),
             reads=[b_gs], writes=[b_gs])
        c.op("V", lambda e: e.tensor_scalar_mul(gsm[:, 0:8], gsm[:, 0:8], gsm[:, 19:20]),
             reads=[b_gs], writes=[b_gs])
        c.op("V", lambda e: e.tensor_add(gsm[:, 8:16], gsm[:, 0:8], biasbc[:, :]),
             reads=[b_gs, b_bb], writes=[b_gs])
        c.op("V", lambda e: e.reduce_max(gsm[:, 16:17], gsm[:, 8:16], axis=AX.X),
             reads=[b_gs], writes=[b_gs])
        c.op("V", lambda e: e.tensor_scalar(gsm[:, 20:28], gsm[:, 8:16], gsm[:, 16:17],
                                            None, ALU.is_ge), reads=[b_gs], writes=[b_gs])
        c.op("V", lambda e: e.scalar_tensor_tensor(gsm[:, 20:28], gsm[:, 20:28], -1e30,
                                                   gsm[:, 8:16], ALU.mult, ALU.add),
             reads=[b_gs], writes=[b_gs])
        c.op("V", lambda e: e.reduce_max(gsm[:, 16:17], gsm[:, 20:28], axis=AX.X),
             reads=[b_gs], writes=[b_gs])
        c.op("V", lambda e: e.tensor_scalar(gsm[:, 20:28], gsm[:, 8:16], gsm[:, 16:17],
                                            None, ALU.is_ge), reads=[b_gs], writes=[b_gs])
        c.op("V", lambda e: e.tensor_mul(gti[:, :], gsm[:, 0:8], gsm[:, 20:28]),
             reads=[b_gs], writes=[b_gti])
        c.transpose(ptr[0:8, :], gti[:, :], P.ident_bf[:, :], reads=[b_gti, P.b_ident],
                    writes=[b_ptr])
        c.op("V", lambda e, K=tk: e.tensor_copy(gate_f8[:, K * 128:(K + 1) * 128], ptr[0:8, :]),
             reads=[b_ptr], writes=[b_g8])
    c.pop_scope()
    c.dma("Y", gate_hbm[:, :], gate_f8[:, :], reads=[b_g8], writes=[b_ghb])

    h_bf = c.sb("l2f_h", [128, DC * T], BF16)
    b_h = [Buf() for _ in range(DC)]
    yacc = c.sb("l2f_y", [128, DC * T], F32)
    b_y = [Buf() for _ in range(DC)]
    gbc = c.sb("l2f_gbc", [128, TT], F32)
    b_gbc = Buf()
    tmp1 = c.sb("l2f_t1", [128, 2 * TT], F32)
    b_t1 = [Buf(), Buf()]
    tmp2 = c.sb("l2f_t2", [128, 2 * TT], F32)
    b_t2 = [Buf(), Buf()]
    w13 = c.sb("l2f_w13", [128, 4 * 1024], BF16)
    b_w13 = [Buf(), Buf(), Buf(), Buf()]
    w2b = c.sb("l2f_w2b", [128, 2 * 1024], BF16)
    b_w2b = [Buf(), Buf()]
    pb1 = [c.ps(f"l2f_ph{i}", [128, TT]) for i in range(2)]
    b_pb1 = [Buf(), Buf()]
    pb3 = [c.ps(f"l2f_ph3{i}", [128, TT]) for i in range(2)]
    b_pb3 = [Buf(), Buf()]
    pyy = [c.ps(f"l2f_py{i}", [128, TT]) for i in range(2)]
    b_py = [Buf(), Buf()]
    pgb = c.ps("l2f_pgb", [128, TT])
    b_pgb = Buf()

    def rhs_xnf(kc, tt):
        return xn[:, kc * T:][:, tt * TT:(tt + 1) * TT]

    def rhs_h(kc, tt):
        return h_bf[:, kc * T:][:, tt * TT:(tt + 1) * TT]

    gbc4 = c.sb("l2f_gbc4", [128, NTT * TT], F32)
    b_gbc4 = [Buf() for _ in range(NTT)]
    for eb in range(10):
        shared = eb >= 8
        sh = eb - 8
        if not shared:
            for tt in range(NTT):
                gw = slice((tt % 2) * TT, (tt % 2 + 1) * TT)
                c.dma("Y", grow[0:1, gw], gate_hbm[eb:eb + 1, tt * TT:(tt + 1) * TT],
                      reads=[b_ghb], writes=[b_grow[tt % 2]])
                c.mm_group([(pgb[:, :], P.ones_row_bf[0:1, :], grow[0:1, gw], True, True)],
                           reads=[b_grow[tt % 2], P.b_ones_row], writes=[b_pgb])
                c.op("V", lambda e, t=tt: e.tensor_copy(gbc4[:, t * TT:(t + 1) * TT], pgb[:, :]),
                     reads=[b_pgb], writes=[b_gbc4[tt]])
        for ic in range(DC):
            pb = ic % 2
            if shared:
                c.dma("P", w13[:, pb * 1024:(pb + 1) * 1024],
                      P.hbm["w_l2_sw1"][:, (sh * 8 + ic) * 1024:(sh * 8 + ic + 1) * 1024],
                      reads=[P.hbuf["w_l2_sw1"]], writes=[b_w13[pb]])
                c.dma("P", w13[:, 2048 + pb * 1024: 2048 + (pb + 1) * 1024],
                      P.hbm["w_l2_sw3"][:, (sh * 8 + ic) * 1024:(sh * 8 + ic + 1) * 1024],
                      reads=[P.hbuf["w_l2_sw3"]], writes=[b_w13[2 + pb]])
            else:
                c.dma("P", w13[:, pb * 1024:(pb + 1) * 1024],
                      P.hbm["w_l2_w1"][:, eb * 8192 + ic * 1024: eb * 8192 + (ic + 1) * 1024],
                      reads=[P.hbuf["w_l2_w1"]], writes=[b_w13[pb]])
                c.dma("P", w13[:, 2048 + pb * 1024: 2048 + (pb + 1) * 1024],
                      P.hbm["w_l2_w3"][:, eb * 8192 + ic * 1024: eb * 8192 + (ic + 1) * 1024],
                      reads=[P.hbuf["w_l2_w3"]], writes=[b_w13[2 + pb]])
            for tt in range(NTT):
                p1, bp1s = pb1[tt % 2], b_pb1[tt % 2]
                p3, bp3s = pb3[tt % 2], b_pb3[tt % 2]
                it1, it3 = [], []
                for kc in range(DC):
                    it1.append((p1[:, :], w13[:, pb * 1024 + kc * 128: pb * 1024 + (kc + 1) * 128],
                                rhs_xnf(kc, tt), kc == 0, kc == DC - 1))
                    it3.append((p3[:, :], w13[:, 2048 + pb * 1024 + kc * 128: 2048 + pb * 1024 + (kc + 1) * 128],
                                rhs_xnf(kc, tt), kc == 0, kc == DC - 1))
                c.mm_group(it1, reads=b_xn + [b_w13[pb]], writes=[bp1s])
                c.mm_group(it3, reads=b_xn + [b_w13[2 + pb]], writes=[bp3s])
                wl = slice((tt % 2) * TT, (tt % 2 + 1) * TT)
                c.op("S", lambda e, p=p1, W=wl: e.activation(tmp1[:, W], p[:, :], AF.Silu),
                     reads=[bp1s], writes=[b_t1[tt % 2]])
                if not shared:
                    c.op("P", lambda e, t=tt, W=wl: e.tensor_mul(tmp2[:, W], tmp1[:, W],
                                                                 gbc4[:, t * TT:(t + 1) * TT]),
                         reads=[b_t1[tt % 2], b_gbc4[tt]], writes=[b_t2[tt % 2]])
                    tsrc, btsrc = tmp2, b_t2[tt % 2]
                else:
                    tsrc, btsrc = tmp1, b_t1[tt % 2]
                c.op("V", lambda e, i=ic, t=tt, p=p3, ts=tsrc, W=wl: e.scalar_tensor_tensor(
                    h_bf[:, i * T + t * TT: i * T + (t + 1) * TT], p[:, :], 1.0, ts[:, W],
                    ALU.mult, ALU.mult),
                    reads=[bp3s, btsrc], writes=[b_h[ic]])
        for mc in range(DC):
            pb = mc % 2
            if shared:
                c.dma("P", w2b[:, pb * 1024:(pb + 1) * 1024],
                      P.hbm["w_l2_sw2"][:, mc * 2048 + sh * 1024: mc * 2048 + (sh + 1) * 1024],
                      reads=[P.hbuf["w_l2_sw2"]], writes=[b_w2b[pb]])
            else:
                c.dma("P", w2b[:, pb * 1024:(pb + 1) * 1024],
                      P.hbm["w_l2_w2"][:, eb * 8192 + mc * 1024: eb * 8192 + (mc + 1) * 1024],
                      reads=[P.hbuf["w_l2_w2"]], writes=[b_w2b[pb]])
            for tt in range(NTT):
                py, bpy = pyy[tt % 2], b_py[tt % 2]
                items = []
                for kc in range(DC):
                    items.append((py[:, :], w2b[:, pb * 1024 + kc * 128: pb * 1024 + (kc + 1) * 128],
                                  rhs_h(kc, tt), kc == 0, kc == DC - 1))
                c.mm_group(items, reads=b_h + [b_w2b[pb]], writes=[bpy])
                if eb == 0:
                    c.op("V", lambda e, m=mc, t=tt, p=py: e.tensor_copy(
                        yacc[:, m * T + t * TT: m * T + (t + 1) * TT], p[:, :]),
                        reads=[bpy], writes=[b_y[mc]])
                else:
                    c.op("V", lambda e, m=mc, t=tt, p=py: e.tensor_add(
                        yacc[:, m * T + t * TT: m * T + (t + 1) * TT],
                        yacc[:, m * T + t * TT: m * T + (t + 1) * TT], p[:, :]),
                        reads=[bpy, b_y[mc]], writes=[b_y[mc]])
    gtile = c.sb("l2f_gt", [128, 2 * TT], BF16)
    b_gt2 = [Buf(), Buf()]
    for mc in range(DC):
        for tt in range(NTT):
            wl = slice((tt % 2) * TT, (tt % 2 + 1) * TT)
            c.op("S", lambda e, m=mc, t=tt, W=wl: e.activation(
                gtile[:, W], yacc[:, m * T + t * TT: m * T + (t + 1) * TT], AF.Silu),
                reads=[b_y[mc]], writes=[b_gt2[tt % 2]])
            c.dma("Y", g_hbm[:, mc * T + tt * TT: mc * T + (tt + 1) * TT], gtile[:, wl],
                  reads=[b_gt2[tt % 2]], writes=[b_gh[mc]])
    c.pop_scope()
    c.pop_scope()
    emit_combine(P, x_hbm, x_buf, P.hbm["sp_s2"], b_sh, P.hbm["sp_g2"], b_gh,
                 out_hbm, out_buf)



def emit_l3(P, x_hbm, x_buf, out_hbm, out_buf):
    c = P.ctx
    CH = 64
    NC = T // CH
    c.push_scope()
    xn = c.sb("l3_xn", [128, DC * T], BF16)
    b_xn = [Buf() for _ in range(DC)]
    emit_layernorm2(P, x_hbm, x_buf, xn, b_xn)
    s_hbm = P.dram_scratch("sp_s3", [128, DC * T], F32)
    b_sh = P.hbuf8["sp_s3"]
    g_hbm = P.dram_scratch("sp_g3", [128, DC * T], BF16)
    b_gh = P.hbuf8["sp_g3"]

    def rhs_xn(kc, tt):
        return xn[:, kc * T:][:, tt * TT:(tt + 1) * TT]

    c.push_scope()
    TP = T + 4
    qkv = c.sb("l3_qkv", [128, 12 * TP], BF16)
    b_qkv = [Buf() for _ in range(12)]
    for i in range(12):
        c.op("V", lambda e, I=i: e.memset(qkv[:, I * TP:I * TP + 4], 0.0), writes=[b_qkv[i]])
    gate = c.sb("l3_gate", [128, DC * T], BF16)
    b_gate = [Buf() for _ in range(DC)]
    btm = c.sb("l3_btm", [128, 128], F32)
    b_btm = Buf()
    blk2e = c.sb("l3_blk2e", [128, 2], BF16)
    b_b2e = Buf()
    sel2e = c.sb("l3_sel2e", [2, 128], BF16)
    b_s2e = Buf()
    it_a = c.sb("l3_ita", [128, 128], I32)
    it_b = c.sb("l3_itb", [128, 128], I32)
    b_ita, b_itb = Buf(), Buf()
    c.op("P", lambda e: e.iota(it_a[:, 0:2], [[0, 2]], channel_multiplier=1,
                               allow_small_or_imprecise_dtypes=True), writes=[b_ita])
    c.op("P", lambda e: e.iota(it_b[:, 0:2], [[1, 2]], channel_multiplier=0,
                               allow_small_or_imprecise_dtypes=True), writes=[b_itb])
    c.op("V", lambda e: e.tensor_single_scalar(it_a[:, 0:2], it_a[:, 0:2], 6,
                                               ALU.arith_shift_right),
         reads=[b_ita], writes=[b_ita])
    c.op("V", lambda e: e.tensor_tensor(blk2e[:, :], it_a[:, 0:2], it_b[:, 0:2], ALU.is_equal),
         reads=[b_ita, b_itb], writes=[b_b2e])
    c.op("P", lambda e: e.iota(it_a[0:2, 0:128], [[1, 128]], channel_multiplier=0,
                               allow_small_or_imprecise_dtypes=True), writes=[b_ita])
    c.op("P", lambda e: e.iota(it_b[0:2, 0:128], [[0, 128]], channel_multiplier=1,
                               allow_small_or_imprecise_dtypes=True), writes=[b_itb])
    c.op("V", lambda e: e.tensor_single_scalar(it_a[0:2, 0:128], it_a[0:2, 0:128], 6,
                                               ALU.arith_shift_right),
         reads=[b_ita], writes=[b_ita])
    c.op("V", lambda e: e.tensor_tensor(sel2e[:, :], it_a[0:2, 0:128], it_b[0:2, 0:128],
                                        ALU.is_equal),
         reads=[b_ita, b_itb], writes=[b_s2e])

    c.push_scope()
    prj = c.sb("l3_prj", [128, TP], BF16)
    b_prj = Buf()
    c.op("V", lambda e: e.memset(prj[:, 0:4], 0.0), writes=[b_prj])
    diag = c.sb("l3_diag", [128, 4 * 128], BF16)
    b_diag = Buf()
    ppool = ([c.ps(f"l3_pp{i}", [128, TT]) for i in range(2)], [Buf(), Buf()])
    pcv = c.ps("l3_pcv", [128, TT])
    b_pcv = Buf()
    for name, wkey, nch, cwcol, cwbuf, qoff in [
            ("q", "w_l3_q", 8, "t_c_l3_cwq", "b_c_l3_cwq", 0),
            ("k", "w_l3_k", 2, "t_c_l3_cwk", "b_c_l3_cwk", 8),
            ("v", "w_l3_v", 2, "t_c_l3_cwv", "b_c_l3_cwv", 10)]:
        cw = getattr(P, cwcol)
        bcw = getattr(P, cwbuf)

        def ev_prj(mc, tt, ps, bp, qoff=qoff, cw=cw, bcw=bcw):
            c.op("V", lambda e, t=tt, p=ps: e.tensor_copy(prj[:, 4 + t * TT: 4 + (t + 1) * TT], p[:, :]),
                 reads=[bp], writes=[b_prj])
            if tt == NTT - 1:
                for j in range(4):
                    c.op("V", lambda e, m=mc, jj=j: e.tensor_scalar_mul(
                        diag[:, jj * 128:(jj + 1) * 128], P.ident_bf[:, :], cw[:, m * 4 + jj:m * 4 + jj + 1]),
                        reads=[P.b_ident, bcw], writes=[b_diag])
                for t2 in range(NTT):
                    items = []
                    for j in range(4):
                        sh = slice(4 + t2 * TT - (3 - j), 4 + (t2 + 1) * TT - (3 - j))
                        items.append((pcv[:, :], diag[:, j * 128:(j + 1) * 128], prj[:, sh],
                                      j == 0, j == 3))
                    c.mm_group(items, reads=[b_diag, b_prj], writes=[b_pcv])
                    ix = qoff + mc
                    c.op("V", lambda e, I=ix, t=t2: e.tensor_copy(
                        qkv[:, I * TP + 4 + t * TT: I * TP + 4 + (t + 1) * TT], pcv[:, :]),
                        reads=[b_pcv], writes=[b_qkv[ix]])
        gemm_fm(P, wkey, 8, range(nch), rhs_xn, b_xn, ev_prj, pool=ppool)
    gemm_fm(P, "w_l3_gate", 8, range(8), rhs_xn, b_xn,
            lambda mc, tt, ps, bp: c.op("S", lambda e, m=mc, t=tt, p=ps: e.activation(
                gate[:, m * T + t * TT: m * T + (t + 1) * TT], p[:, :], AF.Silu),
                reads=[bp], writes=[b_gate[mc]]), pool=ppool)

    bg = c.sb("l3_bg", [64, T], F32)
    b_bg = Buf()
    wbg = c.sb("l3_wbg", [128, 512], BF16)
    b_wbg = Buf()
    c.dma("P", wbg[:, :], P.hbm["w_l3_bg"], reads=[P.hbuf["w_l3_bg"]], writes=[b_wbg])
    pbg = c.ps("l3_pbg", [64, TT])
    b_pbg = Buf()
    for tt in range(NTT):
        items = []
        for kc in range(DC):
            items.append((pbg[:, :], wbg[:, kc * 64:(kc + 1) * 64], rhs_xn(kc, tt),
                          kc == 0, kc == DC - 1))
        c.mm_group(items, reads=b_xn + [b_wbg], writes=[b_pbg])
        c.op("V", lambda e, t=tt: e.tensor_copy(bg[:, t * TT:(t + 1) * TT], pbg[:, :]),
             reads=[b_pbg], writes=[b_bg])
    c.op("S", lambda e: e.activation(bg[0:4, :], bg[0:4, :], AF.Sigmoid,
                                     bias=P.t_c_l3_bb64[0:4, 0:1]),
         reads=[b_bg, P.b_c_l3_bb64], writes=[b_bg])
    gsc = c.sb("l3_gsc", [64, 2 * T], F32)
    b_gsc = Buf()
    c.op("S", lambda e: e.activation(gsc[32:64, 0:T], bg[32:64, :], AF.Abs,
                                     bias=P.t_c_l3_dt64[32:64, 0:1]),
         reads=[b_bg, P.b_c_l3_dt64], writes=[b_gsc])
    c.op("S", lambda e: e.activation(gsc[32:64, 0:T], gsc[32:64, 0:T], AF.Exp, scale=-1.0),
         reads=[b_gsc], writes=[b_gsc])
    c.op("S", lambda e: e.activation(gsc[32:64, 0:T], gsc[32:64, 0:T], AF.Ln, bias=1.0),
         reads=[b_gsc], writes=[b_gsc])
    c.op("S", lambda e: e.activation(gsc[32:64, T:2 * T], bg[32:64, :], AF.Relu,
                                     bias=P.t_c_l3_dt64[32:64, 0:1]),
         reads=[b_bg, P.b_c_l3_dt64], writes=[b_gsc])
    c.op("V", lambda e: e.tensor_add(bg[32:64, :], gsc[32:64, 0:T], gsc[32:64, T:2 * T]),
         reads=[b_gsc], writes=[b_bg])
    negA = c.sb("l3_negA", [64, 1], F32)
    b_nA = Buf()
    c.op("S", lambda e: e.activation(negA[32:64, :], P.t_c_l3_al64[32:64, 0:1], AF.Exp),
         reads=[P.b_c_l3_al64], writes=[b_nA])
    c.op("V", lambda e: e.tensor_scalar_mul(negA[32:64, :], negA[32:64, :], -1.0),
         reads=[b_nA], writes=[b_nA])
    c.op("V", lambda e: e.tensor_scalar_mul(bg[32:64, :], bg[32:64, :], negA[32:64, 0:1]),
         reads=[b_bg, b_nA], writes=[b_bg])
    zero4 = c.sb("l3_z4", [64, 1], F32)
    b_z4 = Buf()
    c.op("V", lambda e: e.memset(zero4[:, :], 0.0), writes=[b_z4])
    for ci in range(NC):
        c.op("V", lambda e, i=ci: e.tensor_tensor_scan(
            bg[32:64, i * CH:(i + 1) * CH], bg[32:64, i * CH:(i + 1) * CH],
            zero4[32:64, 0:1].broadcast_to([32, CH]), 0.0, ALU.add, ALU.add),
            reads=[b_bg, b_z4], writes=[b_bg])
    bounce = P.dram_scratch("sp_l3bg", [8, T], F32)
    b_bo = P.hbuf["sp_l3bg"]
    c.dma("Y", P.hbm["sp_l3bg"][0:4, :], bg[0:4, :], reads=[b_bg], writes=[b_bo])
    c.dma("Y", P.hbm["sp_l3bg"][4:8, :], bg[32:36, :], reads=[b_bg], writes=[b_bo])
    if DEBUG_SPILL:
        dbg2 = P.dram_scratch("sp_l3bg2", [64, T], F32)
        c.dma("Y", P.hbm["sp_l3bg2"], bg[:, :], reads=[b_bg], writes=[P.hbuf["sp_l3bg2"]])
    bgb_b = c.sb("l3_bgb_b", [4, T], BF16)
    bgb_g = c.sb("l3_bgb_g", [4, T], BF16)
    b_bgb = Buf()
    c.op("V", lambda e: e.tensor_copy(bgb_b[0:4, :], bg[0:4, :]), reads=[b_bg], writes=[b_bgb])
    c.op("V", lambda e: e.tensor_copy(bgb_g[0:4, :], bg[32:64, :][0:4, :]), reads=[b_bg], writes=[b_bgb])
    ptb = c.ps("l3_ptb", [128, 8], BF16)
    b_ptb = Buf()
    for tk in range(16):
        c.transpose(ptb[:, 0:4], bgb_b[0:4, tk * 128:(tk + 1) * 128], P.ident_bf[0:4, 0:4],
                    reads=[b_bgb, P.b_ident], writes=[b_ptb])
        c.op("V", lambda e, K=tk: e.tensor_copy(btm[:, K * 4:(K + 1) * 4], ptb[:, 0:4]),
             reads=[b_ptb], writes=[b_btm])
        c.transpose(ptb[:, 4:8], bgb_g[0:4, tk * 128:(tk + 1) * 128], P.ident_bf[0:4, 0:4],
                    reads=[b_bgb, P.b_ident], writes=[b_ptb])
        c.op("V", lambda e, K=tk: e.tensor_copy(btm[:, 64 + K * 4:64 + (K + 1) * 4], ptb[:, 4:8]),
             reads=[b_ptb], writes=[b_btm])

    if DEBUG_SPILL:
        dbt = P.dram_scratch("sp_l3btm", [128, 128], F32)
        c.dma("Y", P.hbm["sp_l3btm"], btm[:, :], reads=[b_btm], writes=[P.hbuf["sp_l3btm"]])
    nsq = c.sb("l3_nsq", [128, TT], BF16)
    b_nsq = Buf()
    nrst = c.sb("l3_nrst", [2, TT], F32)
    b_nrst = Buf()
    nrsb = c.sb("l3_nrsb", [2, TT], BF16)
    b_nrsb = Buf()
    pn2 = c.ps("l3_pn2", [2, TT])
    b_pn2 = Buf()
    pnb = c.ps("l3_pnb", [128, TT])
    b_pnb = Buf()
    for ix in range(10):
        for tt in range(NTT):
            sl = slice(ix * TP + 4 + tt * TT, ix * TP + 4 + (tt + 1) * TT)
            c.op("S", lambda e, s=sl: e.activation(nsq[:, :], qkv[:, s], AF.Square),
                 reads=[b_qkv[ix]], writes=[b_nsq])
            c.mm_group([(pn2[0:2, 0:TT], blk2e[:, :], nsq[:, :], True, True)],
                       reads=[b_nsq, b_b2e], writes=[b_pn2])
            c.op("V", lambda e: e.tensor_scalar_add(nrst[:, :], pn2[0:2, 0:TT], 1e-12),
                 reads=[b_pn2], writes=[b_nrst])
            c.op("S", lambda e: e.activation(nrst[:, :], nrst[:, :], AF.Ln),
                 reads=[b_nrst], writes=[b_nrst])
            c.op("S", lambda e: e.activation(nrst[:, :], nrst[:, :], AF.Exp, scale=-0.5),
                 reads=[b_nrst], writes=[b_nrst])
            c.op("V", lambda e: e.tensor_copy(nrsb[:, :], nrst[:, :]), reads=[b_nrst], writes=[b_nrsb])
            c.mm_group([(pnb[:, :], sel2e[:, :], nrsb[0:2, :], True, True)],
                       reads=[b_nrsb, b_s2e], writes=[b_pnb])
            c.op("V", lambda e, s=sl: e.tensor_mul(qkv[:, s], qkv[:, s], pnb[:, :]),
                 reads=[b_qkv[ix], b_pnb], writes=[b_qkv[ix]])
    c.pop_scope()

    mlo = c.sb("l3_mlo", [128, 192], BF16)
    b_ml = Buf()
    io64 = c.sb("l3_io64", [128, 64], I32)
    b_i64 = Buf()
    c.op("P", lambda e: e.iota(io64[:, :], [[1, 64]], channel_multiplier=-1,
                               allow_small_or_imprecise_dtypes=True), writes=[b_i64])
    c.op("V", lambda e: e.tensor_single_scalar(mlo[0:64, 0:64], io64[0:64, :], 0, ALU.is_lt),
         reads=[b_i64], writes=[b_ml])
    c.op("V", lambda e: e.tensor_single_scalar(mlo[0:64, 64:128], io64[0:64, :], 0, ALU.is_gt),
         reads=[b_i64], writes=[b_ml])
    c.op("V", lambda e: e.tensor_single_scalar(mlo[0:64, 128:192], io64[0:64, :], 0, ALU.is_ge),
         reads=[b_i64], writes=[b_ml])
    c.op("V", lambda e: e.tensor_scalar_mul(mlo[0:64, 0:128], mlo[0:64, 0:128], -1.0),
         reads=[b_ml], writes=[b_ml])
    rowbuf2 = [c.sb(f"l3_rowbuf{i}", [1, 768], F32) for i in range(2)]
    b_fl2 = [Buf(), Buf()]

    o_fm = c.sb("l3_o", [128, DC * T], BF16)
    b_o = [Buf() for _ in range(DC)]
    Sf = c.sb("l3_Sf", [64, 4 * 64], F32)
    b_Sf = [Buf() for _ in range(4)]
    Sb = c.sb("l3_Sb", [64, 4 * 64], BF16)
    b_Sb = [Buf() for _ in range(4)]
    for h in range(4):
        c.op("V", lambda e, H=h: e.memset(Sf[:, H * 64:(H + 1) * 64], 0.0), writes=[b_Sf[h]])
        c.op("V", lambda e, H=h: e.memset(Sb[:, H * 64:(H + 1) * 64], 0.0), writes=[b_Sb[h]])

    scr4 = [c.sb(f"l3_scr{k}", [64, 9 * 64], F32) for k in range(4)]
    b_scr4 = [Buf() for _ in range(4)]
    sbb4 = [c.sb(f"l3_sbb{k}", [64, 15 * 64], BF16) for k in range(4)]
    b_sbb4 = [Buf() for _ in range(4)]
    col4 = [c.sb(f"l3_col{k}", [64, 8], F32) for k in range(4)]
    b_col4 = [Buf() for _ in range(4)]
    rb4 = [c.sb(f"l3_rb{k}", [1, 64], F32) for k in range(4)]
    b_rb4 = [Buf() for _ in range(4)]
    pf4 = [[c.ps(f"l3_pf{k}_{i}", [128, TT]) for i in range(2)] for k in range(4)]
    b_pf4 = [[Buf(), Buf()] for k in range(4)]

    def kq_slice(idx, rh, ci):
        return qkv[:, idx * TP + 4:][:, ci * CH:(ci + 1) * CH][64 * rh:64 * rh + 64, :]

    def emit_kv_instance(ci, kv, tk, sub, rowbuf, b_fl):
        kidx, krh = 8 + kv // 2, kv % 2
        scr, b_scrL = scr4[kv], b_scr4[kv]
        sbb, b_sbbL = sbb4[kv], b_sbb4[kv]
        col, b_colL = col4[kv], b_col4[kv]
        rbk, b_rbk = rb4[kv], b_rb4[kv]
        pA, pB = pf4[kv]
        bA, bB = b_pf4[kv][0], b_pf4[kv][1]
        pAb = pA[:, :].bitcast(BF16)
        pBb = pB[:, :].bitcast(BF16)

        def S(slot):
            return sbb[:, slot * 64:(slot + 1) * 64]

        def R(slot):
            return scr[:, slot * 64:(slot + 1) * 64]

        def stg(slot, idx, rh):
            c.op("V", lambda e, s=slot, i=idx, r=rh, CI=ci: e.tensor_copy(
                S(s), kq_slice(i, r, CI)), reads=[b_qkv[idx]], writes=[b_sbbL])
            return S(slot)

        gf = rowbuf[0:1, 256 + kv * CH: 256 + (kv + 1) * CH]
        nb = rowbuf[0:1, kv * CH: (kv + 1) * CH]
        gc1 = rowbuf[0:1, 256 + kv * CH + CH - 1: 256 + (kv + 1) * CH]
        c.op("V", lambda e, Sb_=sub, K=tk, V=kv: e.tensor_copy(col[:, 0:1],
             btm[64 * Sb_:64 * Sb_ + 64, 64 + K * 4 + V:64 + K * 4 + V + 1]),
             reads=[b_btm], writes=[b_colL])
        c.op("V", lambda e, Sb_=sub, K=tk, V=kv: e.tensor_copy(col[:, 2:3],
             btm[64 * Sb_:64 * Sb_ + 64, K * 4 + V:K * 4 + V + 1]),
             reads=[b_btm], writes=[b_colL])
        yield
        c.op("S", lambda e: e.activation(col[:, 1:2], col[:, 0:1], AF.Exp),
             reads=[b_colL], writes=[b_colL])
        c.op("V", lambda e: e.tensor_mul(col[:, 3:4], col[:, 1:2], col[:, 2:3]),
             reads=[b_colL], writes=[b_colL])
        c.op("V", lambda e: e.tensor_scalar_mul(col[:, 3:4], col[:, 3:4], -1.0),
             reads=[b_colL], writes=[b_colL])
        yield
        c.mm_group([(pA[0:64, 0:CH], P.ones_row[0:1, 0:64], gf, True, True)],
                   reads=[b_fl, P.b_ones_row], writes=[bA])
        c.op("V", lambda e: e.tensor_scalar(R(4), pA[0:64, 0:CH], col[:, 0:1], None,
                                            ALU.subtract),
             reads=[bA, b_colL], writes=[b_scrL])
        yield
        c.op("S", lambda e: e.activation(R(0), R(4), AF.Exp), reads=[b_scrL], writes=[b_scrL])
        c.op("S", lambda e: e.activation(R(1), R(4), AF.Exp, scale=-1.0),
             reads=[b_scrL], writes=[b_scrL])
        yield
        c.mm_group([(pA[0:64, 0:CH], P.ones_row[0:1, 0:64], nb, True, True)],
                   reads=[b_fl, P.b_ones_row], writes=[bA])
        c.op("S", lambda e: e.activation(R(3), pA[0:64, 0:CH], AF.Copy), reads=[bA], writes=[b_scrL])
        yield
        c.mm_group([(pA[0:64, 0:1], P.ones_row[0:1, 0:64], gc1, True, True)],
                   reads=[b_fl, P.b_ones_row], writes=[bA])
        c.op("S", lambda e: e.activation(col[:, 5:6], pA[0:64, 0:1], AF.Exp),
             reads=[bA], writes=[b_colL])
        c.op("V", lambda e: e.tensor_sub(col[:, 6:7], pA[0:64, 0:1], col[:, 0:1]),
             reads=[bA, b_colL], writes=[b_colL])
        c.op("S", lambda e: e.activation(col[:, 4:5], col[:, 6:7], AF.Exp),
             reads=[b_colL], writes=[b_colL])
        yield
        ks = stg(12, kidx, krh)
        c.mm_group([(pA[0:64, 0:CH], ks, ks, True, True)], reads=[b_sbbL], writes=[bA])
        yield
        c.op("V", lambda e: e.tensor_mul(R(5), pA[0:64, 0:CH], R(1)),
             reads=[bA, b_scrL], writes=[b_scrL])
        c.op("V", lambda e: e.tensor_scalar_mul(R(5), R(5), col[:, 2:3]),
             reads=[b_scrL, b_colL], writes=[b_scrL])
        c.op("V", lambda e: e.tensor_mul(S(1), R(5), mlo[0:64, 0:64]),
             reads=[b_scrL, b_ml], writes=[b_sbbL])
        yield
        c.op("V", lambda e: e.tensor_mul(R(6), pA[0:64, 0:CH], R(0)),
             reads=[bA, b_scrL], writes=[b_scrL])
        c.op("V", lambda e: e.tensor_mul(R(6), R(6), R(3)),
             reads=[b_scrL], writes=[b_scrL])
        c.op("V", lambda e: e.tensor_mul(S(0), R(6), mlo[0:64, 64:128]),
             reads=[b_scrL, b_ml], writes=[b_sbbL])
        yield
        c.op("V", lambda e: e.tensor_add(S(6), S(0), P.ident_bf[0:64, 0:64]),
             reads=[b_sbbL, P.b_ident], writes=[b_sbbL])
        yield
        np_, tp_ = 1, 0
        for step in range(5):
            dn, dt = (2, 3) if step % 2 == 0 else (4, 5)
            c.mm_group([(pA[0:64, 0:CH], S(tp_), S(np_), True, True)],
                       reads=[b_sbbL], writes=[bA])
            if step < 4:
                c.mm_group([(pB[0:64, 0:CH], S(np_), S(tp_), True, True)],
                           reads=[b_sbbL], writes=[bB])
            yield
            c.op("S", lambda e, d=dn: e.activation(S(d), pA[0:64, 0:CH], AF.Copy),
                 reads=[bA], writes=[b_sbbL])
            if step < 4:
                c.op("S", lambda e, d=dt: e.activation(S(d), pB[0:64, 0:CH], AF.Copy),
                     reads=[bB], writes=[b_sbbL])
            yield
            c.mm_group([(pA[0:64, 0:CH], S(dn), S(6), True, True)],
                       reads=[b_sbbL], writes=[bA])
            yield
            c.op("V", lambda e: e.tensor_add(S(6), S(6), pA[0:64, 0:CH]),
                 reads=[bA, b_sbbL], writes=[b_sbbL])
            np_, tp_ = dn, dt
            yield
        vs = stg(13, kidx + 2, krh)
        c.transpose(pAb[0:64, 0:CH], vs, P.ident_bf[0:64, 0:64],
                    reads=[b_sbbL, P.b_ident], writes=[bA])
        c.op("V", lambda e: e.tensor_scalar_mul(S(7), pAb[0:64, 0:CH], col[:, 2:3]),
             reads=[bA, b_colL], writes=[b_sbbL])
        yield
        c.transpose(pBb[0:64, 0:CH], ks, P.ident_bf[0:64, 0:64],
                    reads=[b_sbbL], writes=[bB])
        c.op("V", lambda e: e.tensor_scalar_mul(S(8), pBb[0:64, 0:CH], col[:, 4:5]),
             reads=[bB, b_colL], writes=[b_sbbL])
        yield
        c.mm_group([(pA[0:64, 0:CH], ks, Sb[:, kv * 64:(kv + 1) * 64], True, True)],
                   reads=[b_sbbL, b_Sb[kv]], writes=[bA])
        c.op("V", lambda e: e.scalar_tensor_tensor(S(10), pA[0:64, 0:CH], col[:, 3:4],
                                                   S(7), ALU.mult, ALU.add),
             reads=[bA, b_colL, b_sbbL], writes=[b_sbbL])
        yield
        c.mm_group([(pA[0:64, 0:CH], S(6), S(10), True, True)],
                   reads=[b_sbbL], writes=[bA])
        c.op("S", lambda e: e.activation(S(9), pA[0:64, 0:CH], AF.Copy), reads=[bA], writes=[b_sbbL])
        yield
        c.op("V", lambda e: e.tensor_mul(R(7), R(0), mlo[0:64, 128:192]),
             reads=[b_scrL, b_ml], writes=[b_scrL])
        c.op("S", lambda e, G=gf: e.activation(rbk[0:1, :], G, AF.Exp),
             reads=[b_fl], writes=[b_rbk])
        c.op("V", lambda e: e.tensor_scalar_mul(rbk[0:1, :], rbk[0:1, :], 0.125),
             reads=[b_rbk], writes=[b_rbk])
        yield
        c.mm_group([(pA[0:64, 0:CH], P.ones_row[0:1, 0:64], rbk[0:1, :], True, True)],
                   reads=[b_rbk, P.b_ones_row], writes=[bA])
        c.op("S", lambda e: e.activation(R(2), pA[0:64, 0:CH], AF.Copy), reads=[bA], writes=[b_scrL])
        yield
        for rr in range(4):
            qh = kv * 4 + rr
            qidx, qrh = qh // 2, qh % 2
            qs = stg(14, qidx, qrh)
            c.mm_group([(pA[0:64, 0:CH], ks, qs, True, True)],
                       reads=[b_sbbL], writes=[bA])
            yield
            c.op("V", lambda e: e.tensor_mul(S(11), pA[0:64, 0:CH], R(7)),
                 reads=[bA, b_scrL], writes=[b_sbbL])
            c.mm_group([(pB[0:64, 0:CH], Sb[:, kv * 64:(kv + 1) * 64], qs, True, True)],
                       reads=[b_Sb[kv], b_sbbL], writes=[bB])
            yield
            c.op("V", lambda e: e.tensor_mul(R(8), pB[0:64, 0:CH], R(2)),
                 reads=[bB, b_scrL], writes=[b_scrL])
            c.mm_group([(pA[0:64, 0:CH], S(9), S(11), True, True)],
                       reads=[b_sbbL], writes=[bA])
            yield
            och, orh = qh // 2, qh % 2
            c.op("V", lambda e, OC=och, OR=orh, CI=ci: e.scalar_tensor_tensor(
                o_fm[64 * OR:64 * OR + 64, OC * T + CI * CH: OC * T + (CI + 1) * CH],
                pA[0:64, 0:CH], 0.125, R(8), ALU.mult, ALU.add),
                reads=[bA, b_scrL], writes=[b_o[och]])
            yield
        c.mm_group([(pA[0:64, 0:CH], S(8), S(9), True, True)],
                   reads=[b_sbbL], writes=[bA])
        yield
        c.op("V", lambda e, V=kv: e.scalar_tensor_tensor(Sf[:, V * 64:(V + 1) * 64],
                                                         Sf[:, V * 64:(V + 1) * 64],
                                                         col[:, 5:6], pA[0:64, 0:CH],
                                                         ALU.mult, ALU.add),
             reads=[bA, b_colL, b_Sf[kv]], writes=[b_Sf[kv]])
        c.op("V", lambda e, V=kv: e.tensor_copy(Sb[:, V * 64:(V + 1) * 64],
                                                Sf[:, V * 64:(V + 1) * 64]),
             reads=[b_Sf[kv]], writes=[b_Sb[kv]])
        if DEBUG_SPILL and ci == 1 and kv == 0:
            dsb = P.dram_scratch("sp_l3sbb", [64, 15 * 64], BF16)
            c.dma("Y", P.hbm["sp_l3sbb"], sbb[:, 0:15 * 64], reads=[b_sbbL], writes=[P.hbuf["sp_l3sbb"]])

    for ci in range(NC):
        tk, sub = divmod(ci, 2)
        rowbuf, b_fl = rowbuf2[ci % 2], b_fl2[ci % 2]
        c.dma("Y", rowbuf[0:1, 0:512], P.hbm["sp_l3bg"][:, ci * CH:(ci + 1) * CH],
              reads=[b_bo], writes=[b_fl])
        alive = [emit_kv_instance(ci, kv, tk, sub, rowbuf, b_fl) for kv in range(4)]
        while alive:
            nxt = []
            for gen in alive:
                try:
                    next(gen)
                    nxt.append(gen)
                except StopIteration:
                    pass
            alive = nxt

    blk2, b_b2 = blk2e, b_b2e
    sel2, b_s2 = sel2e, b_s2e
    osq = c.sb("l3_osq", [128, TT], BF16)
    b_osq = Buf()
    rst = c.sb("l3_rst", [2, TT], F32)
    b_rst = Buf()
    rstb = c.sb("l3_rstb", [2, TT], BF16)
    b_rstb = Buf()
    og2, b_og2 = o_fm, b_o
    p2s, b_p2s = pf4[0][0], b_pf4[0][0]
    pbc2, b_pbc2 = pf4[0][1], b_pf4[0][1]
    for chn in range(DC):
        for tt in range(NTT):
            sl = slice(chn * T + tt * TT, chn * T + (tt + 1) * TT)
            c.op("S", lambda e, s=sl: e.activation(osq[:, :], o_fm[:, s], AF.Square),
                 reads=[b_o[chn]], writes=[b_osq])
            c.mm_group([(p2s[0:2, 0:TT], blk2[:, :], osq[:, :], True, True)],
                       reads=[b_osq, b_b2], writes=[b_p2s])
            c.op("V", lambda e: e.tensor_scalar(rst[:, :], p2s[0:2, 0:TT], 1.0 / 64, None, ALU.mult),
                 reads=[b_p2s], writes=[b_rst])
            c.op("V", lambda e: e.tensor_scalar_add(rst[:, :], rst[:, :], 1e-5),
                 reads=[b_rst], writes=[b_rst])
            c.op("S", lambda e: e.activation(rst[:, :], rst[:, :], AF.Ln),
                 reads=[b_rst], writes=[b_rst])
            c.op("S", lambda e: e.activation(rst[:, :], rst[:, :], AF.Exp, scale=-0.5),
                 reads=[b_rst], writes=[b_rst])
            c.op("V", lambda e: e.tensor_copy(rstb[:, :], rst[:, :]), reads=[b_rst], writes=[b_rstb])
            c.mm_group([(pbc2[:, :], sel2[:, :], rstb[0:2, :], True, True)],
                       reads=[b_rstb, b_s2], writes=[b_pbc2])
            c.op("V", lambda e, s=sl: e.scalar_tensor_tensor(
                osq[:, :], o_fm[:, s], P.t_r_l3_rmsw[:, 0:1], pbc2[:, :], ALU.mult, ALU.mult),
                reads=[b_o[chn], b_pbc2, P.b_r_l3_rmsw], writes=[b_osq])
            c.op("V", lambda e, s=sl: e.tensor_mul(og2[:, s], osq[:, :], gate[:, s]),
                 reads=[b_osq, b_gate[chn]], writes=[b_og2[chn]])

    s_sb = c.sb("l3_s", [128, TT], F32)
    b_s = Buf()

    def rhs_og(kc, tt):
        return og2[:, kc * T:][:, tt * TT:(tt + 1) * TT]

    def ev_s(mc, tt, ps, bp):
        c.op("V", lambda e, p=ps: e.tensor_copy(s_sb[:, :], p[:, :]), reads=[bp], writes=[b_s])
        c.dma("Y", s_hbm[:, mc * T + tt * TT: mc * T + (tt + 1) * TT], s_sb[:, :],
              reads=[b_s], writes=[b_sh[mc]])
    gemm_fm(P, "w_l3_proj", 8, range(8), rhs_og, b_og2, ev_s, pool=(pf4[1], b_pf4[1]))
    c.pop_scope()

    c.push_scope()
    y1t = c.sb("l3f_y1t", [128, 8 * 1024], BF16)
    b_y1 = [Buf() for _ in range(8)]
    wa2 = c.sb("l3f_wa2", [128, 8 * 1024], BF16)
    b_wa2 = Buf()
    wc = c.sb("l3f_wc", [128, 8 * 1024], BF16)
    b_wc = [Buf() for _ in range(8)]
    wtb = c.sb("l3f_wtb", [128, 8 * 1024], BF16)
    b_wtb = Buf()
    c.dma("P", wtb[:, :], P.hbm["w_l3f_wT"], reads=[P.hbuf["w_l3f_wT"]], writes=[b_wtb])
    c.dma("P", wa2[:, :], P.hbm["w_l3f_a2"], reads=[P.hbuf["w_l3f_a2"]], writes=[b_wa2])
    gpool = ([c.ps(f"l3f_gp{i}", [128, TT]) for i in range(2)], [Buf(), Buf()])

    def rhs_wt(kc, tt):
        return wtb[:, kc * 1024 + tt * TT: kc * 1024 + (tt + 1) * TT]

    def ev_y1(mc, tt, ps, bp):
        c.op("V", lambda e, m=mc, t=tt, p=ps: e.tensor_copy(
            y1t[:, m * 1024 + t * TT: m * 1024 + (t + 1) * TT], p[:, :]),
            reads=[bp], writes=[b_y1[mc]])
    gemm_fm(P, "w_l3f_a1", 8, range(8), rhs_wt, [b_wtb], ev_y1, ntt=2, pool=gpool)
    pwc = gpool[0]
    for mc in range(8):
        for tt in range(2):
            ps, bp = gpool[0][tt % 2], gpool[1][tt % 2]
            items = []
            for kc in range(8):
                items.append((ps[:, :], y1t[:, kc * 1024 + mc * 128: kc * 1024 + (mc + 1) * 128],
                              wa2[:, kc * 1024 + tt * TT: kc * 1024 + (tt + 1) * TT],
                              kc == 0, kc == 7))
            c.mm_group(items, reads=b_y1 + [b_wa2], writes=[bp])
            c.op("V", lambda e, m=mc, t=tt, p=ps: e.tensor_copy(
                wc[:, m * 1024 + t * TT: m * 1024 + (t + 1) * TT], p[:, :]),
                reads=[bp], writes=[b_wc[mc]])
    tnh = c.sb("l3f_tnh", [128, TT], F32)
    b_tnh = Buf()
    gt3 = c.sb("l3f_gt", [128, TT], BF16)
    b_gt3 = Buf()
    for mc in range(8):
        for tt in range(NTT):
            ps, bp = gpool[0][tt % 2], gpool[1][tt % 2]
            items = []
            for kc in range(8):
                items.append((ps[:, :], wc[:, kc * 1024 + mc * 128: kc * 1024 + (mc + 1) * 128],
                              rhs_xn(kc, tt), kc == 0, kc == 7))
            c.mm_group(items, reads=b_wc + b_xn, writes=[bp])
            c.op("S", lambda e, m=mc, p=ps: e.activation(
                tnh[:, :], p[:, :], AF.Tanh, bias=P.t_c_l3f_b[:, m:m + 1]),
                reads=[bp, P.b_c_l3f_b], writes=[b_tnh])
            c.op("S", lambda e: e.activation(gt3[:, :], tnh[:, :], AF.Silu),
                 reads=[b_tnh], writes=[b_gt3])
            c.dma("Y", g_hbm[:, mc * T + tt * TT: mc * T + (tt + 1) * TT], gt3[:, :],
                  reads=[b_gt3], writes=[b_gh[mc]])
    c.pop_scope()
    c.pop_scope()
    emit_combine(P, x_hbm, x_buf, P.hbm["sp_s3"], b_sh, P.hbm["sp_g3"], b_gh,
                 out_hbm, out_buf)



_CACHE = {}


def build_program(layers):
    P = Prog()
    nc = P.nc
    P.dram_in("x_in", [D, T])
    weight_names = [
        ("w_l0_in", [128, 4096 * 8]), ("w_l0_mix", [128, 2048 * 16]),
        ("w_l0_out", [128, 1024 * 16]),
        ("w_l0f_w1", [128, 4096 * 8]), ("w_l0f_w2", [128, 4096 * 8]),
        ("w_l0f_h1", [128, 4096 * 32]), ("w_l0f_h2", [128, 4096 * 32]),
        ("w_l0f_w3", [128, 1024 * 32]),
        ("w_l1_q", [128, 1024 * 8]), ("w_l1_kv", [128, 256 * 8]),
        ("w_l1_kup", [128, 1024 * 2]), ("w_l1_vup", [128, 2 * 1024]),
        ("w_l1_out", [128, 1024 * 8]), ("w_l1_base", [128, 1024 * 8]),
        ("w_l1_enc", [128, 16384]), ("w_l1_dec", [128, 16384]),
        ("w_l1_dn", [128, 4096]),
        ("r_l1_bd", [8, 64]), ("r_l1_bu", [8, 1024]),
        ("w_l2_hv", [128, 8 * 2048]), ("w_l2_hg", [128, 16 * 1024]),
        ("w_l2_qk", [128, 1024]), ("w_l2_out", [128, 8 * 2048]),
        ("r_l2_bhv", [1, 2048]),
        ("w_l2_gw", [128, 64]), ("r_l2_gb", [1, 8]),
        ("w_l2_w1", [128, 65536]), ("w_l2_w3", [128, 65536]), ("w_l2_w2", [128, 65536]),
        ("w_l2_sw1", [128, 16384]), ("w_l2_sw3", [128, 16384]), ("w_l2_sw2", [128, 16384]),
        ("w_l3_q", [128, 8192]), ("w_l3_k", [128, 2048]), ("w_l3_v", [128, 2048]),
        ("w_l3_gate", [128, 8192]), ("w_l3_proj", [128, 8192]), ("w_l3_bg", [128, 512]),
        ("w_l3f_a1", [128, 8192]), ("w_l3f_wT", [128, 8192]), ("w_l3f_a2", [128, 8192]),
    ]
    col_names = [
        ("c_l0_gamma", [128, 8]), ("c_l0_beta", [128, 8]), ("c_l0_inb", [128, 32]),
        ("c_l0_dwb", [128, 16]), ("c_l0_mixb", [128, 16]), ("c_l0_outb", [128, 8]),
        ("c_l0_dec", [128, 16]), ("c_l0_dww", [128, 112]),
        ("c_l0f_s1", [128, 32]), ("c_l0f_s2", [128, 32]), ("c_l0f_s3", [128, 32]),
        ("c_l0f_s4", [128, 32]), ("c_l0f_s5", [128, 8]),
        ("c_l1_bb", [128, 8]), ("c_l1_benc", [128, 16]), ("c_l1_bdec", [128, 64]),
        ("c_l2_scale", [128, 8]), ("c_l2_bias", [128, 8]), ("c_l2_bhg", [128, 16]),
        ("c_l2_bqk", [128, 1]), ("c_l2_osw", [128, 2]), ("c_l2_osb", [128, 2]),
        ("c_l2_bout", [128, 8]),
        ("c_l3_cwq", [128, 32]), ("c_l3_cwk", [128, 8]), ("c_l3_cwv", [128, 8]),
        ("c_l3_bb64", [64, 1]), ("c_l3_dt64", [64, 1]), ("c_l3_al64", [64, 1]),
        ("c_l3f_b", [128, 8]), ("r_l3_rmsw", [128, 1]),
    ]
    for n, s in weight_names + col_names:
        P.dram_in(n, s)
    P.dram_in("w_l1_up", [64, 8192])
    ap = nc.dram_tensor("i_l1_conn", [8, 1], I32, kind="ExternalInput").ap()
    P.hbm["i_l1_conn"] = ap
    P.hbuf["i_l1_conn"] = Buf("i_l1_conn")
    y_out = P.dram_out("y_out", [D, T])

    with P.stack:
        c = P.ctx
        emit_constants(P)
        load_const_cols(P, col_names)
        x_cur = P.hbm["x_in"]
        b_xcur = P.hbuf8["x_in"]
        for li in layers:
            is_last = li == layers[-1]
            out_hbm = y_out if is_last else P.dram_scratch(f"x_l{li + 1}", [D, T])
            out_buf = P.hbuf8["y_out"] if is_last else P.hbuf8[f"x_l{li + 1}"]
            if li == 0:
                emit_l0(P, x_cur, b_xcur, out_hbm, out_buf)
            elif li == 1:
                emit_l1(P, x_cur, b_xcur, out_hbm, out_buf)
            elif li == 2:
                emit_l2(P, x_cur, b_xcur, out_hbm, out_buf)
            elif li == 3:
                emit_l3(P, x_cur, b_xcur, out_hbm, out_buf)
            else:
                raise NotImplementedError(f"layer {li}")
            x_cur, b_xcur = out_hbm, out_buf
        c.barrier()
        with nc.Block() as block:
            block.tensor(lambda e: c.replay("T", e))
            block.vector(lambda e: c.replay("V", e))
            block.scalar(lambda e: c.replay("S", e))
            block.gpsimd(lambda e: c.replay("P", e))
            block.sync(lambda e: c.replay("Y", e))
    return P


def kernel(**inputs):
    layers = tuple(DEBUG_LAYERS)
    g, per_core = prep_host(inputs)
    P = build_program(list(layers))
    in_maps = []
    for cid in range(8):
        m = dict(g)
        m.update(per_core[cid])
        in_maps.append(m)
    res = run_bass_kernel_spmd(P.nc, in_maps, core_ids=list(range(8)))
    _LAST_RES[0] = res
    outs = []
    for cid in range(8):
        y = res.results[cid]["y_out"]
        outs.append(np.ascontiguousarray(y.T))
    return np.stack(outs).astype(np.float32)


# version 06
# speedup vs baseline: 1.1855x; 1.1855x over previous
import numpy as np
from contextlib import ExitStack

import concourse.bass as bass
import concourse.mybir as mybir
from concourse.bass_utils import run_bass_kernel_spmd

F32 = mybir.dt.float32
BF16 = mybir.dt.bfloat16
I32 = mybir.dt.int32
AF = mybir.ActivationFunctionType
ALU = mybir.AluOpType
AX = mybir.AxisListType

T = 2048
D = 1024
NTT = 4
TT = 512
DC = 8

DEBUG_LAYERS = [0, 1, 2, 3]
DEBUG_SPILL = False
_LAST_RES = [None]


COMPUTE = ("T", "V", "S", "P", "Y")
DMA_LANES = {"Y": 10, "P": 10, "S": 4}
SAME_ENGINE_SYNC = True


class Buf:
    __slots__ = ("writer", "readers", "name")

    def __init__(self, name=""):
        self.writer = None
        self.readers = []
        self.name = name


class Ctx:
    def __init__(self, nc, stack):
        self.nc = nc
        self.stack = stack
        self.streams = {e: [] for e in COMPUTE}
        self.sems = {}
        self.cnt = {}
        self.lane_rr = {}
        keys = ["T", "V", "S", "P"]
        for q, n in DMA_LANES.items():
            keys += [f"d{q}{i}" for i in range(n)]
            self.lane_rr[q] = 0
        for key in keys:
            self.sems[key] = stack.enter_context(nc.semaphore("sem_" + key))
            self.cnt[key] = 0
        self.waited = {e: {} for e in COMPUTE}
        self.scopes = []
        self._names = {}

    def _uname(self, base):
        n = self._names.get(base, 0)
        self._names[base] = n + 1
        return f"{base}{n}" if n else base

    def sb(self, name, shape, dtype, scope=None):
        if scope is False:
            st = self.stack
        else:
            st = self.scopes[-1] if self.scopes and scope is None else self.stack
        t = st.enter_context(self.nc.sbuf_tensor(self._uname(name), shape, dtype))
        return t

    def ps(self, name, shape, dtype=F32, scope=None):
        st = self.scopes[-1] if self.scopes and scope is None else self.stack
        t = st.enter_context(self.nc.psum_tensor(self._uname(name), shape, dtype))
        return t

    def push_scope(self):
        self.scopes.append(ExitStack())

    def pop_scope(self):
        self.barrier()
        self.scopes.pop().close()

    def barrier(self):
        snap = dict(self.cnt)
        for e in COMPUTE:
            for key, v in snap.items():
                if v and self.waited[e].get(key, 0) < v:
                    tgt = v * 16 if key.startswith("d") else v
                    self.streams[e].append(("wait", key, tgt))
                    self.waited[e][key] = v

    def _collect_waits(self, eng, reads, writes):
        need = {}
        for b in reads:
            if b is None:
                continue
            t = b.writer
            if t:
                k, v = t
                need[k] = max(need.get(k, 0), v)
        for b in writes:
            if b is None:
                continue
            toks = list(b.readers)
            if b.writer:
                toks.append(b.writer)
            for k, v in toks:
                need[k] = max(need.get(k, 0), v)
        out = []
        for k, v in need.items():
            if not SAME_ENGINE_SYNC and k == eng:
                continue
            if k == eng and v >= self.cnt[k]:
                assert v <= self.cnt[k], f"bad same-engine wait {k} {v} > {self.cnt[k]}"
            if self.waited[eng].get(k, 0) >= v:
                continue
            out.append((k, v * 16 if k.startswith("d") else v))
            self.waited[eng][k] = v
        return out

    def _finish(self, eng, semkey, reads, writes):
        self.cnt[semkey] += 1
        tok = (semkey, self.cnt[semkey])
        for b in reads:
            if b is not None:
                b.readers.append(tok)
        for b in writes:
            if b is not None:
                b.writer = tok
                b.readers = []
        return tok

    def op(self, eng, build, reads=(), writes=()):
        reads = list(reads)
        writes = list(writes)
        waits = self._collect_waits(eng, reads, writes)
        for k, v in waits:
            self.streams[eng].append(("wait", k, v))
        tok_val = self.cnt[eng] + 1
        self.streams[eng].append(("op", build, eng, tok_val))
        return self._finish(eng, eng, reads, writes)

    def dma(self, q, dst_ap, src_ap, reads=(), writes=()):
        lane = self.lane_rr[q]
        self.lane_rr[q] = (lane + 1) % DMA_LANES[q]
        semkey = f"d{q}{lane}"
        reads = list(reads)
        writes = list(writes)
        waits = self._collect_waits(q, reads, writes)
        prev = self.cnt[semkey]
        if prev and self.waited[q].get(semkey, 0) < prev:
            waits.append((semkey, prev * 16))
            self.waited[q][semkey] = prev
        for k, v in waits:
            self.streams[q].append(("wait", k, v))
        self.cnt[semkey] += 1
        self.streams[q].append(("dma", dst_ap, src_ap, semkey, self.cnt[semkey] * 16))
        tok = (semkey, self.cnt[semkey])
        for b in reads:
            if b is not None:
                b.readers.append(tok)
        for b in writes:
            if b is not None:
                b.writer = tok
                b.readers = []
        return tok

    def mm_group(self, items, reads=(), writes=()):
        reads = list(reads)
        writes = list(writes)
        waits = self._collect_waits("T", reads, writes)
        for k, v in waits:
            self.streams["T"].append(("wait", k, v))
        n = len(items)
        for i, it in enumerate(items):
            last = i == n - 1
            self.streams["T"].append(("mm", it, "T", self.cnt["T"] + 1 if last else None))
        return self._finish("T", "T", reads, writes)

    def replay(self, eng_key, eng):
        sems = self.sems
        for entry in self.streams[eng_key]:
            kind = entry[0]
            if kind == "wait":
                eng.wait_ge(sems[entry[1]], entry[2])
            elif kind == "op":
                _, build, semkey, val = entry
                ins = build(eng)
                ins.then_inc(sems[semkey], 1)
            elif kind == "dma":
                _, dst, src, semkey, tgt = entry
                eng.dma_start(dst, src).then_inc(sems[semkey], 16)
            elif kind == "mm":
                _, it, semkey, val = entry
                tp = it[5] if len(it) > 5 else None
                ins = eng.matmul(it[0], it[1], it[2], start=it[3], stop=it[4],
                                 tile_position=tp)
                if val is not None:
                    ins.then_inc(sems[semkey], 1)
            elif kind == "tr":
                _, out_ap, in_ap, ident_ap, semkey, val = entry
                ins = eng.transpose(out_ap, in_ap, ident_ap)
                if val is not None:
                    ins.then_inc(sems[semkey], 1)

    def transpose(self, out_ap, in_ap, ident_ap, reads=(), writes=()):
        reads = list(reads)
        writes = list(writes)
        waits = self._collect_waits("T", reads, writes)
        for k, v in waits:
            self.streams["T"].append(("wait", k, v))
        self.streams["T"].append(("tr", out_ap, in_ap, ident_ap, "T", self.cnt["T"] + 1))
        return self._finish("T", "T", reads, writes)



def img_lhsT(w):
    K, M = w.shape
    assert K % 128 == 0 and M % 128 == 0, (K, M)
    Kc, Mc = K // 128, M // 128
    return np.ascontiguousarray(
        w.reshape(Kc, 128, Mc, 128).transpose(1, 2, 0, 3).reshape(128, Mc * Kc * 128)
    )


def img_rhs(w):
    K, N = w.shape
    assert K % 128 == 0
    Kc = K // 128
    return np.ascontiguousarray(w.reshape(Kc, 128, N).transpose(1, 0, 2).reshape(128, Kc * N))


def cols(v, parts=128):
    v = np.asarray(v)
    n = v.size // parts
    return np.ascontiguousarray(v.reshape(n, parts).T)


def prep_host(inputs):
    g = {}
    f = lambda k: np.asarray(inputs[k], np.float32)

    g["w_l0_in"] = img_lhsT(f("L0_attn_in_w"))
    g["w_l0_mix"] = img_lhsT(f("L0_attn_mix_w"))
    g["w_l0_out"] = img_lhsT(f("L0_attn_out_w"))
    g["c_l0_gamma"] = cols(f("L0_attn_norm_gamma"))
    g["c_l0_beta"] = cols(f("L0_attn_norm_beta"))
    g["c_l0_inb"] = cols(f("L0_attn_in_b"))
    g["c_l0_dwb"] = cols(f("L0_attn_dw_b"))
    g["c_l0_mixb"] = cols(f("L0_attn_mix_b"))
    g["c_l0_outb"] = cols(f("L0_attn_out_b"))
    g["c_l0_dec"] = cols(f("L0_attn_log_decay"))
    dw = f("L0_attn_dw_w")
    g["c_l0_dww"] = np.ascontiguousarray(
        dw.reshape(16, 128, 7).transpose(1, 0, 2).reshape(128, 112))

    g["w_l0f_w1"] = img_lhsT(np.ascontiguousarray(f("L0_ffn_w1_w").T))
    g["w_l0f_w2"] = img_lhsT(np.ascontiguousarray(f("L0_ffn_w2_w").T))
    g["w_l0f_h1"] = img_lhsT(np.ascontiguousarray(f("L0_ffn_h1_w").T))
    g["w_l0f_h2"] = img_lhsT(np.ascontiguousarray(f("L0_ffn_h2_w").T))
    g["w_l0f_w3"] = img_lhsT(np.ascontiguousarray(f("L0_ffn_w3_w").T))
    g["c_l0f_s1"] = cols(f("L0_ffn_w1_s"))
    g["c_l0f_s2"] = cols(f("L0_ffn_w2_s"))
    g["c_l0f_s3"] = cols(f("L0_ffn_h1_s"))
    g["c_l0f_s4"] = cols(f("L0_ffn_h2_s"))
    g["c_l0f_s5"] = cols(f("L0_ffn_w3_s"))

    g["w_l1_q"] = img_lhsT(f("L1_attn_Wq").reshape(1024, 1024))
    g["w_l1_kv"] = img_lhsT(f("L1_attn_Wkv_down"))
    g["w_l1_kup"] = img_lhsT(f("L1_attn_Wk_up").reshape(256, 1024))
    g["w_l1_vup"] = img_rhs(f("L1_attn_Wv_up").reshape(256, 1024))
    g["w_l1_out"] = img_lhsT(f("L1_attn_Wout").reshape(1024, 1024))
    g["w_l1_base"] = img_lhsT(f("L1_ffn_W_base"))
    g["w_l1_enc"] = np.concatenate([img_lhsT(f("L1_ffn_W_enc")[e]) for e in range(8)], axis=1)
    g["w_l1_dec"] = np.concatenate([img_lhsT(f("L1_ffn_W_dec")[e]) for e in range(8)], axis=1)
    wd = f("L1_ffn_W_down")
    g["w_l1_dn"] = np.ascontiguousarray(
        wd.reshape(8, 8, 128, 64).transpose(2, 0, 1, 3).reshape(128, 8 * 8 * 64))
    wu = f("L1_ffn_W_up")
    g["w_l1_up"] = np.ascontiguousarray(
        wu.reshape(8, 64, 8, 128).transpose(1, 0, 2, 3).reshape(64, 8 * 8 * 128))
    g["c_l1_bb"] = cols(f("L1_ffn_b_base"))
    g["c_l1_benc"] = np.ascontiguousarray(f("L1_ffn_b_enc").reshape(8, 2, 128).transpose(2, 0, 1).reshape(128, 16))
    g["c_l1_bdec"] = np.ascontiguousarray(f("L1_ffn_b_dec").reshape(8, 8, 128).transpose(2, 0, 1).reshape(128, 64))
    g["r_l1_bd"] = np.ascontiguousarray(f("L1_ffn_b_down"))
    g["r_l1_bu"] = np.ascontiguousarray(f("L1_ffn_b_up"))
    g["i_l1_conn"] = np.ascontiguousarray(np.asarray(inputs["L1_ffn_connected_idx"], np.int32).reshape(8, 1))

    wh = f("L2_attn_W_hidden")
    g["w_l2_hv"] = img_rhs(np.ascontiguousarray(wh[:, :2048]))
    g["w_l2_hg"] = img_lhsT(np.ascontiguousarray(wh[:, 2048:]))
    g["w_l2_qk"] = img_lhsT(f("L2_attn_W_qk"))
    g["w_l2_out"] = img_lhsT(f("L2_attn_W_out"))
    g["c_l2_scale"] = cols(f("L2_attn_ln_scale"))
    g["c_l2_bias"] = cols(f("L2_attn_ln_bias"))
    g["r_l2_bhv"] = np.ascontiguousarray(f("L2_attn_b_hidden")[:2048].reshape(1, 2048))
    g["c_l2_bhg"] = cols(f("L2_attn_b_hidden")[2048:])
    g["c_l2_bqk"] = cols(f("L2_attn_b_qk"))
    g["c_l2_osw"] = np.ascontiguousarray(f("L2_attn_os_weight").T)
    g["c_l2_osb"] = np.ascontiguousarray(f("L2_attn_os_bias").T)
    g["c_l2_bout"] = cols(f("L2_attn_b_out"))
    g["w_l2_gw"] = img_rhs(np.ascontiguousarray(f("L2_ffn_gate_w").T))
    g["r_l2_gb"] = np.ascontiguousarray(f("L2_ffn_gate_bias").reshape(1, 8))
    g["w_l2_w1"] = np.concatenate([img_lhsT(np.ascontiguousarray(f("L2_ffn_w1")[e].T)) for e in range(8)], axis=1)
    g["w_l2_w3"] = np.concatenate([img_lhsT(np.ascontiguousarray(f("L2_ffn_w3")[e].T)) for e in range(8)], axis=1)
    g["w_l2_w2"] = np.concatenate([img_lhsT(np.ascontiguousarray(f("L2_ffn_w2")[e].T)) for e in range(8)], axis=1)
    g["w_l2_sw1"] = img_lhsT(np.ascontiguousarray(f("L2_ffn_sw1").T))
    g["w_l2_sw3"] = img_lhsT(np.ascontiguousarray(f("L2_ffn_sw3").T))
    g["w_l2_sw2"] = img_lhsT(np.ascontiguousarray(f("L2_ffn_sw2").T))

    g["w_l3_q"] = img_lhsT(np.ascontiguousarray(f("L3_attn_Wq").T))
    g["w_l3_k"] = img_lhsT(np.ascontiguousarray(f("L3_attn_Wk").T))
    g["w_l3_v"] = img_lhsT(np.ascontiguousarray(f("L3_attn_Wv").T))
    g["w_l3_gate"] = img_lhsT(np.ascontiguousarray(f("L3_attn_Wgate").T))
    g["w_l3_proj"] = img_lhsT(np.ascontiguousarray(f("L3_attn_Wproj").T))
    wbg = np.zeros((1024, 64), np.float32)
    wbg[:, 0:4] = f("L3_attn_Wbeta").T
    wbg[:, 32:36] = f("L3_attn_Wgk").T
    g["w_l3_bg"] = np.ascontiguousarray(
        wbg.reshape(8, 128, 64).transpose(1, 0, 2).reshape(128, 512))
    g["c_l3_cwq"] = np.ascontiguousarray(f("L3_attn_cwq").reshape(8, 128, 4).transpose(1, 0, 2).reshape(128, 32))
    g["c_l3_cwk"] = np.ascontiguousarray(f("L3_attn_cwk").reshape(2, 128, 4).transpose(1, 0, 2).reshape(128, 8))
    g["c_l3_cwv"] = np.ascontiguousarray(f("L3_attn_cwv").reshape(2, 128, 4).transpose(1, 0, 2).reshape(128, 8))
    bb64 = np.zeros((64, 1), np.float32); bb64[0:4, 0] = f("L3_attn_b_beta")
    dt64 = np.zeros((64, 1), np.float32); dt64[32:36, 0] = f("L3_attn_dt_bias")
    al64 = np.zeros((64, 1), np.float32); al64[32:36, 0] = f("L3_attn_A_log")
    g["c_l3_bb64"] = bb64; g["c_l3_dt64"] = dt64; g["c_l3_al64"] = al64
    g["r_l3_rmsw"] = np.ascontiguousarray(np.tile(f("L3_attn_rms_w"), 2).reshape(128, 1))
    g["w_l3f_a1"] = img_lhsT(f("L3_ffn_aux_weight1"))
    g["w_l3f_wT"] = img_rhs(np.ascontiguousarray(f("L3_ffn_weight").T))
    g["w_l3f_a2"] = img_rhs(f("L3_ffn_aux_weight2"))
    g["c_l3f_b"] = cols(f("L3_ffn_bias"))

    per_core = []
    x = np.asarray(inputs["L0_attn_x"], np.float32)
    for c in range(8):
        per_core.append({"x_in": np.ascontiguousarray(x[c].T)})
    return g, per_core



class Prog:

    def __init__(self):
        self.nc = bass.Bass("TRN2", target_bir_lowering=False)
        self.stack = ExitStack()
        self.ctx = Ctx(self.nc, self.stack)
        self.hbm = {}
        self.hbuf = {}
        self.hbuf8 = {}
        self.hbuf8 = {}

    def dram_in(self, name, shape):
        ap = self.nc.dram_tensor(name, list(shape), F32, kind="ExternalInput").ap()
        self.hbm[name] = ap
        self.hbuf[name] = Buf(name)
        self.hbuf8[name] = [Buf(f"{name}.{i}") for i in range(DC)]
        return ap

    def dram_scratch(self, name, shape, dtype=F32):
        kind = "ExternalOutput" if DEBUG_SPILL else "Internal"
        ap = self.nc.dram_tensor(name, list(shape), dtype, kind=kind).ap()
        self.hbm[name] = ap
        self.hbuf[name] = Buf(name)
        self.hbuf8[name] = [Buf(f"{name}.{i}") for i in range(DC)]
        return ap

    def dram_out(self, name, shape):
        ap = self.nc.dram_tensor(name, list(shape), F32, kind="ExternalOutput").ap()
        self.hbm[name] = ap
        self.hbuf[name] = Buf(name)
        self.hbuf8[name] = [Buf(f"{name}.{i}") for i in range(DC)]
        return ap


def emit_constants(P):
    c = P.ctx
    P.ones_col = c.sb("ones_col", [128, 1], F32)
    P.ones_row = c.sb("ones_row", [1, 128], F32)
    P.ident_bf = c.sb("ident_bf", [128, 128], BF16)
    P.ln_tmp4 = [c.sb(f"ln_tmp{i}", [128, TT], F32) for i in range(4)]
    P.b_ln_tmp4 = [Buf() for _ in range(4)]
    P.ln_rows_h = c.sb("ln_rows_h", [1, 2 * T], F32)
    P.b_lnrh = Buf("ln_rows_h")
    P.ones_col_bf = c.sb("ones_col_bf", [128, 1], BF16)
    P.ones_row_bf = c.sb("ones_row_bf", [1, 128], BF16)
    P.conn_i = c.sb("conn_i", [8, 1], I32)
    P.b_conn = Buf("conn")
    c.push_scope()
    iot = c.sb("iota_i", [128, 128], I32)
    P.b_ones_col = Buf("ones_col")
    P.b_ones_row = Buf("ones_row")
    P.b_ident = Buf("ident")
    b_iot = Buf("iota")
    c.op("V", lambda e: e.memset(P.ones_col_bf[:, :], 1.0), writes=[P.b_ones_col])
    c.op("V", lambda e: e.memset(P.ones_col[:, :], 1.0), writes=[P.b_ones_col])
    c.op("V", lambda e: e.memset(P.ones_row[:, :], 1.0), writes=[P.b_ones_row])
    c.op("V", lambda e: e.memset(P.ones_row_bf[:, :], 1.0), writes=[P.b_ones_row])
    c.op("P", lambda e: e.iota(iot[:, :], [[1, 128]], channel_multiplier=-1,
                               allow_small_or_imprecise_dtypes=True), writes=[b_iot])
    c.op("V", lambda e: e.tensor_single_scalar(P.ident_bf[:, :], iot[:, :], 0, ALU.is_equal),
         reads=[b_iot], writes=[P.b_ident])
    c.pop_scope()


def load_const_cols(P, names_shapes):
    c = P.ctx
    for name, shape in names_shapes:
        t = c.sb("c_" + name, list(shape), F32)
        b = Buf(name)
        c.dma("Y", t[:, :] if len(shape) == 2 else t[:], P.hbm[name],
              reads=[P.hbuf[name]], writes=[b])
        setattr(P, "t_" + name, t)
        setattr(P, "b_" + name, b)



def emit_layernorm2(P, src_hbm, src_buf, xn_sb, xn_buf, gamma_col=None, beta_col=None,
                    src_sb=None, eps=1e-5, pre_stats=False):
    c = P.ctx
    c.push_scope()
    from_hbm = src_sb is None
    if from_hbm:
        xch = c.sb("ln_xch", [128, 2 * T], F32)
        b_xch = [Buf(), Buf()]
        xb, bxb = xn_sb, xn_buf
    else:
        xb, bxb = src_sb
    if pre_stats:
        rows = P.ln_rows_h
        b_rows = P.b_lnrh
    else:
        rows = c.sb("ln_rows", [1, 2 * T], F32)
        b_rows = Buf("ln_rows")
    mrow = c.sb("ln_mrow", [1, T], F32)
    b_mrow = Buf("ln_mrow")
    onesb = P.ones_col_bf[:, 0:1]

    c.push_scope()
    if not pre_stats:
        xsq = c.sb("ln_xsq", [128, T], BF16)
        b_xsq = [Buf()]
        p_s = [c.ps(f"ln_ps{tt}", [1, TT]) for tt in range(NTT)]
        p_q = [c.ps(f"ln_pq{tt}", [1, TT]) for tt in range(NTT)]
        b_ps = [Buf() for _ in range(NTT)]
        b_pq = [Buf() for _ in range(NTT)]
    for ch in range(DC if not pre_stats else 0):
        pb = ch % 2
        if from_hbm:
            c.dma("Y", xch[:, pb * T:(pb + 1) * T], src_hbm[ch * 128:(ch + 1) * 128, :],
                  reads=[src_buf[ch]], writes=[b_xch[pb]])
            c.op("V", lambda e, i=ch, p=pb: e.tensor_copy(xb[:, i * T:(i + 1) * T],
                                                          xch[:, p * T:(p + 1) * T]),
                 reads=[b_xch[pb]], writes=[bxb[ch]])
        c.op("S", lambda e, i=ch: e.activation(xsq[:, 0:T],
                                               xb[:, i * T:(i + 1) * T], AF.Square),
             reads=[bxb[ch]], writes=[b_xsq[0]])
        for tt in range(NTT):
            sl = slice(tt * TT, (tt + 1) * TT)
            c.mm_group([(p_s[tt][0:1, :], onesb, xb[:, ch * T:][:, sl], ch == 0, ch == DC - 1)],
                       reads=[bxb[ch], P.b_ones_col], writes=[b_ps[tt]])
            c.mm_group([(p_q[tt][0:1, :], onesb, xsq[:, sl], ch == 0, ch == DC - 1)],
                       reads=[b_xsq[0], P.b_ones_col], writes=[b_pq[tt]])
    for tt in range(NTT if not pre_stats else 0):
        c.op("V", lambda e, t=tt: e.tensor_copy(rows[0:1, t * TT:(t + 1) * TT], p_s[t][0:1, :]),
             reads=[b_ps[tt]], writes=[b_rows])
        c.op("V", lambda e, t=tt: e.tensor_copy(rows[0:1, T + t * TT:T + (t + 1) * TT], p_q[t][0:1, :]),
             reads=[b_pq[tt]], writes=[b_rows])
    c.pop_scope()

    c.op("V", lambda e: e.tensor_scalar_mul(mrow[0:1, :], rows[0:1, 0:T], 1.0 / D),
         reads=[b_rows], writes=[b_mrow])
    c.op("S", lambda e: e.activation(rows[0:1, 0:T], mrow[0:1, :], AF.Square),
         reads=[b_mrow], writes=[b_rows])
    c.op("V", lambda e: e.scalar_tensor_tensor(rows[0:1, T:2 * T], rows[0:1, T:2 * T],
                                               1.0 / D, rows[0:1, 0:T], ALU.mult, ALU.subtract),
         reads=[b_rows], writes=[b_rows])
    c.op("V", lambda e: e.tensor_scalar_add(rows[0:1, T:2 * T], rows[0:1, T:2 * T], eps),
         reads=[b_rows], writes=[b_rows])
    c.op("S", lambda e: e.activation(rows[0:1, T:2 * T], rows[0:1, T:2 * T], AF.Ln),
         reads=[b_rows], writes=[b_rows])
    c.op("S", lambda e: e.activation(rows[0:1, 0:T], rows[0:1, T:2 * T], AF.Exp, scale=-0.5),
         reads=[b_rows], writes=[b_rows])
    c.op("V", lambda e: e.tensor_mul(rows[0:1, T:2 * T], mrow[0:1, :], rows[0:1, 0:T]),
         reads=[b_rows, b_mrow], writes=[b_rows])
    c.op("V", lambda e: e.tensor_scalar_mul(rows[0:1, T:2 * T], rows[0:1, T:2 * T], -1.0),
         reads=[b_rows], writes=[b_rows])

    if hasattr(P, "dbg_rows") and P.dbg_rows is not None:
        yr, b_yr = P.dbg_rows
        c.dma("Y", yr[0:1, 0:2 * T], rows[0:1, :], reads=[b_rows], writes=[b_yr])
        c.dma("Y", yr[0:1, 2 * T:3 * T], mrow[0:1, :], reads=[b_mrow], writes=[b_yr])
        P.dbg_rows = None
    p_bc = [c.ps(f"ln_bc{i}", [128, 2 * TT]) for i in range(2)]
    b_bc = [Buf(), Buf()]
    if from_hbm:
        xt = c.sb("ln_xt", [128, 2 * TT], F32)
        b_xt = [Buf(), Buf()]
    for tt in range(NTT):
        pb2 = tt % 2
        pbc, bbc = p_bc[pb2], b_bc[pb2]
        c.mm_group([(pbc[:, 0:TT], P.ones_row[0:1, :], rows[0:1, tt * TT:(tt + 1) * TT], True, True),
                    (pbc[:, TT:2 * TT], P.ones_row[0:1, :], rows[0:1, T + tt * TT:T + (tt + 1) * TT], True, True)],
                   reads=[b_rows, P.b_ones_row], writes=[bbc])
        for ch in range(DC):
            sl = slice(tt * TT, (tt + 1) * TT)
            lt = P.ln_tmp4[ch % 4]
            b_lt = P.b_ln_tmp4[ch % 4]
            if from_hbm:
                px = ch % 2
                c.dma("Y", xt[:, px * TT:(px + 1) * TT],
                      src_hbm[ch * 128:(ch + 1) * 128, tt * TT:(tt + 1) * TT],
                      reads=[src_buf[ch]], writes=[b_xt[px]])
                xsrc = xt[:, px * TT:(px + 1) * TT]
                bsrc = b_xt[px]
            else:
                xsrc = xb[:, ch * T:][:, sl]
                bsrc = bxb[ch]
            c.op("V", lambda e, xs=xsrc, p=pbc, L=lt: e.tensor_mul(L[:, 0:TT], xs, p[:, 0:TT]),
                 reads=[bsrc, bbc], writes=[b_lt])
            if gamma_col is None:
                c.op("V", lambda e, i=ch, t=tt, p=pbc, L=lt: e.tensor_add(
                    xn_sb[:, i * T + t * TT: i * T + (t + 1) * TT], L[:, 0:TT], p[:, TT:2 * TT]),
                    reads=[b_lt, bbc], writes=[xn_buf[ch]])
            else:
                gcol, gbuf = gamma_col
                bcol, bbuf = beta_col
                c.op("V", lambda e, p=pbc, L=lt: e.tensor_add(
                    L[:, 0:TT], L[:, 0:TT], p[:, TT:2 * TT]),
                    reads=[b_lt, bbc], writes=[b_lt])
                c.op("V", lambda e, i=ch, t=tt, L=lt: e.tensor_scalar(
                    xn_sb[:, i * T + t * TT: i * T + (t + 1) * TT], L[:, 0:TT],
                    gcol[:, i:i + 1], bcol[:, i:i + 1], ALU.mult, ALU.add),
                    reads=[b_lt, gbuf, bbuf], writes=[xn_buf[ch]])
    c.pop_scope()


def gemm_fm(P, wkey, Kc, mcs, rhs_fn, rhs_bufs, evac, wcols=128, ntt=NTT, ttw=TT,
            col_off=0, extra_reads=(), pool=None):
    c = P.ctx
    wbuf = c.sb("gw_" + wkey, [128, 2 * Kc * wcols], BF16)
    b_w = [Buf(), Buf()]
    if pool is None:
        pss = [c.ps("gp_" + wkey, [128, ttw]) for i in range(2)]
        b_ps = [Buf(), Buf()]
    else:
        pss, b_ps = pool
    wimg = P.hbm[wkey]
    b_wimg = P.hbuf[wkey]
    for i, mc in enumerate(mcs):
        pb = i % 2
        seg = Kc * wcols
        c.dma("P", wbuf[:, pb * seg:(pb + 1) * seg],
              wimg[:, col_off + mc * seg: col_off + (mc + 1) * seg],
              reads=[b_wimg], writes=[b_w[pb]])
        for tt in range(ntt):
            ps, bp = pss[tt % 2], b_ps[tt % 2]
            items = []
            for kc in range(Kc):
                items.append((ps[:, :], wbuf[:, pb * seg + kc * wcols: pb * seg + (kc + 1) * wcols],
                              rhs_fn(kc, tt), kc == 0, kc == Kc - 1))
            c.mm_group(items, reads=[b_w[pb]] + list(rhs_bufs) + list(extra_reads), writes=[bp])
            evac(mc, tt, ps, bp)



def emit_l0(P, x_hbm, x_buf, out_hbm, out_buf, pre_stats=False, collect=False):
    c = P.ctx
    H = 2048
    HC = 16
    TP = T + 8
    KC = 8

    c.push_scope()
    xn2 = c.sb("l0_xn2", [128, DC * T], BF16)
    b_xn2 = [Buf() for _ in range(DC)]
    vgv = c.sb("l0_vgv", [128, HC * TP], BF16)
    b_v = [Buf(f"v{h}") for h in range(HC)]
    for h in range(HC):
        c.op("V", lambda e, hh=h: e.memset(vgv[:, hh * TP:hh * TP + 8], 0.0), writes=[b_v[h]])

    c.push_scope()
    xn_sb = c.sb("l0_xn0", [128, DC * T], BF16)
    xn_bufs = [Buf() for _ in range(DC)]
    emit_layernorm2(P, x_hbm, x_buf, xn_sb, xn_bufs, pre_stats=pre_stats)
    xn_hbm = P.dram_scratch("sp_xn0", [128, DC * T], BF16)
    b_xnh = P.hbuf["sp_xn0"]
    for ch in range(DC):
        c.dma("Y", P.hbm["sp_xn0"][:, ch * T:(ch + 1) * T], xn_sb[:, ch * T:(ch + 1) * T],
              reads=[xn_bufs[ch]], writes=[b_xnh])
    emit_layernorm2(P, None, None, xn2, b_xn2,
                    gamma_col=(P.t_c_l0_gamma[:, :], P.b_c_l0_gamma),
                    beta_col=(P.t_c_l0_beta[:, :], P.b_c_l0_beta),
                    src_sb=(xn_sb, xn_bufs))
    c.pop_scope()

    c.push_scope()
    wbuf = c.sb("l0_wbuf", [128, 2 * KC * 128], BF16)
    b_w = [Buf(), Buf()]
    p_slots = [c.ps(f"l0_pc{i}", [128, TT]) for i in range(4)]
    b_p = [Buf() for _ in range(4)]
    w_img = P.hbm["w_l0_in"]
    b_wimg = P.hbuf["w_l0_in"]
    for i, mc in enumerate(range(16, 32)):
        pb = i % 2
        c.dma("P", wbuf[:, pb * 1024:(pb + 1) * 1024],
              w_img[:, (mc * KC) * 128:(mc * KC + KC) * 128],
              reads=[b_wimg], writes=[b_w[pb]])
        hc = mc - 16
        for tt in range(NTT):
            sl = slice(tt * TT, (tt + 1) * TT)
            ps = p_slots[(i * NTT + tt) % 4]
            bp = b_p[(i * NTT + tt) % 4]
            items = []
            for kc in range(KC):
                items.append((ps[:, :], wbuf[:, pb * 1024 + kc * 128: pb * 1024 + (kc + 1) * 128],
                              xn2[:, kc * T:][:, sl], kc == 0, kc == KC - 1))
            c.mm_group(items, reads=[b_w[pb]] + b_xn2, writes=[bp])
            c.op("V", lambda e, h=hc, t=tt, p=ps, m=mc: e.tensor_scalar(
                vgv[:, h * TP + 8 + t * TT: h * TP + 8 + (t + 1) * TT], p[:, :],
                P.t_c_l0_inb[:, m:m + 1], None, ALU.add),
                reads=[bp, P.b_c_l0_inb], writes=[b_v[hc]])

    scan_bf = c.sb("l0_scan", [128, HC * T], BF16)
    b_scan = [Buf() for _ in range(HC)]
    dec = c.sb("l0_dec", [128, HC], F32)
    b_dec = Buf()
    scan_f = c.sb("l0_scanf", [128, T // 2], F32)
    b_scanf = Buf()
    c.op("S", lambda e: e.activation(dec[:, :], P.t_c_l0_dec[:, :], AF.Sigmoid),
         reads=[P.b_c_l0_dec], writes=[b_dec])
    carry = c.sb("l0_carry", [128, 1], F32)
    b_carry = Buf()
    TH2 = T // 2
    for hc in range(HC):
        for hf in range(2):
            init = 0.0 if hf == 0 else carry[:, 0:1]
            rd = [b_dec, b_v[hc]] + ([b_carry] if hf else [])
            c.op("V", lambda e, h=hc, f=hf, ini=init: e.tensor_tensor_scan(
                scan_f[:, :], dec[:, h:h + 1].broadcast_to([128, TH2]),
                vgv[:, h * TP + 8 + f * TH2: h * TP + 8 + (f + 1) * TH2],
                ini, ALU.mult, ALU.add),
                reads=rd, writes=[b_scanf])
            if hf == 0:
                c.op("V", lambda e: e.tensor_copy(carry[:, 0:1], scan_f[:, TH2 - 1:TH2]),
                     reads=[b_scanf], writes=[b_carry])
            c.op("P", lambda e, h=hc, f=hf: e.tensor_copy(
                scan_bf[:, h * T + f * TH2: h * T + (f + 1) * TH2], scan_f[:, :]),
                reads=[b_scanf], writes=[b_scan[hc]])

    diag = c.sb("l0_diag", [128, 7 * 128], BF16)
    b_diag = Buf()
    bsum = c.sb("l0_bsum", [128, HC], F32)
    b_bsum = Buf()
    c.op("V", lambda e: e.tensor_add(bsum[:, :], P.t_c_l0_dwb[:, :], P.t_c_l0_mixb[:, :]),
         reads=[P.b_c_l0_dwb, P.b_c_l0_mixb], writes=[b_bsum])

    wmix = P.hbm["w_l0_mix"]
    b_wmix = P.hbuf["w_l0_mix"]
    wbuf2 = c.sb("l0_wbuf2", [128, 16 * 128], BF16)
    b_w2 = [Buf(), Buf()]
    g_sb = c.sb("l0_g", [128, 2 * TT], F32)
    b_g = [Buf(), Buf()]
    KCH = 16
    for hc in range(HC):
        pb = hc % 2
        c.dma("P", wbuf2[:, 0:2048],
              wmix[:, (hc * KCH) * 128:(hc * KCH + KCH) * 128],
              reads=[b_wmix], writes=[b_w2[0]])
        c.dma("P", wbuf[:, pb * 1024:(pb + 1) * 1024],
              w_img[:, (hc * KC) * 128:(hc * KC + KC) * 128],
              reads=[b_wimg], writes=[b_w[pb]])
        for j in range(7):
            c.op("V", lambda e, h=hc, jj=j: e.tensor_scalar_mul(
                diag[:, jj * 128:(jj + 1) * 128], P.ident_bf[:, :],
                P.t_c_l0_dww[:, h * 7 + jj:h * 7 + jj + 1]),
                reads=[P.b_ident, P.b_c_l0_dww], writes=[b_diag])
        for tt in range(NTT):
            sl = slice(tt * TT, (tt + 1) * TT)
            items = []
            for kc in range(KCH):
                items.append((p_slots[tt][:, :],
                              wbuf2[:, kc * 128:(kc + 1) * 128],
                              scan_bf[:, kc * T:][:, sl], kc == 0, False))
            for j in range(7):
                sh = slice(hc * TP + 8 + tt * TT - (6 - j), hc * TP + 8 + (tt + 1) * TT - (6 - j))
                items.append((p_slots[tt][:, :], diag[:, j * 128:(j + 1) * 128], vgv[:, sh],
                              False, j == 6))
            c.mm_group(items, reads=[b_w2[0], b_diag, b_v[hc]] + b_scan, writes=[b_p[tt]])
        if hc == 0:
            pg2 = [c.ps(f"l0_pg{i}", [128, TT]) for i in range(2)]
            bpg2 = [Buf(), Buf()]
        for tt in range(NTT):
            sl = slice(tt * TT, (tt + 1) * TT)
            pg, bpg = pg2[tt % 2], bpg2[tt % 2]
            wl = slice((tt % 2) * TT, (tt % 2 + 1) * TT)
            itg = []
            for kc in range(KC):
                itg.append((pg[:, :], wbuf[:, pb * 1024 + kc * 128:pb * 1024 + (kc + 1) * 128],
                            xn2[:, kc * T:][:, sl], kc == 0, kc == KC - 1))
            c.mm_group(itg, reads=[b_w[pb]] + b_xn2, writes=[bpg])
            c.op("S", lambda e, h=hc, p=pg, W=wl: e.activation(
                g_sb[:, W], p[:, :], AF.Sigmoid, bias=P.t_c_l0_inb[:, h:h + 1]),
                reads=[bpg, P.b_c_l0_inb], writes=[b_g[tt % 2]])
            c.op("V", lambda e, h=hc, t=tt, W=wl: e.scalar_tensor_tensor(
                vgv[:, h * TP + 8 + t * TT: h * TP + 8 + (t + 1) * TT], p_slots[t][:, :],
                bsum[:, h:h + 1], g_sb[:, W], ALU.add, ALU.mult),
                reads=[b_p[tt], b_g[tt % 2], b_bsum], writes=[b_v[hc]])
    c.pop_scope()

    c.push_scope()
    s_hbm = P.dram_scratch("sp_s0", [128, DC * T], F32)
    b_sh = P.hbuf8["sp_s0"]
    s_sb = c.sb("l0_s", [128, 2 * TT], F32)
    b_s = [Buf(), Buf()]
    xn_tile = c.sb("l0_xnt", [128, T], BF16)
    b_xnt = Buf()
    wout = P.hbm["w_l0_out"]
    b_wout = P.hbuf["w_l0_out"]
    wbuf3 = c.sb("l0_wbuf3", [128, 2 * 16 * 128], BF16)
    b_w3 = [Buf(), Buf()]
    p_f = [c.ps(f"l0_pf{i}", [128, TT]) for i in range(2)]
    b_pf = [Buf(), Buf()]
    for mc in range(DC):
        pb = mc % 2
        c.dma("P", wbuf3[:, pb * 2048:(pb + 1) * 2048],
              wout[:, (mc * 16) * 128:(mc * 16 + 16) * 128],
              reads=[b_wout], writes=[b_w3[pb]])
        c.dma("Y", xn_tile[:, :], P.hbm["sp_xn0"][:, mc * T:(mc + 1) * T],
              reads=[b_xnh], writes=[b_xnt])
        for tt in range(NTT):
            sl = slice(tt * TT, (tt + 1) * TT)
            ps = p_f[tt % 2]
            bp = b_pf[tt % 2]
            items = []
            for kc in range(16):
                items.append((ps[:, :], wbuf3[:, pb * 2048 + kc * 128:pb * 2048 + (kc + 1) * 128],
                              vgv[:, kc * TP + 8:][:, sl], kc == 0, kc == 15))
            c.mm_group(items, reads=[b_w3[pb]] + b_v, writes=[bp])
            wl = slice((tt % 2) * TT, (tt % 2 + 1) * TT)
            c.op("V", lambda e, p=ps, m=mc, t=tt, W=wl: e.scalar_tensor_tensor(
                s_sb[:, W], p[:, :], P.t_c_l0_outb[:, m:m + 1], xn_tile[:, t * TT:(t + 1) * TT],
                ALU.add, ALU.add),
                reads=[bp, b_xnt, P.b_c_l0_outb], writes=[b_s[tt % 2]])
            c.dma("Y", s_hbm[:, mc * T + tt * TT: mc * T + (tt + 1) * TT], s_sb[:, wl],
                  reads=[b_s[tt % 2]], writes=[b_sh[mc]])
    c.pop_scope()
    c.pop_scope()

    g_hbm = P.dram_scratch("sp_g0", [128, DC * T], BF16)
    b_gh = P.hbuf8["sp_g0"]
    emit_l0_ffn(P, g_hbm, b_gh)

    emit_combine(P, x_hbm, x_buf, P.hbm["sp_s0"], b_sh, P.hbm["sp_g0"], b_gh,
                 out_hbm, out_buf, collect_stats=collect)


def emit_l0_ffn(P, g_hbm, b_gh):
    c = P.ctx
    HC = 32
    KC8 = 8
    KC32 = 32
    c.push_scope()
    TH = T // 2
    NT2 = TH // TT
    xn_t = c.sb("f_xn", [128, DC * TH], BF16)
    b_xn = [Buf() for _ in range(DC)]
    A = c.sb("f_A", [128, HC * TH], BF16)
    b_A = [Buf() for _ in range(HC)]
    Bt = c.sb("f_B", [128, HC * TH], BF16)
    b_B = [Buf() for _ in range(HC)]
    wb1 = c.sb("f_w1", [128, 2 * KC8 * 128], BF16)
    wb2 = c.sb("f_w2", [128, 2 * KC8 * 128], BF16)
    b_w1 = [Buf(), Buf()]
    b_w2 = [Buf(), Buf()]
    tmp = c.sb("f_tmp", [128, 2 * TT], F32)
    b_tmp = [Buf(), Buf()]
    wbig = c.sb("f_wbig", [128, 2 * KC32 * 128], BF16)
    b_wbig = [Buf(), Buf()]
    gt = c.sb("f_g", [128, 2 * TT], BF16)
    b_gt = [Buf(), Buf()]
    p_slots = [c.ps(f"f_p{i}", [128, TT]) for i in range(4)]
    b_p = [Buf() for _ in range(4)]
    for half in range(2):
        for ch in range(DC):
            c.dma("Y", xn_t[:, ch * TH:(ch + 1) * TH],
                  P.hbm["sp_xn0"][:, ch * T + half * TH: ch * T + (half + 1) * TH],
                  reads=[P.hbuf["sp_xn0"]], writes=[b_xn[ch]])
        for mc in range(HC):
            pb = mc % 2
            c.dma("P", wb1[:, pb * 1024:(pb + 1) * 1024],
                  P.hbm["w_l0f_w1"][:, mc * KC8 * 128:(mc * KC8 + KC8) * 128],
                  reads=[P.hbuf["w_l0f_w1"]], writes=[b_w1[pb]])
            c.dma("P", wb2[:, pb * 1024:(pb + 1) * 1024],
                  P.hbm["w_l0f_w2"][:, mc * KC8 * 128:(mc * KC8 + KC8) * 128],
                  reads=[P.hbuf["w_l0f_w2"]], writes=[b_w2[pb]])
            for tt in range(NT2):
                sl = slice(tt * TT, (tt + 1) * TT)
                ps1, bp1 = p_slots[2 * (tt % 2)], b_p[2 * (tt % 2)]
                ps2, bp2 = p_slots[2 * (tt % 2) + 1], b_p[2 * (tt % 2) + 1]
                it1, it2 = [], []
                for kc in range(KC8):
                    it1.append((ps1[:, :], wb1[:, pb * 1024 + kc * 128:pb * 1024 + (kc + 1) * 128],
                                xn_t[:, kc * TH:][:, sl], kc == 0, kc == KC8 - 1))
                    it2.append((ps2[:, :], wb2[:, pb * 1024 + kc * 128:pb * 1024 + (kc + 1) * 128],
                                xn_t[:, kc * TH:][:, sl], kc == 0, kc == KC8 - 1))
                c.mm_group(it1, reads=[b_w1[pb]] + b_xn, writes=[bp1])
                c.mm_group(it2, reads=[b_w2[pb]] + b_xn, writes=[bp2])
                wl = slice((tt % 2) * TT, (tt % 2 + 1) * TT)
                c.op("S", lambda e, p=ps1, m=mc, W=wl: e.activation(
                    tmp[:, W], p[:, :], AF.Silu, scale=P.t_c_l0f_s1[:, m:m + 1]),
                    reads=[bp1, P.b_c_l0f_s1], writes=[b_tmp[tt % 2]])
                c.op("V", lambda e, p=ps2, m=mc, t=tt, W=wl: e.scalar_tensor_tensor(
                    A[:, m * TH + t * TT: m * TH + (t + 1) * TT], p[:, :],
                    P.t_c_l0f_s2[:, m:m + 1], tmp[:, W], ALU.mult, ALU.mult),
                    reads=[bp2, b_tmp[tt % 2], P.b_c_l0f_s2], writes=[b_A[mc]])
        for stage, (wkey, skey, scol, src, bsrc, dst, bdst) in enumerate([
                ("w_l0f_h1", "s3", P.t_c_l0f_s3, A, b_A, Bt, b_B),
                ("w_l0f_h2", "s4", P.t_c_l0f_s4, Bt, b_B, A, b_A)]):
            for mc in range(HC):
                pb = mc % 2
                c.dma("P", wbig[:, pb * KC32 * 128:(pb + 1) * KC32 * 128],
                      P.hbm[wkey][:, mc * KC32 * 128:(mc * KC32 + KC32) * 128],
                      reads=[P.hbuf[wkey]], writes=[b_wbig[pb]])
                for tt in range(NT2):
                    sl = slice(tt * TT, (tt + 1) * TT)
                    ps, bp = p_slots[tt % 2], b_p[tt % 2]
                    items = []
                    for kc in range(KC32):
                        items.append((ps[:, :],
                                      wbig[:, pb * KC32 * 128 + kc * 128:pb * KC32 * 128 + (kc + 1) * 128],
                                      src[:, kc * TH:][:, sl], kc == 0, kc == KC32 - 1))
                    c.mm_group(items, reads=[b_wbig[pb]] + bsrc, writes=[bp])
                    c.op("S", lambda e, p=ps, m=mc, t=tt, sc=scol, d=dst: e.activation(
                        d[:, m * TH + t * TT: m * TH + (t + 1) * TT], p[:, :], AF.Gelu,
                        scale=sc[:, m:m + 1]),
                        reads=[bp], writes=[bdst[mc]])
        for mc in range(DC):
            pb = mc % 2
            c.dma("P", wbig[:, pb * KC32 * 128:(pb + 1) * KC32 * 128],
                  P.hbm["w_l0f_w3"][:, mc * KC32 * 128:(mc * KC32 + KC32) * 128],
                  reads=[P.hbuf["w_l0f_w3"]], writes=[b_wbig[pb]])
            for tt in range(NT2):
                sl = slice(tt * TT, (tt + 1) * TT)
                ps, bp = p_slots[tt % 2], b_p[tt % 2]
                items = []
                for kc in range(KC32):
                    items.append((ps[:, :],
                                  wbig[:, pb * KC32 * 128 + kc * 128:pb * KC32 * 128 + (kc + 1) * 128],
                                  A[:, kc * TH:][:, sl], kc == 0, kc == KC32 - 1))
                c.mm_group(items, reads=[b_wbig[pb]] + b_A, writes=[bp])
                wl = slice((tt % 2) * TT, (tt % 2 + 1) * TT)
                c.op("S", lambda e, p=ps, m=mc, W=wl: e.activation(
                    gt[:, W], p[:, :], AF.Silu, scale=P.t_c_l0f_s5[:, m:m + 1]),
                    reads=[bp, P.b_c_l0f_s5], writes=[b_gt[tt % 2]])
                c.dma("Y", g_hbm[:, mc * T + half * TH + tt * TT: mc * T + half * TH + (tt + 1) * TT],
                      gt[:, wl], reads=[b_gt[tt % 2]], writes=[b_gh[mc]])
    c.pop_scope()


def emit_combine(P, x_hbm, x_buf, s_hbm, b_sh, g_hbm, b_gh, out_hbm, out_buf,
                 collect_stats=False):
    c = P.ctx
    c.push_scope()
    NL = 4
    xt = c.sb("cb_x", [128, NL * TT], F32)
    st = c.sb("cb_s", [128, NL * TT], F32)
    gt = c.sb("cb_g", [128, NL * TT], BF16)
    ot = c.sb("cb_o", [128, NL * TT], F32)
    bx = [Buf() for _ in range(NL)]
    bs = [Buf() for _ in range(NL)]
    bg = [Buf() for _ in range(NL)]
    bo = [Buf() for _ in range(NL)]
    if collect_stats:
        ob = c.sb("cb_ob", [128, NL * TT], BF16)
        osq = c.sb("cb_osq", [128, NL * TT], BF16)
        bob = [Buf() for _ in range(NL)]
        bosq = [Buf() for _ in range(NL)]
        p_s = [c.ps(f"cb_ps{t}", [1, TT]) for t in range(NTT)]
        p_q = [c.ps(f"cb_pq{t}", [1, TT]) for t in range(NTT)]
        b_ps = [Buf() for _ in range(NTT)]
        b_pq = [Buf() for _ in range(NTT)]
    i = 0
    for ch in range(DC):
        for tt in range(NTT):
            ln = i % NL
            i += 1
            w = slice(ln * TT, (ln + 1) * TT)
            sl = slice(ch * T + tt * TT, ch * T + (tt + 1) * TT)
            c.dma("Y", xt[:, w], x_hbm[ch * 128:(ch + 1) * 128, tt * TT:(tt + 1) * TT]
                  if x_hbm.shape[0] == D else x_hbm[:, sl],
                  reads=[x_buf[ch]], writes=[bx[ln]])
            c.dma("Y", st[:, w], s_hbm[:, sl], reads=[b_sh[ch]], writes=[bs[ln]])
            c.dma("P", gt[:, w], g_hbm[:, sl], reads=[b_gh[ch]], writes=[bg[ln]])
            c.op("V", lambda e, W=w: e.tensor_mul(ot[:, W], st[:, W], gt[:, W]),
                 reads=[bs[ln], bg[ln]], writes=[bo[ln]])
            c.op("P", lambda e, W=w: e.tensor_add(ot[:, W], ot[:, W], xt[:, W]),
                 reads=[bo[ln], bx[ln]], writes=[bo[ln]])
            c.dma("Y", out_hbm[ch * 128:(ch + 1) * 128, tt * TT:(tt + 1) * TT]
                  if out_hbm.shape[0] == D else out_hbm[:, sl],
                  ot[:, w], reads=[bo[ln]], writes=[out_buf[ch]])
            if collect_stats:
                c.op("P", lambda e, W=w: e.tensor_copy(ob[:, W], ot[:, W]),
                     reads=[bo[ln]], writes=[bob[ln]])
                c.op("S", lambda e, W=w: e.activation(osq[:, W], ot[:, W], AF.Square),
                     reads=[bo[ln]], writes=[bosq[ln]])
                c.mm_group([(p_s[tt][0:1, :], P.ones_col_bf[:, 0:1], ob[:, w],
                             ch == 0, ch == DC - 1)],
                           reads=[bob[ln], P.b_ones_col], writes=[b_ps[tt]])
                c.mm_group([(p_q[tt][0:1, :], P.ones_col_bf[:, 0:1], osq[:, w],
                             ch == 0, ch == DC - 1)],
                           reads=[bosq[ln], P.b_ones_col], writes=[b_pq[tt]])
    if collect_stats:
        for tt in range(NTT):
            c.op("V", lambda e, t=tt: e.tensor_copy(
                P.ln_rows_h[0:1, t * TT:(t + 1) * TT], p_s[t][0:1, :]),
                reads=[b_ps[tt]], writes=[P.b_lnrh])
            c.op("V", lambda e, t=tt: e.tensor_copy(
                P.ln_rows_h[0:1, T + t * TT:T + (t + 1) * TT], p_q[t][0:1, :]),
                reads=[b_pq[tt]], writes=[P.b_lnrh])
    c.pop_scope()



def emit_l1(P, x_hbm, x_buf, out_hbm, out_buf, pre_stats=False, collect=False):
    c = P.ctx
    H, HD = 16, 64
    c.push_scope()
    s_hbm = P.dram_scratch("sp_s1", [128, DC * T], F32)
    b_sh = P.hbuf8["sp_s1"]
    g_hbm = P.dram_scratch("sp_g1", [128, DC * T], BF16)
    b_gh = P.hbuf8["sp_g1"]
    xn_hbm = P.dram_scratch("sp_xn1", [128, DC * T], BF16)
    b_xnh = P.hbuf8["sp_xn1"]

    c.push_scope()
    q = c.sb("l1_q", [128, DC * T], BF16)
    b_q = [Buf() for _ in range(DC)]
    k = c.sb("l1_k", [128, DC * T], BF16)
    b_k = [Buf() for _ in range(DC)]
    v_tm = c.sb("l1_v", [128, 16 * 1040], BF16)
    b_v = [Buf() for _ in range(16)]

    c.push_scope()
    xn = c.sb("l1_xn", [128, DC * T], BF16)
    b_xn = [Buf() for _ in range(DC)]
    emit_layernorm2(P, x_hbm, x_buf, xn, b_xn, pre_stats=pre_stats)
    for ch in range(DC):
        c.dma("Y", xn_hbm[:, ch * T:(ch + 1) * T], xn[:, ch * T:(ch + 1) * T],
              reads=[b_xn[ch]], writes=[b_xnh[ch]])

    def rhs_xn(kc, tt):
        return xn[:, kc * T:][:, tt * TT:(tt + 1) * TT]
    gemm_fm(P, "w_l1_q", 8, range(8), rhs_xn, b_xn,
            lambda mc, tt, ps, bp: c.op("V", lambda e, m=mc, t=tt, p=ps: e.tensor_copy(
                q[:, m * T + t * TT: m * T + (t + 1) * TT], p[:, :]),
                reads=[bp], writes=[b_q[mc]]))
    c.push_scope()
    lat = c.sb("l1_lat", [128, 2 * T], BF16)
    b_lat = [Buf(), Buf()]
    gemm_fm(P, "w_l1_kv", 8, range(2), rhs_xn, b_xn,
            lambda mc, tt, ps, bp: c.op("V", lambda e, m=mc, t=tt, p=ps: e.tensor_copy(
                lat[:, m * T + t * TT: m * T + (t + 1) * TT], p[:, :]),
                reads=[bp], writes=[b_lat[mc]]))

    def rhs_lat(kc, tt):
        return lat[:, kc * T:][:, tt * TT:(tt + 1) * TT]
    gemm_fm(P, "w_l1_kup", 2, range(8), rhs_lat, b_lat,
            lambda mc, tt, ps, bp: c.op("V", lambda e, m=mc, t=tt, p=ps: e.tensor_copy(
                k[:, m * T + t * TT: m * T + (t + 1) * TT], p[:, :]),
                reads=[bp], writes=[b_k[mc]]))
    wv = c.sb("l1_wv", [128, 2 * 1024], BF16)
    b_wv = Buf()
    c.dma("P", wv[:, :], P.hbm["w_l1_vup"][:, :], reads=[P.hbuf["w_l1_vup"]], writes=[b_wv])
    pv = [c.ps(f"l1_pv{i}", [128, TT]) for i in range(2)]
    b_pv = [Buf(), Buf()]
    for kt in range(16):
        c.op("V", lambda e, K=kt: e.memset(
            v_tm[:, K * 1040:(K + 1) * 1040].rearrange("p (h f) -> p h f", h=16)[:, :, 64:65], 1.0),
            writes=[b_v[kt]])
        for nb in range(2):
            ps, bp = pv[nb], b_pv[nb]
            items = []
            for kc in range(2):
                items.append((ps[:, :], lat[:, kc * T + kt * 128: kc * T + (kt + 1) * 128],
                              wv[:, kc * 1024 + nb * 512: kc * 1024 + (nb + 1) * 512],
                              kc == 0, kc == 1))
            c.mm_group(items, reads=b_lat + [b_wv], writes=[bp])
            c.op("V", lambda e, K=kt, n=nb, p=ps: e.tensor_copy(
                v_tm[:, K * 1040 + n * 8 * 65:][:, 0:520].rearrange("p (h f) -> p h f", h=8)[:, :, 0:64],
                p[:, :].rearrange("p (h f) -> p h f", h=8)),
                reads=[bp], writes=[b_v[kt]])
    c.pop_scope()
    c.pop_scope()
    y = c.sb("l1_y", [128, DC * T], BF16)
    b_y = [Buf() for _ in range(DC)]

    cneg = c.sb("l1_cneg", [128, 1], F32)
    b_cn = Buf()
    c.push_scope()
    sq = c.sb("l1_sq", [128, T], BF16)
    b_sq = Buf()
    mxr = c.sb("l1_mx", [1, 64], F32)
    b_mx = Buf()
    p1 = c.ps("l1_p1", [1, TT])
    b_p1 = Buf()
    for half, (src, bufs) in enumerate([(q, b_q), (k, b_k)]):
        for ch in range(DC):
            c.op("S", lambda e, s=src, i=ch: e.activation(sq[:, :], s[:, i * T:(i + 1) * T], AF.Square),
                 reads=[bufs[ch]], writes=[b_sq])
            for tt in range(NTT):
                c.mm_group([(p1[0:1, :], P.ones_col_bf[:, 0:1], sq[:, tt * TT:(tt + 1) * TT], True, True)],
                           reads=[b_sq, P.b_ones_col], writes=[b_p1])
                c.op("V", lambda e, hf=half, i=ch, t=tt: e.reduce_max(
                    mxr[0:1, hf * 32 + i * 4 + t: hf * 32 + i * 4 + t + 1], p1[0:1, :], axis=AX.X),
                    reads=[b_p1], writes=[b_mx])
    crow = c.sb("l1_cr", [1, 4], F32)
    b_cr = Buf()
    c.op("V", lambda e: e.tensor_reduce(crow[0:1, 0:2], mxr[0:1, :].rearrange("p (a b) -> p a b", a=2),
                                        AX.X, ALU.max), reads=[b_mx], writes=[b_cr])
    c.op("S", lambda e: e.activation(crow[0:1, 2:4], crow[0:1, 0:2], AF.Ln), reads=[b_cr], writes=[b_cr])
    c.op("V", lambda e: e.reduce_sum(crow[0:1, 0:1], crow[0:1, 2:4], axis=AX.X), reads=[b_cr], writes=[b_cr])
    c.op("S", lambda e: e.activation(crow[0:1, 1:2], crow[0:1, 0:1], AF.Exp, scale=0.5),
         reads=[b_cr], writes=[b_cr])
    c.op("V", lambda e: e.tensor_scalar_mul(crow[0:1, 0:1], crow[0:1, 1:2], -0.125),
         reads=[b_cr], writes=[b_cr])
    pcb = c.ps("l1_pcb", [128, 1])
    b_pcb = Buf()
    c.mm_group([(pcb[:, 0:1], P.ones_row[0:1, :], crow[0:1, 0:1], True, True)],
               reads=[b_cr, P.b_ones_row], writes=[b_pcb])
    c.op("V", lambda e: e.tensor_copy(cneg[:, :], pcb[:, 0:1]), reads=[b_pcb], writes=[b_cn])
    c.pop_scope()

    masks = c.sb("l1_masks", [128, 4 * TT], BF16)
    b_mk = Buf()
    c.push_scope()
    iot2 = c.sb("l1_iot2", [128, TT], I32)
    b_io = Buf()
    c.op("P", lambda e: e.iota(iot2[:, :], [[1, TT]], channel_multiplier=-1,
                               allow_small_or_imprecise_dtypes=True), writes=[b_io])
    for r in range(4):
        c.op("V", lambda e, rr=r: e.tensor_single_scalar(
            masks[:, rr * TT:(rr + 1) * TT], iot2[:, :], 128 * rr, ALU.is_ge),
            reads=[b_io], writes=[b_mk])
    c.pop_scope()

    expS2 = [c.sb(f"l1_es{i}", [128, 16 * TT], BF16) for i in range(2)]
    b_es2 = [Buf(), Buf()]
    av2 = [c.sb(f"l1_av{i}", [128, TT], F32) for i in range(2)]
    b_av2 = [Buf(), Buf()]
    rc2 = [c.sb(f"l1_rc{i}", [1, TT], F32) for i in range(2)]
    b_rc2 = [Buf(), Buf()]
    psc4 = [c.ps(f"l1_psc{i}", [128, TT]) for i in range(4)]
    b_psc4 = [Buf() for _ in range(4)]
    pav2 = [c.ps(f"l1_pav{i}", [128, TT]) for i in range(2)]
    b_pav2 = [Buf(), Buf()]
    pbc2_ = [c.ps(f"l1_pbc{i}", [128, TT]) for i in range(2)]
    b_pbc2_ = [Buf(), Buf()]

    def emit_head(qb, ch, hh):
        lane = hh
        expS, b_es = expS2[lane], b_es2[lane]
        av_sb, b_av = av2[lane], b_av2[lane]
        rc, b_rc = rc2[lane], b_rc2[lane]
        pav, b_pav = pav2[lane], b_pav2[lane]
        pbc, b_pbc = pbc2_[lane], b_pbc2_[lane]
        head = 2 * ch + hh
        rs = slice(64 * hh, 64 * hh + 64)
        tp = (64 * hh, 0)
        nkb = 4 * qb + 4
        for kb in range(nkb):
            ps, bp = psc4[2 * lane + kb % 2], b_psc4[2 * lane + kb % 2]
            c.mm_group([(ps[:, :], k[rs, ch * T + kb * 128: ch * T + (kb + 1) * 128],
                         q[rs, ch * T + qb * TT: ch * T + (qb + 1) * TT], True, True, tp)],
                       reads=[b_k[ch], b_q[ch]], writes=[bp])
            c.op("S", lambda e, K=kb, p=ps: e.activation(
                expS[:, K * TT:(K + 1) * TT], p[:, :], AF.Exp, bias=cneg[:, 0:1], scale=0.125),
                reads=[bp, b_cn], writes=[b_es])
            if kb >= 4 * qb:
                r = kb - 4 * qb
                c.op("V", lambda e, K=kb, rr=r: e.tensor_mul(
                    expS[:, K * TT:(K + 1) * TT], expS[:, K * TT:(K + 1) * TT],
                    masks[:, rr * TT:(rr + 1) * TT]),
                    reads=[b_es, b_mk], writes=[b_es])
            yield
        items = []
        for kb in range(nkb):
            items.append((pav[0:65, :], v_tm[:, kb * 1040 + head * 65: kb * 1040 + head * 65 + 65],
                          expS[:, kb * TT:(kb + 1) * TT], kb == 0, kb == nkb - 1))
        c.mm_group(items, reads=b_v + [b_es], writes=[b_pav])
        yield
        c.op("V", lambda e: e.tensor_copy(av_sb[0:65, :], pav[0:65, :]),
             reads=[b_pav], writes=[b_av])
        c.op("V", lambda e: e.reciprocal(rc[0:1, :], av_sb[64:65, :]),
             reads=[b_av], writes=[b_rc])
        yield
        c.mm_group([(pbc[0:64, :], P.ones_row[0:1, 0:64], rc[0:1, :], True, True)],
                   reads=[b_rc, P.b_ones_row], writes=[b_pbc])
        yield
        c.op("V", lambda e, i=ch, h=hh, t=qb: e.tensor_mul(
            y[64 * h:64 * h + 64, i * T + t * TT: i * T + (t + 1) * TT],
            av_sb[0:64, :], pbc[0:64, :]),
            reads=[b_av, b_pbc], writes=[b_y[ch]])

    for qb in range(NTT):
        for ch in range(DC):
            alive = [emit_head(qb, ch, 0), emit_head(qb, ch, 1)]
            while alive:
                nxt = []
                for gen in alive:
                    try:
                        next(gen)
                        nxt.append(gen)
                    except StopIteration:
                        pass
                alive = nxt

    s_sb = c.sb("l1_s", [128, TT], F32)
    b_s = Buf()

    def rhs_y(kc, tt):
        return y[:, kc * T:][:, tt * TT:(tt + 1) * TT]

    def ev_s(mc, tt, ps, bp):
        c.op("V", lambda e, m=mc, t=tt, p=ps: e.tensor_copy(s_sb[:, :], p[:, :]),
             reads=[bp], writes=[b_s])
        c.dma("Y", s_hbm[:, mc * T + tt * TT: mc * T + (tt + 1) * TT], s_sb[:, :],
              reads=[b_s], writes=[b_sh[mc]])
    gemm_fm(P, "w_l1_out", 8, range(8), rhs_y, b_y, ev_s,
            pool=([psc4[0], psc4[1]], [b_psc4[0], b_psc4[1]]))
    c.pop_scope()

    c.push_scope()
    xn = c.sb("l1f_xn", [128, DC * T], BF16)
    b_xn = [Buf() for _ in range(DC)]
    for ch in range(DC):
        c.dma("Y", xn[:, ch * T:(ch + 1) * T], xn_hbm[:, ch * T:(ch + 1) * T],
              reads=[b_xnh[ch]], writes=[b_xn[ch]])

    def rhs_xn(kc, tt):
        return xn[:, kc * T:][:, tt * TT:(tt + 1) * TT]
    gpool = ([c.ps(f"l1f_gp{i}", [128, TT]) for i in range(2)], [Buf(), Buf()])
    h_sb = c.sb("l1f_h", [128, 2 * T], BF16)
    b_h = [Buf(), Buf()]
    diff = c.sb("l1f_diff", [128, TT], F32)
    b_df = Buf()
    sqd = c.sb("l1f_sqd", [128, TT], BF16)
    b_sqd = Buf()
    lcols = c.sb("l1f_lc", [128, 256], F32)
    b_lc = Buf()
    for ei in range(8):
        def ev_h(mc, tt, ps, bp, ei=ei):
            c.op("S", lambda e, m=mc, t=tt, p=ps, E=ei: e.activation(
                h_sb[:, m * T + t * TT: m * T + (t + 1) * TT], p[:, :], AF.Relu,
                bias=P.t_c_l1_benc[:, E * 2 + m:E * 2 + m + 1]),
                reads=[bp, P.b_c_l1_benc], writes=[b_h[mc]])
        gemm_fm(P, "w_l1_enc", 8, range(2), rhs_xn, b_xn, ev_h, col_off=ei * 2048, pool=gpool)

        def rhs_h(kc, tt):
            return h_sb[:, kc * T:][:, tt * TT:(tt + 1) * TT]

        def ev_rec(mc, tt, ps, bp, ei=ei):
            c.op("V", lambda e, m=mc, t=tt, p=ps, E=ei: e.scalar_tensor_tensor(
                diff[:, :], p[:, :], P.t_c_l1_bdec[:, E * 8 + m:E * 8 + m + 1],
                xn[:, m * T + t * TT: m * T + (t + 1) * TT], ALU.add, ALU.subtract),
                reads=[bp, b_xn[mc], P.b_c_l1_bdec], writes=[b_df])
            c.op("S", lambda e, m=mc, t=tt, E=ei: e.activation(
                sqd[:, :], diff[:, :], AF.Square,
                accum_out=lcols[:, E * 32 + m * 4 + t: E * 32 + m * 4 + t + 1]),
                reads=[b_df], writes=[b_sqd, b_lc])
        gemm_fm(P, "w_l1_dec", 2, range(8), rhs_h, b_h, ev_rec, col_off=ei * 2048, pool=gpool)
    ohbc = c.sb("l1f_ohbc", [128, 8], F32)
    b_ohbc = Buf()
    bdu = c.sb("l1f_bdu", [128, 9], F32)
    b_bdu = Buf()
    c.push_scope()
    lsum = c.sb("l1f_ls", [128, 8], F32)
    b_lsum = Buf()
    c.op("V", lambda e: e.tensor_reduce(lsum[:, :], lcols[:, :].rearrange("p (a b) -> p a b", a=8),
                                        AX.X, ALU.add), reads=[b_lc], writes=[b_lsum])
    pl8 = c.ps("l1f_pl8", [1, 8])
    b_pl8 = Buf()
    c.mm_group([(pl8[0:1, :], P.ones_col[:, 0:1], lsum[:, :], True, True)],
               reads=[b_lsum, P.b_ones_col], writes=[b_pl8])
    small = c.sb("l1f_small", [1, 32], F32)
    b_sm = Buf()
    c.op("V", lambda e: e.tensor_copy(small[0:1, 0:8], pl8[0:1, :]), reads=[b_pl8], writes=[b_sm])
    c.op("V", lambda e: e.tensor_reduce(small[0:1, 8:9], small[0:1, 0:8], AX.X, ALU.min),
         reads=[b_sm], writes=[b_sm])
    c.op("V", lambda e: e.tensor_scalar(small[0:1, 9:17], small[0:1, 0:8], small[0:1, 8:9], None,
                                        ALU.is_le), reads=[b_sm], writes=[b_sm])
    oh_bf = c.sb("l1f_ohbf", [1, 8], BF16)
    b_ohbf = Buf()
    c.op("V", lambda e: e.tensor_copy(oh_bf[0:1, :], small[0:1, 9:17]), reads=[b_sm], writes=[b_ohbf])
    connf = c.sb("l1f_connf", [8, 1], F32)
    b_cf = Buf()
    c.dma("Y", P.conn_i[0:8, 0:1], P.hbm["i_l1_conn"], reads=[P.hbuf["i_l1_conn"]], writes=[P.b_conn])
    c.op("V", lambda e: e.tensor_copy(connf[:, :], P.conn_i[0:8, 0:1]), reads=[P.b_conn], writes=[b_cf])
    io8 = c.sb("l1f_io8", [8, 8], I32)
    b_io8 = Buf()
    c.op("P", lambda e: e.iota(io8[:, :], [[1, 8]], channel_multiplier=0,
                               allow_small_or_imprecise_dtypes=True), writes=[b_io8])
    iof = c.sb("l1f_iof", [8, 8], F32)
    b_iof = Buf()
    c.op("V", lambda e: e.tensor_copy(iof[:, :], io8[:, :]), reads=[b_io8], writes=[b_iof])
    P8 = c.sb("l1f_P8", [8, 8], BF16)
    b_P8 = Buf()
    c.op("V", lambda e: e.tensor_scalar(P8[:, :], iof[:, :], connf[0:8, 0:1], None, ALU.is_equal),
         reads=[b_iof, b_cf], writes=[b_P8])
    p81 = c.ps("l1f_p81", [8, 8], BF16)
    b_p81 = Buf()
    c.transpose(p81[0:8, 0:1], oh_bf[0:1, 0:8], P.ident_bf[0:1, 0:1], reads=[b_ohbf, P.b_ident],
                writes=[b_p81])
    ohc = c.sb("l1f_ohc", [8, 1], BF16)
    b_ohc = Buf()
    c.op("V", lambda e: e.tensor_copy(ohc[:, :], p81[0:8, 0:1]), reads=[b_p81], writes=[b_ohc])
    p82 = c.ps("l1f_p82", [8, 1])
    b_p82 = Buf()
    c.mm_group([(p82[0:8, 0:1], P8[:, :], ohc[:, :], True, True)], reads=[b_P8, b_ohc], writes=[b_p82])

    oh2f = c.sb("l1f_oh2f", [8, 1], F32)
    b_oh2f = Buf()
    c.op("V", lambda e: e.tensor_copy(oh2f[:, :], p82[0:8, 0:1]), reads=[b_p82], writes=[b_oh2f])
    d8 = c.sb("l1f_d8", [8, 8], BF16)
    b_d8 = Buf()
    c.op("V", lambda e: e.tensor_scalar_mul(d8[:, :], P.ident_bf[0:8, 0:8], oh2f[0:8, 0:1]),
         reads=[P.b_ident, b_oh2f], writes=[b_d8])
    ones8 = c.sb("l1f_o8", [8, 128], BF16)
    b_o8 = Buf()
    c.op("V", lambda e: e.memset(ones8[:, :], 1.0), writes=[b_o8])
    pbc8 = c.ps("l1f_pbc8", [128, 8])
    b_pbc8 = Buf()
    c.mm_group([(pbc8[:, :], ones8[:, :], d8[:, :], True, True)], reads=[b_o8, b_d8], writes=[b_pbc8])
    c.op("V", lambda e: e.tensor_copy(ohbc[:, :], pbc8[:, :]), reads=[b_pbc8], writes=[b_ohbc])
    bdn = c.sb("l1f_bdn", [8, 64], F32)
    b_bdn = Buf()
    bun = c.sb("l1f_bun", [8, 1024], F32)
    b_bun = Buf()
    c.dma("Y", bdn[:, :], P.hbm["r_l1_bd"], reads=[P.hbuf["r_l1_bd"]], writes=[b_bdn])
    c.dma("Y", bun[:, :], P.hbm["r_l1_bu"], reads=[P.hbuf["r_l1_bu"]], writes=[b_bun])
    pb1 = c.ps("l1f_pb1", [128, 2])
    b_pb1 = Buf()
    c.mm_group([(pb1[0:64, 0:1], bdn[:, :], oh2f[:, :], True, True)],
               reads=[b_bdn, b_oh2f], writes=[b_pb1])
    c.op("V", lambda e: e.tensor_copy(bdu[0:64, 0:1], pb1[0:64, 0:1]), reads=[b_pb1], writes=[b_bdu])
    for mc in range(8):
        c.mm_group([(pb1[:, 1:2], bun[:, mc * 128:(mc + 1) * 128], oh2f[:, :], True, True)],
                   reads=[b_bun, b_oh2f], writes=[b_pb1])
        c.op("V", lambda e, m=mc: e.tensor_copy(bdu[:, 1 + m:2 + m], pb1[:, 1:2]),
             reads=[b_pb1], writes=[b_bdu])
    c.pop_scope()
    wd_all = c.sb("l1f_wda", [128, 4096], BF16)
    b_wda = Buf()
    wu_all = c.sb("l1f_wua", [64, 8192], BF16)
    b_wua = Buf()
    c.dma("P", wd_all[:, :], P.hbm["w_l1_dn"][:, :], reads=[P.hbuf["w_l1_dn"]], writes=[b_wda])
    c.dma("P", wu_all[:, :], P.hbm["w_l1_up"][:, :], reads=[P.hbuf["w_l1_up"]], writes=[b_wua])
    for ei in range(8):
        c.op("V", lambda e, E=ei: e.tensor_scalar_mul(
            wd_all[:, E * 512:(E + 1) * 512], wd_all[:, E * 512:(E + 1) * 512], ohbc[:, E:E + 1]),
            reads=[b_ohbc], writes=[b_wda])
        c.op("V", lambda e, E=ei: e.tensor_scalar_mul(
            wu_all[:, E * 1024:(E + 1) * 1024], wu_all[:, E * 1024:(E + 1) * 1024], ohbc[0:64, E:E + 1]),
            reads=[b_ohbc], writes=[b_wua])

    base = c.sb("l1f_base", [128, DC * T], BF16)
    b_base = [Buf() for _ in range(DC)]
    gemm_fm(P, "w_l1_base", 8, range(8), rhs_xn, b_xn,
            lambda mc, tt, ps, bp: c.op("V", lambda e, m=mc, t=tt, p=ps: e.tensor_scalar(
                base[:, m * T + t * TT: m * T + (t + 1) * TT], p[:, :],
                P.t_c_l1_bb[:, m:m + 1], None, ALU.add),
                reads=[bp, P.b_c_l1_bb], writes=[b_base[mc]]), pool=gpool)

    mid = c.sb("l1f_mid", [64, T], BF16)
    b_mid = Buf()
    pm = [c.ps(f"l1f_pm{i}", [128, TT]) for i in range(2)]
    b_pm = [Buf(), Buf()]
    for tt in range(NTT):
        ps, bp = pm[tt % 2], b_pm[tt % 2]
        items = []
        n = 0
        for ei in range(8):
            for kc in range(8):
                items.append((ps[0:64, :], wd_all[:, (ei * 8 + kc) * 64:(ei * 8 + kc + 1) * 64],
                              rhs_xn(kc, tt), n == 0, n == 63))
                n += 1
        c.mm_group(items, reads=[b_wda] + b_xn, writes=[bp])
        c.op("S", lambda e, t=tt, p=ps: e.activation(
            mid[:, t * TT:(t + 1) * TT], p[0:64, :], AF.Relu, bias=bdu[0:64, 0:1]),
            reads=[bp, b_bdu], writes=[b_mid])

    tmp = c.sb("l1f_tmp", [128, TT], F32)
    b_tmp = Buf()
    gt = c.sb("l1f_gt", [128, TT], BF16)
    b_gt = Buf()
    for mc in range(8):
        for tt in range(NTT):
            ps, bp = pm[tt % 2], b_pm[tt % 2]
            items = []
            for ei in range(8):
                items.append((ps[:, :], wu_all[:, (ei * 8 + mc) * 128:(ei * 8 + mc + 1) * 128],
                              mid[:, tt * TT:(tt + 1) * TT], ei == 0, ei == 7))
            c.mm_group(items, reads=[b_wua, b_mid], writes=[bp])
            c.op("V", lambda e, m=mc, t=tt, p=ps: e.scalar_tensor_tensor(
                tmp[:, :], p[:, :], bdu[:, 1 + m:2 + m],
                base[:, m * T + t * TT: m * T + (t + 1) * TT], ALU.add, ALU.add),
                reads=[bp, b_base[mc], b_bdu], writes=[b_tmp])
            c.op("S", lambda e: e.activation(gt[:, :], tmp[:, :], AF.Silu),
                 reads=[b_tmp], writes=[b_gt])
            c.dma("Y", g_hbm[:, mc * T + tt * TT: mc * T + (tt + 1) * TT], gt[:, :],
                  reads=[b_gt], writes=[b_gh[mc]])
    c.pop_scope()
    c.pop_scope()

    emit_combine(P, x_hbm, x_buf, P.hbm["sp_s1"], b_sh, P.hbm["sp_g1"], b_gh,
                 out_hbm, out_buf, collect_stats=collect)



def emit_l2(P, x_hbm, x_buf, out_hbm, out_buf, pre_stats=False, collect=False):
    c = P.ctx
    c.push_scope()
    s_hbm = P.dram_scratch("sp_s2", [128, DC * T], F32)
    b_sh = P.hbuf8["sp_s2"]
    g_hbm = P.dram_scratch("sp_g2", [128, DC * T], BF16)
    b_gh = P.hbuf8["sp_g2"]
    xn_hbm = P.dram_scratch("sp_xn2", [128, DC * T], BF16)
    b_xnh = P.hbuf8["sp_xn2"]

    c.push_scope()
    xn2 = c.sb("l2_xn2", [128, DC * T], BF16)
    b_xn2 = [Buf() for _ in range(DC)]
    c.push_scope()
    xn = c.sb("l2_xn", [128, DC * T], BF16)
    b_xn = [Buf() for _ in range(DC)]
    emit_layernorm2(P, x_hbm, x_buf, xn, b_xn, pre_stats=pre_stats)
    for ch in range(DC):
        c.dma("Y", xn_hbm[:, ch * T:(ch + 1) * T], xn[:, ch * T:(ch + 1) * T],
              reads=[b_xn[ch]], writes=[b_xnh[ch]])
    emit_layernorm2(P, None, None, xn2, b_xn2,
                    gamma_col=(P.t_c_l2_scale[:, :], P.b_c_l2_scale),
                    beta_col=(P.t_c_l2_bias[:, :], P.b_c_l2_bias),
                    src_sb=(xn, b_xn))
    c.pop_scope()

    def rhs_xn2(kc, tt):
        return xn2[:, kc * T:][:, tt * TT:(tt + 1) * TT]

    v_tm = c.sb("l2_v", [128, 16 * T], BF16)
    b_v = [Buf() for _ in range(16)]
    bhv = c.sb("l2_bhv", [1, 2048], BF16)
    b_bhv = Buf()
    c.dma("P", bhv[:, :], P.hbm["r_l2_bhv"], reads=[P.hbuf["r_l2_bhv"]], writes=[b_bhv])
    c.push_scope()
    wv = c.sb("l2_wv", [128, 2 * 4096], BF16)
    b_wv = [Buf(), Buf()]
    pv = [c.ps(f"l2_pv{i}", [128, TT]) for i in range(2)]
    b_pv = [Buf(), Buf()]
    for nb in range(4):
        pb = nb % 2
        c.dma("P", wv[:, pb * 4096:(pb + 1) * 4096].rearrange("p (k n) -> p k n", k=8),
              P.hbm["w_l2_hv"].rearrange("p (k n) -> p k n", k=8)[:, :, nb * TT:(nb + 1) * TT],
              reads=[P.hbuf["w_l2_hv"]], writes=[b_wv[pb]])
        for kt in range(16):
            ps, bp = pv[kt % 2], b_pv[kt % 2]
            items = [(ps[:, :], P.ones_row_bf[0:1, 0:128],
                      bhv[0:1, nb * TT:(nb + 1) * TT], True, False)]
            for kc in range(DC):
                items.append((ps[:, :], xn2[:, kc * T + kt * 128: kc * T + (kt + 1) * 128],
                              wv[:, pb * 4096 + kc * TT: pb * 4096 + (kc + 1) * TT],
                              False, kc == DC - 1))
            c.mm_group(items, reads=b_xn2 + [b_wv[pb], b_bhv, P.b_ones_row], writes=[bp])
            c.op("S", lambda e, K=kt, n=nb, p=ps: e.activation(
                v_tm[:, K * T + n * TT: K * T + (n + 1) * TT], p[:, :], AF.Silu),
                reads=[bp], writes=[b_v[kt]])
    c.pop_scope()

    qb_sb = c.sb("l2_q", [128, T], BF16)
    kb_sb = c.sb("l2_k", [128, T], BF16)
    b_qb, b_kb = Buf(), Buf()
    qkf = c.sb("l2_qkf", [128, TT], F32)
    b_qkf = Buf()

    def ev_qk(mc, tt, ps, bp):
        c.op("S", lambda e, p=ps: e.activation(qkf[:, :], p[:, :], AF.Silu,
                                               bias=P.t_c_l2_bqk[:, 0:1]),
             reads=[bp, P.b_c_l2_bqk], writes=[b_qkf])
        c.op("V", lambda e, t=tt: e.tensor_scalar(
            qb_sb[:, t * TT:(t + 1) * TT], qkf[:, :], P.t_c_l2_osw[:, 0:1],
            P.t_c_l2_osb[:, 0:1], ALU.mult, ALU.add),
            reads=[b_qkf, P.b_c_l2_osw, P.b_c_l2_osb], writes=[b_qb])
        c.op("V", lambda e, t=tt: e.tensor_scalar(
            kb_sb[:, t * TT:(t + 1) * TT], qkf[:, :], P.t_c_l2_osw[:, 1:2],
            P.t_c_l2_osb[:, 1:2], ALU.mult, ALU.add),
            reads=[b_qkf, P.b_c_l2_osw, P.b_c_l2_osb], writes=[b_kb])
    c.push_scope()
    gemm_fm(P, "w_l2_qk", 8, range(1), rhs_xn2, b_xn2, ev_qk)
    c.pop_scope()

    relu2 = c.sb("l2_r2", [128, 16 * TT], BF16)
    b_r2 = [Buf() for _ in range(16)]
    og = c.sb("l2_og", [128, 16 * TT], BF16)
    b_og = [Buf() for _ in range(16)]
    tsc = c.sb("l2_tsc", [128, 2 * TT], BF16)
    b_tsc = [Buf(), Buf()]
    gsc = c.sb("l2_gsc", [128, 2 * TT], F32)
    b_gsc = [Buf(), Buf()]
    s_sb = c.sb("l2_s", [128, 2 * TT], F32)
    b_s = [Buf(), Buf()]
    psim = [c.ps(f"l2_psim{i}", [128, TT]) for i in range(2)]
    b_psim = [Buf(), Buf()]
    pav = c.ps("l2_pav", [128, TT])
    b_pav = Buf()
    pg = c.ps("l2_pg", [128, TT])
    b_pg = Buf()
    wgbuf = c.sb("l2_wg", [128, 2 * 1024], BF16)
    b_wg = [Buf(), Buf()]
    wobuf = c.sb("l2_wo", [128, 2 * 2048], BF16)
    b_wo = [Buf(), Buf()]
    pso = [c.ps(f"l2_pso{i}", [128, TT]) for i in range(2)]
    b_pso = [Buf(), Buf()]
    for qb in range(NTT):
        for kb in range(16):
            ps, bp = psim[kb % 2], b_psim[kb % 2]
            w2_ = slice((kb % 2) * TT, (kb % 2 + 1) * TT)
            c.mm_group([(ps[:, :], kb_sb[:, kb * 128:(kb + 1) * 128],
                         qb_sb[:, qb * TT:(qb + 1) * TT], True, True)],
                       reads=[b_kb, b_qb], writes=[bp])
            c.op("S", lambda e, p=ps, W=w2_: e.activation(tsc[:, W], p[:, :], AF.Relu,
                                                          scale=1.0 / T),
                 reads=[bp], writes=[b_tsc[kb % 2]])
            c.op("S", lambda e, K=kb, W=w2_: e.activation(relu2[:, K * TT:(K + 1) * TT],
                                                          tsc[:, W], AF.Square),
                 reads=[b_tsc[kb % 2]], writes=[b_r2[kb]])
        for vd in range(16):
            pb = vd % 2
            c.dma("P", wgbuf[:, pb * 1024:(pb + 1) * 1024],
                  P.hbm["w_l2_hg"][:, vd * 1024:(vd + 1) * 1024],
                  reads=[P.hbuf["w_l2_hg"]], writes=[b_wg[pb]])
            wv2 = slice((vd % 2) * TT, (vd % 2 + 1) * TT)
            items = []
            for kb in range(16):
                items.append((pav[:, :], v_tm[:, kb * T + vd * 128: kb * T + (vd + 1) * 128],
                              relu2[:, kb * TT:(kb + 1) * TT], kb == 0, kb == 15))
            c.mm_group(items, reads=b_v + b_r2, writes=[b_pav])
            itg = []
            for kc in range(DC):
                itg.append((pg[:, :], wgbuf[:, pb * 1024 + kc * 128: pb * 1024 + (kc + 1) * 128],
                            rhs_xn2(kc, qb), kc == 0, kc == DC - 1))
            c.mm_group(itg, reads=b_xn2 + [b_wg[pb]], writes=[b_pg])
            c.op("S", lambda e, m=vd, W=wv2: e.activation(gsc[:, W], pg[:, :], AF.Silu,
                                                          bias=P.t_c_l2_bhg[:, m:m + 1]),
                 reads=[b_pg, P.b_c_l2_bhg], writes=[b_gsc[vd % 2]])
            c.op("V", lambda e, m=vd, W=wv2: e.tensor_mul(og[:, m * TT:(m + 1) * TT],
                                                          pav[:, :], gsc[:, W]),
                 reads=[b_pav, b_gsc[vd % 2]], writes=[b_og[vd]])
        for mc in range(DC):
            pb = mc % 2
            c.dma("P", wobuf[:, pb * 2048:(pb + 1) * 2048],
                  P.hbm["w_l2_out"][:, mc * 2048:(mc + 1) * 2048],
                  reads=[P.hbuf["w_l2_out"]], writes=[b_wo[pb]])
            items = []
            for kc in range(16):
                items.append((pso[mc % 2][:, :], wobuf[:, pb * 2048 + kc * 128: pb * 2048 + (kc + 1) * 128],
                              og[:, kc * TT:(kc + 1) * TT], kc == 0, kc == 15))
            c.mm_group(items, reads=b_og + [b_wo[pb]], writes=[b_pso[mc % 2]])
            ws2 = slice((mc % 2) * TT, (mc % 2 + 1) * TT)
            c.op("V", lambda e, m=mc, p=pso[mc % 2], W=ws2: e.tensor_scalar(
                s_sb[:, W], p[:, :], P.t_c_l2_bout[:, m:m + 1], None, ALU.add),
                reads=[b_pso[mc % 2], P.b_c_l2_bout], writes=[b_s[mc % 2]])
            c.dma("Y", s_hbm[:, mc * T + qb * TT: mc * T + (qb + 1) * TT], s_sb[:, ws2],
                  reads=[b_s[mc % 2]], writes=[b_sh[mc]])
    c.pop_scope()

    c.push_scope()
    xn = c.sb("l2f_xn", [128, DC * T], BF16)
    b_xn = [Buf() for _ in range(DC)]
    for ch in range(DC):
        c.dma("Y", xn[:, ch * T:(ch + 1) * T], xn_hbm[:, ch * T:(ch + 1) * T],
              reads=[b_xnh[ch]], writes=[b_xn[ch]])
    gate_f8 = c.sb("l2f_g8", [8, 16 * 128], BF16)
    b_g8 = Buf()
    gate_hbm = P.dram_scratch("sp_l2gate", [8, T], BF16)
    b_ghb = P.hbuf["sp_l2gate"]
    grow = c.sb("l2f_grow", [1, 2 * TT], BF16)
    b_grow = [Buf(), Buf()]
    c.push_scope()
    wgt = c.sb("l2f_wgt", [128, 64], BF16)
    b_wgt = Buf()
    c.dma("P", wgt[:, :], P.hbm["w_l2_gw"], reads=[P.hbuf["w_l2_gw"]], writes=[b_wgt])
    gbr = c.sb("l2f_gbr", [1, 8], F32)
    b_gbr = Buf()
    c.dma("Y", gbr[:, :], P.hbm["r_l2_gb"], reads=[P.hbuf["r_l2_gb"]], writes=[b_gbr])
    pgl = c.ps("l2f_pgl", [128, 8])
    b_pgl = Buf()
    pbb = c.ps("l2f_pbb", [128, 8])
    b_pbb = Buf()
    biasbc = c.sb("l2f_bb", [128, 8], F32)
    b_bb = Buf()
    c.mm_group([(pbb[:, :], P.ones_row[0:1, :], gbr[0:1, :], True, True)],
               reads=[b_gbr, P.b_ones_row], writes=[b_pbb])
    c.op("V", lambda e: e.tensor_copy(biasbc[:, :], pbb[:, :]), reads=[b_pbb], writes=[b_bb])
    gsm = c.sb("l2f_gsm", [128, 32], F32)
    b_gs = Buf()
    gti = c.sb("l2f_gti", [128, 8], BF16)
    b_gti = Buf()
    ptr = c.ps("l2f_ptr", [8, 128], BF16)
    b_ptr = Buf()
    for tk in range(16):
        items = []
        for kc in range(DC):
            items.append((pgl[:, :], xn[:, kc * T + tk * 128: kc * T + (tk + 1) * 128],
                          wgt[:, kc * 8:(kc + 1) * 8], kc == 0, kc == DC - 1))
        c.mm_group(items, reads=b_xn + [b_wgt], writes=[b_pgl])
        c.op("V", lambda e: e.reduce_max(gsm[:, 16:17], pgl[:, :], axis=AX.X),
             reads=[b_pgl], writes=[b_gs])
        c.op("V", lambda e: e.tensor_scalar_mul(gsm[:, 17:18], gsm[:, 16:17], -1.0),
             reads=[b_gs], writes=[b_gs])
        c.op("S", lambda e: e.activation(gsm[:, 0:8], pgl[:, :], AF.Exp, bias=gsm[:, 17:18]),
             reads=[b_pgl, b_gs], writes=[b_gs])
        c.op("V", lambda e: e.reduce_sum(gsm[:, 18:19], gsm[:, 0:8], axis=AX.X),
             reads=[b_gs], writes=[b_gs])
        c.op("V", lambda e: e.reciprocal(gsm[:, 19:20], gsm[:, 18:19]),
             reads=[b_gs], writes=[b_gs])
        c.op("V", lambda e: e.tensor_scalar_mul(gsm[:, 0:8], gsm[:, 0:8], gsm[:, 19:20]),
             reads=[b_gs], writes=[b_gs])
        c.op("V", lambda e: e.tensor_add(gsm[:, 8:16], gsm[:, 0:8], biasbc[:, :]),
             reads=[b_gs, b_bb], writes=[b_gs])
        c.op("V", lambda e: e.reduce_max(gsm[:, 16:17], gsm[:, 8:16], axis=AX.X),
             reads=[b_gs], writes=[b_gs])
        c.op("V", lambda e: e.tensor_scalar(gsm[:, 20:28], gsm[:, 8:16], gsm[:, 16:17],
                                            None, ALU.is_ge), reads=[b_gs], writes=[b_gs])
        c.op("V", lambda e: e.scalar_tensor_tensor(gsm[:, 20:28], gsm[:, 20:28], -1e30,
                                                   gsm[:, 8:16], ALU.mult, ALU.add),
             reads=[b_gs], writes=[b_gs])
        c.op("V", lambda e: e.reduce_max(gsm[:, 16:17], gsm[:, 20:28], axis=AX.X),
             reads=[b_gs], writes=[b_gs])
        c.op("V", lambda e: e.tensor_scalar(gsm[:, 20:28], gsm[:, 8:16], gsm[:, 16:17],
                                            None, ALU.is_ge), reads=[b_gs], writes=[b_gs])
        c.op("V", lambda e: e.tensor_mul(gti[:, :], gsm[:, 0:8], gsm[:, 20:28]),
             reads=[b_gs], writes=[b_gti])
        c.transpose(ptr[0:8, :], gti[:, :], P.ident_bf[:, :], reads=[b_gti, P.b_ident],
                    writes=[b_ptr])
        c.op("V", lambda e, K=tk: e.tensor_copy(gate_f8[:, K * 128:(K + 1) * 128], ptr[0:8, :]),
             reads=[b_ptr], writes=[b_g8])
    c.pop_scope()
    c.dma("Y", gate_hbm[:, :], gate_f8[:, :], reads=[b_g8], writes=[b_ghb])

    h_bf = c.sb("l2f_h", [128, DC * T], BF16)
    b_h = [Buf() for _ in range(DC)]
    yacc = c.sb("l2f_y", [128, DC * T], F32)
    b_y = [Buf() for _ in range(DC)]
    gbc = c.sb("l2f_gbc", [128, TT], F32)
    b_gbc = Buf()
    tmp1 = c.sb("l2f_t1", [128, 2 * TT], F32)
    b_t1 = [Buf(), Buf()]
    tmp2 = c.sb("l2f_t2", [128, 2 * TT], F32)
    b_t2 = [Buf(), Buf()]
    w13 = c.sb("l2f_w13", [128, 4 * 1024], BF16)
    b_w13 = [Buf(), Buf(), Buf(), Buf()]
    w2b = c.sb("l2f_w2b", [128, 2 * 1024], BF16)
    b_w2b = [Buf(), Buf()]
    pb1 = [c.ps(f"l2f_ph{i}", [128, TT]) for i in range(2)]
    b_pb1 = [Buf(), Buf()]
    pb3 = [c.ps(f"l2f_ph3{i}", [128, TT]) for i in range(2)]
    b_pb3 = [Buf(), Buf()]
    pyy = [c.ps(f"l2f_py{i}", [128, TT]) for i in range(2)]
    b_py = [Buf(), Buf()]
    pgb = c.ps("l2f_pgb", [128, TT])
    b_pgb = Buf()

    def rhs_xnf(kc, tt):
        return xn[:, kc * T:][:, tt * TT:(tt + 1) * TT]

    def rhs_h(kc, tt):
        return h_bf[:, kc * T:][:, tt * TT:(tt + 1) * TT]

    gbc4 = c.sb("l2f_gbc4", [128, NTT * TT], F32)
    b_gbc4 = [Buf() for _ in range(NTT)]
    for eb in range(10):
        shared = eb >= 8
        sh = eb - 8
        if not shared:
            for tt in range(NTT):
                gw = slice((tt % 2) * TT, (tt % 2 + 1) * TT)
                c.dma("Y", grow[0:1, gw], gate_hbm[eb:eb + 1, tt * TT:(tt + 1) * TT],
                      reads=[b_ghb], writes=[b_grow[tt % 2]])
                c.mm_group([(pgb[:, :], P.ones_row_bf[0:1, :], grow[0:1, gw], True, True)],
                           reads=[b_grow[tt % 2], P.b_ones_row], writes=[b_pgb])
                c.op("V", lambda e, t=tt: e.tensor_copy(gbc4[:, t * TT:(t + 1) * TT], pgb[:, :]),
                     reads=[b_pgb], writes=[b_gbc4[tt]])
        for ic in range(DC):
            pb = ic % 2
            if shared:
                c.dma("P", w13[:, pb * 1024:(pb + 1) * 1024],
                      P.hbm["w_l2_sw1"][:, (sh * 8 + ic) * 1024:(sh * 8 + ic + 1) * 1024],
                      reads=[P.hbuf["w_l2_sw1"]], writes=[b_w13[pb]])
                c.dma("P", w13[:, 2048 + pb * 1024: 2048 + (pb + 1) * 1024],
                      P.hbm["w_l2_sw3"][:, (sh * 8 + ic) * 1024:(sh * 8 + ic + 1) * 1024],
                      reads=[P.hbuf["w_l2_sw3"]], writes=[b_w13[2 + pb]])
            else:
                c.dma("P", w13[:, pb * 1024:(pb + 1) * 1024],
                      P.hbm["w_l2_w1"][:, eb * 8192 + ic * 1024: eb * 8192 + (ic + 1) * 1024],
                      reads=[P.hbuf["w_l2_w1"]], writes=[b_w13[pb]])
                c.dma("P", w13[:, 2048 + pb * 1024: 2048 + (pb + 1) * 1024],
                      P.hbm["w_l2_w3"][:, eb * 8192 + ic * 1024: eb * 8192 + (ic + 1) * 1024],
                      reads=[P.hbuf["w_l2_w3"]], writes=[b_w13[2 + pb]])
            for tt in range(NTT):
                p1, bp1s = pb1[tt % 2], b_pb1[tt % 2]
                p3, bp3s = pb3[tt % 2], b_pb3[tt % 2]
                it1, it3 = [], []
                for kc in range(DC):
                    it1.append((p1[:, :], w13[:, pb * 1024 + kc * 128: pb * 1024 + (kc + 1) * 128],
                                rhs_xnf(kc, tt), kc == 0, kc == DC - 1))
                    it3.append((p3[:, :], w13[:, 2048 + pb * 1024 + kc * 128: 2048 + pb * 1024 + (kc + 1) * 128],
                                rhs_xnf(kc, tt), kc == 0, kc == DC - 1))
                c.mm_group(it1, reads=b_xn + [b_w13[pb]], writes=[bp1s])
                c.mm_group(it3, reads=b_xn + [b_w13[2 + pb]], writes=[bp3s])
                wl = slice((tt % 2) * TT, (tt % 2 + 1) * TT)
                c.op("S", lambda e, p=p1, W=wl: e.activation(tmp1[:, W], p[:, :], AF.Silu),
                     reads=[bp1s], writes=[b_t1[tt % 2]])
                if not shared:
                    c.op("P", lambda e, t=tt, W=wl: e.tensor_mul(tmp2[:, W], tmp1[:, W],
                                                                 gbc4[:, t * TT:(t + 1) * TT]),
                         reads=[b_t1[tt % 2], b_gbc4[tt]], writes=[b_t2[tt % 2]])
                    tsrc, btsrc = tmp2, b_t2[tt % 2]
                else:
                    tsrc, btsrc = tmp1, b_t1[tt % 2]
                c.op("V", lambda e, i=ic, t=tt, p=p3, ts=tsrc, W=wl: e.scalar_tensor_tensor(
                    h_bf[:, i * T + t * TT: i * T + (t + 1) * TT], p[:, :], 1.0, ts[:, W],
                    ALU.mult, ALU.mult),
                    reads=[bp3s, btsrc], writes=[b_h[ic]])
        for mc in range(DC):
            pb = mc % 2
            if shared:
                c.dma("P", w2b[:, pb * 1024:(pb + 1) * 1024],
                      P.hbm["w_l2_sw2"][:, mc * 2048 + sh * 1024: mc * 2048 + (sh + 1) * 1024],
                      reads=[P.hbuf["w_l2_sw2"]], writes=[b_w2b[pb]])
            else:
                c.dma("P", w2b[:, pb * 1024:(pb + 1) * 1024],
                      P.hbm["w_l2_w2"][:, eb * 8192 + mc * 1024: eb * 8192 + (mc + 1) * 1024],
                      reads=[P.hbuf["w_l2_w2"]], writes=[b_w2b[pb]])
            for tt in range(NTT):
                py, bpy = pyy[tt % 2], b_py[tt % 2]
                items = []
                for kc in range(DC):
                    items.append((py[:, :], w2b[:, pb * 1024 + kc * 128: pb * 1024 + (kc + 1) * 128],
                                  rhs_h(kc, tt), kc == 0, kc == DC - 1))
                c.mm_group(items, reads=b_h + [b_w2b[pb]], writes=[bpy])
                if eb == 0:
                    c.op("V", lambda e, m=mc, t=tt, p=py: e.tensor_copy(
                        yacc[:, m * T + t * TT: m * T + (t + 1) * TT], p[:, :]),
                        reads=[bpy], writes=[b_y[mc]])
                else:
                    c.op("V", lambda e, m=mc, t=tt, p=py: e.tensor_add(
                        yacc[:, m * T + t * TT: m * T + (t + 1) * TT],
                        yacc[:, m * T + t * TT: m * T + (t + 1) * TT], p[:, :]),
                        reads=[bpy, b_y[mc]], writes=[b_y[mc]])
    gtile = c.sb("l2f_gt", [128, 2 * TT], BF16)
    b_gt2 = [Buf(), Buf()]
    for mc in range(DC):
        for tt in range(NTT):
            wl = slice((tt % 2) * TT, (tt % 2 + 1) * TT)
            c.op("S", lambda e, m=mc, t=tt, W=wl: e.activation(
                gtile[:, W], yacc[:, m * T + t * TT: m * T + (t + 1) * TT], AF.Silu),
                reads=[b_y[mc]], writes=[b_gt2[tt % 2]])
            c.dma("Y", g_hbm[:, mc * T + tt * TT: mc * T + (tt + 1) * TT], gtile[:, wl],
                  reads=[b_gt2[tt % 2]], writes=[b_gh[mc]])
    c.pop_scope()
    c.pop_scope()
    emit_combine(P, x_hbm, x_buf, P.hbm["sp_s2"], b_sh, P.hbm["sp_g2"], b_gh,
                 out_hbm, out_buf, collect_stats=collect)



def emit_l3(P, x_hbm, x_buf, out_hbm, out_buf, pre_stats=False, collect=False):
    c = P.ctx
    CH = 64
    NC = T // CH
    c.push_scope()
    xn = c.sb("l3_xn", [128, DC * T], BF16)
    b_xn = [Buf() for _ in range(DC)]
    emit_layernorm2(P, x_hbm, x_buf, xn, b_xn, pre_stats=pre_stats)
    s_hbm = P.dram_scratch("sp_s3", [128, DC * T], F32)
    b_sh = P.hbuf8["sp_s3"]
    g_hbm = P.dram_scratch("sp_g3", [128, DC * T], BF16)
    b_gh = P.hbuf8["sp_g3"]

    def rhs_xn(kc, tt):
        return xn[:, kc * T:][:, tt * TT:(tt + 1) * TT]

    c.push_scope()
    TP = T + 4
    qkv = c.sb("l3_qkv", [128, 12 * TP], BF16)
    b_qkv = [Buf() for _ in range(12)]
    for i in range(12):
        c.op("V", lambda e, I=i: e.memset(qkv[:, I * TP:I * TP + 4], 0.0), writes=[b_qkv[i]])
    gate = c.sb("l3_gate", [128, DC * T], BF16)
    b_gate = [Buf() for _ in range(DC)]
    btm = c.sb("l3_btm", [128, 128], F32)
    b_btm = Buf()
    blk2e = c.sb("l3_blk2e", [128, 2], BF16)
    b_b2e = Buf()
    sel2e = c.sb("l3_sel2e", [2, 128], BF16)
    b_s2e = Buf()
    it_a = c.sb("l3_ita", [128, 2], I32)
    it_b = c.sb("l3_itb", [128, 2], I32)
    b_ita, b_itb = Buf(), Buf()
    c.op("P", lambda e: e.iota(it_a[:, 0:2], [[0, 2]], channel_multiplier=1,
                               allow_small_or_imprecise_dtypes=True), writes=[b_ita])
    c.op("P", lambda e: e.iota(it_b[:, 0:2], [[1, 2]], channel_multiplier=0,
                               allow_small_or_imprecise_dtypes=True), writes=[b_itb])
    c.op("V", lambda e: e.tensor_single_scalar(it_a[:, 0:2], it_a[:, 0:2], 6,
                                               ALU.arith_shift_right),
         reads=[b_ita], writes=[b_ita])
    c.op("V", lambda e: e.tensor_tensor(blk2e[:, :], it_a[:, 0:2], it_b[:, 0:2], ALU.is_equal),
         reads=[b_ita, b_itb], writes=[b_b2e])
    it_c = c.sb("l3_itc", [2, 128], I32)
    it_d = c.sb("l3_itd", [2, 128], I32)
    c.op("P", lambda e: e.iota(it_c[0:2, 0:128], [[1, 128]], channel_multiplier=0,
                               allow_small_or_imprecise_dtypes=True), writes=[b_ita])
    c.op("P", lambda e: e.iota(it_d[0:2, 0:128], [[0, 128]], channel_multiplier=1,
                               allow_small_or_imprecise_dtypes=True), writes=[b_itb])
    c.op("V", lambda e: e.tensor_single_scalar(it_c[0:2, 0:128], it_c[0:2, 0:128], 6,
                                               ALU.arith_shift_right),
         reads=[b_ita], writes=[b_ita])
    c.op("V", lambda e: e.tensor_tensor(sel2e[:, :], it_c[0:2, 0:128], it_d[0:2, 0:128],
                                        ALU.is_equal),
         reads=[b_ita, b_itb], writes=[b_s2e])

    c.push_scope()
    prj = c.sb("l3_prj", [128, TP], BF16)
    b_prj = Buf()
    c.op("V", lambda e: e.memset(prj[:, 0:4], 0.0), writes=[b_prj])
    diag = c.sb("l3_diag", [128, 4 * 128], BF16)
    b_diag = Buf()
    ppool = ([c.ps(f"l3_pp{i}", [128, TT]) for i in range(2)], [Buf(), Buf()])
    pcv = c.ps("l3_pcv", [128, TT])
    b_pcv = Buf()
    for name, wkey, nch, cwcol, cwbuf, qoff in [
            ("q", "w_l3_q", 8, "t_c_l3_cwq", "b_c_l3_cwq", 0),
            ("k", "w_l3_k", 2, "t_c_l3_cwk", "b_c_l3_cwk", 8),
            ("v", "w_l3_v", 2, "t_c_l3_cwv", "b_c_l3_cwv", 10)]:
        cw = getattr(P, cwcol)
        bcw = getattr(P, cwbuf)

        def ev_prj(mc, tt, ps, bp, qoff=qoff, cw=cw, bcw=bcw):
            c.op("V", lambda e, t=tt, p=ps: e.tensor_copy(prj[:, 4 + t * TT: 4 + (t + 1) * TT], p[:, :]),
                 reads=[bp], writes=[b_prj])
            if tt == NTT - 1:
                for j in range(4):
                    c.op("V", lambda e, m=mc, jj=j: e.tensor_scalar_mul(
                        diag[:, jj * 128:(jj + 1) * 128], P.ident_bf[:, :], cw[:, m * 4 + jj:m * 4 + jj + 1]),
                        reads=[P.b_ident, bcw], writes=[b_diag])
                for t2 in range(NTT):
                    items = []
                    for j in range(4):
                        sh = slice(4 + t2 * TT - (3 - j), 4 + (t2 + 1) * TT - (3 - j))
                        items.append((pcv[:, :], diag[:, j * 128:(j + 1) * 128], prj[:, sh],
                                      j == 0, j == 3))
                    c.mm_group(items, reads=[b_diag, b_prj], writes=[b_pcv])
                    ix = qoff + mc
                    c.op("V", lambda e, I=ix, t=t2: e.tensor_copy(
                        qkv[:, I * TP + 4 + t * TT: I * TP + 4 + (t + 1) * TT], pcv[:, :]),
                        reads=[b_pcv], writes=[b_qkv[ix]])
        gemm_fm(P, wkey, 8, range(nch), rhs_xn, b_xn, ev_prj, pool=ppool)
    gemm_fm(P, "w_l3_gate", 8, range(8), rhs_xn, b_xn,
            lambda mc, tt, ps, bp: c.op("S", lambda e, m=mc, t=tt, p=ps: e.activation(
                gate[:, m * T + t * TT: m * T + (t + 1) * TT], p[:, :], AF.Silu),
                reads=[bp], writes=[b_gate[mc]]), pool=ppool)

    bg = c.sb("l3_bg", [64, T], F32)
    b_bg = Buf()
    wbg = c.sb("l3_wbg", [128, 512], BF16)
    b_wbg = Buf()
    c.dma("P", wbg[:, :], P.hbm["w_l3_bg"], reads=[P.hbuf["w_l3_bg"]], writes=[b_wbg])
    pbg = c.ps("l3_pbg", [64, TT])
    b_pbg = Buf()
    for tt in range(NTT):
        items = []
        for kc in range(DC):
            items.append((pbg[:, :], wbg[:, kc * 64:(kc + 1) * 64], rhs_xn(kc, tt),
                          kc == 0, kc == DC - 1))
        c.mm_group(items, reads=b_xn + [b_wbg], writes=[b_pbg])
        c.op("V", lambda e, t=tt: e.tensor_copy(bg[:, t * TT:(t + 1) * TT], pbg[:, :]),
             reads=[b_pbg], writes=[b_bg])
    c.op("S", lambda e: e.activation(bg[0:4, :], bg[0:4, :], AF.Sigmoid,
                                     bias=P.t_c_l3_bb64[0:4, 0:1]),
         reads=[b_bg, P.b_c_l3_bb64], writes=[b_bg])
    gsc = c.sb("l3_gsc", [64, 2 * T], F32)
    b_gsc = Buf()
    c.op("S", lambda e: e.activation(gsc[32:64, 0:T], bg[32:64, :], AF.Abs,
                                     bias=P.t_c_l3_dt64[32:64, 0:1]),
         reads=[b_bg, P.b_c_l3_dt64], writes=[b_gsc])
    c.op("S", lambda e: e.activation(gsc[32:64, 0:T], gsc[32:64, 0:T], AF.Exp, scale=-1.0),
         reads=[b_gsc], writes=[b_gsc])
    c.op("S", lambda e: e.activation(gsc[32:64, 0:T], gsc[32:64, 0:T], AF.Ln, bias=1.0),
         reads=[b_gsc], writes=[b_gsc])
    c.op("S", lambda e: e.activation(gsc[32:64, T:2 * T], bg[32:64, :], AF.Relu,
                                     bias=P.t_c_l3_dt64[32:64, 0:1]),
         reads=[b_bg, P.b_c_l3_dt64], writes=[b_gsc])
    c.op("V", lambda e: e.tensor_add(bg[32:64, :], gsc[32:64, 0:T], gsc[32:64, T:2 * T]),
         reads=[b_gsc], writes=[b_bg])
    negA = c.sb("l3_negA", [64, 1], F32)
    b_nA = Buf()
    c.op("S", lambda e: e.activation(negA[32:64, :], P.t_c_l3_al64[32:64, 0:1], AF.Exp),
         reads=[P.b_c_l3_al64], writes=[b_nA])
    c.op("V", lambda e: e.tensor_scalar_mul(negA[32:64, :], negA[32:64, :], -1.0),
         reads=[b_nA], writes=[b_nA])
    c.op("V", lambda e: e.tensor_scalar_mul(bg[32:64, :], bg[32:64, :], negA[32:64, 0:1]),
         reads=[b_bg, b_nA], writes=[b_bg])
    zero4 = c.sb("l3_z4", [64, 1], F32)
    b_z4 = Buf()
    c.op("V", lambda e: e.memset(zero4[:, :], 0.0), writes=[b_z4])
    for ci in range(NC):
        c.op("V", lambda e, i=ci: e.tensor_tensor_scan(
            bg[32:64, i * CH:(i + 1) * CH], bg[32:64, i * CH:(i + 1) * CH],
            zero4[32:64, 0:1].broadcast_to([32, CH]), 0.0, ALU.add, ALU.add),
            reads=[b_bg, b_z4], writes=[b_bg])
    bounce = P.dram_scratch("sp_l3bg", [8, T], F32)
    b_bo = P.hbuf["sp_l3bg"]
    c.dma("Y", P.hbm["sp_l3bg"][0:4, :], bg[0:4, :], reads=[b_bg], writes=[b_bo])
    c.dma("Y", P.hbm["sp_l3bg"][4:8, :], bg[32:36, :], reads=[b_bg], writes=[b_bo])
    if DEBUG_SPILL:
        dbg2 = P.dram_scratch("sp_l3bg2", [64, T], F32)
        c.dma("Y", P.hbm["sp_l3bg2"], bg[:, :], reads=[b_bg], writes=[P.hbuf["sp_l3bg2"]])
    bgb_b = c.sb("l3_bgb_b", [4, T], BF16)
    bgb_g = c.sb("l3_bgb_g", [4, T], BF16)
    b_bgb = Buf()
    c.op("V", lambda e: e.tensor_copy(bgb_b[0:4, :], bg[0:4, :]), reads=[b_bg], writes=[b_bgb])
    c.op("V", lambda e: e.tensor_copy(bgb_g[0:4, :], bg[32:64, :][0:4, :]), reads=[b_bg], writes=[b_bgb])
    ptb = c.ps("l3_ptb", [128, 8], BF16)
    b_ptb = Buf()
    for tk in range(16):
        c.transpose(ptb[:, 0:4], bgb_b[0:4, tk * 128:(tk + 1) * 128], P.ident_bf[0:4, 0:4],
                    reads=[b_bgb, P.b_ident], writes=[b_ptb])
        c.op("V", lambda e, K=tk: e.tensor_copy(btm[:, K * 4:(K + 1) * 4], ptb[:, 0:4]),
             reads=[b_ptb], writes=[b_btm])
        c.transpose(ptb[:, 4:8], bgb_g[0:4, tk * 128:(tk + 1) * 128], P.ident_bf[0:4, 0:4],
                    reads=[b_bgb, P.b_ident], writes=[b_ptb])
        c.op("V", lambda e, K=tk: e.tensor_copy(btm[:, 64 + K * 4:64 + (K + 1) * 4], ptb[:, 4:8]),
             reads=[b_ptb], writes=[b_btm])

    if DEBUG_SPILL:
        dbt = P.dram_scratch("sp_l3btm", [128, 128], F32)
        c.dma("Y", P.hbm["sp_l3btm"], btm[:, :], reads=[b_btm], writes=[P.hbuf["sp_l3btm"]])
    nsq = c.sb("l3_nsq", [128, TT], BF16)
    b_nsq = Buf()
    nrst = c.sb("l3_nrst", [2, TT], F32)
    b_nrst = Buf()
    nrsb = c.sb("l3_nrsb", [2, TT], BF16)
    b_nrsb = Buf()
    pn2 = c.ps("l3_pn2", [2, TT])
    b_pn2 = Buf()
    pnb = c.ps("l3_pnb", [128, TT])
    b_pnb = Buf()
    for ix in range(10):
        for tt in range(NTT):
            sl = slice(ix * TP + 4 + tt * TT, ix * TP + 4 + (tt + 1) * TT)
            c.op("S", lambda e, s=sl: e.activation(nsq[:, :], qkv[:, s], AF.Square),
                 reads=[b_qkv[ix]], writes=[b_nsq])
            c.mm_group([(pn2[0:2, 0:TT], blk2e[:, :], nsq[:, :], True, True)],
                       reads=[b_nsq, b_b2e], writes=[b_pn2])
            c.op("V", lambda e: e.tensor_scalar_add(nrst[:, :], pn2[0:2, 0:TT], 1e-12),
                 reads=[b_pn2], writes=[b_nrst])
            c.op("S", lambda e: e.activation(nrst[:, :], nrst[:, :], AF.Ln),
                 reads=[b_nrst], writes=[b_nrst])
            c.op("S", lambda e: e.activation(nrst[:, :], nrst[:, :], AF.Exp, scale=-0.5),
                 reads=[b_nrst], writes=[b_nrst])
            c.op("V", lambda e: e.tensor_copy(nrsb[:, :], nrst[:, :]), reads=[b_nrst], writes=[b_nrsb])
            c.mm_group([(pnb[:, :], sel2e[:, :], nrsb[0:2, :], True, True)],
                       reads=[b_nrsb, b_s2e], writes=[b_pnb])
            c.op("V", lambda e, s=sl: e.tensor_mul(qkv[:, s], qkv[:, s], pnb[:, :]),
                 reads=[b_qkv[ix], b_pnb], writes=[b_qkv[ix]])
    c.pop_scope()

    mlo = c.sb("l3_mlo", [128, 192], BF16)
    b_ml = Buf()
    io64 = c.sb("l3_io64", [128, 64], I32)
    b_i64 = Buf()
    c.op("P", lambda e: e.iota(io64[:, :], [[1, 64]], channel_multiplier=-1,
                               allow_small_or_imprecise_dtypes=True), writes=[b_i64])
    c.op("V", lambda e: e.tensor_single_scalar(mlo[0:64, 0:64], io64[0:64, :], 0, ALU.is_lt),
         reads=[b_i64], writes=[b_ml])
    c.op("V", lambda e: e.tensor_single_scalar(mlo[0:64, 64:128], io64[0:64, :], 0, ALU.is_gt),
         reads=[b_i64], writes=[b_ml])
    c.op("V", lambda e: e.tensor_single_scalar(mlo[0:64, 128:192], io64[0:64, :], 0, ALU.is_ge),
         reads=[b_i64], writes=[b_ml])
    c.op("V", lambda e: e.tensor_scalar_mul(mlo[0:64, 0:128], mlo[0:64, 0:128], -1.0),
         reads=[b_ml], writes=[b_ml])
    rowbuf2 = [c.sb(f"l3_rowbuf{i}", [1, 576], F32) for i in range(2)]
    b_fl2 = [Buf(), Buf()]

    o_fm = c.sb("l3_o", [128, DC * T], BF16)
    b_o = [Buf() for _ in range(DC)]
    Sf = c.sb("l3_Sf", [64, 4 * 64], F32)
    b_Sf = [Buf() for _ in range(4)]
    Sb = c.sb("l3_Sb", [64, 4 * 64], BF16)
    b_Sb = [Buf() for _ in range(4)]
    for h in range(4):
        c.op("V", lambda e, H=h: e.memset(Sf[:, H * 64:(H + 1) * 64], 0.0), writes=[b_Sf[h]])
        c.op("V", lambda e, H=h: e.memset(Sb[:, H * 64:(H + 1) * 64], 0.0), writes=[b_Sb[h]])

    scr4 = [c.sb(f"l3_scr{k}", [64, 9 * 64], F32) for k in range(4)]
    b_scr4 = [Buf() for _ in range(4)]
    sbb4 = [c.sb(f"l3_sbb{k}", [64, 15 * 64], BF16) for k in range(4)]
    b_sbb4 = [Buf() for _ in range(4)]
    col4 = [c.sb(f"l3_col{k}", [64, 8], F32) for k in range(4)]
    b_col4 = [Buf() for _ in range(4)]
    rb4 = [c.sb(f"l3_rb{k}", [1, 64], F32) for k in range(4)]
    b_rb4 = [Buf() for _ in range(4)]
    pf4 = [[c.ps(f"l3_pf{k}_{i}", [128, TT]) for i in range(2)] for k in range(4)]
    b_pf4 = [[Buf(), Buf()] for k in range(4)]

    def kq_slice(idx, rh, ci):
        return qkv[:, idx * TP + 4:][:, ci * CH:(ci + 1) * CH][64 * rh:64 * rh + 64, :]

    def emit_kv_instance(ci, kv, tk, sub, rowbuf, b_fl):
        kidx, krh = 8 + kv // 2, kv % 2
        scr, b_scrL = scr4[kv], b_scr4[kv]
        sbb, b_sbbL = sbb4[kv], b_sbb4[kv]
        col, b_colL = col4[kv], b_col4[kv]
        rbk, b_rbk = rb4[kv], b_rb4[kv]
        pA, pB = pf4[kv]
        bA, bB = b_pf4[kv][0], b_pf4[kv][1]
        pAb = pA[:, :].bitcast(BF16)
        pBb = pB[:, :].bitcast(BF16)

        def S(slot):
            return sbb[:, slot * 64:(slot + 1) * 64]

        def R(slot):
            return scr[:, slot * 64:(slot + 1) * 64]

        def stg(slot, idx, rh):
            c.op("V", lambda e, s=slot, i=idx, r=rh, CI=ci: e.tensor_copy(
                S(s), kq_slice(i, r, CI)), reads=[b_qkv[idx]], writes=[b_sbbL])
            return S(slot)

        gf = rowbuf[0:1, 256 + kv * CH: 256 + (kv + 1) * CH]
        nb = rowbuf[0:1, kv * CH: (kv + 1) * CH]
        gc1 = rowbuf[0:1, 256 + kv * CH + CH - 1: 256 + (kv + 1) * CH]
        c.op("V", lambda e, Sb_=sub, K=tk, V=kv: e.tensor_copy(col[:, 0:1],
             btm[64 * Sb_:64 * Sb_ + 64, 64 + K * 4 + V:64 + K * 4 + V + 1]),
             reads=[b_btm], writes=[b_colL])
        c.op("V", lambda e, Sb_=sub, K=tk, V=kv: e.tensor_copy(col[:, 2:3],
             btm[64 * Sb_:64 * Sb_ + 64, K * 4 + V:K * 4 + V + 1]),
             reads=[b_btm], writes=[b_colL])
        yield
        c.op("S", lambda e: e.activation(col[:, 1:2], col[:, 0:1], AF.Exp),
             reads=[b_colL], writes=[b_colL])
        c.op("V", lambda e: e.tensor_mul(col[:, 3:4], col[:, 1:2], col[:, 2:3]),
             reads=[b_colL], writes=[b_colL])
        c.op("V", lambda e: e.tensor_scalar_mul(col[:, 3:4], col[:, 3:4], -1.0),
             reads=[b_colL], writes=[b_colL])
        yield
        c.mm_group([(pA[0:64, 0:CH], P.ones_row[0:1, 0:64], gf, True, True)],
                   reads=[b_fl, P.b_ones_row], writes=[bA])
        c.op("V", lambda e: e.tensor_scalar(R(4), pA[0:64, 0:CH], col[:, 0:1], None,
                                            ALU.subtract),
             reads=[bA, b_colL], writes=[b_scrL])
        yield
        c.op("S", lambda e: e.activation(R(0), R(4), AF.Exp), reads=[b_scrL], writes=[b_scrL])
        c.op("S", lambda e: e.activation(R(1), R(4), AF.Exp, scale=-1.0),
             reads=[b_scrL], writes=[b_scrL])
        yield
        c.mm_group([(pA[0:64, 0:CH], P.ones_row[0:1, 0:64], nb, True, True)],
                   reads=[b_fl, P.b_ones_row], writes=[bA])
        c.op("S", lambda e: e.activation(R(3), pA[0:64, 0:CH], AF.Copy), reads=[bA], writes=[b_scrL])
        yield
        c.mm_group([(pA[0:64, 0:1], P.ones_row[0:1, 0:64], gc1, True, True)],
                   reads=[b_fl, P.b_ones_row], writes=[bA])
        c.op("S", lambda e: e.activation(col[:, 5:6], pA[0:64, 0:1], AF.Exp),
             reads=[bA], writes=[b_colL])
        c.op("V", lambda e: e.tensor_sub(col[:, 6:7], pA[0:64, 0:1], col[:, 0:1]),
             reads=[bA, b_colL], writes=[b_colL])
        c.op("S", lambda e: e.activation(col[:, 4:5], col[:, 6:7], AF.Exp),
             reads=[b_colL], writes=[b_colL])
        yield
        ks = stg(12, kidx, krh)
        c.mm_group([(pA[0:64, 0:CH], ks, ks, True, True)], reads=[b_sbbL], writes=[bA])
        yield
        c.op("V", lambda e: e.tensor_mul(R(5), pA[0:64, 0:CH], R(1)),
             reads=[bA, b_scrL], writes=[b_scrL])
        c.op("V", lambda e: e.tensor_scalar_mul(R(5), R(5), col[:, 2:3]),
             reads=[b_scrL, b_colL], writes=[b_scrL])
        c.op("V", lambda e: e.tensor_mul(S(1), R(5), mlo[0:64, 0:64]),
             reads=[b_scrL, b_ml], writes=[b_sbbL])
        yield
        c.op("V", lambda e: e.tensor_mul(R(6), pA[0:64, 0:CH], R(0)),
             reads=[bA, b_scrL], writes=[b_scrL])
        c.op("V", lambda e: e.tensor_mul(R(6), R(6), R(3)),
             reads=[b_scrL], writes=[b_scrL])
        c.op("V", lambda e: e.tensor_mul(S(0), R(6), mlo[0:64, 64:128]),
             reads=[b_scrL, b_ml], writes=[b_sbbL])
        yield
        c.op("V", lambda e: e.tensor_add(S(6), S(0), P.ident_bf[0:64, 0:64]),
             reads=[b_sbbL, P.b_ident], writes=[b_sbbL])
        yield
        np_, tp_ = 1, 0
        for step in range(5):
            dn, dt = (2, 3) if step % 2 == 0 else (4, 5)
            c.mm_group([(pA[0:64, 0:CH], S(tp_), S(np_), True, True)],
                       reads=[b_sbbL], writes=[bA])
            if step < 4:
                c.mm_group([(pB[0:64, 0:CH], S(np_), S(tp_), True, True)],
                           reads=[b_sbbL], writes=[bB])
            yield
            c.op("S", lambda e, d=dn: e.activation(S(d), pA[0:64, 0:CH], AF.Copy),
                 reads=[bA], writes=[b_sbbL])
            if step < 4:
                c.op("S", lambda e, d=dt: e.activation(S(d), pB[0:64, 0:CH], AF.Copy),
                     reads=[bB], writes=[b_sbbL])
            yield
            c.mm_group([(pA[0:64, 0:CH], S(dn), S(6), True, True)],
                       reads=[b_sbbL], writes=[bA])
            yield
            c.op("V", lambda e: e.tensor_add(S(6), S(6), pA[0:64, 0:CH]),
                 reads=[bA, b_sbbL], writes=[b_sbbL])
            np_, tp_ = dn, dt
            yield
        vs = stg(13, kidx + 2, krh)
        c.transpose(pAb[0:64, 0:CH], vs, P.ident_bf[0:64, 0:64],
                    reads=[b_sbbL, P.b_ident], writes=[bA])
        c.op("V", lambda e: e.tensor_scalar_mul(S(7), pAb[0:64, 0:CH], col[:, 2:3]),
             reads=[bA, b_colL], writes=[b_sbbL])
        yield
        c.transpose(pBb[0:64, 0:CH], ks, P.ident_bf[0:64, 0:64],
                    reads=[b_sbbL], writes=[bB])
        c.op("V", lambda e: e.tensor_scalar_mul(S(8), pBb[0:64, 0:CH], col[:, 4:5]),
             reads=[bB, b_colL], writes=[b_sbbL])
        yield
        c.mm_group([(pA[0:64, 0:CH], ks, Sb[:, kv * 64:(kv + 1) * 64], True, True)],
                   reads=[b_sbbL, b_Sb[kv]], writes=[bA])
        c.op("V", lambda e: e.scalar_tensor_tensor(S(10), pA[0:64, 0:CH], col[:, 3:4],
                                                   S(7), ALU.mult, ALU.add),
             reads=[bA, b_colL, b_sbbL], writes=[b_sbbL])
        yield
        c.mm_group([(pA[0:64, 0:CH], S(6), S(10), True, True)],
                   reads=[b_sbbL], writes=[bA])
        c.op("S", lambda e: e.activation(S(9), pA[0:64, 0:CH], AF.Copy), reads=[bA], writes=[b_sbbL])
        yield
        c.op("V", lambda e: e.tensor_mul(R(7), R(0), mlo[0:64, 128:192]),
             reads=[b_scrL, b_ml], writes=[b_scrL])
        c.op("S", lambda e, G=gf: e.activation(rbk[0:1, :], G, AF.Exp),
             reads=[b_fl], writes=[b_rbk])
        c.op("V", lambda e: e.tensor_scalar_mul(rbk[0:1, :], rbk[0:1, :], 0.125),
             reads=[b_rbk], writes=[b_rbk])
        yield
        c.mm_group([(pA[0:64, 0:CH], P.ones_row[0:1, 0:64], rbk[0:1, :], True, True)],
                   reads=[b_rbk, P.b_ones_row], writes=[bA])
        c.op("S", lambda e: e.activation(R(2), pA[0:64, 0:CH], AF.Copy), reads=[bA], writes=[b_scrL])
        yield
        for rr in range(4):
            qh = kv * 4 + rr
            qidx, qrh = qh // 2, qh % 2
            qs = stg(14, qidx, qrh)
            c.mm_group([(pA[0:64, 0:CH], ks, qs, True, True)],
                       reads=[b_sbbL], writes=[bA])
            yield
            c.op("V", lambda e: e.tensor_mul(S(11), pA[0:64, 0:CH], R(7)),
                 reads=[bA, b_scrL], writes=[b_sbbL])
            c.mm_group([(pB[0:64, 0:CH], Sb[:, kv * 64:(kv + 1) * 64], qs, True, True)],
                       reads=[b_Sb[kv], b_sbbL], writes=[bB])
            yield
            c.op("V", lambda e: e.tensor_mul(R(8), pB[0:64, 0:CH], R(2)),
                 reads=[bB, b_scrL], writes=[b_scrL])
            c.mm_group([(pA[0:64, 0:CH], S(9), S(11), True, True)],
                       reads=[b_sbbL], writes=[bA])
            yield
            och, orh = qh // 2, qh % 2
            c.op("V", lambda e, OC=och, OR=orh, CI=ci: e.scalar_tensor_tensor(
                o_fm[64 * OR:64 * OR + 64, OC * T + CI * CH: OC * T + (CI + 1) * CH],
                pA[0:64, 0:CH], 0.125, R(8), ALU.mult, ALU.add),
                reads=[bA, b_scrL], writes=[b_o[och]])
            yield
        c.mm_group([(pA[0:64, 0:CH], S(8), S(9), True, True)],
                   reads=[b_sbbL], writes=[bA])
        yield
        c.op("V", lambda e, V=kv: e.scalar_tensor_tensor(Sf[:, V * 64:(V + 1) * 64],
                                                         Sf[:, V * 64:(V + 1) * 64],
                                                         col[:, 5:6], pA[0:64, 0:CH],
                                                         ALU.mult, ALU.add),
             reads=[bA, b_colL, b_Sf[kv]], writes=[b_Sf[kv]])
        c.op("V", lambda e, V=kv: e.tensor_copy(Sb[:, V * 64:(V + 1) * 64],
                                                Sf[:, V * 64:(V + 1) * 64]),
             reads=[b_Sf[kv]], writes=[b_Sb[kv]])
        if DEBUG_SPILL and ci == 1 and kv == 0:
            dsb = P.dram_scratch("sp_l3sbb", [64, 15 * 64], BF16)
            c.dma("Y", P.hbm["sp_l3sbb"], sbb[:, 0:15 * 64], reads=[b_sbbL], writes=[P.hbuf["sp_l3sbb"]])

    for ci in range(NC):
        tk, sub = divmod(ci, 2)
        rowbuf, b_fl = rowbuf2[ci % 2], b_fl2[ci % 2]
        c.dma("Y", rowbuf[0:1, 0:512], P.hbm["sp_l3bg"][:, ci * CH:(ci + 1) * CH],
              reads=[b_bo], writes=[b_fl])
        alive = [emit_kv_instance(ci, kv, tk, sub, rowbuf, b_fl) for kv in range(4)]
        while alive:
            nxt = []
            for gen in alive:
                try:
                    next(gen)
                    nxt.append(gen)
                except StopIteration:
                    pass
            alive = nxt

    blk2, b_b2 = blk2e, b_b2e
    sel2, b_s2 = sel2e, b_s2e
    osq = c.sb("l3_osq", [128, TT], BF16)
    b_osq = Buf()
    rst = c.sb("l3_rst", [2, TT], F32)
    b_rst = Buf()
    rstb = c.sb("l3_rstb", [2, TT], BF16)
    b_rstb = Buf()
    og2, b_og2 = o_fm, b_o
    p2s, b_p2s = pf4[0][0], b_pf4[0][0]
    pbc2, b_pbc2 = pf4[0][1], b_pf4[0][1]
    for chn in range(DC):
        for tt in range(NTT):
            sl = slice(chn * T + tt * TT, chn * T + (tt + 1) * TT)
            c.op("S", lambda e, s=sl: e.activation(osq[:, :], o_fm[:, s], AF.Square),
                 reads=[b_o[chn]], writes=[b_osq])
            c.mm_group([(p2s[0:2, 0:TT], blk2[:, :], osq[:, :], True, True)],
                       reads=[b_osq, b_b2], writes=[b_p2s])
            c.op("V", lambda e: e.tensor_scalar(rst[:, :], p2s[0:2, 0:TT], 1.0 / 64, None, ALU.mult),
                 reads=[b_p2s], writes=[b_rst])
            c.op("V", lambda e: e.tensor_scalar_add(rst[:, :], rst[:, :], 1e-5),
                 reads=[b_rst], writes=[b_rst])
            c.op("S", lambda e: e.activation(rst[:, :], rst[:, :], AF.Ln),
                 reads=[b_rst], writes=[b_rst])
            c.op("S", lambda e: e.activation(rst[:, :], rst[:, :], AF.Exp, scale=-0.5),
                 reads=[b_rst], writes=[b_rst])
            c.op("V", lambda e: e.tensor_copy(rstb[:, :], rst[:, :]), reads=[b_rst], writes=[b_rstb])
            c.mm_group([(pbc2[:, :], sel2[:, :], rstb[0:2, :], True, True)],
                       reads=[b_rstb, b_s2], writes=[b_pbc2])
            c.op("V", lambda e, s=sl: e.scalar_tensor_tensor(
                osq[:, :], o_fm[:, s], P.t_r_l3_rmsw[:, 0:1], pbc2[:, :], ALU.mult, ALU.mult),
                reads=[b_o[chn], b_pbc2, P.b_r_l3_rmsw], writes=[b_osq])
            c.op("V", lambda e, s=sl: e.tensor_mul(og2[:, s], osq[:, :], gate[:, s]),
                 reads=[b_osq, b_gate[chn]], writes=[b_og2[chn]])

    s_sb = c.sb("l3_s", [128, TT], F32)
    b_s = Buf()

    def rhs_og(kc, tt):
        return og2[:, kc * T:][:, tt * TT:(tt + 1) * TT]

    def ev_s(mc, tt, ps, bp):
        c.op("V", lambda e, p=ps: e.tensor_copy(s_sb[:, :], p[:, :]), reads=[bp], writes=[b_s])
        c.dma("Y", s_hbm[:, mc * T + tt * TT: mc * T + (tt + 1) * TT], s_sb[:, :],
              reads=[b_s], writes=[b_sh[mc]])
    gemm_fm(P, "w_l3_proj", 8, range(8), rhs_og, b_og2, ev_s, pool=(pf4[1], b_pf4[1]))
    c.pop_scope()

    c.push_scope()
    y1t = c.sb("l3f_y1t", [128, 8 * 1024], BF16)
    b_y1 = [Buf() for _ in range(8)]
    wa2 = c.sb("l3f_wa2", [128, 8 * 1024], BF16)
    b_wa2 = Buf()
    wc = c.sb("l3f_wc", [128, 8 * 1024], BF16)
    b_wc = [Buf() for _ in range(8)]
    wtb = c.sb("l3f_wtb", [128, 8 * 1024], BF16)
    b_wtb = Buf()
    c.dma("P", wtb[:, :], P.hbm["w_l3f_wT"], reads=[P.hbuf["w_l3f_wT"]], writes=[b_wtb])
    c.dma("P", wa2[:, :], P.hbm["w_l3f_a2"], reads=[P.hbuf["w_l3f_a2"]], writes=[b_wa2])
    gpool = ([c.ps(f"l3f_gp{i}", [128, TT]) for i in range(2)], [Buf(), Buf()])

    def rhs_wt(kc, tt):
        return wtb[:, kc * 1024 + tt * TT: kc * 1024 + (tt + 1) * TT]

    def ev_y1(mc, tt, ps, bp):
        c.op("V", lambda e, m=mc, t=tt, p=ps: e.tensor_copy(
            y1t[:, m * 1024 + t * TT: m * 1024 + (t + 1) * TT], p[:, :]),
            reads=[bp], writes=[b_y1[mc]])
    gemm_fm(P, "w_l3f_a1", 8, range(8), rhs_wt, [b_wtb], ev_y1, ntt=2, pool=gpool)
    pwc = gpool[0]
    for mc in range(8):
        for tt in range(2):
            ps, bp = gpool[0][tt % 2], gpool[1][tt % 2]
            items = []
            for kc in range(8):
                items.append((ps[:, :], y1t[:, kc * 1024 + mc * 128: kc * 1024 + (mc + 1) * 128],
                              wa2[:, kc * 1024 + tt * TT: kc * 1024 + (tt + 1) * TT],
                              kc == 0, kc == 7))
            c.mm_group(items, reads=b_y1 + [b_wa2], writes=[bp])
            c.op("V", lambda e, m=mc, t=tt, p=ps: e.tensor_copy(
                wc[:, m * 1024 + t * TT: m * 1024 + (t + 1) * TT], p[:, :]),
                reads=[bp], writes=[b_wc[mc]])
    tnh = c.sb("l3f_tnh", [128, TT], F32)
    b_tnh = Buf()
    gt3 = c.sb("l3f_gt", [128, TT], BF16)
    b_gt3 = Buf()
    for mc in range(8):
        for tt in range(NTT):
            ps, bp = gpool[0][tt % 2], gpool[1][tt % 2]
            items = []
            for kc in range(8):
                items.append((ps[:, :], wc[:, kc * 1024 + mc * 128: kc * 1024 + (mc + 1) * 128],
                              rhs_xn(kc, tt), kc == 0, kc == 7))
            c.mm_group(items, reads=b_wc + b_xn, writes=[bp])
            c.op("S", lambda e, m=mc, p=ps: e.activation(
                tnh[:, :], p[:, :], AF.Tanh, bias=P.t_c_l3f_b[:, m:m + 1]),
                reads=[bp, P.b_c_l3f_b], writes=[b_tnh])
            c.op("S", lambda e: e.activation(gt3[:, :], tnh[:, :], AF.Silu),
                 reads=[b_tnh], writes=[b_gt3])
            c.dma("Y", g_hbm[:, mc * T + tt * TT: mc * T + (tt + 1) * TT], gt3[:, :],
                  reads=[b_gt3], writes=[b_gh[mc]])
    c.pop_scope()
    c.pop_scope()
    emit_combine(P, x_hbm, x_buf, P.hbm["sp_s3"], b_sh, P.hbm["sp_g3"], b_gh,
                 out_hbm, out_buf, collect_stats=collect)



_CACHE = {}


def build_program(layers):
    P = Prog()
    nc = P.nc
    P.dram_in("x_in", [D, T])
    weight_names = [
        ("w_l0_in", [128, 4096 * 8]), ("w_l0_mix", [128, 2048 * 16]),
        ("w_l0_out", [128, 1024 * 16]),
        ("w_l0f_w1", [128, 4096 * 8]), ("w_l0f_w2", [128, 4096 * 8]),
        ("w_l0f_h1", [128, 4096 * 32]), ("w_l0f_h2", [128, 4096 * 32]),
        ("w_l0f_w3", [128, 1024 * 32]),
        ("w_l1_q", [128, 1024 * 8]), ("w_l1_kv", [128, 256 * 8]),
        ("w_l1_kup", [128, 1024 * 2]), ("w_l1_vup", [128, 2 * 1024]),
        ("w_l1_out", [128, 1024 * 8]), ("w_l1_base", [128, 1024 * 8]),
        ("w_l1_enc", [128, 16384]), ("w_l1_dec", [128, 16384]),
        ("w_l1_dn", [128, 4096]),
        ("r_l1_bd", [8, 64]), ("r_l1_bu", [8, 1024]),
        ("w_l2_hv", [128, 8 * 2048]), ("w_l2_hg", [128, 16 * 1024]),
        ("w_l2_qk", [128, 1024]), ("w_l2_out", [128, 8 * 2048]),
        ("r_l2_bhv", [1, 2048]),
        ("w_l2_gw", [128, 64]), ("r_l2_gb", [1, 8]),
        ("w_l2_w1", [128, 65536]), ("w_l2_w3", [128, 65536]), ("w_l2_w2", [128, 65536]),
        ("w_l2_sw1", [128, 16384]), ("w_l2_sw3", [128, 16384]), ("w_l2_sw2", [128, 16384]),
        ("w_l3_q", [128, 8192]), ("w_l3_k", [128, 2048]), ("w_l3_v", [128, 2048]),
        ("w_l3_gate", [128, 8192]), ("w_l3_proj", [128, 8192]), ("w_l3_bg", [128, 512]),
        ("w_l3f_a1", [128, 8192]), ("w_l3f_wT", [128, 8192]), ("w_l3f_a2", [128, 8192]),
    ]
    col_names = [
        ("c_l0_gamma", [128, 8]), ("c_l0_beta", [128, 8]), ("c_l0_inb", [128, 32]),
        ("c_l0_dwb", [128, 16]), ("c_l0_mixb", [128, 16]), ("c_l0_outb", [128, 8]),
        ("c_l0_dec", [128, 16]), ("c_l0_dww", [128, 112]),
        ("c_l0f_s1", [128, 32]), ("c_l0f_s2", [128, 32]), ("c_l0f_s3", [128, 32]),
        ("c_l0f_s4", [128, 32]), ("c_l0f_s5", [128, 8]),
        ("c_l1_bb", [128, 8]), ("c_l1_benc", [128, 16]), ("c_l1_bdec", [128, 64]),
        ("c_l2_scale", [128, 8]), ("c_l2_bias", [128, 8]), ("c_l2_bhg", [128, 16]),
        ("c_l2_bqk", [128, 1]), ("c_l2_osw", [128, 2]), ("c_l2_osb", [128, 2]),
        ("c_l2_bout", [128, 8]),
        ("c_l3_cwq", [128, 32]), ("c_l3_cwk", [128, 8]), ("c_l3_cwv", [128, 8]),
        ("c_l3_bb64", [64, 1]), ("c_l3_dt64", [64, 1]), ("c_l3_al64", [64, 1]),
        ("c_l3f_b", [128, 8]), ("r_l3_rmsw", [128, 1]),
    ]
    for n, s in weight_names + col_names:
        P.dram_in(n, s)
    P.dram_in("w_l1_up", [64, 8192])
    ap = nc.dram_tensor("i_l1_conn", [8, 1], I32, kind="ExternalInput").ap()
    P.hbm["i_l1_conn"] = ap
    P.hbuf["i_l1_conn"] = Buf("i_l1_conn")
    y_out = P.dram_out("y_out", [D, T])

    with P.stack:
        c = P.ctx
        emit_constants(P)
        load_const_cols(P, col_names)
        x_cur = P.hbm["x_in"]
        b_xcur = P.hbuf8["x_in"]
        for li in layers:
            is_last = li == layers[-1]
            out_hbm = y_out if is_last else P.dram_scratch(f"x_l{li + 1}", [D, T])
            out_buf = P.hbuf8["y_out"] if is_last else P.hbuf8[f"x_l{li + 1}"]
            fn = {0: emit_l0, 1: emit_l1, 2: emit_l2, 3: emit_l3}[li]
            fn(P, x_cur, b_xcur, out_hbm, out_buf,
               pre_stats=(li != layers[0]), collect=not is_last)
            x_cur, b_xcur = out_hbm, out_buf
        c.barrier()
        with nc.Block() as block:
            block.tensor(lambda e: c.replay("T", e))
            block.vector(lambda e: c.replay("V", e))
            block.scalar(lambda e: c.replay("S", e))
            block.gpsimd(lambda e: c.replay("P", e))
            block.sync(lambda e: c.replay("Y", e))
    return P


def kernel(**inputs):
    layers = tuple(DEBUG_LAYERS)
    g, per_core = prep_host(inputs)
    P = build_program(list(layers))
    in_maps = []
    for cid in range(8):
        m = dict(g)
        m.update(per_core[cid])
        in_maps.append(m)
    res = run_bass_kernel_spmd(P.nc, in_maps, core_ids=list(range(8)))
    _LAST_RES[0] = res
    outs = []
    for cid in range(8):
        y = res.results[cid]["y_out"]
        outs.append(np.ascontiguousarray(y.T))
    return np.stack(outs).astype(np.float32)


# version 07
# speedup vs baseline: 1.1928x; 1.0061x over previous
import numpy as np
from contextlib import ExitStack

import concourse.bass as bass
import concourse.mybir as mybir
from concourse.bass_utils import run_bass_kernel_spmd

F32 = mybir.dt.float32
BF16 = mybir.dt.bfloat16
I32 = mybir.dt.int32
AF = mybir.ActivationFunctionType
ALU = mybir.AluOpType
AX = mybir.AxisListType

T = 2048
D = 1024
NTT = 4
TT = 512
DC = 8

DEBUG_LAYERS = [0, 1, 2, 3]
DEBUG_SPILL = False
_LAST_RES = [None]


COMPUTE = ("T", "V", "S", "P", "Y")
DMA_LANES = {"Y": 10, "P": 10, "S": 4}
SAME_ENGINE_SYNC = True


class Buf:
    __slots__ = ("writer", "readers", "name")

    def __init__(self, name=""):
        self.writer = None
        self.readers = []
        self.name = name


class Ctx:
    def __init__(self, nc, stack):
        self.nc = nc
        self.stack = stack
        self.streams = {e: [] for e in COMPUTE}
        self.sems = {}
        self.cnt = {}
        self.lane_rr = {}
        keys = ["T", "V", "S", "P"]
        for q, n in DMA_LANES.items():
            keys += [f"d{q}{i}" for i in range(n)]
            self.lane_rr[q] = 0
        for key in keys:
            self.sems[key] = stack.enter_context(nc.semaphore("sem_" + key))
            self.cnt[key] = 0
        self.waited = {e: {} for e in COMPUTE}
        self.scopes = []
        self._names = {}

    def _uname(self, base):
        n = self._names.get(base, 0)
        self._names[base] = n + 1
        return f"{base}{n}" if n else base

    def sb(self, name, shape, dtype, scope=None):
        if scope is False:
            st = self.stack
        else:
            st = self.scopes[-1] if self.scopes and scope is None else self.stack
        t = st.enter_context(self.nc.sbuf_tensor(self._uname(name), shape, dtype))
        return t

    def ps(self, name, shape, dtype=F32, scope=None):
        st = self.scopes[-1] if self.scopes and scope is None else self.stack
        t = st.enter_context(self.nc.psum_tensor(self._uname(name), shape, dtype))
        return t

    def push_scope(self):
        self.scopes.append(ExitStack())

    def pop_scope(self):
        self.barrier()
        self.scopes.pop().close()

    def barrier(self):
        snap = dict(self.cnt)
        for e in COMPUTE:
            for key, v in snap.items():
                if v and self.waited[e].get(key, 0) < v:
                    tgt = v * 16 if key.startswith("d") else v
                    self.streams[e].append(("wait", key, tgt))
                    self.waited[e][key] = v

    def _collect_waits(self, eng, reads, writes):
        need = {}
        for b in reads:
            if b is None:
                continue
            t = b.writer
            if t:
                k, v = t
                need[k] = max(need.get(k, 0), v)
        for b in writes:
            if b is None:
                continue
            toks = list(b.readers)
            if b.writer:
                toks.append(b.writer)
            for k, v in toks:
                need[k] = max(need.get(k, 0), v)
        out = []
        for k, v in need.items():
            if not SAME_ENGINE_SYNC and k == eng:
                continue
            if k == eng and v >= self.cnt[k]:
                assert v <= self.cnt[k], f"bad same-engine wait {k} {v} > {self.cnt[k]}"
            if self.waited[eng].get(k, 0) >= v:
                continue
            out.append((k, v * 16 if k.startswith("d") else v))
            self.waited[eng][k] = v
        return out

    def _finish(self, eng, semkey, reads, writes):
        self.cnt[semkey] += 1
        tok = (semkey, self.cnt[semkey])
        for b in reads:
            if b is not None:
                b.readers.append(tok)
        for b in writes:
            if b is not None:
                b.writer = tok
                b.readers = []
        return tok

    def op(self, eng, build, reads=(), writes=()):
        reads = list(reads)
        writes = list(writes)
        waits = self._collect_waits(eng, reads, writes)
        for k, v in waits:
            self.streams[eng].append(("wait", k, v))
        tok_val = self.cnt[eng] + 1
        self.streams[eng].append(("op", build, eng, tok_val))
        return self._finish(eng, eng, reads, writes)

    def dma(self, q, dst_ap, src_ap, reads=(), writes=()):
        lane = self.lane_rr[q]
        self.lane_rr[q] = (lane + 1) % DMA_LANES[q]
        semkey = f"d{q}{lane}"
        reads = list(reads)
        writes = list(writes)
        waits = self._collect_waits(q, reads, writes)
        prev = self.cnt[semkey]
        if prev and self.waited[q].get(semkey, 0) < prev:
            waits.append((semkey, prev * 16))
            self.waited[q][semkey] = prev
        for k, v in waits:
            self.streams[q].append(("wait", k, v))
        self.cnt[semkey] += 1
        self.streams[q].append(("dma", dst_ap, src_ap, semkey, self.cnt[semkey] * 16))
        tok = (semkey, self.cnt[semkey])
        for b in reads:
            if b is not None:
                b.readers.append(tok)
        for b in writes:
            if b is not None:
                b.writer = tok
                b.readers = []
        return tok

    def mm_group(self, items, reads=(), writes=()):
        reads = list(reads)
        writes = list(writes)
        waits = self._collect_waits("T", reads, writes)
        for k, v in waits:
            self.streams["T"].append(("wait", k, v))
        n = len(items)
        for i, it in enumerate(items):
            last = i == n - 1
            self.streams["T"].append(("mm", it, "T", self.cnt["T"] + 1 if last else None))
        return self._finish("T", "T", reads, writes)

    def replay(self, eng_key, eng):
        sems = self.sems
        for entry in self.streams[eng_key]:
            kind = entry[0]
            if kind == "wait":
                eng.wait_ge(sems[entry[1]], entry[2])
            elif kind == "op":
                _, build, semkey, val = entry
                ins = build(eng)
                ins.then_inc(sems[semkey], 1)
            elif kind == "dma":
                _, dst, src, semkey, tgt = entry
                eng.dma_start(dst, src).then_inc(sems[semkey], 16)
            elif kind == "mm":
                _, it, semkey, val = entry
                tp = it[5] if len(it) > 5 else None
                ins = eng.matmul(it[0], it[1], it[2], start=it[3], stop=it[4],
                                 tile_position=tp)
                if val is not None:
                    ins.then_inc(sems[semkey], 1)
            elif kind == "tr":
                _, out_ap, in_ap, ident_ap, semkey, val = entry
                ins = eng.transpose(out_ap, in_ap, ident_ap)
                if val is not None:
                    ins.then_inc(sems[semkey], 1)

    def transpose(self, out_ap, in_ap, ident_ap, reads=(), writes=()):
        reads = list(reads)
        writes = list(writes)
        waits = self._collect_waits("T", reads, writes)
        for k, v in waits:
            self.streams["T"].append(("wait", k, v))
        self.streams["T"].append(("tr", out_ap, in_ap, ident_ap, "T", self.cnt["T"] + 1))
        return self._finish("T", "T", reads, writes)



def img_lhsT(w):
    K, M = w.shape
    assert K % 128 == 0 and M % 128 == 0, (K, M)
    Kc, Mc = K // 128, M // 128
    return np.ascontiguousarray(
        w.reshape(Kc, 128, Mc, 128).transpose(1, 2, 0, 3).reshape(128, Mc * Kc * 128)
    )


def img_rhs(w):
    K, N = w.shape
    assert K % 128 == 0
    Kc = K // 128
    return np.ascontiguousarray(w.reshape(Kc, 128, N).transpose(1, 0, 2).reshape(128, Kc * N))


def cols(v, parts=128):
    v = np.asarray(v)
    n = v.size // parts
    return np.ascontiguousarray(v.reshape(n, parts).T)


def prep_host(inputs):
    g = {}
    f = lambda k: np.asarray(inputs[k], np.float32)

    g["w_l0_in"] = img_lhsT(f("L0_attn_in_w"))
    g["w_l0_mix"] = img_lhsT(f("L0_attn_mix_w"))
    g["w_l0_out"] = img_lhsT(f("L0_attn_out_w"))
    g["c_l0_gamma"] = cols(f("L0_attn_norm_gamma"))
    g["c_l0_beta"] = cols(f("L0_attn_norm_beta"))
    g["c_l0_inb"] = cols(f("L0_attn_in_b"))
    g["c_l0_dwb"] = cols(f("L0_attn_dw_b"))
    g["c_l0_mixb"] = cols(f("L0_attn_mix_b"))
    g["c_l0_outb"] = cols(f("L0_attn_out_b"))
    g["c_l0_dec"] = cols(f("L0_attn_log_decay"))
    dw = f("L0_attn_dw_w")
    g["c_l0_dww"] = np.ascontiguousarray(
        dw.reshape(16, 128, 7).transpose(1, 0, 2).reshape(128, 112))

    g["w_l0f_w1"] = img_lhsT(np.ascontiguousarray(f("L0_ffn_w1_w").T))
    g["w_l0f_w2"] = img_lhsT(np.ascontiguousarray(f("L0_ffn_w2_w").T))
    g["w_l0f_h1"] = img_lhsT(np.ascontiguousarray(f("L0_ffn_h1_w").T))
    g["w_l0f_h2"] = img_lhsT(np.ascontiguousarray(f("L0_ffn_h2_w").T))
    g["w_l0f_w3"] = img_lhsT(np.ascontiguousarray(f("L0_ffn_w3_w").T))
    g["c_l0f_s1"] = cols(f("L0_ffn_w1_s"))
    g["c_l0f_s2"] = cols(f("L0_ffn_w2_s"))
    g["c_l0f_s3"] = cols(f("L0_ffn_h1_s"))
    g["c_l0f_s4"] = cols(f("L0_ffn_h2_s"))
    g["c_l0f_s5"] = cols(f("L0_ffn_w3_s"))

    g["w_l1_q"] = img_lhsT(f("L1_attn_Wq").reshape(1024, 1024))
    g["w_l1_kv"] = img_lhsT(f("L1_attn_Wkv_down"))
    g["w_l1_kup"] = img_lhsT(f("L1_attn_Wk_up").reshape(256, 1024))
    g["w_l1_vup"] = img_rhs(f("L1_attn_Wv_up").reshape(256, 1024))
    g["w_l1_out"] = img_lhsT(f("L1_attn_Wout").reshape(1024, 1024))
    g["w_l1_base"] = img_lhsT(f("L1_ffn_W_base"))
    g["w_l1_enc"] = np.concatenate([img_lhsT(f("L1_ffn_W_enc")[e]) for e in range(8)], axis=1)
    g["w_l1_dec"] = np.concatenate([img_lhsT(f("L1_ffn_W_dec")[e]) for e in range(8)], axis=1)
    wd = f("L1_ffn_W_down")
    g["w_l1_dn"] = np.ascontiguousarray(
        wd.reshape(8, 8, 128, 64).transpose(2, 0, 1, 3).reshape(128, 8 * 8 * 64))
    wu = f("L1_ffn_W_up")
    g["w_l1_up"] = np.ascontiguousarray(
        wu.reshape(8, 64, 8, 128).transpose(1, 0, 2, 3).reshape(64, 8 * 8 * 128))
    g["c_l1_bb"] = cols(f("L1_ffn_b_base"))
    g["c_l1_benc"] = np.ascontiguousarray(f("L1_ffn_b_enc").reshape(8, 2, 128).transpose(2, 0, 1).reshape(128, 16))
    g["c_l1_bdec"] = np.ascontiguousarray(f("L1_ffn_b_dec").reshape(8, 8, 128).transpose(2, 0, 1).reshape(128, 64))
    g["r_l1_bd"] = np.ascontiguousarray(f("L1_ffn_b_down"))
    g["r_l1_bu"] = np.ascontiguousarray(f("L1_ffn_b_up"))
    g["i_l1_conn"] = np.ascontiguousarray(np.asarray(inputs["L1_ffn_connected_idx"], np.int32).reshape(8, 1))

    wh = f("L2_attn_W_hidden")
    g["w_l2_hv"] = img_rhs(np.ascontiguousarray(wh[:, :2048]))
    g["w_l2_hg"] = img_lhsT(np.ascontiguousarray(wh[:, 2048:]))
    g["w_l2_qk"] = img_lhsT(f("L2_attn_W_qk"))
    g["w_l2_out"] = img_lhsT(f("L2_attn_W_out"))
    g["c_l2_scale"] = cols(f("L2_attn_ln_scale"))
    g["c_l2_bias"] = cols(f("L2_attn_ln_bias"))
    g["r_l2_bhv"] = np.ascontiguousarray(f("L2_attn_b_hidden")[:2048].reshape(1, 2048))
    g["c_l2_bhg"] = cols(f("L2_attn_b_hidden")[2048:])
    g["c_l2_bqk"] = cols(f("L2_attn_b_qk"))
    g["c_l2_osw"] = np.ascontiguousarray(f("L2_attn_os_weight").T)
    g["c_l2_osb"] = np.ascontiguousarray(f("L2_attn_os_bias").T)
    g["c_l2_bout"] = cols(f("L2_attn_b_out"))
    g["w_l2_gw"] = img_rhs(np.ascontiguousarray(f("L2_ffn_gate_w").T))
    g["r_l2_gb"] = np.ascontiguousarray(f("L2_ffn_gate_bias").reshape(1, 8))
    g["w_l2_w1"] = np.concatenate([img_lhsT(np.ascontiguousarray(f("L2_ffn_w1")[e].T)) for e in range(8)], axis=1)
    g["w_l2_w3"] = np.concatenate([img_lhsT(np.ascontiguousarray(f("L2_ffn_w3")[e].T)) for e in range(8)], axis=1)
    g["w_l2_w2"] = np.concatenate([img_lhsT(np.ascontiguousarray(f("L2_ffn_w2")[e].T)) for e in range(8)], axis=1)
    g["w_l2_sw1"] = img_lhsT(np.ascontiguousarray(f("L2_ffn_sw1").T))
    g["w_l2_sw3"] = img_lhsT(np.ascontiguousarray(f("L2_ffn_sw3").T))
    g["w_l2_sw2"] = img_lhsT(np.ascontiguousarray(f("L2_ffn_sw2").T))

    g["w_l3_q"] = img_lhsT(np.ascontiguousarray(f("L3_attn_Wq").T))
    g["w_l3_k"] = img_lhsT(np.ascontiguousarray(f("L3_attn_Wk").T))
    g["w_l3_v"] = img_lhsT(np.ascontiguousarray(f("L3_attn_Wv").T))
    g["w_l3_gate"] = img_lhsT(np.ascontiguousarray(f("L3_attn_Wgate").T))
    g["w_l3_proj"] = img_lhsT(np.ascontiguousarray(f("L3_attn_Wproj").T))
    wbg = np.zeros((1024, 64), np.float32)
    wbg[:, 0:4] = f("L3_attn_Wbeta").T
    wbg[:, 32:36] = f("L3_attn_Wgk").T
    g["w_l3_bg"] = np.ascontiguousarray(
        wbg.reshape(8, 128, 64).transpose(1, 0, 2).reshape(128, 512))
    g["c_l3_cwq"] = np.ascontiguousarray(f("L3_attn_cwq").reshape(8, 128, 4).transpose(1, 0, 2).reshape(128, 32))
    g["c_l3_cwk"] = np.ascontiguousarray(f("L3_attn_cwk").reshape(2, 128, 4).transpose(1, 0, 2).reshape(128, 8))
    g["c_l3_cwv"] = np.ascontiguousarray(f("L3_attn_cwv").reshape(2, 128, 4).transpose(1, 0, 2).reshape(128, 8))
    bb64 = np.zeros((64, 1), np.float32); bb64[0:4, 0] = f("L3_attn_b_beta")
    dt64 = np.zeros((64, 1), np.float32); dt64[32:36, 0] = f("L3_attn_dt_bias")
    al64 = np.zeros((64, 1), np.float32); al64[32:36, 0] = f("L3_attn_A_log")
    g["c_l3_bb64"] = bb64; g["c_l3_dt64"] = dt64; g["c_l3_al64"] = al64
    g["r_l3_rmsw"] = np.ascontiguousarray(np.tile(f("L3_attn_rms_w"), 2).reshape(128, 1))
    g["w_l3f_a1"] = img_lhsT(f("L3_ffn_aux_weight1"))
    g["w_l3f_wT"] = img_rhs(np.ascontiguousarray(f("L3_ffn_weight").T))
    g["w_l3f_a2"] = img_rhs(f("L3_ffn_aux_weight2"))
    g["c_l3f_b"] = cols(f("L3_ffn_bias"))

    per_core = []
    x = np.asarray(inputs["L0_attn_x"], np.float32)
    for c in range(8):
        per_core.append({"x_in": np.ascontiguousarray(x[c].T)})
    return g, per_core



class Prog:

    def __init__(self):
        self.nc = bass.Bass("TRN2", target_bir_lowering=False)
        self.stack = ExitStack()
        self.ctx = Ctx(self.nc, self.stack)
        self.hbm = {}
        self.hbuf = {}
        self.hbuf8 = {}
        self.hbuf8 = {}

    def dram_in(self, name, shape):
        ap = self.nc.dram_tensor(name, list(shape), F32, kind="ExternalInput").ap()
        self.hbm[name] = ap
        self.hbuf[name] = Buf(name)
        self.hbuf8[name] = [Buf(f"{name}.{i}") for i in range(DC)]
        return ap

    def dram_scratch(self, name, shape, dtype=F32):
        kind = "ExternalOutput" if DEBUG_SPILL else "Internal"
        ap = self.nc.dram_tensor(name, list(shape), dtype, kind=kind).ap()
        self.hbm[name] = ap
        self.hbuf[name] = Buf(name)
        self.hbuf8[name] = [Buf(f"{name}.{i}") for i in range(DC)]
        return ap

    def dram_out(self, name, shape):
        ap = self.nc.dram_tensor(name, list(shape), F32, kind="ExternalOutput").ap()
        self.hbm[name] = ap
        self.hbuf[name] = Buf(name)
        self.hbuf8[name] = [Buf(f"{name}.{i}") for i in range(DC)]
        return ap


def emit_constants(P):
    c = P.ctx
    P.ones_col = c.sb("ones_col", [128, 1], F32)
    P.ones_row = c.sb("ones_row", [1, 128], F32)
    P.ident_bf = c.sb("ident_bf", [128, 128], BF16)
    P.ln_tmp4 = [c.sb(f"ln_tmp{i}", [128, TT], F32) for i in range(4)]
    P.b_ln_tmp4 = [Buf() for _ in range(4)]
    P.ln_rows_h = c.sb("ln_rows_h", [1, 2 * T], F32)
    P.b_lnrh = Buf("ln_rows_h")
    P.ones_col_bf = c.sb("ones_col_bf", [128, 1], BF16)
    P.ones_row_bf = c.sb("ones_row_bf", [1, 128], BF16)
    P.conn_i = c.sb("conn_i", [8, 1], I32)
    P.b_conn = Buf("conn")
    c.push_scope()
    iot = c.sb("iota_i", [128, 128], I32)
    P.b_ones_col = Buf("ones_col")
    P.b_ones_row = Buf("ones_row")
    P.b_ident = Buf("ident")
    b_iot = Buf("iota")
    c.op("V", lambda e: e.memset(P.ones_col_bf[:, :], 1.0), writes=[P.b_ones_col])
    c.op("V", lambda e: e.memset(P.ones_col[:, :], 1.0), writes=[P.b_ones_col])
    c.op("V", lambda e: e.memset(P.ones_row[:, :], 1.0), writes=[P.b_ones_row])
    c.op("V", lambda e: e.memset(P.ones_row_bf[:, :], 1.0), writes=[P.b_ones_row])
    c.op("P", lambda e: e.iota(iot[:, :], [[1, 128]], channel_multiplier=-1,
                               allow_small_or_imprecise_dtypes=True), writes=[b_iot])
    c.op("V", lambda e: e.tensor_single_scalar(P.ident_bf[:, :], iot[:, :], 0, ALU.is_equal),
         reads=[b_iot], writes=[P.b_ident])
    c.pop_scope()


def load_const_cols(P, names_shapes):
    c = P.ctx
    for name, shape in names_shapes:
        t = c.sb("c_" + name, list(shape), F32)
        b = Buf(name)
        c.dma("Y", t[:, :] if len(shape) == 2 else t[:], P.hbm[name],
              reads=[P.hbuf[name]], writes=[b])
        setattr(P, "t_" + name, t)
        setattr(P, "b_" + name, b)



def emit_layernorm2(P, src_hbm, src_buf, xn_sb, xn_buf, gamma_col=None, beta_col=None,
                    src_sb=None, eps=1e-5, pre_stats=False):
    c = P.ctx
    c.push_scope()
    from_hbm = src_sb is None
    if from_hbm:
        xch = c.sb("ln_xch", [128, 2 * T], F32)
        b_xch = [Buf(), Buf()]
        xb, bxb = xn_sb, xn_buf
    else:
        xb, bxb = src_sb
    if pre_stats:
        rows = P.ln_rows_h
        b_rows = P.b_lnrh
    else:
        rows = c.sb("ln_rows", [1, 2 * T], F32)
        b_rows = Buf("ln_rows")
    mrow = c.sb("ln_mrow", [1, T], F32)
    b_mrow = Buf("ln_mrow")
    onesb = P.ones_col_bf[:, 0:1]

    c.push_scope()
    if not pre_stats:
        xsq = c.sb("ln_xsq", [128, T], BF16)
        b_xsq = [Buf()]
        p_s = [c.ps(f"ln_ps{tt}", [1, TT]) for tt in range(NTT)]
        p_q = [c.ps(f"ln_pq{tt}", [1, TT]) for tt in range(NTT)]
        b_ps = [Buf() for _ in range(NTT)]
        b_pq = [Buf() for _ in range(NTT)]
    for ch in range(DC if not pre_stats else 0):
        pb = ch % 2
        if from_hbm:
            c.dma("Y", xch[:, pb * T:(pb + 1) * T], src_hbm[ch * 128:(ch + 1) * 128, :],
                  reads=[src_buf[ch]], writes=[b_xch[pb]])
            c.op("V", lambda e, i=ch, p=pb: e.tensor_copy(xb[:, i * T:(i + 1) * T],
                                                          xch[:, p * T:(p + 1) * T]),
                 reads=[b_xch[pb]], writes=[bxb[ch]])
        c.op("S", lambda e, i=ch: e.activation(xsq[:, 0:T],
                                               xb[:, i * T:(i + 1) * T], AF.Square),
             reads=[bxb[ch]], writes=[b_xsq[0]])
        for tt in range(NTT):
            sl = slice(tt * TT, (tt + 1) * TT)
            c.mm_group([(p_s[tt][0:1, :], onesb, xb[:, ch * T:][:, sl], ch == 0, ch == DC - 1)],
                       reads=[bxb[ch], P.b_ones_col], writes=[b_ps[tt]])
            c.mm_group([(p_q[tt][0:1, :], onesb, xsq[:, sl], ch == 0, ch == DC - 1)],
                       reads=[b_xsq[0], P.b_ones_col], writes=[b_pq[tt]])
    for tt in range(NTT if not pre_stats else 0):
        c.op("V", lambda e, t=tt: e.tensor_copy(rows[0:1, t * TT:(t + 1) * TT], p_s[t][0:1, :]),
             reads=[b_ps[tt]], writes=[b_rows])
        c.op("V", lambda e, t=tt: e.tensor_copy(rows[0:1, T + t * TT:T + (t + 1) * TT], p_q[t][0:1, :]),
             reads=[b_pq[tt]], writes=[b_rows])
    c.pop_scope()

    c.op("V", lambda e: e.tensor_scalar_mul(mrow[0:1, :], rows[0:1, 0:T], 1.0 / D),
         reads=[b_rows], writes=[b_mrow])
    c.op("S", lambda e: e.activation(rows[0:1, 0:T], mrow[0:1, :], AF.Square),
         reads=[b_mrow], writes=[b_rows])
    c.op("V", lambda e: e.scalar_tensor_tensor(rows[0:1, T:2 * T], rows[0:1, T:2 * T],
                                               1.0 / D, rows[0:1, 0:T], ALU.mult, ALU.subtract),
         reads=[b_rows], writes=[b_rows])
    c.op("V", lambda e: e.tensor_scalar_add(rows[0:1, T:2 * T], rows[0:1, T:2 * T], eps),
         reads=[b_rows], writes=[b_rows])
    c.op("S", lambda e: e.activation(rows[0:1, T:2 * T], rows[0:1, T:2 * T], AF.Ln),
         reads=[b_rows], writes=[b_rows])
    c.op("S", lambda e: e.activation(rows[0:1, 0:T], rows[0:1, T:2 * T], AF.Exp, scale=-0.5),
         reads=[b_rows], writes=[b_rows])
    c.op("V", lambda e: e.tensor_mul(rows[0:1, T:2 * T], mrow[0:1, :], rows[0:1, 0:T]),
         reads=[b_rows, b_mrow], writes=[b_rows])
    c.op("V", lambda e: e.tensor_scalar_mul(rows[0:1, T:2 * T], rows[0:1, T:2 * T], -1.0),
         reads=[b_rows], writes=[b_rows])

    if hasattr(P, "dbg_rows") and P.dbg_rows is not None:
        yr, b_yr = P.dbg_rows
        c.dma("Y", yr[0:1, 0:2 * T], rows[0:1, :], reads=[b_rows], writes=[b_yr])
        c.dma("Y", yr[0:1, 2 * T:3 * T], mrow[0:1, :], reads=[b_mrow], writes=[b_yr])
        P.dbg_rows = None
    p_bc = [c.ps(f"ln_bc{i}", [128, 2 * TT]) for i in range(2)]
    b_bc = [Buf(), Buf()]
    if from_hbm:
        xt = c.sb("ln_xt", [128, 2 * TT], F32)
        b_xt = [Buf(), Buf()]
    for tt in range(NTT):
        pb2 = tt % 2
        pbc, bbc = p_bc[pb2], b_bc[pb2]
        c.mm_group([(pbc[:, 0:TT], P.ones_row[0:1, :], rows[0:1, tt * TT:(tt + 1) * TT], True, True),
                    (pbc[:, TT:2 * TT], P.ones_row[0:1, :], rows[0:1, T + tt * TT:T + (tt + 1) * TT], True, True)],
                   reads=[b_rows, P.b_ones_row], writes=[bbc])
        for ch in range(DC):
            sl = slice(tt * TT, (tt + 1) * TT)
            lt = P.ln_tmp4[ch % 4]
            b_lt = P.b_ln_tmp4[ch % 4]
            if from_hbm:
                px = ch % 2
                c.dma("Y", xt[:, px * TT:(px + 1) * TT],
                      src_hbm[ch * 128:(ch + 1) * 128, tt * TT:(tt + 1) * TT],
                      reads=[src_buf[ch]], writes=[b_xt[px]])
                xsrc = xt[:, px * TT:(px + 1) * TT]
                bsrc = b_xt[px]
            else:
                xsrc = xb[:, ch * T:][:, sl]
                bsrc = bxb[ch]
            c.op("V", lambda e, xs=xsrc, p=pbc, L=lt: e.tensor_mul(L[:, 0:TT], xs, p[:, 0:TT]),
                 reads=[bsrc, bbc], writes=[b_lt])
            if gamma_col is None:
                c.op("V", lambda e, i=ch, t=tt, p=pbc, L=lt: e.tensor_add(
                    xn_sb[:, i * T + t * TT: i * T + (t + 1) * TT], L[:, 0:TT], p[:, TT:2 * TT]),
                    reads=[b_lt, bbc], writes=[xn_buf[ch]])
            else:
                gcol, gbuf = gamma_col
                bcol, bbuf = beta_col
                c.op("V", lambda e, p=pbc, L=lt: e.tensor_add(
                    L[:, 0:TT], L[:, 0:TT], p[:, TT:2 * TT]),
                    reads=[b_lt, bbc], writes=[b_lt])
                c.op("V", lambda e, i=ch, t=tt, L=lt: e.tensor_scalar(
                    xn_sb[:, i * T + t * TT: i * T + (t + 1) * TT], L[:, 0:TT],
                    gcol[:, i:i + 1], bcol[:, i:i + 1], ALU.mult, ALU.add),
                    reads=[b_lt, gbuf, bbuf], writes=[xn_buf[ch]])
    c.pop_scope()


def gemm_fm(P, wkey, Kc, mcs, rhs_fn, rhs_bufs, evac, wcols=128, ntt=NTT, ttw=TT,
            col_off=0, extra_reads=(), pool=None):
    c = P.ctx
    wbuf = c.sb("gw_" + wkey, [128, 2 * Kc * wcols], BF16)
    b_w = [Buf(), Buf()]
    if pool is None:
        pss = [c.ps("gp_" + wkey, [128, ttw]) for i in range(2)]
        b_ps = [Buf(), Buf()]
    else:
        pss, b_ps = pool
    wimg = P.hbm[wkey]
    b_wimg = P.hbuf[wkey]
    for i, mc in enumerate(mcs):
        pb = i % 2
        seg = Kc * wcols
        c.dma("P", wbuf[:, pb * seg:(pb + 1) * seg],
              wimg[:, col_off + mc * seg: col_off + (mc + 1) * seg],
              reads=[b_wimg], writes=[b_w[pb]])
        for tt in range(ntt):
            ps, bp = pss[tt % 2], b_ps[tt % 2]
            items = []
            for kc in range(Kc):
                items.append((ps[:, :], wbuf[:, pb * seg + kc * wcols: pb * seg + (kc + 1) * wcols],
                              rhs_fn(kc, tt), kc == 0, kc == Kc - 1))
            c.mm_group(items, reads=[b_w[pb]] + list(rhs_bufs) + list(extra_reads), writes=[bp])
            evac(mc, tt, ps, bp)



def emit_l0(P, x_hbm, x_buf, out_hbm, out_buf, pre_stats=False, collect=False):
    c = P.ctx
    H = 2048
    HC = 16
    TP = T + 8
    KC = 8

    c.push_scope()
    xn2 = c.sb("l0_xn2", [128, DC * T], BF16)
    b_xn2 = [Buf() for _ in range(DC)]
    vgv = c.sb("l0_vgv", [128, HC * TP], BF16)
    b_v = [Buf(f"v{h}") for h in range(HC)]
    for h in range(HC):
        c.op("V", lambda e, hh=h: e.memset(vgv[:, hh * TP:hh * TP + 8], 0.0), writes=[b_v[h]])

    c.push_scope()
    xn_sb = c.sb("l0_xn0", [128, DC * T], BF16)
    xn_bufs = [Buf() for _ in range(DC)]
    emit_layernorm2(P, x_hbm, x_buf, xn_sb, xn_bufs, pre_stats=pre_stats)
    xn_hbm = P.dram_scratch("sp_xn0", [128, DC * T], BF16)
    b_xnh = P.hbuf["sp_xn0"]
    for ch in range(DC):
        c.dma("Y", P.hbm["sp_xn0"][:, ch * T:(ch + 1) * T], xn_sb[:, ch * T:(ch + 1) * T],
              reads=[xn_bufs[ch]], writes=[b_xnh])
    emit_layernorm2(P, None, None, xn2, b_xn2,
                    gamma_col=(P.t_c_l0_gamma[:, :], P.b_c_l0_gamma),
                    beta_col=(P.t_c_l0_beta[:, :], P.b_c_l0_beta),
                    src_sb=(xn_sb, xn_bufs))
    c.pop_scope()

    c.push_scope()
    wbuf = c.sb("l0_wbuf", [128, 2 * KC * 128], BF16)
    b_w = [Buf(), Buf()]
    p_slots = [c.ps(f"l0_pc{i}", [128, TT]) for i in range(4)]
    b_p = [Buf() for _ in range(4)]
    w_img = P.hbm["w_l0_in"]
    b_wimg = P.hbuf["w_l0_in"]
    for i, mc in enumerate(range(16, 32)):
        pb = i % 2
        c.dma("P", wbuf[:, pb * 1024:(pb + 1) * 1024],
              w_img[:, (mc * KC) * 128:(mc * KC + KC) * 128],
              reads=[b_wimg], writes=[b_w[pb]])
        hc = mc - 16
        for tt in range(NTT):
            sl = slice(tt * TT, (tt + 1) * TT)
            ps = p_slots[(i * NTT + tt) % 4]
            bp = b_p[(i * NTT + tt) % 4]
            items = []
            for kc in range(KC):
                items.append((ps[:, :], wbuf[:, pb * 1024 + kc * 128: pb * 1024 + (kc + 1) * 128],
                              xn2[:, kc * T:][:, sl], kc == 0, kc == KC - 1))
            c.mm_group(items, reads=[b_w[pb]] + b_xn2, writes=[bp])
            c.op("V", lambda e, h=hc, t=tt, p=ps, m=mc: e.tensor_scalar(
                vgv[:, h * TP + 8 + t * TT: h * TP + 8 + (t + 1) * TT], p[:, :],
                P.t_c_l0_inb[:, m:m + 1], None, ALU.add),
                reads=[bp, P.b_c_l0_inb], writes=[b_v[hc]])

    scan_bf = c.sb("l0_scan", [128, HC * T], BF16)
    b_scan = [Buf() for _ in range(HC)]
    dec = c.sb("l0_dec", [128, HC], F32)
    b_dec = Buf()
    scan_f = c.sb("l0_scanf", [128, T // 2], F32)
    b_scanf = Buf()
    c.op("S", lambda e: e.activation(dec[:, :], P.t_c_l0_dec[:, :], AF.Sigmoid),
         reads=[P.b_c_l0_dec], writes=[b_dec])
    carry = c.sb("l0_carry", [128, 1], F32)
    b_carry = Buf()
    TH2 = T // 2
    for hc in range(HC):
        for hf in range(2):
            init = 0.0 if hf == 0 else carry[:, 0:1]
            rd = [b_dec, b_v[hc]] + ([b_carry] if hf else [])
            c.op("V", lambda e, h=hc, f=hf, ini=init: e.tensor_tensor_scan(
                scan_f[:, :], dec[:, h:h + 1].broadcast_to([128, TH2]),
                vgv[:, h * TP + 8 + f * TH2: h * TP + 8 + (f + 1) * TH2],
                ini, ALU.mult, ALU.add),
                reads=rd, writes=[b_scanf])
            if hf == 0:
                c.op("V", lambda e: e.tensor_copy(carry[:, 0:1], scan_f[:, TH2 - 1:TH2]),
                     reads=[b_scanf], writes=[b_carry])
            c.op("P", lambda e, h=hc, f=hf: e.tensor_copy(
                scan_bf[:, h * T + f * TH2: h * T + (f + 1) * TH2], scan_f[:, :]),
                reads=[b_scanf], writes=[b_scan[hc]])

    diag = c.sb("l0_diag", [128, 7 * 128], BF16)
    b_diag = Buf()
    bsum = c.sb("l0_bsum", [128, HC], F32)
    b_bsum = Buf()
    c.op("V", lambda e: e.tensor_add(bsum[:, :], P.t_c_l0_dwb[:, :], P.t_c_l0_mixb[:, :]),
         reads=[P.b_c_l0_dwb, P.b_c_l0_mixb], writes=[b_bsum])

    wmix = P.hbm["w_l0_mix"]
    b_wmix = P.hbuf["w_l0_mix"]
    wbuf2 = c.sb("l0_wbuf2", [128, 16 * 128], BF16)
    b_w2 = [Buf(), Buf()]
    g_sb = c.sb("l0_g", [128, 2 * TT], F32)
    b_g = [Buf(), Buf()]
    KCH = 16
    for hc in range(HC):
        pb = hc % 2
        c.dma("P", wbuf2[:, 0:2048],
              wmix[:, (hc * KCH) * 128:(hc * KCH + KCH) * 128],
              reads=[b_wmix], writes=[b_w2[0]])
        c.dma("P", wbuf[:, pb * 1024:(pb + 1) * 1024],
              w_img[:, (hc * KC) * 128:(hc * KC + KC) * 128],
              reads=[b_wimg], writes=[b_w[pb]])
        for j in range(7):
            c.op("V", lambda e, h=hc, jj=j: e.tensor_scalar_mul(
                diag[:, jj * 128:(jj + 1) * 128], P.ident_bf[:, :],
                P.t_c_l0_dww[:, h * 7 + jj:h * 7 + jj + 1]),
                reads=[P.b_ident, P.b_c_l0_dww], writes=[b_diag])
        for tt in range(NTT):
            sl = slice(tt * TT, (tt + 1) * TT)
            items = []
            for kc in range(KCH):
                items.append((p_slots[tt][:, :],
                              wbuf2[:, kc * 128:(kc + 1) * 128],
                              scan_bf[:, kc * T:][:, sl], kc == 0, False))
            for j in range(7):
                sh = slice(hc * TP + 8 + tt * TT - (6 - j), hc * TP + 8 + (tt + 1) * TT - (6 - j))
                items.append((p_slots[tt][:, :], diag[:, j * 128:(j + 1) * 128], vgv[:, sh],
                              False, j == 6))
            c.mm_group(items, reads=[b_w2[0], b_diag, b_v[hc]] + b_scan, writes=[b_p[tt]])
        if hc == 0:
            pg2 = [c.ps(f"l0_pg{i}", [128, TT]) for i in range(2)]
            bpg2 = [Buf(), Buf()]
        for tt in range(NTT):
            sl = slice(tt * TT, (tt + 1) * TT)
            pg, bpg = pg2[tt % 2], bpg2[tt % 2]
            wl = slice((tt % 2) * TT, (tt % 2 + 1) * TT)
            itg = []
            for kc in range(KC):
                itg.append((pg[:, :], wbuf[:, pb * 1024 + kc * 128:pb * 1024 + (kc + 1) * 128],
                            xn2[:, kc * T:][:, sl], kc == 0, kc == KC - 1))
            c.mm_group(itg, reads=[b_w[pb]] + b_xn2, writes=[bpg])
            c.op("S", lambda e, h=hc, p=pg, W=wl: e.activation(
                g_sb[:, W], p[:, :], AF.Sigmoid, bias=P.t_c_l0_inb[:, h:h + 1]),
                reads=[bpg, P.b_c_l0_inb], writes=[b_g[tt % 2]])
            c.op("V", lambda e, h=hc, t=tt, W=wl: e.scalar_tensor_tensor(
                vgv[:, h * TP + 8 + t * TT: h * TP + 8 + (t + 1) * TT], p_slots[t][:, :],
                bsum[:, h:h + 1], g_sb[:, W], ALU.add, ALU.mult),
                reads=[b_p[tt], b_g[tt % 2], b_bsum], writes=[b_v[hc]])
    c.pop_scope()

    c.push_scope()
    s_hbm = P.dram_scratch("sp_s0", [128, DC * T], F32)
    b_sh = P.hbuf8["sp_s0"]
    s_sb = c.sb("l0_s", [128, 2 * TT], F32)
    b_s = [Buf(), Buf()]
    xn_tile = c.sb("l0_xnt", [128, T], BF16)
    b_xnt = Buf()
    wout = P.hbm["w_l0_out"]
    b_wout = P.hbuf["w_l0_out"]
    wbuf3 = c.sb("l0_wbuf3", [128, 2 * 16 * 128], BF16)
    b_w3 = [Buf(), Buf()]
    p_f = [c.ps(f"l0_pf{i}", [128, TT]) for i in range(2)]
    b_pf = [Buf(), Buf()]
    for mc in range(DC):
        pb = mc % 2
        c.dma("P", wbuf3[:, pb * 2048:(pb + 1) * 2048],
              wout[:, (mc * 16) * 128:(mc * 16 + 16) * 128],
              reads=[b_wout], writes=[b_w3[pb]])
        c.dma("Y", xn_tile[:, :], P.hbm["sp_xn0"][:, mc * T:(mc + 1) * T],
              reads=[b_xnh], writes=[b_xnt])
        for tt in range(NTT):
            sl = slice(tt * TT, (tt + 1) * TT)
            ps = p_f[tt % 2]
            bp = b_pf[tt % 2]
            items = []
            for kc in range(16):
                items.append((ps[:, :], wbuf3[:, pb * 2048 + kc * 128:pb * 2048 + (kc + 1) * 128],
                              vgv[:, kc * TP + 8:][:, sl], kc == 0, kc == 15))
            c.mm_group(items, reads=[b_w3[pb]] + b_v, writes=[bp])
            wl = slice((tt % 2) * TT, (tt % 2 + 1) * TT)
            c.op("V", lambda e, p=ps, m=mc, t=tt, W=wl: e.scalar_tensor_tensor(
                s_sb[:, W], p[:, :], P.t_c_l0_outb[:, m:m + 1], xn_tile[:, t * TT:(t + 1) * TT],
                ALU.add, ALU.add),
                reads=[bp, b_xnt, P.b_c_l0_outb], writes=[b_s[tt % 2]])
            c.dma("Y", s_hbm[:, mc * T + tt * TT: mc * T + (tt + 1) * TT], s_sb[:, wl],
                  reads=[b_s[tt % 2]], writes=[b_sh[mc]])
    c.pop_scope()
    c.pop_scope()

    g_hbm = P.dram_scratch("sp_g0", [128, DC * T], BF16)
    b_gh = P.hbuf8["sp_g0"]
    emit_l0_ffn(P, g_hbm, b_gh)

    emit_combine(P, x_hbm, x_buf, P.hbm["sp_s0"], b_sh, P.hbm["sp_g0"], b_gh,
                 out_hbm, out_buf, collect_stats=collect)


def emit_l0_ffn(P, g_hbm, b_gh):
    c = P.ctx
    HC = 32
    KC8 = 8
    KC32 = 32
    c.push_scope()
    TH = T // 2
    NT2 = TH // TT
    xn_t = c.sb("f_xn", [128, DC * TH], BF16)
    b_xn = [Buf() for _ in range(DC)]
    A = c.sb("f_A", [128, HC * TH], BF16)
    b_A = [Buf() for _ in range(HC)]
    Bt = c.sb("f_B", [128, HC * TH], BF16)
    b_B = [Buf() for _ in range(HC)]
    wb1 = c.sb("f_w1", [128, 2 * KC8 * 128], BF16)
    wb2 = c.sb("f_w2", [128, 2 * KC8 * 128], BF16)
    b_w1 = [Buf(), Buf()]
    b_w2 = [Buf(), Buf()]
    tmp = c.sb("f_tmp", [128, 2 * TT], F32)
    b_tmp = [Buf(), Buf()]
    wbig = c.sb("f_wbig", [128, 2 * KC32 * 128], BF16)
    b_wbig = [Buf(), Buf()]
    gt = c.sb("f_g", [128, 2 * TT], BF16)
    b_gt = [Buf(), Buf()]
    p_slots = [c.ps(f"f_p{i}", [128, TT]) for i in range(4)]
    b_p = [Buf() for _ in range(4)]
    for half in range(2):
        for ch in range(DC):
            c.dma("Y", xn_t[:, ch * TH:(ch + 1) * TH],
                  P.hbm["sp_xn0"][:, ch * T + half * TH: ch * T + (half + 1) * TH],
                  reads=[P.hbuf["sp_xn0"]], writes=[b_xn[ch]])
        for mc in range(HC):
            pb = mc % 2
            c.dma("P", wb1[:, pb * 1024:(pb + 1) * 1024],
                  P.hbm["w_l0f_w1"][:, mc * KC8 * 128:(mc * KC8 + KC8) * 128],
                  reads=[P.hbuf["w_l0f_w1"]], writes=[b_w1[pb]])
            c.dma("P", wb2[:, pb * 1024:(pb + 1) * 1024],
                  P.hbm["w_l0f_w2"][:, mc * KC8 * 128:(mc * KC8 + KC8) * 128],
                  reads=[P.hbuf["w_l0f_w2"]], writes=[b_w2[pb]])
            for tt in range(NT2):
                sl = slice(tt * TT, (tt + 1) * TT)
                ps1, bp1 = p_slots[2 * (tt % 2)], b_p[2 * (tt % 2)]
                ps2, bp2 = p_slots[2 * (tt % 2) + 1], b_p[2 * (tt % 2) + 1]
                it1, it2 = [], []
                for kc in range(KC8):
                    it1.append((ps1[:, :], wb1[:, pb * 1024 + kc * 128:pb * 1024 + (kc + 1) * 128],
                                xn_t[:, kc * TH:][:, sl], kc == 0, kc == KC8 - 1))
                    it2.append((ps2[:, :], wb2[:, pb * 1024 + kc * 128:pb * 1024 + (kc + 1) * 128],
                                xn_t[:, kc * TH:][:, sl], kc == 0, kc == KC8 - 1))
                c.mm_group(it1, reads=[b_w1[pb]] + b_xn, writes=[bp1])
                c.mm_group(it2, reads=[b_w2[pb]] + b_xn, writes=[bp2])
                wl = slice((tt % 2) * TT, (tt % 2 + 1) * TT)
                c.op("S", lambda e, p=ps1, m=mc, W=wl: e.activation(
                    tmp[:, W], p[:, :], AF.Silu, scale=P.t_c_l0f_s1[:, m:m + 1]),
                    reads=[bp1, P.b_c_l0f_s1], writes=[b_tmp[tt % 2]])
                c.op("V", lambda e, p=ps2, m=mc, t=tt, W=wl: e.scalar_tensor_tensor(
                    A[:, m * TH + t * TT: m * TH + (t + 1) * TT], p[:, :],
                    P.t_c_l0f_s2[:, m:m + 1], tmp[:, W], ALU.mult, ALU.mult),
                    reads=[bp2, b_tmp[tt % 2], P.b_c_l0f_s2], writes=[b_A[mc]])
        for stage, (wkey, skey, scol, src, bsrc, dst, bdst) in enumerate([
                ("w_l0f_h1", "s3", P.t_c_l0f_s3, A, b_A, Bt, b_B),
                ("w_l0f_h2", "s4", P.t_c_l0f_s4, Bt, b_B, A, b_A)]):
            for mc in range(HC):
                pb = mc % 2
                c.dma("P", wbig[:, pb * KC32 * 128:(pb + 1) * KC32 * 128],
                      P.hbm[wkey][:, mc * KC32 * 128:(mc * KC32 + KC32) * 128],
                      reads=[P.hbuf[wkey]], writes=[b_wbig[pb]])
                for tt in range(NT2):
                    sl = slice(tt * TT, (tt + 1) * TT)
                    ps, bp = p_slots[tt % 2], b_p[tt % 2]
                    items = []
                    for kc in range(KC32):
                        items.append((ps[:, :],
                                      wbig[:, pb * KC32 * 128 + kc * 128:pb * KC32 * 128 + (kc + 1) * 128],
                                      src[:, kc * TH:][:, sl], kc == 0, kc == KC32 - 1))
                    c.mm_group(items, reads=[b_wbig[pb]] + bsrc, writes=[bp])
                    c.op("S", lambda e, p=ps, m=mc, t=tt, sc=scol, d=dst: e.activation(
                        d[:, m * TH + t * TT: m * TH + (t + 1) * TT], p[:, :], AF.Gelu,
                        scale=sc[:, m:m + 1]),
                        reads=[bp], writes=[bdst[mc]])
        for mc in range(DC):
            pb = mc % 2
            c.dma("P", wbig[:, pb * KC32 * 128:(pb + 1) * KC32 * 128],
                  P.hbm["w_l0f_w3"][:, mc * KC32 * 128:(mc * KC32 + KC32) * 128],
                  reads=[P.hbuf["w_l0f_w3"]], writes=[b_wbig[pb]])
            for tt in range(NT2):
                sl = slice(tt * TT, (tt + 1) * TT)
                ps, bp = p_slots[tt % 2], b_p[tt % 2]
                items = []
                for kc in range(KC32):
                    items.append((ps[:, :],
                                  wbig[:, pb * KC32 * 128 + kc * 128:pb * KC32 * 128 + (kc + 1) * 128],
                                  A[:, kc * TH:][:, sl], kc == 0, kc == KC32 - 1))
                c.mm_group(items, reads=[b_wbig[pb]] + b_A, writes=[bp])
                wl = slice((tt % 2) * TT, (tt % 2 + 1) * TT)
                c.op("S", lambda e, p=ps, m=mc, W=wl: e.activation(
                    gt[:, W], p[:, :], AF.Silu, scale=P.t_c_l0f_s5[:, m:m + 1]),
                    reads=[bp, P.b_c_l0f_s5], writes=[b_gt[tt % 2]])
                c.dma("Y", g_hbm[:, mc * T + half * TH + tt * TT: mc * T + half * TH + (tt + 1) * TT],
                      gt[:, wl], reads=[b_gt[tt % 2]], writes=[b_gh[mc]])
    c.pop_scope()


def emit_combine(P, x_hbm, x_buf, s_hbm, b_sh, g_hbm, b_gh, out_hbm, out_buf,
                 collect_stats=False):
    c = P.ctx
    c.push_scope()
    NL = 4
    xt = c.sb("cb_x", [128, NL * TT], F32)
    st = c.sb("cb_s", [128, NL * TT], F32)
    gt = c.sb("cb_g", [128, NL * TT], BF16)
    ot = c.sb("cb_o", [128, NL * TT], F32)
    bx = [Buf() for _ in range(NL)]
    bs = [Buf() for _ in range(NL)]
    bg = [Buf() for _ in range(NL)]
    bo = [Buf() for _ in range(NL)]
    if collect_stats:
        ob = c.sb("cb_ob", [128, NL * TT], BF16)
        osq = c.sb("cb_osq", [128, NL * TT], BF16)
        bob = [Buf() for _ in range(NL)]
        bosq = [Buf() for _ in range(NL)]
        p_s = [c.ps(f"cb_ps{t}", [1, TT]) for t in range(NTT)]
        p_q = [c.ps(f"cb_pq{t}", [1, TT]) for t in range(NTT)]
        b_ps = [Buf() for _ in range(NTT)]
        b_pq = [Buf() for _ in range(NTT)]
    i = 0
    for ch in range(DC):
        for tt in range(NTT):
            ln = i % NL
            i += 1
            w = slice(ln * TT, (ln + 1) * TT)
            sl = slice(ch * T + tt * TT, ch * T + (tt + 1) * TT)
            c.dma("Y", xt[:, w], x_hbm[ch * 128:(ch + 1) * 128, tt * TT:(tt + 1) * TT]
                  if x_hbm.shape[0] == D else x_hbm[:, sl],
                  reads=[x_buf[ch]], writes=[bx[ln]])
            c.dma("Y", st[:, w], s_hbm[:, sl], reads=[b_sh[ch]], writes=[bs[ln]])
            c.dma("P", gt[:, w], g_hbm[:, sl], reads=[b_gh[ch]], writes=[bg[ln]])
            c.op("V", lambda e, W=w: e.tensor_mul(ot[:, W], st[:, W], gt[:, W]),
                 reads=[bs[ln], bg[ln]], writes=[bo[ln]])
            c.op("P", lambda e, W=w: e.tensor_add(ot[:, W], ot[:, W], xt[:, W]),
                 reads=[bo[ln], bx[ln]], writes=[bo[ln]])
            c.dma("Y", out_hbm[ch * 128:(ch + 1) * 128, tt * TT:(tt + 1) * TT]
                  if out_hbm.shape[0] == D else out_hbm[:, sl],
                  ot[:, w], reads=[bo[ln]], writes=[out_buf[ch]])
            if collect_stats:
                c.op("P", lambda e, W=w: e.tensor_copy(ob[:, W], ot[:, W]),
                     reads=[bo[ln]], writes=[bob[ln]])
                c.op("S", lambda e, W=w: e.activation(osq[:, W], ot[:, W], AF.Square),
                     reads=[bo[ln]], writes=[bosq[ln]])
                c.mm_group([(p_s[tt][0:1, :], P.ones_col_bf[:, 0:1], ob[:, w],
                             ch == 0, ch == DC - 1)],
                           reads=[bob[ln], P.b_ones_col], writes=[b_ps[tt]])
                c.mm_group([(p_q[tt][0:1, :], P.ones_col_bf[:, 0:1], osq[:, w],
                             ch == 0, ch == DC - 1)],
                           reads=[bosq[ln], P.b_ones_col], writes=[b_pq[tt]])
    if collect_stats:
        for tt in range(NTT):
            c.op("V", lambda e, t=tt: e.tensor_copy(
                P.ln_rows_h[0:1, t * TT:(t + 1) * TT], p_s[t][0:1, :]),
                reads=[b_ps[tt]], writes=[P.b_lnrh])
            c.op("V", lambda e, t=tt: e.tensor_copy(
                P.ln_rows_h[0:1, T + t * TT:T + (t + 1) * TT], p_q[t][0:1, :]),
                reads=[b_pq[tt]], writes=[P.b_lnrh])
    c.pop_scope()



def emit_l1(P, x_hbm, x_buf, out_hbm, out_buf, pre_stats=False, collect=False):
    c = P.ctx
    H, HD = 16, 64
    c.push_scope()
    s_hbm = P.dram_scratch("sp_s1", [128, DC * T], F32)
    b_sh = P.hbuf8["sp_s1"]
    g_hbm = P.dram_scratch("sp_g1", [128, DC * T], BF16)
    b_gh = P.hbuf8["sp_g1"]
    xn_hbm = P.dram_scratch("sp_xn1", [128, DC * T], BF16)
    b_xnh = P.hbuf8["sp_xn1"]

    c.push_scope()
    q = c.sb("l1_q", [128, DC * T], BF16)
    b_q = [Buf() for _ in range(DC)]
    k = c.sb("l1_k", [128, DC * T], BF16)
    b_k = [Buf() for _ in range(DC)]
    v_tm = c.sb("l1_v", [128, 16 * 1040], BF16)
    b_v = [Buf() for _ in range(16)]

    c.push_scope()
    xn = c.sb("l1_xn", [128, DC * T], BF16)
    b_xn = [Buf() for _ in range(DC)]
    emit_layernorm2(P, x_hbm, x_buf, xn, b_xn, pre_stats=pre_stats)
    for ch in range(DC):
        c.dma("Y", xn_hbm[:, ch * T:(ch + 1) * T], xn[:, ch * T:(ch + 1) * T],
              reads=[b_xn[ch]], writes=[b_xnh[ch]])

    def rhs_xn(kc, tt):
        return xn[:, kc * T:][:, tt * TT:(tt + 1) * TT]
    gemm_fm(P, "w_l1_q", 8, range(8), rhs_xn, b_xn,
            lambda mc, tt, ps, bp: c.op("V", lambda e, m=mc, t=tt, p=ps: e.tensor_copy(
                q[:, m * T + t * TT: m * T + (t + 1) * TT], p[:, :]),
                reads=[bp], writes=[b_q[mc]]))
    c.push_scope()
    lat = c.sb("l1_lat", [128, 2 * T], BF16)
    b_lat = [Buf(), Buf()]
    gemm_fm(P, "w_l1_kv", 8, range(2), rhs_xn, b_xn,
            lambda mc, tt, ps, bp: c.op("V", lambda e, m=mc, t=tt, p=ps: e.tensor_copy(
                lat[:, m * T + t * TT: m * T + (t + 1) * TT], p[:, :]),
                reads=[bp], writes=[b_lat[mc]]))

    def rhs_lat(kc, tt):
        return lat[:, kc * T:][:, tt * TT:(tt + 1) * TT]
    gemm_fm(P, "w_l1_kup", 2, range(8), rhs_lat, b_lat,
            lambda mc, tt, ps, bp: c.op("V", lambda e, m=mc, t=tt, p=ps: e.tensor_copy(
                k[:, m * T + t * TT: m * T + (t + 1) * TT], p[:, :]),
                reads=[bp], writes=[b_k[mc]]))
    wv = c.sb("l1_wv", [128, 2 * 1024], BF16)
    b_wv = Buf()
    c.dma("P", wv[:, :], P.hbm["w_l1_vup"][:, :], reads=[P.hbuf["w_l1_vup"]], writes=[b_wv])
    pv = [c.ps(f"l1_pv{i}", [128, TT]) for i in range(2)]
    b_pv = [Buf(), Buf()]
    for kt in range(16):
        c.op("V", lambda e, K=kt: e.memset(
            v_tm[:, K * 1040:(K + 1) * 1040].rearrange("p (h f) -> p h f", h=16)[:, :, 64:65], 1.0),
            writes=[b_v[kt]])
        for nb in range(2):
            ps, bp = pv[nb], b_pv[nb]
            items = []
            for kc in range(2):
                items.append((ps[:, :], lat[:, kc * T + kt * 128: kc * T + (kt + 1) * 128],
                              wv[:, kc * 1024 + nb * 512: kc * 1024 + (nb + 1) * 512],
                              kc == 0, kc == 1))
            c.mm_group(items, reads=b_lat + [b_wv], writes=[bp])
            c.op("V", lambda e, K=kt, n=nb, p=ps: e.tensor_copy(
                v_tm[:, K * 1040 + n * 8 * 65:][:, 0:520].rearrange("p (h f) -> p h f", h=8)[:, :, 0:64],
                p[:, :].rearrange("p (h f) -> p h f", h=8)),
                reads=[bp], writes=[b_v[kt]])
    c.pop_scope()
    c.pop_scope()
    y = c.sb("l1_y", [128, DC * T], BF16)
    b_y = [Buf() for _ in range(DC)]

    cneg = c.sb("l1_cneg", [128, 1], F32)
    b_cn = Buf()
    c.push_scope()
    sq = c.sb("l1_sq", [128, T], BF16)
    b_sq = Buf()
    mxr = c.sb("l1_mx", [1, 64], F32)
    b_mx = Buf()
    p1 = c.ps("l1_p1", [1, TT])
    b_p1 = Buf()
    for half, (src, bufs) in enumerate([(q, b_q), (k, b_k)]):
        for ch in range(DC):
            c.op("S", lambda e, s=src, i=ch: e.activation(sq[:, :], s[:, i * T:(i + 1) * T], AF.Square),
                 reads=[bufs[ch]], writes=[b_sq])
            for tt in range(NTT):
                c.mm_group([(p1[0:1, :], P.ones_col_bf[:, 0:1], sq[:, tt * TT:(tt + 1) * TT], True, True)],
                           reads=[b_sq, P.b_ones_col], writes=[b_p1])
                c.op("V", lambda e, hf=half, i=ch, t=tt: e.reduce_max(
                    mxr[0:1, hf * 32 + i * 4 + t: hf * 32 + i * 4 + t + 1], p1[0:1, :], axis=AX.X),
                    reads=[b_p1], writes=[b_mx])
    crow = c.sb("l1_cr", [1, 4], F32)
    b_cr = Buf()
    c.op("V", lambda e: e.tensor_reduce(crow[0:1, 0:2], mxr[0:1, :].rearrange("p (a b) -> p a b", a=2),
                                        AX.X, ALU.max), reads=[b_mx], writes=[b_cr])
    c.op("S", lambda e: e.activation(crow[0:1, 2:4], crow[0:1, 0:2], AF.Ln), reads=[b_cr], writes=[b_cr])
    c.op("V", lambda e: e.reduce_sum(crow[0:1, 0:1], crow[0:1, 2:4], axis=AX.X), reads=[b_cr], writes=[b_cr])
    c.op("S", lambda e: e.activation(crow[0:1, 1:2], crow[0:1, 0:1], AF.Exp, scale=0.5),
         reads=[b_cr], writes=[b_cr])
    c.op("V", lambda e: e.tensor_scalar_mul(crow[0:1, 0:1], crow[0:1, 1:2], -0.125),
         reads=[b_cr], writes=[b_cr])
    pcb = c.ps("l1_pcb", [128, 1])
    b_pcb = Buf()
    c.mm_group([(pcb[:, 0:1], P.ones_row[0:1, :], crow[0:1, 0:1], True, True)],
               reads=[b_cr, P.b_ones_row], writes=[b_pcb])
    c.op("V", lambda e: e.tensor_copy(cneg[:, :], pcb[:, 0:1]), reads=[b_pcb], writes=[b_cn])
    c.pop_scope()

    masks = c.sb("l1_masks", [128, 4 * TT], BF16)
    b_mk = Buf()
    c.push_scope()
    iot2 = c.sb("l1_iot2", [128, TT], I32)
    b_io = Buf()
    c.op("P", lambda e: e.iota(iot2[:, :], [[1, TT]], channel_multiplier=-1,
                               allow_small_or_imprecise_dtypes=True), writes=[b_io])
    for r in range(4):
        c.op("V", lambda e, rr=r: e.tensor_single_scalar(
            masks[:, rr * TT:(rr + 1) * TT], iot2[:, :], 128 * rr, ALU.is_ge),
            reads=[b_io], writes=[b_mk])
    c.pop_scope()

    expS2 = [c.sb(f"l1_es{i}", [128, 16 * TT], BF16) for i in range(2)]
    b_es2 = [Buf(), Buf()]
    av2 = [c.sb(f"l1_av{i}", [128, TT], F32) for i in range(2)]
    b_av2 = [Buf(), Buf()]
    rc2 = [c.sb(f"l1_rc{i}", [1, TT], F32) for i in range(2)]
    b_rc2 = [Buf(), Buf()]
    psc4 = [c.ps(f"l1_psc{i}", [128, TT]) for i in range(4)]
    b_psc4 = [Buf() for _ in range(4)]
    pav2 = [c.ps(f"l1_pav{i}", [128, TT]) for i in range(2)]
    b_pav2 = [Buf(), Buf()]
    pbc2_ = [c.ps(f"l1_pbc{i}", [128, TT]) for i in range(2)]
    b_pbc2_ = [Buf(), Buf()]

    def emit_head(qb, ch, hh):
        lane = hh
        expS, b_es = expS2[lane], b_es2[lane]
        av_sb, b_av = av2[lane], b_av2[lane]
        rc, b_rc = rc2[lane], b_rc2[lane]
        pav, b_pav = pav2[lane], b_pav2[lane]
        pbc, b_pbc = pbc2_[lane], b_pbc2_[lane]
        head = 2 * ch + hh
        rs = slice(64 * hh, 64 * hh + 64)
        tp = (64 * hh, 0)
        nkb = 4 * qb + 4
        for kb in range(nkb):
            ps, bp = psc4[2 * lane + kb % 2], b_psc4[2 * lane + kb % 2]
            c.mm_group([(ps[:, :], k[rs, ch * T + kb * 128: ch * T + (kb + 1) * 128],
                         q[rs, ch * T + qb * TT: ch * T + (qb + 1) * TT], True, True, tp)],
                       reads=[b_k[ch], b_q[ch]], writes=[bp])
            c.op("S", lambda e, K=kb, p=ps: e.activation(
                expS[:, K * TT:(K + 1) * TT], p[:, :], AF.Exp, bias=cneg[:, 0:1], scale=0.125),
                reads=[bp, b_cn], writes=[b_es])
            if kb >= 4 * qb:
                r = kb - 4 * qb
                c.op("V", lambda e, K=kb, rr=r: e.tensor_mul(
                    expS[:, K * TT:(K + 1) * TT], expS[:, K * TT:(K + 1) * TT],
                    masks[:, rr * TT:(rr + 1) * TT]),
                    reads=[b_es, b_mk], writes=[b_es])
            yield
        items = []
        for kb in range(nkb):
            items.append((pav[0:65, :], v_tm[:, kb * 1040 + head * 65: kb * 1040 + head * 65 + 65],
                          expS[:, kb * TT:(kb + 1) * TT], kb == 0, kb == nkb - 1))
        c.mm_group(items, reads=b_v + [b_es], writes=[b_pav])
        yield
        c.op("V", lambda e: e.tensor_copy(av_sb[0:65, :], pav[0:65, :]),
             reads=[b_pav], writes=[b_av])
        c.op("V", lambda e: e.reciprocal(rc[0:1, :], av_sb[64:65, :]),
             reads=[b_av], writes=[b_rc])
        yield
        c.mm_group([(pbc[0:64, :], P.ones_row[0:1, 0:64], rc[0:1, :], True, True)],
                   reads=[b_rc, P.b_ones_row], writes=[b_pbc])
        yield
        c.op("V", lambda e, i=ch, h=hh, t=qb: e.tensor_mul(
            y[64 * h:64 * h + 64, i * T + t * TT: i * T + (t + 1) * TT],
            av_sb[0:64, :], pbc[0:64, :]),
            reads=[b_av, b_pbc], writes=[b_y[ch]])

    for qb in range(NTT):
        for ch in range(DC):
            alive = [emit_head(qb, ch, 0), emit_head(qb, ch, 1)]
            while alive:
                nxt = []
                for gen in alive:
                    try:
                        next(gen)
                        nxt.append(gen)
                    except StopIteration:
                        pass
                alive = nxt

    s_sb = c.sb("l1_s", [128, TT], F32)
    b_s = Buf()

    def rhs_y(kc, tt):
        return y[:, kc * T:][:, tt * TT:(tt + 1) * TT]

    def ev_s(mc, tt, ps, bp):
        c.op("V", lambda e, m=mc, t=tt, p=ps: e.tensor_copy(s_sb[:, :], p[:, :]),
             reads=[bp], writes=[b_s])
        c.dma("Y", s_hbm[:, mc * T + tt * TT: mc * T + (tt + 1) * TT], s_sb[:, :],
              reads=[b_s], writes=[b_sh[mc]])
    gemm_fm(P, "w_l1_out", 8, range(8), rhs_y, b_y, ev_s,
            pool=([psc4[0], psc4[1]], [b_psc4[0], b_psc4[1]]))
    c.pop_scope()

    c.push_scope()
    xn = c.sb("l1f_xn", [128, DC * T], BF16)
    b_xn = [Buf() for _ in range(DC)]
    for ch in range(DC):
        c.dma("Y", xn[:, ch * T:(ch + 1) * T], xn_hbm[:, ch * T:(ch + 1) * T],
              reads=[b_xnh[ch]], writes=[b_xn[ch]])

    def rhs_xn(kc, tt):
        return xn[:, kc * T:][:, tt * TT:(tt + 1) * TT]
    gpool = ([c.ps(f"l1f_gp{i}", [128, TT]) for i in range(2)], [Buf(), Buf()])
    h_sb = c.sb("l1f_h", [128, 2 * T], BF16)
    b_h = [Buf(), Buf()]
    diff = c.sb("l1f_diff", [128, TT], F32)
    b_df = Buf()
    sqd = c.sb("l1f_sqd", [128, TT], BF16)
    b_sqd = Buf()
    lcols = c.sb("l1f_lc", [128, 256], F32)
    b_lc = Buf()
    for ei in range(8):
        def ev_h(mc, tt, ps, bp, ei=ei):
            c.op("S", lambda e, m=mc, t=tt, p=ps, E=ei: e.activation(
                h_sb[:, m * T + t * TT: m * T + (t + 1) * TT], p[:, :], AF.Relu,
                bias=P.t_c_l1_benc[:, E * 2 + m:E * 2 + m + 1]),
                reads=[bp, P.b_c_l1_benc], writes=[b_h[mc]])
        gemm_fm(P, "w_l1_enc", 8, range(2), rhs_xn, b_xn, ev_h, col_off=ei * 2048, pool=gpool)

        def rhs_h(kc, tt):
            return h_sb[:, kc * T:][:, tt * TT:(tt + 1) * TT]

        def ev_rec(mc, tt, ps, bp, ei=ei):
            c.op("V", lambda e, m=mc, t=tt, p=ps, E=ei: e.scalar_tensor_tensor(
                diff[:, :], p[:, :], P.t_c_l1_bdec[:, E * 8 + m:E * 8 + m + 1],
                xn[:, m * T + t * TT: m * T + (t + 1) * TT], ALU.add, ALU.subtract),
                reads=[bp, b_xn[mc], P.b_c_l1_bdec], writes=[b_df])
            c.op("S", lambda e, m=mc, t=tt, E=ei: e.activation(
                sqd[:, :], diff[:, :], AF.Square,
                accum_out=lcols[:, E * 32 + m * 4 + t: E * 32 + m * 4 + t + 1]),
                reads=[b_df], writes=[b_sqd, b_lc])
        gemm_fm(P, "w_l1_dec", 2, range(8), rhs_h, b_h, ev_rec, col_off=ei * 2048, pool=gpool)
    ohbc = c.sb("l1f_ohbc", [128, 8], F32)
    b_ohbc = Buf()
    bdu = c.sb("l1f_bdu", [128, 9], F32)
    b_bdu = Buf()
    c.push_scope()
    lsum = c.sb("l1f_ls", [128, 8], F32)
    b_lsum = Buf()
    c.op("V", lambda e: e.tensor_reduce(lsum[:, :], lcols[:, :].rearrange("p (a b) -> p a b", a=8),
                                        AX.X, ALU.add), reads=[b_lc], writes=[b_lsum])
    pl8 = c.ps("l1f_pl8", [1, 8])
    b_pl8 = Buf()
    c.mm_group([(pl8[0:1, :], P.ones_col[:, 0:1], lsum[:, :], True, True)],
               reads=[b_lsum, P.b_ones_col], writes=[b_pl8])
    small = c.sb("l1f_small", [1, 32], F32)
    b_sm = Buf()
    c.op("V", lambda e: e.tensor_copy(small[0:1, 0:8], pl8[0:1, :]), reads=[b_pl8], writes=[b_sm])
    c.op("V", lambda e: e.tensor_reduce(small[0:1, 8:9], small[0:1, 0:8], AX.X, ALU.min),
         reads=[b_sm], writes=[b_sm])
    c.op("V", lambda e: e.tensor_scalar(small[0:1, 9:17], small[0:1, 0:8], small[0:1, 8:9], None,
                                        ALU.is_le), reads=[b_sm], writes=[b_sm])
    oh_bf = c.sb("l1f_ohbf", [1, 8], BF16)
    b_ohbf = Buf()
    c.op("V", lambda e: e.tensor_copy(oh_bf[0:1, :], small[0:1, 9:17]), reads=[b_sm], writes=[b_ohbf])
    connf = c.sb("l1f_connf", [8, 1], F32)
    b_cf = Buf()
    c.dma("Y", P.conn_i[0:8, 0:1], P.hbm["i_l1_conn"], reads=[P.hbuf["i_l1_conn"]], writes=[P.b_conn])
    c.op("V", lambda e: e.tensor_copy(connf[:, :], P.conn_i[0:8, 0:1]), reads=[P.b_conn], writes=[b_cf])
    io8 = c.sb("l1f_io8", [8, 8], I32)
    b_io8 = Buf()
    c.op("P", lambda e: e.iota(io8[:, :], [[1, 8]], channel_multiplier=0,
                               allow_small_or_imprecise_dtypes=True), writes=[b_io8])
    iof = c.sb("l1f_iof", [8, 8], F32)
    b_iof = Buf()
    c.op("V", lambda e: e.tensor_copy(iof[:, :], io8[:, :]), reads=[b_io8], writes=[b_iof])
    P8 = c.sb("l1f_P8", [8, 8], BF16)
    b_P8 = Buf()
    c.op("V", lambda e: e.tensor_scalar(P8[:, :], iof[:, :], connf[0:8, 0:1], None, ALU.is_equal),
         reads=[b_iof, b_cf], writes=[b_P8])
    p81 = c.ps("l1f_p81", [8, 8], BF16)
    b_p81 = Buf()
    c.transpose(p81[0:8, 0:1], oh_bf[0:1, 0:8], P.ident_bf[0:1, 0:1], reads=[b_ohbf, P.b_ident],
                writes=[b_p81])
    ohc = c.sb("l1f_ohc", [8, 1], BF16)
    b_ohc = Buf()
    c.op("V", lambda e: e.tensor_copy(ohc[:, :], p81[0:8, 0:1]), reads=[b_p81], writes=[b_ohc])
    p82 = c.ps("l1f_p82", [8, 1])
    b_p82 = Buf()
    c.mm_group([(p82[0:8, 0:1], P8[:, :], ohc[:, :], True, True)], reads=[b_P8, b_ohc], writes=[b_p82])

    oh2f = c.sb("l1f_oh2f", [8, 1], F32)
    b_oh2f = Buf()
    c.op("V", lambda e: e.tensor_copy(oh2f[:, :], p82[0:8, 0:1]), reads=[b_p82], writes=[b_oh2f])
    d8 = c.sb("l1f_d8", [8, 8], BF16)
    b_d8 = Buf()
    c.op("V", lambda e: e.tensor_scalar_mul(d8[:, :], P.ident_bf[0:8, 0:8], oh2f[0:8, 0:1]),
         reads=[P.b_ident, b_oh2f], writes=[b_d8])
    ones8 = c.sb("l1f_o8", [8, 128], BF16)
    b_o8 = Buf()
    c.op("V", lambda e: e.memset(ones8[:, :], 1.0), writes=[b_o8])
    pbc8 = c.ps("l1f_pbc8", [128, 8])
    b_pbc8 = Buf()
    c.mm_group([(pbc8[:, :], ones8[:, :], d8[:, :], True, True)], reads=[b_o8, b_d8], writes=[b_pbc8])
    c.op("V", lambda e: e.tensor_copy(ohbc[:, :], pbc8[:, :]), reads=[b_pbc8], writes=[b_ohbc])
    bdn = c.sb("l1f_bdn", [8, 64], F32)
    b_bdn = Buf()
    bun = c.sb("l1f_bun", [8, 1024], F32)
    b_bun = Buf()
    c.dma("Y", bdn[:, :], P.hbm["r_l1_bd"], reads=[P.hbuf["r_l1_bd"]], writes=[b_bdn])
    c.dma("Y", bun[:, :], P.hbm["r_l1_bu"], reads=[P.hbuf["r_l1_bu"]], writes=[b_bun])
    pb1 = c.ps("l1f_pb1", [128, 2])
    b_pb1 = Buf()
    c.mm_group([(pb1[0:64, 0:1], bdn[:, :], oh2f[:, :], True, True)],
               reads=[b_bdn, b_oh2f], writes=[b_pb1])
    c.op("V", lambda e: e.tensor_copy(bdu[0:64, 0:1], pb1[0:64, 0:1]), reads=[b_pb1], writes=[b_bdu])
    for mc in range(8):
        c.mm_group([(pb1[:, 1:2], bun[:, mc * 128:(mc + 1) * 128], oh2f[:, :], True, True)],
                   reads=[b_bun, b_oh2f], writes=[b_pb1])
        c.op("V", lambda e, m=mc: e.tensor_copy(bdu[:, 1 + m:2 + m], pb1[:, 1:2]),
             reads=[b_pb1], writes=[b_bdu])
    c.pop_scope()
    wd_all = c.sb("l1f_wda", [128, 4096], BF16)
    b_wda = Buf()
    wu_all = c.sb("l1f_wua", [64, 8192], BF16)
    b_wua = Buf()
    c.dma("P", wd_all[:, :], P.hbm["w_l1_dn"][:, :], reads=[P.hbuf["w_l1_dn"]], writes=[b_wda])
    c.dma("P", wu_all[:, :], P.hbm["w_l1_up"][:, :], reads=[P.hbuf["w_l1_up"]], writes=[b_wua])
    for ei in range(8):
        c.op("V", lambda e, E=ei: e.tensor_scalar_mul(
            wd_all[:, E * 512:(E + 1) * 512], wd_all[:, E * 512:(E + 1) * 512], ohbc[:, E:E + 1]),
            reads=[b_ohbc], writes=[b_wda])
        c.op("V", lambda e, E=ei: e.tensor_scalar_mul(
            wu_all[:, E * 1024:(E + 1) * 1024], wu_all[:, E * 1024:(E + 1) * 1024], ohbc[0:64, E:E + 1]),
            reads=[b_ohbc], writes=[b_wua])

    base = c.sb("l1f_base", [128, DC * T], BF16)
    b_base = [Buf() for _ in range(DC)]
    gemm_fm(P, "w_l1_base", 8, range(8), rhs_xn, b_xn,
            lambda mc, tt, ps, bp: c.op("V", lambda e, m=mc, t=tt, p=ps: e.tensor_scalar(
                base[:, m * T + t * TT: m * T + (t + 1) * TT], p[:, :],
                P.t_c_l1_bb[:, m:m + 1], None, ALU.add),
                reads=[bp, P.b_c_l1_bb], writes=[b_base[mc]]), pool=gpool)

    mid = c.sb("l1f_mid", [64, T], BF16)
    b_mid = Buf()
    pm = [c.ps(f"l1f_pm{i}", [128, TT]) for i in range(2)]
    b_pm = [Buf(), Buf()]
    for tt in range(NTT):
        ps, bp = pm[tt % 2], b_pm[tt % 2]
        items = []
        n = 0
        for ei in range(8):
            for kc in range(8):
                items.append((ps[0:64, :], wd_all[:, (ei * 8 + kc) * 64:(ei * 8 + kc + 1) * 64],
                              rhs_xn(kc, tt), n == 0, n == 63))
                n += 1
        c.mm_group(items, reads=[b_wda] + b_xn, writes=[bp])
        c.op("S", lambda e, t=tt, p=ps: e.activation(
            mid[:, t * TT:(t + 1) * TT], p[0:64, :], AF.Relu, bias=bdu[0:64, 0:1]),
            reads=[bp, b_bdu], writes=[b_mid])

    tmp = c.sb("l1f_tmp", [128, TT], F32)
    b_tmp = Buf()
    gt = c.sb("l1f_gt", [128, TT], BF16)
    b_gt = Buf()
    for mc in range(8):
        for tt in range(NTT):
            ps, bp = pm[tt % 2], b_pm[tt % 2]
            items = []
            for ei in range(8):
                items.append((ps[:, :], wu_all[:, (ei * 8 + mc) * 128:(ei * 8 + mc + 1) * 128],
                              mid[:, tt * TT:(tt + 1) * TT], ei == 0, ei == 7))
            c.mm_group(items, reads=[b_wua, b_mid], writes=[bp])
            c.op("V", lambda e, m=mc, t=tt, p=ps: e.scalar_tensor_tensor(
                tmp[:, :], p[:, :], bdu[:, 1 + m:2 + m],
                base[:, m * T + t * TT: m * T + (t + 1) * TT], ALU.add, ALU.add),
                reads=[bp, b_base[mc], b_bdu], writes=[b_tmp])
            c.op("S", lambda e: e.activation(gt[:, :], tmp[:, :], AF.Silu),
                 reads=[b_tmp], writes=[b_gt])
            c.dma("Y", g_hbm[:, mc * T + tt * TT: mc * T + (tt + 1) * TT], gt[:, :],
                  reads=[b_gt], writes=[b_gh[mc]])
    c.pop_scope()
    c.pop_scope()

    emit_combine(P, x_hbm, x_buf, P.hbm["sp_s1"], b_sh, P.hbm["sp_g1"], b_gh,
                 out_hbm, out_buf, collect_stats=collect)



def emit_l2(P, x_hbm, x_buf, out_hbm, out_buf, pre_stats=False, collect=False):
    c = P.ctx
    c.push_scope()
    s_hbm = P.dram_scratch("sp_s2", [128, DC * T], F32)
    b_sh = P.hbuf8["sp_s2"]
    g_hbm = P.dram_scratch("sp_g2", [128, DC * T], BF16)
    b_gh = P.hbuf8["sp_g2"]
    xn_hbm = P.dram_scratch("sp_xn2", [128, DC * T], BF16)
    b_xnh = P.hbuf8["sp_xn2"]

    c.push_scope()
    xn2 = c.sb("l2_xn2", [128, DC * T], BF16)
    b_xn2 = [Buf() for _ in range(DC)]
    c.push_scope()
    xn = c.sb("l2_xn", [128, DC * T], BF16)
    b_xn = [Buf() for _ in range(DC)]
    emit_layernorm2(P, x_hbm, x_buf, xn, b_xn, pre_stats=pre_stats)
    for ch in range(DC):
        c.dma("Y", xn_hbm[:, ch * T:(ch + 1) * T], xn[:, ch * T:(ch + 1) * T],
              reads=[b_xn[ch]], writes=[b_xnh[ch]])
    emit_layernorm2(P, None, None, xn2, b_xn2,
                    gamma_col=(P.t_c_l2_scale[:, :], P.b_c_l2_scale),
                    beta_col=(P.t_c_l2_bias[:, :], P.b_c_l2_bias),
                    src_sb=(xn, b_xn))
    c.pop_scope()

    def rhs_xn2(kc, tt):
        return xn2[:, kc * T:][:, tt * TT:(tt + 1) * TT]

    v_tm = c.sb("l2_v", [128, 16 * T], BF16)
    b_v = [Buf() for _ in range(16)]
    bhv = c.sb("l2_bhv", [1, 2048], BF16)
    b_bhv = Buf()
    c.dma("P", bhv[:, :], P.hbm["r_l2_bhv"], reads=[P.hbuf["r_l2_bhv"]], writes=[b_bhv])
    c.push_scope()
    wv = c.sb("l2_wv", [128, 2 * 4096], BF16)
    b_wv = [Buf(), Buf()]
    pv = [c.ps(f"l2_pv{i}", [128, TT]) for i in range(2)]
    b_pv = [Buf(), Buf()]
    for nb in range(4):
        pb = nb % 2
        c.dma("P", wv[:, pb * 4096:(pb + 1) * 4096].rearrange("p (k n) -> p k n", k=8),
              P.hbm["w_l2_hv"].rearrange("p (k n) -> p k n", k=8)[:, :, nb * TT:(nb + 1) * TT],
              reads=[P.hbuf["w_l2_hv"]], writes=[b_wv[pb]])
        for kt in range(16):
            ps, bp = pv[kt % 2], b_pv[kt % 2]
            items = [(ps[:, :], P.ones_row_bf[0:1, 0:128],
                      bhv[0:1, nb * TT:(nb + 1) * TT], True, False)]
            for kc in range(DC):
                items.append((ps[:, :], xn2[:, kc * T + kt * 128: kc * T + (kt + 1) * 128],
                              wv[:, pb * 4096 + kc * TT: pb * 4096 + (kc + 1) * TT],
                              False, kc == DC - 1))
            c.mm_group(items, reads=b_xn2 + [b_wv[pb], b_bhv, P.b_ones_row], writes=[bp])
            c.op("S", lambda e, K=kt, n=nb, p=ps: e.activation(
                v_tm[:, K * T + n * TT: K * T + (n + 1) * TT], p[:, :], AF.Silu),
                reads=[bp], writes=[b_v[kt]])
    c.pop_scope()

    qb_sb = c.sb("l2_q", [128, T], BF16)
    kb_sb = c.sb("l2_k", [128, T], BF16)
    b_qb, b_kb = Buf(), Buf()
    qkf = c.sb("l2_qkf", [128, TT], F32)
    b_qkf = Buf()

    def ev_qk(mc, tt, ps, bp):
        c.op("S", lambda e, p=ps: e.activation(qkf[:, :], p[:, :], AF.Silu,
                                               bias=P.t_c_l2_bqk[:, 0:1]),
             reads=[bp, P.b_c_l2_bqk], writes=[b_qkf])
        c.op("V", lambda e, t=tt: e.tensor_scalar(
            qb_sb[:, t * TT:(t + 1) * TT], qkf[:, :], P.t_c_l2_osw[:, 0:1],
            P.t_c_l2_osb[:, 0:1], ALU.mult, ALU.add),
            reads=[b_qkf, P.b_c_l2_osw, P.b_c_l2_osb], writes=[b_qb])
        c.op("V", lambda e, t=tt: e.tensor_scalar(
            kb_sb[:, t * TT:(t + 1) * TT], qkf[:, :], P.t_c_l2_osw[:, 1:2],
            P.t_c_l2_osb[:, 1:2], ALU.mult, ALU.add),
            reads=[b_qkf, P.b_c_l2_osw, P.b_c_l2_osb], writes=[b_kb])
    c.push_scope()
    gemm_fm(P, "w_l2_qk", 8, range(1), rhs_xn2, b_xn2, ev_qk)
    c.pop_scope()

    relu2 = c.sb("l2_r2", [128, 16 * TT], BF16)
    b_r2 = [Buf() for _ in range(16)]
    og = c.sb("l2_og", [128, 16 * TT], BF16)
    b_og = [Buf() for _ in range(16)]
    tsc = c.sb("l2_tsc", [128, 2 * TT], BF16)
    b_tsc = [Buf(), Buf()]
    gsc = c.sb("l2_gsc", [128, 2 * TT], F32)
    b_gsc = [Buf(), Buf()]
    s_sb = c.sb("l2_s", [128, 2 * TT], F32)
    b_s = [Buf(), Buf()]
    psim = [c.ps(f"l2_psim{i}", [128, TT]) for i in range(2)]
    b_psim = [Buf(), Buf()]
    pav = c.ps("l2_pav", [128, TT])
    b_pav = Buf()
    pg = c.ps("l2_pg", [128, TT])
    b_pg = Buf()
    wgbuf = c.sb("l2_wg", [128, 2 * 1024], BF16)
    b_wg = [Buf(), Buf()]
    wobuf = c.sb("l2_wo", [128, 2 * 2048], BF16)
    b_wo = [Buf(), Buf()]
    pso = [c.ps(f"l2_pso{i}", [128, TT]) for i in range(2)]
    b_pso = [Buf(), Buf()]
    for qb in range(NTT):
        for kb in range(16):
            ps, bp = psim[kb % 2], b_psim[kb % 2]
            w2_ = slice((kb % 2) * TT, (kb % 2 + 1) * TT)
            c.mm_group([(ps[:, :], kb_sb[:, kb * 128:(kb + 1) * 128],
                         qb_sb[:, qb * TT:(qb + 1) * TT], True, True)],
                       reads=[b_kb, b_qb], writes=[bp])
            c.op("S", lambda e, p=ps, W=w2_: e.activation(tsc[:, W], p[:, :], AF.Relu,
                                                          scale=1.0 / T),
                 reads=[bp], writes=[b_tsc[kb % 2]])
            c.op("S", lambda e, K=kb, W=w2_: e.activation(relu2[:, K * TT:(K + 1) * TT],
                                                          tsc[:, W], AF.Square),
                 reads=[b_tsc[kb % 2]], writes=[b_r2[kb]])
        for vd in range(16):
            pb = vd % 2
            c.dma("P", wgbuf[:, pb * 1024:(pb + 1) * 1024],
                  P.hbm["w_l2_hg"][:, vd * 1024:(vd + 1) * 1024],
                  reads=[P.hbuf["w_l2_hg"]], writes=[b_wg[pb]])
            wv2 = slice((vd % 2) * TT, (vd % 2 + 1) * TT)
            items = []
            for kb in range(16):
                items.append((pav[:, :], v_tm[:, kb * T + vd * 128: kb * T + (vd + 1) * 128],
                              relu2[:, kb * TT:(kb + 1) * TT], kb == 0, kb == 15))
            c.mm_group(items, reads=b_v + b_r2, writes=[b_pav])
            itg = []
            for kc in range(DC):
                itg.append((pg[:, :], wgbuf[:, pb * 1024 + kc * 128: pb * 1024 + (kc + 1) * 128],
                            rhs_xn2(kc, qb), kc == 0, kc == DC - 1))
            c.mm_group(itg, reads=b_xn2 + [b_wg[pb]], writes=[b_pg])
            c.op("S", lambda e, m=vd, W=wv2: e.activation(gsc[:, W], pg[:, :], AF.Silu,
                                                          bias=P.t_c_l2_bhg[:, m:m + 1]),
                 reads=[b_pg, P.b_c_l2_bhg], writes=[b_gsc[vd % 2]])
            c.op("V", lambda e, m=vd, W=wv2: e.tensor_mul(og[:, m * TT:(m + 1) * TT],
                                                          pav[:, :], gsc[:, W]),
                 reads=[b_pav, b_gsc[vd % 2]], writes=[b_og[vd]])
        for mc in range(DC):
            pb = mc % 2
            c.dma("P", wobuf[:, pb * 2048:(pb + 1) * 2048],
                  P.hbm["w_l2_out"][:, mc * 2048:(mc + 1) * 2048],
                  reads=[P.hbuf["w_l2_out"]], writes=[b_wo[pb]])
            items = []
            for kc in range(16):
                items.append((pso[mc % 2][:, :], wobuf[:, pb * 2048 + kc * 128: pb * 2048 + (kc + 1) * 128],
                              og[:, kc * TT:(kc + 1) * TT], kc == 0, kc == 15))
            c.mm_group(items, reads=b_og + [b_wo[pb]], writes=[b_pso[mc % 2]])
            ws2 = slice((mc % 2) * TT, (mc % 2 + 1) * TT)
            c.op("V", lambda e, m=mc, p=pso[mc % 2], W=ws2: e.tensor_scalar(
                s_sb[:, W], p[:, :], P.t_c_l2_bout[:, m:m + 1], None, ALU.add),
                reads=[b_pso[mc % 2], P.b_c_l2_bout], writes=[b_s[mc % 2]])
            c.dma("Y", s_hbm[:, mc * T + qb * TT: mc * T + (qb + 1) * TT], s_sb[:, ws2],
                  reads=[b_s[mc % 2]], writes=[b_sh[mc]])
    c.pop_scope()

    c.push_scope()
    xn = c.sb("l2f_xn", [128, DC * T], BF16)
    b_xn = [Buf() for _ in range(DC)]
    for ch in range(DC):
        c.dma("Y", xn[:, ch * T:(ch + 1) * T], xn_hbm[:, ch * T:(ch + 1) * T],
              reads=[b_xnh[ch]], writes=[b_xn[ch]])
    gate_f8 = c.sb("l2f_g8", [8, 16 * 128], BF16)
    b_g8 = Buf()
    gate_hbm = P.dram_scratch("sp_l2gate", [8, T], BF16)
    b_ghb = P.hbuf["sp_l2gate"]
    grow = c.sb("l2f_grow", [1, 2 * TT], BF16)
    b_grow = [Buf(), Buf()]
    c.push_scope()
    wgt = c.sb("l2f_wgt", [128, 64], BF16)
    b_wgt = Buf()
    c.dma("P", wgt[:, :], P.hbm["w_l2_gw"], reads=[P.hbuf["w_l2_gw"]], writes=[b_wgt])
    gbr = c.sb("l2f_gbr", [1, 8], F32)
    b_gbr = Buf()
    c.dma("Y", gbr[:, :], P.hbm["r_l2_gb"], reads=[P.hbuf["r_l2_gb"]], writes=[b_gbr])
    pgl = c.ps("l2f_pgl", [128, 8])
    b_pgl = Buf()
    pbb = c.ps("l2f_pbb", [128, 8])
    b_pbb = Buf()
    biasbc = c.sb("l2f_bb", [128, 8], F32)
    b_bb = Buf()
    c.mm_group([(pbb[:, :], P.ones_row[0:1, :], gbr[0:1, :], True, True)],
               reads=[b_gbr, P.b_ones_row], writes=[b_pbb])
    c.op("V", lambda e: e.tensor_copy(biasbc[:, :], pbb[:, :]), reads=[b_pbb], writes=[b_bb])
    gsm = c.sb("l2f_gsm", [128, 32], F32)
    b_gs = Buf()
    gti = c.sb("l2f_gti", [128, 8], BF16)
    b_gti = Buf()
    ptr = c.ps("l2f_ptr", [8, 128], BF16)
    b_ptr = Buf()
    for tk in range(16):
        items = []
        for kc in range(DC):
            items.append((pgl[:, :], xn[:, kc * T + tk * 128: kc * T + (tk + 1) * 128],
                          wgt[:, kc * 8:(kc + 1) * 8], kc == 0, kc == DC - 1))
        c.mm_group(items, reads=b_xn + [b_wgt], writes=[b_pgl])
        c.op("V", lambda e: e.reduce_max(gsm[:, 16:17], pgl[:, :], axis=AX.X),
             reads=[b_pgl], writes=[b_gs])
        c.op("V", lambda e: e.tensor_scalar_mul(gsm[:, 17:18], gsm[:, 16:17], -1.0),
             reads=[b_gs], writes=[b_gs])
        c.op("S", lambda e: e.activation(gsm[:, 0:8], pgl[:, :], AF.Exp, bias=gsm[:, 17:18]),
             reads=[b_pgl, b_gs], writes=[b_gs])
        c.op("V", lambda e: e.reduce_sum(gsm[:, 18:19], gsm[:, 0:8], axis=AX.X),
             reads=[b_gs], writes=[b_gs])
        c.op("V", lambda e: e.reciprocal(gsm[:, 19:20], gsm[:, 18:19]),
             reads=[b_gs], writes=[b_gs])
        c.op("V", lambda e: e.tensor_scalar_mul(gsm[:, 0:8], gsm[:, 0:8], gsm[:, 19:20]),
             reads=[b_gs], writes=[b_gs])
        c.op("V", lambda e: e.tensor_add(gsm[:, 8:16], gsm[:, 0:8], biasbc[:, :]),
             reads=[b_gs, b_bb], writes=[b_gs])
        c.op("V", lambda e: e.reduce_max(gsm[:, 16:17], gsm[:, 8:16], axis=AX.X),
             reads=[b_gs], writes=[b_gs])
        c.op("V", lambda e: e.tensor_scalar(gsm[:, 20:28], gsm[:, 8:16], gsm[:, 16:17],
                                            None, ALU.is_ge), reads=[b_gs], writes=[b_gs])
        c.op("V", lambda e: e.scalar_tensor_tensor(gsm[:, 20:28], gsm[:, 20:28], -1e30,
                                                   gsm[:, 8:16], ALU.mult, ALU.add),
             reads=[b_gs], writes=[b_gs])
        c.op("V", lambda e: e.reduce_max(gsm[:, 16:17], gsm[:, 20:28], axis=AX.X),
             reads=[b_gs], writes=[b_gs])
        c.op("V", lambda e: e.tensor_scalar(gsm[:, 20:28], gsm[:, 8:16], gsm[:, 16:17],
                                            None, ALU.is_ge), reads=[b_gs], writes=[b_gs])
        c.op("V", lambda e: e.tensor_mul(gti[:, :], gsm[:, 0:8], gsm[:, 20:28]),
             reads=[b_gs], writes=[b_gti])
        c.transpose(ptr[0:8, :], gti[:, :], P.ident_bf[:, :], reads=[b_gti, P.b_ident],
                    writes=[b_ptr])
        c.op("V", lambda e, K=tk: e.tensor_copy(gate_f8[:, K * 128:(K + 1) * 128], ptr[0:8, :]),
             reads=[b_ptr], writes=[b_g8])
    c.pop_scope()
    c.dma("Y", gate_hbm[:, :], gate_f8[:, :], reads=[b_g8], writes=[b_ghb])

    h_bf = c.sb("l2f_h", [128, DC * T], BF16)
    b_h = [Buf() for _ in range(DC)]
    yacc = c.sb("l2f_y", [128, DC * T], F32)
    b_y = [Buf() for _ in range(DC)]
    gbc = c.sb("l2f_gbc", [128, TT], F32)
    b_gbc = Buf()
    tmp1 = c.sb("l2f_t1", [128, 2 * TT], F32)
    b_t1 = [Buf(), Buf()]
    tmp2 = c.sb("l2f_t2", [128, 2 * TT], F32)
    b_t2 = [Buf(), Buf()]
    w13 = c.sb("l2f_w13", [128, 4 * 1024], BF16)
    b_w13 = [Buf(), Buf(), Buf(), Buf()]
    w2b = c.sb("l2f_w2b", [128, 2 * 1024], BF16)
    b_w2b = [Buf(), Buf()]
    pb1 = [c.ps(f"l2f_ph{i}", [128, TT]) for i in range(2)]
    b_pb1 = [Buf(), Buf()]
    pb3 = [c.ps(f"l2f_ph3{i}", [128, TT]) for i in range(2)]
    b_pb3 = [Buf(), Buf()]
    pyy = [c.ps(f"l2f_py{i}", [128, TT]) for i in range(2)]
    b_py = [Buf(), Buf()]
    pgb = c.ps("l2f_pgb", [128, TT])
    b_pgb = Buf()

    def rhs_xnf(kc, tt):
        return xn[:, kc * T:][:, tt * TT:(tt + 1) * TT]

    def rhs_h(kc, tt):
        return h_bf[:, kc * T:][:, tt * TT:(tt + 1) * TT]

    gbc4 = c.sb("l2f_gbc4", [128, NTT * TT], F32)
    b_gbc4 = [Buf() for _ in range(NTT)]
    for eb in range(10):
        shared = eb >= 8
        sh = eb - 8
        if not shared:
            for tt in range(NTT):
                gw = slice((tt % 2) * TT, (tt % 2 + 1) * TT)
                c.dma("Y", grow[0:1, gw], gate_hbm[eb:eb + 1, tt * TT:(tt + 1) * TT],
                      reads=[b_ghb], writes=[b_grow[tt % 2]])
                c.mm_group([(pgb[:, :], P.ones_row_bf[0:1, :], grow[0:1, gw], True, True)],
                           reads=[b_grow[tt % 2], P.b_ones_row], writes=[b_pgb])
                c.op("V", lambda e, t=tt: e.tensor_copy(gbc4[:, t * TT:(t + 1) * TT], pgb[:, :]),
                     reads=[b_pgb], writes=[b_gbc4[tt]])
        for ic in range(DC):
            pb = ic % 2
            if shared:
                c.dma("P", w13[:, pb * 1024:(pb + 1) * 1024],
                      P.hbm["w_l2_sw1"][:, (sh * 8 + ic) * 1024:(sh * 8 + ic + 1) * 1024],
                      reads=[P.hbuf["w_l2_sw1"]], writes=[b_w13[pb]])
                c.dma("P", w13[:, 2048 + pb * 1024: 2048 + (pb + 1) * 1024],
                      P.hbm["w_l2_sw3"][:, (sh * 8 + ic) * 1024:(sh * 8 + ic + 1) * 1024],
                      reads=[P.hbuf["w_l2_sw3"]], writes=[b_w13[2 + pb]])
            else:
                c.dma("P", w13[:, pb * 1024:(pb + 1) * 1024],
                      P.hbm["w_l2_w1"][:, eb * 8192 + ic * 1024: eb * 8192 + (ic + 1) * 1024],
                      reads=[P.hbuf["w_l2_w1"]], writes=[b_w13[pb]])
                c.dma("P", w13[:, 2048 + pb * 1024: 2048 + (pb + 1) * 1024],
                      P.hbm["w_l2_w3"][:, eb * 8192 + ic * 1024: eb * 8192 + (ic + 1) * 1024],
                      reads=[P.hbuf["w_l2_w3"]], writes=[b_w13[2 + pb]])
            for tt in range(NTT):
                p1, bp1s = pb1[tt % 2], b_pb1[tt % 2]
                p3, bp3s = pb3[tt % 2], b_pb3[tt % 2]
                it1, it3 = [], []
                for kc in range(DC):
                    it1.append((p1[:, :], w13[:, pb * 1024 + kc * 128: pb * 1024 + (kc + 1) * 128],
                                rhs_xnf(kc, tt), kc == 0, kc == DC - 1))
                    it3.append((p3[:, :], w13[:, 2048 + pb * 1024 + kc * 128: 2048 + pb * 1024 + (kc + 1) * 128],
                                rhs_xnf(kc, tt), kc == 0, kc == DC - 1))
                c.mm_group(it1, reads=b_xn + [b_w13[pb]], writes=[bp1s])
                c.mm_group(it3, reads=b_xn + [b_w13[2 + pb]], writes=[bp3s])
                wl = slice((tt % 2) * TT, (tt % 2 + 1) * TT)
                c.op("S", lambda e, p=p1, W=wl: e.activation(tmp1[:, W], p[:, :], AF.Silu),
                     reads=[bp1s], writes=[b_t1[tt % 2]])
                if not shared:
                    c.op("P", lambda e, t=tt, W=wl: e.tensor_mul(tmp2[:, W], tmp1[:, W],
                                                                 gbc4[:, t * TT:(t + 1) * TT]),
                         reads=[b_t1[tt % 2], b_gbc4[tt]], writes=[b_t2[tt % 2]])
                    tsrc, btsrc = tmp2, b_t2[tt % 2]
                else:
                    tsrc, btsrc = tmp1, b_t1[tt % 2]
                c.op("V", lambda e, i=ic, t=tt, p=p3, ts=tsrc, W=wl: e.scalar_tensor_tensor(
                    h_bf[:, i * T + t * TT: i * T + (t + 1) * TT], p[:, :], 1.0, ts[:, W],
                    ALU.mult, ALU.mult),
                    reads=[bp3s, btsrc], writes=[b_h[ic]])
        for mc in range(DC):
            pb = mc % 2
            if shared:
                c.dma("P", w2b[:, pb * 1024:(pb + 1) * 1024],
                      P.hbm["w_l2_sw2"][:, mc * 2048 + sh * 1024: mc * 2048 + (sh + 1) * 1024],
                      reads=[P.hbuf["w_l2_sw2"]], writes=[b_w2b[pb]])
            else:
                c.dma("P", w2b[:, pb * 1024:(pb + 1) * 1024],
                      P.hbm["w_l2_w2"][:, eb * 8192 + mc * 1024: eb * 8192 + (mc + 1) * 1024],
                      reads=[P.hbuf["w_l2_w2"]], writes=[b_w2b[pb]])
            for tt in range(NTT):
                py, bpy = pyy[tt % 2], b_py[tt % 2]
                items = []
                for kc in range(DC):
                    items.append((py[:, :], w2b[:, pb * 1024 + kc * 128: pb * 1024 + (kc + 1) * 128],
                                  rhs_h(kc, tt), kc == 0, kc == DC - 1))
                c.mm_group(items, reads=b_h + [b_w2b[pb]], writes=[bpy])
                if eb == 0:
                    c.op("V", lambda e, m=mc, t=tt, p=py: e.tensor_copy(
                        yacc[:, m * T + t * TT: m * T + (t + 1) * TT], p[:, :]),
                        reads=[bpy], writes=[b_y[mc]])
                else:
                    c.op("V", lambda e, m=mc, t=tt, p=py: e.tensor_add(
                        yacc[:, m * T + t * TT: m * T + (t + 1) * TT],
                        yacc[:, m * T + t * TT: m * T + (t + 1) * TT], p[:, :]),
                        reads=[bpy, b_y[mc]], writes=[b_y[mc]])
    gtile = c.sb("l2f_gt", [128, 2 * TT], BF16)
    b_gt2 = [Buf(), Buf()]
    for mc in range(DC):
        for tt in range(NTT):
            wl = slice((tt % 2) * TT, (tt % 2 + 1) * TT)
            c.op("S", lambda e, m=mc, t=tt, W=wl: e.activation(
                gtile[:, W], yacc[:, m * T + t * TT: m * T + (t + 1) * TT], AF.Silu),
                reads=[b_y[mc]], writes=[b_gt2[tt % 2]])
            c.dma("Y", g_hbm[:, mc * T + tt * TT: mc * T + (tt + 1) * TT], gtile[:, wl],
                  reads=[b_gt2[tt % 2]], writes=[b_gh[mc]])
    c.pop_scope()
    c.pop_scope()
    emit_combine(P, x_hbm, x_buf, P.hbm["sp_s2"], b_sh, P.hbm["sp_g2"], b_gh,
                 out_hbm, out_buf, collect_stats=collect)



def emit_l3(P, x_hbm, x_buf, out_hbm, out_buf, pre_stats=False, collect=False):
    c = P.ctx
    CH = 64
    NC = T // CH
    c.push_scope()
    xn = c.sb("l3_xn", [128, DC * T], BF16)
    b_xn = [Buf() for _ in range(DC)]
    emit_layernorm2(P, x_hbm, x_buf, xn, b_xn, pre_stats=pre_stats)
    s_hbm = P.dram_scratch("sp_s3", [128, DC * T], F32)
    b_sh = P.hbuf8["sp_s3"]
    g_hbm = P.dram_scratch("sp_g3", [128, DC * T], BF16)
    b_gh = P.hbuf8["sp_g3"]

    def rhs_xn(kc, tt):
        return xn[:, kc * T:][:, tt * TT:(tt + 1) * TT]

    c.push_scope()
    TP = T + 4
    qkv = c.sb("l3_qkv", [128, 12 * TP], BF16)
    b_qkv = [Buf() for _ in range(12)]
    for i in range(12):
        c.op("V", lambda e, I=i: e.memset(qkv[:, I * TP:I * TP + 4], 0.0), writes=[b_qkv[i]])
    gate = c.sb("l3_gate", [128, DC * T], BF16)
    b_gate = [Buf() for _ in range(DC)]
    btm = c.sb("l3_btm", [128, 128], F32)
    b_btm = Buf()
    blk2e = c.sb("l3_blk2e", [128, 2], BF16)
    b_b2e = Buf()
    sel2e = c.sb("l3_sel2e", [2, 128], BF16)
    b_s2e = Buf()
    it_a = c.sb("l3_ita", [128, 2], I32)
    it_b = c.sb("l3_itb", [128, 2], I32)
    b_ita, b_itb = Buf(), Buf()
    c.op("P", lambda e: e.iota(it_a[:, 0:2], [[0, 2]], channel_multiplier=1,
                               allow_small_or_imprecise_dtypes=True), writes=[b_ita])
    c.op("P", lambda e: e.iota(it_b[:, 0:2], [[1, 2]], channel_multiplier=0,
                               allow_small_or_imprecise_dtypes=True), writes=[b_itb])
    c.op("V", lambda e: e.tensor_single_scalar(it_a[:, 0:2], it_a[:, 0:2], 6,
                                               ALU.arith_shift_right),
         reads=[b_ita], writes=[b_ita])
    c.op("V", lambda e: e.tensor_tensor(blk2e[:, :], it_a[:, 0:2], it_b[:, 0:2], ALU.is_equal),
         reads=[b_ita, b_itb], writes=[b_b2e])
    it_c = c.sb("l3_itc", [2, 128], I32)
    it_d = c.sb("l3_itd", [2, 128], I32)
    c.op("P", lambda e: e.iota(it_c[0:2, 0:128], [[1, 128]], channel_multiplier=0,
                               allow_small_or_imprecise_dtypes=True), writes=[b_ita])
    c.op("P", lambda e: e.iota(it_d[0:2, 0:128], [[0, 128]], channel_multiplier=1,
                               allow_small_or_imprecise_dtypes=True), writes=[b_itb])
    c.op("V", lambda e: e.tensor_single_scalar(it_c[0:2, 0:128], it_c[0:2, 0:128], 6,
                                               ALU.arith_shift_right),
         reads=[b_ita], writes=[b_ita])
    c.op("V", lambda e: e.tensor_tensor(sel2e[:, :], it_c[0:2, 0:128], it_d[0:2, 0:128],
                                        ALU.is_equal),
         reads=[b_ita, b_itb], writes=[b_s2e])

    c.push_scope()
    prj = c.sb("l3_prj", [128, TP], BF16)
    b_prj = Buf()
    c.op("V", lambda e: e.memset(prj[:, 0:4], 0.0), writes=[b_prj])
    diag = c.sb("l3_diag", [128, 4 * 128], BF16)
    b_diag = Buf()
    ppool = ([c.ps(f"l3_pp{i}", [128, TT]) for i in range(2)], [Buf(), Buf()])
    pcv = c.ps("l3_pcv", [128, TT])
    b_pcv = Buf()
    for name, wkey, nch, cwcol, cwbuf, qoff in [
            ("q", "w_l3_q", 8, "t_c_l3_cwq", "b_c_l3_cwq", 0),
            ("k", "w_l3_k", 2, "t_c_l3_cwk", "b_c_l3_cwk", 8),
            ("v", "w_l3_v", 2, "t_c_l3_cwv", "b_c_l3_cwv", 10)]:
        cw = getattr(P, cwcol)
        bcw = getattr(P, cwbuf)

        def ev_prj(mc, tt, ps, bp, qoff=qoff, cw=cw, bcw=bcw):
            c.op("V", lambda e, t=tt, p=ps: e.tensor_copy(prj[:, 4 + t * TT: 4 + (t + 1) * TT], p[:, :]),
                 reads=[bp], writes=[b_prj])
            if tt == NTT - 1:
                for j in range(4):
                    c.op("V", lambda e, m=mc, jj=j: e.tensor_scalar_mul(
                        diag[:, jj * 128:(jj + 1) * 128], P.ident_bf[:, :], cw[:, m * 4 + jj:m * 4 + jj + 1]),
                        reads=[P.b_ident, bcw], writes=[b_diag])
                for t2 in range(NTT):
                    items = []
                    for j in range(4):
                        sh = slice(4 + t2 * TT - (3 - j), 4 + (t2 + 1) * TT - (3 - j))
                        items.append((pcv[:, :], diag[:, j * 128:(j + 1) * 128], prj[:, sh],
                                      j == 0, j == 3))
                    c.mm_group(items, reads=[b_diag, b_prj], writes=[b_pcv])
                    ix = qoff + mc
                    c.op("V", lambda e, I=ix, t=t2: e.tensor_copy(
                        qkv[:, I * TP + 4 + t * TT: I * TP + 4 + (t + 1) * TT], pcv[:, :]),
                        reads=[b_pcv], writes=[b_qkv[ix]])
        gemm_fm(P, wkey, 8, range(nch), rhs_xn, b_xn, ev_prj, pool=ppool)
    gemm_fm(P, "w_l3_gate", 8, range(8), rhs_xn, b_xn,
            lambda mc, tt, ps, bp: c.op("S", lambda e, m=mc, t=tt, p=ps: e.activation(
                gate[:, m * T + t * TT: m * T + (t + 1) * TT], p[:, :], AF.Silu),
                reads=[bp], writes=[b_gate[mc]]), pool=ppool)

    bg = c.sb("l3_bg", [64, T], F32)
    b_bg = Buf()
    wbg = c.sb("l3_wbg", [128, 512], BF16)
    b_wbg = Buf()
    c.dma("P", wbg[:, :], P.hbm["w_l3_bg"], reads=[P.hbuf["w_l3_bg"]], writes=[b_wbg])
    pbg = c.ps("l3_pbg", [64, TT])
    b_pbg = Buf()
    for tt in range(NTT):
        items = []
        for kc in range(DC):
            items.append((pbg[:, :], wbg[:, kc * 64:(kc + 1) * 64], rhs_xn(kc, tt),
                          kc == 0, kc == DC - 1))
        c.mm_group(items, reads=b_xn + [b_wbg], writes=[b_pbg])
        c.op("V", lambda e, t=tt: e.tensor_copy(bg[:, t * TT:(t + 1) * TT], pbg[:, :]),
             reads=[b_pbg], writes=[b_bg])
    c.op("S", lambda e: e.activation(bg[0:4, :], bg[0:4, :], AF.Sigmoid,
                                     bias=P.t_c_l3_bb64[0:4, 0:1]),
         reads=[b_bg, P.b_c_l3_bb64], writes=[b_bg])
    gsc = c.sb("l3_gsc", [64, 2 * T], F32)
    b_gsc = Buf()
    c.op("S", lambda e: e.activation(gsc[32:64, 0:T], bg[32:64, :], AF.Abs,
                                     bias=P.t_c_l3_dt64[32:64, 0:1]),
         reads=[b_bg, P.b_c_l3_dt64], writes=[b_gsc])
    c.op("S", lambda e: e.activation(gsc[32:64, 0:T], gsc[32:64, 0:T], AF.Exp, scale=-1.0),
         reads=[b_gsc], writes=[b_gsc])
    c.op("S", lambda e: e.activation(gsc[32:64, 0:T], gsc[32:64, 0:T], AF.Ln, bias=1.0),
         reads=[b_gsc], writes=[b_gsc])
    c.op("S", lambda e: e.activation(gsc[32:64, T:2 * T], bg[32:64, :], AF.Relu,
                                     bias=P.t_c_l3_dt64[32:64, 0:1]),
         reads=[b_bg, P.b_c_l3_dt64], writes=[b_gsc])
    c.op("V", lambda e: e.tensor_add(bg[32:64, :], gsc[32:64, 0:T], gsc[32:64, T:2 * T]),
         reads=[b_gsc], writes=[b_bg])
    negA = c.sb("l3_negA", [64, 1], F32)
    b_nA = Buf()
    c.op("S", lambda e: e.activation(negA[32:64, :], P.t_c_l3_al64[32:64, 0:1], AF.Exp),
         reads=[P.b_c_l3_al64], writes=[b_nA])
    c.op("V", lambda e: e.tensor_scalar_mul(negA[32:64, :], negA[32:64, :], -1.0),
         reads=[b_nA], writes=[b_nA])
    c.op("V", lambda e: e.tensor_scalar_mul(bg[32:64, :], bg[32:64, :], negA[32:64, 0:1]),
         reads=[b_bg, b_nA], writes=[b_bg])
    zero4 = c.sb("l3_z4", [64, 1], F32)
    b_z4 = Buf()
    c.op("V", lambda e: e.memset(zero4[:, :], 0.0), writes=[b_z4])
    for ci in range(NC):
        c.op("V", lambda e, i=ci: e.tensor_tensor_scan(
            bg[32:64, i * CH:(i + 1) * CH], bg[32:64, i * CH:(i + 1) * CH],
            zero4[32:64, 0:1].broadcast_to([32, CH]), 0.0, ALU.add, ALU.add),
            reads=[b_bg, b_z4], writes=[b_bg])
    bounce = P.dram_scratch("sp_l3bg", [8, T], F32)
    b_bo = P.hbuf["sp_l3bg"]
    c.dma("Y", P.hbm["sp_l3bg"][0:4, :], bg[0:4, :], reads=[b_bg], writes=[b_bo])
    c.dma("Y", P.hbm["sp_l3bg"][4:8, :], bg[32:36, :], reads=[b_bg], writes=[b_bo])
    if DEBUG_SPILL:
        dbg2 = P.dram_scratch("sp_l3bg2", [64, T], F32)
        c.dma("Y", P.hbm["sp_l3bg2"], bg[:, :], reads=[b_bg], writes=[P.hbuf["sp_l3bg2"]])
    bgb_b = c.sb("l3_bgb_b", [4, T], BF16)
    bgb_g = c.sb("l3_bgb_g", [4, T], BF16)
    b_bgb = Buf()
    c.op("V", lambda e: e.tensor_copy(bgb_b[0:4, :], bg[0:4, :]), reads=[b_bg], writes=[b_bgb])
    c.op("V", lambda e: e.tensor_copy(bgb_g[0:4, :], bg[32:64, :][0:4, :]), reads=[b_bg], writes=[b_bgb])
    ptb = c.ps("l3_ptb", [128, 8], BF16)
    b_ptb = Buf()
    for tk in range(16):
        c.transpose(ptb[:, 0:4], bgb_b[0:4, tk * 128:(tk + 1) * 128], P.ident_bf[0:4, 0:4],
                    reads=[b_bgb, P.b_ident], writes=[b_ptb])
        c.op("V", lambda e, K=tk: e.tensor_copy(btm[:, K * 4:(K + 1) * 4], ptb[:, 0:4]),
             reads=[b_ptb], writes=[b_btm])
        c.transpose(ptb[:, 4:8], bgb_g[0:4, tk * 128:(tk + 1) * 128], P.ident_bf[0:4, 0:4],
                    reads=[b_bgb, P.b_ident], writes=[b_ptb])
        c.op("V", lambda e, K=tk: e.tensor_copy(btm[:, 64 + K * 4:64 + (K + 1) * 4], ptb[:, 4:8]),
             reads=[b_ptb], writes=[b_btm])

    if DEBUG_SPILL:
        dbt = P.dram_scratch("sp_l3btm", [128, 128], F32)
        c.dma("Y", P.hbm["sp_l3btm"], btm[:, :], reads=[b_btm], writes=[P.hbuf["sp_l3btm"]])
    nsq = c.sb("l3_nsq", [128, TT], BF16)
    b_nsq = Buf()
    nrst = c.sb("l3_nrst", [2, TT], F32)
    b_nrst = Buf()
    nrsb = c.sb("l3_nrsb", [2, TT], BF16)
    b_nrsb = Buf()
    pn2 = c.ps("l3_pn2", [2, TT])
    b_pn2 = Buf()
    pnb = c.ps("l3_pnb", [128, TT])
    b_pnb = Buf()
    for ix in range(10):
        for tt in range(NTT):
            sl = slice(ix * TP + 4 + tt * TT, ix * TP + 4 + (tt + 1) * TT)
            c.op("S", lambda e, s=sl: e.activation(nsq[:, :], qkv[:, s], AF.Square),
                 reads=[b_qkv[ix]], writes=[b_nsq])
            c.mm_group([(pn2[0:2, 0:TT], blk2e[:, :], nsq[:, :], True, True)],
                       reads=[b_nsq, b_b2e], writes=[b_pn2])
            c.op("V", lambda e: e.tensor_scalar_add(nrst[:, :], pn2[0:2, 0:TT], 1e-12),
                 reads=[b_pn2], writes=[b_nrst])
            c.op("S", lambda e: e.activation(nrst[:, :], nrst[:, :], AF.Ln),
                 reads=[b_nrst], writes=[b_nrst])
            c.op("S", lambda e: e.activation(nrst[:, :], nrst[:, :], AF.Exp, scale=-0.5),
                 reads=[b_nrst], writes=[b_nrst])
            c.op("V", lambda e: e.tensor_copy(nrsb[:, :], nrst[:, :]), reads=[b_nrst], writes=[b_nrsb])
            c.mm_group([(pnb[:, :], sel2e[:, :], nrsb[0:2, :], True, True)],
                       reads=[b_nrsb, b_s2e], writes=[b_pnb])
            c.op("V", lambda e, s=sl: e.tensor_mul(qkv[:, s], qkv[:, s], pnb[:, :]),
                 reads=[b_qkv[ix], b_pnb], writes=[b_qkv[ix]])
    c.pop_scope()

    mlo = c.sb("l3_mlo", [128, 192], BF16)
    b_ml = Buf()
    io64 = c.sb("l3_io64", [128, 64], I32)
    b_i64 = Buf()
    c.op("P", lambda e: e.iota(io64[:, :], [[1, 64]], channel_multiplier=-1,
                               allow_small_or_imprecise_dtypes=True), writes=[b_i64])
    c.op("V", lambda e: e.tensor_single_scalar(mlo[0:64, 0:64], io64[0:64, :], 0, ALU.is_lt),
         reads=[b_i64], writes=[b_ml])
    c.op("V", lambda e: e.tensor_single_scalar(mlo[0:64, 64:128], io64[0:64, :], 0, ALU.is_gt),
         reads=[b_i64], writes=[b_ml])
    c.op("V", lambda e: e.tensor_single_scalar(mlo[0:64, 128:192], io64[0:64, :], 0, ALU.is_ge),
         reads=[b_i64], writes=[b_ml])
    c.op("V", lambda e: e.tensor_scalar_mul(mlo[0:64, 0:128], mlo[0:64, 0:128], -1.0),
         reads=[b_ml], writes=[b_ml])
    rowbuf2 = [c.sb(f"l3_rowbuf{i}", [1, 576], F32) for i in range(2)]
    b_fl2 = [Buf(), Buf()]

    o_fm = c.sb("l3_o", [128, DC * T], BF16)
    b_o = [Buf() for _ in range(DC)]
    Sf = c.sb("l3_Sf", [64, 4 * 64], F32)
    b_Sf = [Buf() for _ in range(4)]
    Sb = c.sb("l3_Sb", [64, 4 * 64], BF16)
    b_Sb = [Buf() for _ in range(4)]
    for h in range(4):
        c.op("V", lambda e, H=h: e.memset(Sf[:, H * 64:(H + 1) * 64], 0.0), writes=[b_Sf[h]])
        c.op("V", lambda e, H=h: e.memset(Sb[:, H * 64:(H + 1) * 64], 0.0), writes=[b_Sb[h]])

    scr4 = [c.sb(f"l3_scr{k}", [64, 9 * 64], F32) for k in range(4)]
    b_scr4 = [Buf() for _ in range(4)]
    sbb4 = [c.sb(f"l3_sbb{k}", [64, 15 * 64], BF16) for k in range(4)]
    b_sbb4 = [Buf() for _ in range(4)]
    col4 = [c.sb(f"l3_col{k}", [64, 8], F32) for k in range(4)]
    b_col4 = [Buf() for _ in range(4)]
    rb4 = [c.sb(f"l3_rb{k}", [1, 64], F32) for k in range(4)]
    b_rb4 = [Buf() for _ in range(4)]
    pf4 = [[c.ps(f"l3_pf{k}_{i}", [128, TT]) for i in range(2)] for k in range(4)]
    b_pf4 = [[Buf(), Buf()] for k in range(4)]

    def kq_slice(idx, rh, ci):
        return qkv[:, idx * TP + 4:][:, ci * CH:(ci + 1) * CH][64 * rh:64 * rh + 64, :]

    def emit_kv_instance(ci, kv, tk, sub, rowbuf, b_fl):
        kidx, krh = 8 + kv // 2, kv % 2
        scr, b_scrL = scr4[kv], b_scr4[kv]
        sbb, b_sbbL = sbb4[kv], b_sbb4[kv]
        col, b_colL = col4[kv], b_col4[kv]
        rbk, b_rbk = rb4[kv], b_rb4[kv]
        pA, pB = pf4[kv]
        bA, bB = b_pf4[kv][0], b_pf4[kv][1]
        pAb = pA[:, :].bitcast(BF16)
        pBb = pB[:, :].bitcast(BF16)

        def S(slot):
            return sbb[:, slot * 64:(slot + 1) * 64]

        def R(slot):
            return scr[:, slot * 64:(slot + 1) * 64]

        def stg(slot, idx, rh):
            c.op("P", lambda e, s=slot, i=idx, r=rh, CI=ci: e.tensor_copy(
                S(s), kq_slice(i, r, CI)), reads=[b_qkv[idx]], writes=[b_sbbL])
            return S(slot)

        gf = rowbuf[0:1, 256 + kv * CH: 256 + (kv + 1) * CH]
        nb = rowbuf[0:1, kv * CH: (kv + 1) * CH]
        gc1 = rowbuf[0:1, 256 + kv * CH + CH - 1: 256 + (kv + 1) * CH]
        c.op("V", lambda e, Sb_=sub, K=tk, V=kv: e.tensor_copy(col[:, 0:1],
             btm[64 * Sb_:64 * Sb_ + 64, 64 + K * 4 + V:64 + K * 4 + V + 1]),
             reads=[b_btm], writes=[b_colL])
        c.op("V", lambda e, Sb_=sub, K=tk, V=kv: e.tensor_copy(col[:, 2:3],
             btm[64 * Sb_:64 * Sb_ + 64, K * 4 + V:K * 4 + V + 1]),
             reads=[b_btm], writes=[b_colL])
        yield
        c.op("S", lambda e: e.activation(col[:, 1:2], col[:, 0:1], AF.Exp),
             reads=[b_colL], writes=[b_colL])
        c.op("P", lambda e: e.tensor_mul(col[:, 3:4], col[:, 1:2], col[:, 2:3]),
             reads=[b_colL], writes=[b_colL])
        c.op("P", lambda e: e.tensor_scalar_mul(col[:, 3:4], col[:, 3:4], -1.0),
             reads=[b_colL], writes=[b_colL])
        yield
        c.mm_group([(pA[0:64, 0:CH], P.ones_row[0:1, 0:64], gf, True, True)],
                   reads=[b_fl, P.b_ones_row], writes=[bA])
        c.op("V", lambda e: e.tensor_scalar(R(4), pA[0:64, 0:CH], col[:, 0:1], None,
                                            ALU.subtract),
             reads=[bA, b_colL], writes=[b_scrL])
        yield
        c.op("S", lambda e: e.activation(R(0), R(4), AF.Exp), reads=[b_scrL], writes=[b_scrL])
        c.op("S", lambda e: e.activation(R(1), R(4), AF.Exp, scale=-1.0),
             reads=[b_scrL], writes=[b_scrL])
        yield
        c.mm_group([(pA[0:64, 0:CH], P.ones_row[0:1, 0:64], nb, True, True)],
                   reads=[b_fl, P.b_ones_row], writes=[bA])
        c.op("S", lambda e: e.activation(R(3), pA[0:64, 0:CH], AF.Copy), reads=[bA], writes=[b_scrL])
        yield
        c.mm_group([(pA[0:64, 0:1], P.ones_row[0:1, 0:64], gc1, True, True)],
                   reads=[b_fl, P.b_ones_row], writes=[bA])
        c.op("S", lambda e: e.activation(col[:, 5:6], pA[0:64, 0:1], AF.Exp),
             reads=[bA], writes=[b_colL])
        c.op("V", lambda e: e.tensor_sub(col[:, 6:7], pA[0:64, 0:1], col[:, 0:1]),
             reads=[bA, b_colL], writes=[b_colL])
        c.op("S", lambda e: e.activation(col[:, 4:5], col[:, 6:7], AF.Exp),
             reads=[b_colL], writes=[b_colL])
        yield
        ks = stg(12, kidx, krh)
        c.mm_group([(pA[0:64, 0:CH], ks, ks, True, True)], reads=[b_sbbL], writes=[bA])
        yield
        c.op("V", lambda e: e.tensor_mul(R(5), pA[0:64, 0:CH], R(1)),
             reads=[bA, b_scrL], writes=[b_scrL])
        c.op("P", lambda e: e.tensor_scalar_mul(R(5), R(5), col[:, 2:3]),
             reads=[b_scrL, b_colL], writes=[b_scrL])
        c.op("P", lambda e: e.tensor_mul(S(1), R(5), mlo[0:64, 0:64]),
             reads=[b_scrL, b_ml], writes=[b_sbbL])
        yield
        c.op("V", lambda e: e.tensor_mul(R(6), pA[0:64, 0:CH], R(0)),
             reads=[bA, b_scrL], writes=[b_scrL])
        c.op("P", lambda e: e.tensor_mul(R(6), R(6), R(3)),
             reads=[b_scrL], writes=[b_scrL])
        c.op("P", lambda e: e.tensor_mul(S(0), R(6), mlo[0:64, 64:128]),
             reads=[b_scrL, b_ml], writes=[b_sbbL])
        yield
        c.op("P", lambda e: e.tensor_add(S(6), S(0), P.ident_bf[0:64, 0:64]),
             reads=[b_sbbL, P.b_ident], writes=[b_sbbL])
        yield
        np_, tp_ = 1, 0
        for step in range(5):
            dn, dt = (2, 3) if step % 2 == 0 else (4, 5)
            c.mm_group([(pA[0:64, 0:CH], S(tp_), S(np_), True, True)],
                       reads=[b_sbbL], writes=[bA])
            if step < 4:
                c.mm_group([(pB[0:64, 0:CH], S(np_), S(tp_), True, True)],
                           reads=[b_sbbL], writes=[bB])
            yield
            c.op("S", lambda e, d=dn: e.activation(S(d), pA[0:64, 0:CH], AF.Copy),
                 reads=[bA], writes=[b_sbbL])
            if step < 4:
                c.op("S", lambda e, d=dt: e.activation(S(d), pB[0:64, 0:CH], AF.Copy),
                     reads=[bB], writes=[b_sbbL])
            yield
            c.mm_group([(pA[0:64, 0:CH], S(dn), S(6), True, True)],
                       reads=[b_sbbL], writes=[bA])
            yield
            c.op("V", lambda e: e.tensor_add(S(6), S(6), pA[0:64, 0:CH]),
                 reads=[bA, b_sbbL], writes=[b_sbbL])
            np_, tp_ = dn, dt
            yield
        vs = stg(13, kidx + 2, krh)
        c.transpose(pAb[0:64, 0:CH], vs, P.ident_bf[0:64, 0:64],
                    reads=[b_sbbL, P.b_ident], writes=[bA])
        c.op("V", lambda e: e.tensor_scalar_mul(S(7), pAb[0:64, 0:CH], col[:, 2:3]),
             reads=[bA, b_colL], writes=[b_sbbL])
        yield
        c.transpose(pBb[0:64, 0:CH], ks, P.ident_bf[0:64, 0:64],
                    reads=[b_sbbL], writes=[bB])
        c.op("V", lambda e: e.tensor_scalar_mul(S(8), pBb[0:64, 0:CH], col[:, 4:5]),
             reads=[bB, b_colL], writes=[b_sbbL])
        yield
        c.mm_group([(pA[0:64, 0:CH], ks, Sb[:, kv * 64:(kv + 1) * 64], True, True)],
                   reads=[b_sbbL, b_Sb[kv]], writes=[bA])
        c.op("V", lambda e: e.scalar_tensor_tensor(S(10), pA[0:64, 0:CH], col[:, 3:4],
                                                   S(7), ALU.mult, ALU.add),
             reads=[bA, b_colL, b_sbbL], writes=[b_sbbL])
        yield
        c.mm_group([(pA[0:64, 0:CH], S(6), S(10), True, True)],
                   reads=[b_sbbL], writes=[bA])
        c.op("S", lambda e: e.activation(S(9), pA[0:64, 0:CH], AF.Copy), reads=[bA], writes=[b_sbbL])
        yield
        c.op("P", lambda e: e.tensor_mul(R(7), R(0), mlo[0:64, 128:192]),
             reads=[b_scrL, b_ml], writes=[b_scrL])
        c.op("S", lambda e, G=gf: e.activation(rbk[0:1, :], G, AF.Exp),
             reads=[b_fl], writes=[b_rbk])
        c.op("P", lambda e: e.tensor_scalar_mul(rbk[0:1, :], rbk[0:1, :], 0.125),
             reads=[b_rbk], writes=[b_rbk])
        yield
        c.mm_group([(pA[0:64, 0:CH], P.ones_row[0:1, 0:64], rbk[0:1, :], True, True)],
                   reads=[b_rbk, P.b_ones_row], writes=[bA])
        c.op("S", lambda e: e.activation(R(2), pA[0:64, 0:CH], AF.Copy), reads=[bA], writes=[b_scrL])
        yield
        for rr in range(4):
            qh = kv * 4 + rr
            qidx, qrh = qh // 2, qh % 2
            qs = stg(14, qidx, qrh)
            c.mm_group([(pA[0:64, 0:CH], ks, qs, True, True)],
                       reads=[b_sbbL], writes=[bA])
            yield
            c.op("V", lambda e: e.tensor_mul(S(11), pA[0:64, 0:CH], R(7)),
                 reads=[bA, b_scrL], writes=[b_sbbL])
            c.mm_group([(pB[0:64, 0:CH], Sb[:, kv * 64:(kv + 1) * 64], qs, True, True)],
                       reads=[b_Sb[kv], b_sbbL], writes=[bB])
            yield
            c.op("V", lambda e: e.tensor_mul(R(8), pB[0:64, 0:CH], R(2)),
                 reads=[bB, b_scrL], writes=[b_scrL])
            c.mm_group([(pA[0:64, 0:CH], S(9), S(11), True, True)],
                       reads=[b_sbbL], writes=[bA])
            yield
            och, orh = qh // 2, qh % 2
            c.op("V", lambda e, OC=och, OR=orh, CI=ci: e.scalar_tensor_tensor(
                o_fm[64 * OR:64 * OR + 64, OC * T + CI * CH: OC * T + (CI + 1) * CH],
                pA[0:64, 0:CH], 0.125, R(8), ALU.mult, ALU.add),
                reads=[bA, b_scrL], writes=[b_o[och]])
            yield
        c.mm_group([(pA[0:64, 0:CH], S(8), S(9), True, True)],
                   reads=[b_sbbL], writes=[bA])
        yield
        c.op("V", lambda e, V=kv: e.scalar_tensor_tensor(Sf[:, V * 64:(V + 1) * 64],
                                                         Sf[:, V * 64:(V + 1) * 64],
                                                         col[:, 5:6], pA[0:64, 0:CH],
                                                         ALU.mult, ALU.add),
             reads=[bA, b_colL, b_Sf[kv]], writes=[b_Sf[kv]])
        c.op("P", lambda e, V=kv: e.tensor_copy(Sb[:, V * 64:(V + 1) * 64],
                                                Sf[:, V * 64:(V + 1) * 64]),
             reads=[b_Sf[kv]], writes=[b_Sb[kv]])
        if DEBUG_SPILL and ci == 1 and kv == 0:
            dsb = P.dram_scratch("sp_l3sbb", [64, 15 * 64], BF16)
            c.dma("Y", P.hbm["sp_l3sbb"], sbb[:, 0:15 * 64], reads=[b_sbbL], writes=[P.hbuf["sp_l3sbb"]])

    for ci in range(NC):
        tk, sub = divmod(ci, 2)
        rowbuf, b_fl = rowbuf2[ci % 2], b_fl2[ci % 2]
        c.dma("Y", rowbuf[0:1, 0:512], P.hbm["sp_l3bg"][:, ci * CH:(ci + 1) * CH],
              reads=[b_bo], writes=[b_fl])
        alive = [emit_kv_instance(ci, kv, tk, sub, rowbuf, b_fl) for kv in range(4)]
        while alive:
            nxt = []
            for gen in alive:
                try:
                    next(gen)
                    nxt.append(gen)
                except StopIteration:
                    pass
            alive = nxt

    blk2, b_b2 = blk2e, b_b2e
    sel2, b_s2 = sel2e, b_s2e
    osq = c.sb("l3_osq", [128, TT], BF16)
    b_osq = Buf()
    rst = c.sb("l3_rst", [2, TT], F32)
    b_rst = Buf()
    rstb = c.sb("l3_rstb", [2, TT], BF16)
    b_rstb = Buf()
    og2, b_og2 = o_fm, b_o
    p2s, b_p2s = pf4[0][0], b_pf4[0][0]
    pbc2, b_pbc2 = pf4[0][1], b_pf4[0][1]
    for chn in range(DC):
        for tt in range(NTT):
            sl = slice(chn * T + tt * TT, chn * T + (tt + 1) * TT)
            c.op("S", lambda e, s=sl: e.activation(osq[:, :], o_fm[:, s], AF.Square),
                 reads=[b_o[chn]], writes=[b_osq])
            c.mm_group([(p2s[0:2, 0:TT], blk2[:, :], osq[:, :], True, True)],
                       reads=[b_osq, b_b2], writes=[b_p2s])
            c.op("V", lambda e: e.tensor_scalar(rst[:, :], p2s[0:2, 0:TT], 1.0 / 64, None, ALU.mult),
                 reads=[b_p2s], writes=[b_rst])
            c.op("V", lambda e: e.tensor_scalar_add(rst[:, :], rst[:, :], 1e-5),
                 reads=[b_rst], writes=[b_rst])
            c.op("S", lambda e: e.activation(rst[:, :], rst[:, :], AF.Ln),
                 reads=[b_rst], writes=[b_rst])
            c.op("S", lambda e: e.activation(rst[:, :], rst[:, :], AF.Exp, scale=-0.5),
                 reads=[b_rst], writes=[b_rst])
            c.op("V", lambda e: e.tensor_copy(rstb[:, :], rst[:, :]), reads=[b_rst], writes=[b_rstb])
            c.mm_group([(pbc2[:, :], sel2[:, :], rstb[0:2, :], True, True)],
                       reads=[b_rstb, b_s2], writes=[b_pbc2])
            c.op("V", lambda e, s=sl: e.scalar_tensor_tensor(
                osq[:, :], o_fm[:, s], P.t_r_l3_rmsw[:, 0:1], pbc2[:, :], ALU.mult, ALU.mult),
                reads=[b_o[chn], b_pbc2, P.b_r_l3_rmsw], writes=[b_osq])
            c.op("V", lambda e, s=sl: e.tensor_mul(og2[:, s], osq[:, :], gate[:, s]),
                 reads=[b_osq, b_gate[chn]], writes=[b_og2[chn]])

    s_sb = c.sb("l3_s", [128, TT], F32)
    b_s = Buf()

    def rhs_og(kc, tt):
        return og2[:, kc * T:][:, tt * TT:(tt + 1) * TT]

    def ev_s(mc, tt, ps, bp):
        c.op("V", lambda e, p=ps: e.tensor_copy(s_sb[:, :], p[:, :]), reads=[bp], writes=[b_s])
        c.dma("Y", s_hbm[:, mc * T + tt * TT: mc * T + (tt + 1) * TT], s_sb[:, :],
              reads=[b_s], writes=[b_sh[mc]])
    gemm_fm(P, "w_l3_proj", 8, range(8), rhs_og, b_og2, ev_s, pool=(pf4[1], b_pf4[1]))
    c.pop_scope()

    c.push_scope()
    y1t = c.sb("l3f_y1t", [128, 8 * 1024], BF16)
    b_y1 = [Buf() for _ in range(8)]
    wa2 = c.sb("l3f_wa2", [128, 8 * 1024], BF16)
    b_wa2 = Buf()
    wc = c.sb("l3f_wc", [128, 8 * 1024], BF16)
    b_wc = [Buf() for _ in range(8)]
    wtb = c.sb("l3f_wtb", [128, 8 * 1024], BF16)
    b_wtb = Buf()
    c.dma("P", wtb[:, :], P.hbm["w_l3f_wT"], reads=[P.hbuf["w_l3f_wT"]], writes=[b_wtb])
    c.dma("P", wa2[:, :], P.hbm["w_l3f_a2"], reads=[P.hbuf["w_l3f_a2"]], writes=[b_wa2])
    gpool = ([c.ps(f"l3f_gp{i}", [128, TT]) for i in range(2)], [Buf(), Buf()])

    def rhs_wt(kc, tt):
        return wtb[:, kc * 1024 + tt * TT: kc * 1024 + (tt + 1) * TT]

    def ev_y1(mc, tt, ps, bp):
        c.op("V", lambda e, m=mc, t=tt, p=ps: e.tensor_copy(
            y1t[:, m * 1024 + t * TT: m * 1024 + (t + 1) * TT], p[:, :]),
            reads=[bp], writes=[b_y1[mc]])
    gemm_fm(P, "w_l3f_a1", 8, range(8), rhs_wt, [b_wtb], ev_y1, ntt=2, pool=gpool)
    pwc = gpool[0]
    for mc in range(8):
        for tt in range(2):
            ps, bp = gpool[0][tt % 2], gpool[1][tt % 2]
            items = []
            for kc in range(8):
                items.append((ps[:, :], y1t[:, kc * 1024 + mc * 128: kc * 1024 + (mc + 1) * 128],
                              wa2[:, kc * 1024 + tt * TT: kc * 1024 + (tt + 1) * TT],
                              kc == 0, kc == 7))
            c.mm_group(items, reads=b_y1 + [b_wa2], writes=[bp])
            c.op("V", lambda e, m=mc, t=tt, p=ps: e.tensor_copy(
                wc[:, m * 1024 + t * TT: m * 1024 + (t + 1) * TT], p[:, :]),
                reads=[bp], writes=[b_wc[mc]])
    tnh = c.sb("l3f_tnh", [128, TT], F32)
    b_tnh = Buf()
    gt3 = c.sb("l3f_gt", [128, TT], BF16)
    b_gt3 = Buf()
    for mc in range(8):
        for tt in range(NTT):
            ps, bp = gpool[0][tt % 2], gpool[1][tt % 2]
            items = []
            for kc in range(8):
                items.append((ps[:, :], wc[:, kc * 1024 + mc * 128: kc * 1024 + (mc + 1) * 128],
                              rhs_xn(kc, tt), kc == 0, kc == 7))
            c.mm_group(items, reads=b_wc + b_xn, writes=[bp])
            c.op("S", lambda e, m=mc, p=ps: e.activation(
                tnh[:, :], p[:, :], AF.Tanh, bias=P.t_c_l3f_b[:, m:m + 1]),
                reads=[bp, P.b_c_l3f_b], writes=[b_tnh])
            c.op("S", lambda e: e.activation(gt3[:, :], tnh[:, :], AF.Silu),
                 reads=[b_tnh], writes=[b_gt3])
            c.dma("Y", g_hbm[:, mc * T + tt * TT: mc * T + (tt + 1) * TT], gt3[:, :],
                  reads=[b_gt3], writes=[b_gh[mc]])
    c.pop_scope()
    c.pop_scope()
    emit_combine(P, x_hbm, x_buf, P.hbm["sp_s3"], b_sh, P.hbm["sp_g3"], b_gh,
                 out_hbm, out_buf, collect_stats=collect)



_CACHE = {}


def build_program(layers):
    P = Prog()
    nc = P.nc
    P.dram_in("x_in", [D, T])
    weight_names = [
        ("w_l0_in", [128, 4096 * 8]), ("w_l0_mix", [128, 2048 * 16]),
        ("w_l0_out", [128, 1024 * 16]),
        ("w_l0f_w1", [128, 4096 * 8]), ("w_l0f_w2", [128, 4096 * 8]),
        ("w_l0f_h1", [128, 4096 * 32]), ("w_l0f_h2", [128, 4096 * 32]),
        ("w_l0f_w3", [128, 1024 * 32]),
        ("w_l1_q", [128, 1024 * 8]), ("w_l1_kv", [128, 256 * 8]),
        ("w_l1_kup", [128, 1024 * 2]), ("w_l1_vup", [128, 2 * 1024]),
        ("w_l1_out", [128, 1024 * 8]), ("w_l1_base", [128, 1024 * 8]),
        ("w_l1_enc", [128, 16384]), ("w_l1_dec", [128, 16384]),
        ("w_l1_dn", [128, 4096]),
        ("r_l1_bd", [8, 64]), ("r_l1_bu", [8, 1024]),
        ("w_l2_hv", [128, 8 * 2048]), ("w_l2_hg", [128, 16 * 1024]),
        ("w_l2_qk", [128, 1024]), ("w_l2_out", [128, 8 * 2048]),
        ("r_l2_bhv", [1, 2048]),
        ("w_l2_gw", [128, 64]), ("r_l2_gb", [1, 8]),
        ("w_l2_w1", [128, 65536]), ("w_l2_w3", [128, 65536]), ("w_l2_w2", [128, 65536]),
        ("w_l2_sw1", [128, 16384]), ("w_l2_sw3", [128, 16384]), ("w_l2_sw2", [128, 16384]),
        ("w_l3_q", [128, 8192]), ("w_l3_k", [128, 2048]), ("w_l3_v", [128, 2048]),
        ("w_l3_gate", [128, 8192]), ("w_l3_proj", [128, 8192]), ("w_l3_bg", [128, 512]),
        ("w_l3f_a1", [128, 8192]), ("w_l3f_wT", [128, 8192]), ("w_l3f_a2", [128, 8192]),
    ]
    col_names = [
        ("c_l0_gamma", [128, 8]), ("c_l0_beta", [128, 8]), ("c_l0_inb", [128, 32]),
        ("c_l0_dwb", [128, 16]), ("c_l0_mixb", [128, 16]), ("c_l0_outb", [128, 8]),
        ("c_l0_dec", [128, 16]), ("c_l0_dww", [128, 112]),
        ("c_l0f_s1", [128, 32]), ("c_l0f_s2", [128, 32]), ("c_l0f_s3", [128, 32]),
        ("c_l0f_s4", [128, 32]), ("c_l0f_s5", [128, 8]),
        ("c_l1_bb", [128, 8]), ("c_l1_benc", [128, 16]), ("c_l1_bdec", [128, 64]),
        ("c_l2_scale", [128, 8]), ("c_l2_bias", [128, 8]), ("c_l2_bhg", [128, 16]),
        ("c_l2_bqk", [128, 1]), ("c_l2_osw", [128, 2]), ("c_l2_osb", [128, 2]),
        ("c_l2_bout", [128, 8]),
        ("c_l3_cwq", [128, 32]), ("c_l3_cwk", [128, 8]), ("c_l3_cwv", [128, 8]),
        ("c_l3_bb64", [64, 1]), ("c_l3_dt64", [64, 1]), ("c_l3_al64", [64, 1]),
        ("c_l3f_b", [128, 8]), ("r_l3_rmsw", [128, 1]),
    ]
    for n, s in weight_names + col_names:
        P.dram_in(n, s)
    P.dram_in("w_l1_up", [64, 8192])
    ap = nc.dram_tensor("i_l1_conn", [8, 1], I32, kind="ExternalInput").ap()
    P.hbm["i_l1_conn"] = ap
    P.hbuf["i_l1_conn"] = Buf("i_l1_conn")
    y_out = P.dram_out("y_out", [D, T])

    with P.stack:
        c = P.ctx
        emit_constants(P)
        load_const_cols(P, col_names)
        x_cur = P.hbm["x_in"]
        b_xcur = P.hbuf8["x_in"]
        for li in layers:
            is_last = li == layers[-1]
            out_hbm = y_out if is_last else P.dram_scratch(f"x_l{li + 1}", [D, T])
            out_buf = P.hbuf8["y_out"] if is_last else P.hbuf8[f"x_l{li + 1}"]
            fn = {0: emit_l0, 1: emit_l1, 2: emit_l2, 3: emit_l3}[li]
            fn(P, x_cur, b_xcur, out_hbm, out_buf,
               pre_stats=(li != layers[0]), collect=not is_last)
            x_cur, b_xcur = out_hbm, out_buf
        c.barrier()
        with nc.Block() as block:
            block.tensor(lambda e: c.replay("T", e))
            block.vector(lambda e: c.replay("V", e))
            block.scalar(lambda e: c.replay("S", e))
            block.gpsimd(lambda e: c.replay("P", e))
            block.sync(lambda e: c.replay("Y", e))
    return P


def kernel(**inputs):
    layers = tuple(DEBUG_LAYERS)
    g, per_core = prep_host(inputs)
    P = build_program(list(layers))
    in_maps = []
    for cid in range(8):
        m = dict(g)
        m.update(per_core[cid])
        in_maps.append(m)
    res = run_bass_kernel_spmd(P.nc, in_maps, core_ids=list(range(8)))
    _LAST_RES[0] = res
    outs = []
    for cid in range(8):
        y = res.results[cid]["y_out"]
        outs.append(np.ascontiguousarray(y.T))
    return np.stack(outs).astype(np.float32)
